# version 02
# baseline (speedup 1.0000x reference)
import contextlib
import numpy as np
import ml_dtypes
import concourse.bass as bass
import concourse.mybir as mybir
from concourse.bass_utils import run_bass_kernel_spmd
from concourse.aot_env import allow_ici_all_to_all

F32 = mybir.dt.float32
BF16 = mybir.dt.bfloat16
AF = mybir.ActivationFunctionType
ALU = mybir.AluOpType
AX = mybir.AxisListType
BBF16 = ml_dtypes.bfloat16

N_CORES = 8
D = 1024
T_LOC = 2048
NT = 16
NDC = 8
LN_EPS = 1e-5


class NS:
    pass


def apof(t):
    return t.ap() if callable(getattr(t, "ap", None)) else t


class Sems:
    def __init__(self, nc, stack, names):
        self.h = {}
        self.c = {}
        self.b = {}
        for n in names:
            self.h[n] = stack.enter_context(nc.semaphore(n))
            self.c[n] = 0
            self.b[n] = 0

    def inc(self, instr, name, v=1):
        instr.then_inc(self.h[name], v)
        self.c[name] += v
        return self.c[name]

    def wait(self, eng, name, val=None):
        v = self.c[name] if val is None else val
        if v > 0:
            eng.wait_ge(self.h[name], v)

    def step(self, eng, instr, name):
        self.inc(instr, name)
        eng.wait_ge(self.h[name], self.c[name])

    def rebase(self):
        self.b = dict(self.c)

    def waitb(self, eng, name, k):
        if k > 0:
            eng.wait_ge(self.h[name], self.b[name] + k)

    def hb(self, name, k):
        return self.b[name] + k


FUSED_SEMS = [
    "xld", "wld", "wldg", "lnSq", "lnVar", "lnSqrt", "lnRstd", "lnTok", "tr",
    "xnTe", "xbL", "yst",
    "gmm", "glogE", "gateF", "wfcL", "wprL", "wprUse", "mm1d", "geluE",
    "mm2d", "evacE",
    "qmm", "qe1", "qe2", "qfE", "kvmm", "ke1", "ke2", "kvE", "kvmmB", "kvE2",
    "cc", "ccb", "kvbf", "amm", "adiv", "atr", "aTe", "omm", "oev",
    "w1L", "w2L", "w2U", "wsF",
    "shA", "ccA", "blkL", "pmm", "pev", "cvD", "silu", "l2q", "l2k", "bsig",
    "kbE", "scanD", "fir", "payT", "payD", "ccB", "cl", "clg", "clv", "cst",
    "trP", "trE", "sAm", "sAe", "sZm", "sZe", "sUm", "sUe", "sSm",
    "mixv", "otr", "resv", "st", "vch", "sch",
]


class Pool:

    def __init__(self, nc, stack, fused=False, f32_cols=0, bf16_cols=0):
        self.nc, self.stack, self.fused = nc, stack, fused
        if fused:
            self.m32 = stack.enter_context(
                nc.sbuf_tensor("pool_f32", [128, f32_cols], F32))
            self.m16 = stack.enter_context(
                nc.sbuf_tensor("pool_b16", [128, bf16_cols], BF16))
            self.o32 = self.o16 = 0
            self.hw32 = self.hw16 = 0

    def reset(self):
        if self.fused:
            self.hw32 = max(self.hw32, self.o32)
            self.hw16 = max(self.hw16, self.o16)
            self.o32 = self.o16 = 0

    def f32(self, name, cols):
        if not self.fused:
            return self.stack.enter_context(
                self.nc.sbuf_tensor(name, [128, cols], F32))
        v = self.m32[:, self.o32:self.o32 + cols]
        self.o32 += cols
        return v

    def bf16(self, name, cols):
        if not self.fused:
            return self.stack.enter_context(
                self.nc.sbuf_tensor(name, [128, cols], BF16))
        v = self.m16[:, self.o16:self.o16 + cols]
        self.o16 += cols
        return v


def alloc_psums(nc, stack):
    PS = NS()
    PS.ptA = stack.enter_context(nc.psum_tensor("ptA", [128, 2048], BF16))
    PS.pf = [stack.enter_context(nc.psum_tensor(f"pf{i}", [128, 512], F32))
             for i in range(6)]
    return PS


def alloc_consts(nc, stack):
    C = NS()
    C.ident = stack.enter_context(nc.sbuf_tensor("c_ident", [128, 128], BF16))
    C.ones_row = stack.enter_context(nc.sbuf_tensor("c_ones_row", [1, 128], BF16))
    C.ones_col = stack.enter_context(nc.sbuf_tensor("c_ones_col", [128, 1], BF16))
    C.masks = stack.enter_context(nc.sbuf_tensor("c_masks", [128, 96], F32))
    C.eps = stack.enter_context(nc.sbuf_tensor("c_eps", [128, 2], F32))
    return C


def emit_const_init(nc, block, S, C, dr):

    @block.sync
    def _(sync):
        S.inc(sync.dma_start(out=C.ident[:], in_=dr["identw"]), "wld", 16)
        if "masks" in dr:
            S.inc(sync.dma_start(out=C.masks[:], in_=dr["masks"]), "wld", 16)

    @block.vector
    def _(vector):
        vector.memset(C.ones_row[:], 1.0)
        vector.memset(C.ones_col[:], 1.0)
        vector.memset(C.eps[:, 0:1], 1e-6)
        i = vector.memset(C.eps[:, 1:2], 1e-5)
        S.inc(i, "wld", 1)



def emit_ln_stream(nc, block, S, T, C, PS, x_src, tiles, xnT, xnT_stride):
    tiles = list(tiles)
    NTL = len(tiles)
    st = T.ln_stats
    sums, sumsq = st[:, 0:16], st[:, 16:32]
    mean, rstd, tmp = st[:, 32:48], st[:, 48:64], st[:, 64:80]
    xb, ln_xt, pt = T.xb, T.ln_xt, PS.ptA

    for j, t in enumerate(tiles):
        @block.sync
        def _(sync, j=j, t=t):
            if j == 0:
                S.wait(sync, "yst")
            if j >= 2:
                S.waitb(sync, "lnSq", j - 1)
                S.waitb(sync, "lnVar", j - 1)
            if j > 0:
                sync.wait_ge(S.h["xld"], S.hb("xld", 16 * j))
            S.inc(sync.dma_start(out=xb[:, (j % 2) * D:(j % 2 + 1) * D],
                                 in_=x_src[:, t * D:(t + 1) * D]), "xld", 16)

        @block.vector
        def _(vector, j=j):
            vector.wait_ge(S.h["xld"], S.hb("xld", 16 * (j + 1)))
            i = vector.tensor_reduce(sums[:, j:j + 1],
                                     xb[:, (j % 2) * D:(j % 2 + 1) * D],
                                     AX.X, ALU.add)
            S.inc(i, "lnVar")

        @block.scalar
        def _(scalar, j=j):
            scalar.wait_ge(S.h["lnVar"], S.hb("lnVar", j + 1))
            i = scalar.activation(ln_xt[:, (j % 2) * D:(j % 2 + 1) * D],
                                  xb[:, (j % 2) * D:(j % 2 + 1) * D],
                                  AF.Square, accum_out=sumsq[:, j:j + 1])
            S.inc(i, "lnSq")

    @block.vector
    def _(vector):
        S.waitb(vector, "lnSq", NTL)
        S.step(vector, vector.tensor_scalar(mean[:, 0:NTL], sums[:, 0:NTL],
                                            1.0 / D, None, ALU.mult), "vch")
        S.step(vector, vector.tensor_scalar(tmp[:, 0:NTL], sumsq[:, 0:NTL],
                                            1.0 / D, None, ALU.mult), "vch")
        S.step(vector, vector.tensor_mul(sumsq[:, 0:NTL], mean[:, 0:NTL],
                                         mean[:, 0:NTL]), "vch")
        S.step(vector, vector.tensor_sub(tmp[:, 0:NTL], tmp[:, 0:NTL],
                                         sumsq[:, 0:NTL]), "vch")
        i = vector.tensor_scalar(tmp[:, 0:NTL], tmp[:, 0:NTL], LN_EPS, None, ALU.add)
        S.inc(i, "lnVar")

    @block.scalar
    def _(scalar):
        S.waitb(scalar, "lnVar", NTL + 1)
        S.inc(scalar.sqrt(tmp[:, 0:NTL], tmp[:, 0:NTL]), "lnSqrt")

    @block.vector
    def _(vector):
        S.waitb(vector, "lnSqrt", 1)
        S.step(vector, vector.reciprocal(rstd[:, 0:NTL], tmp[:, 0:NTL]), "vch")
        S.inc(nc.vector.engine_nop(), "lnRstd")

    for j, t in enumerate(tiles):
        @block.sync
        def _(sync, j=j, t=t):
            if j < 2:
                S.waitb(sync, "lnSq", NTL)
                S.waitb(sync, "lnVar", NTL)
            else:
                S.waitb(sync, "lnTok", j - 1)
            sync.wait_ge(S.h["xld"], S.hb("xld", 16 * (NTL + j)))
            S.inc(sync.dma_start(out=xb[:, (j % 2) * D:(j % 2 + 1) * D],
                                 in_=x_src[:, t * D:(t + 1) * D]), "xld", 16)

        @block.vector
        def _(vector, j=j):
            if j == 0:
                S.waitb(vector, "lnRstd", 1)
            vector.wait_ge(S.h["xld"], S.hb("xld", 16 * (NTL + j + 1)))
            i = vector.tensor_scalar(ln_xt[:, (j % 2) * D:(j % 2 + 1) * D],
                                     xb[:, (j % 2) * D:(j % 2 + 1) * D],
                                     mean[:, j:j + 1], rstd[:, j:j + 1],
                                     ALU.subtract, ALU.mult)
            S.inc(i, "lnTok")

        @block.tensor
        def _(tensor, j=j):
            if j == 0:
                S.wait(tensor, "wld")
            S.waitb(tensor, "lnTok", j + 1)
            if j >= 2:
                S.waitb(tensor, "xnTe", j - 1)
            for dc in range(NDC):
                i = tensor.transpose(
                    pt[:, (j % 2) * 1024 + dc * 128:(j % 2) * 1024 + (dc + 1) * 128],
                    ln_xt[:, (j % 2) * D + dc * 128:(j % 2) * D + (dc + 1) * 128],
                    C.ident[:])
            S.inc(i, "tr")

        @block.vector
        def _(vector, j=j):
            S.waitb(vector, "tr", j + 1)
            for dc in range(NDC):
                i = vector.tensor_copy(
                    xnT[:, dc * xnT_stride + j * 128: dc * xnT_stride + j * 128 + 128],
                    pt[:, (j % 2) * 1024 + dc * 128:(j % 2) * 1024 + (dc + 1) * 128])
            S.inc(i, "xnTe")



def alloc_ffn0(nc, stack, P):
    T = NS()
    T.xb = P.f32("f0_xb", 2 * D)
    T.x_blk = P.f32("f0_x_blk", 2 * 4096)
    T.ln_stats = P.f32("f0_ln_stats", 80)
    T.gates = P.f32("f0_gates", NT * 8)
    T.glog = P.f32("f0_glog", NT * 8)
    T.gscr = P.f32("f0_gscr", 8)
    T.gstat = P.f32("f0_gstat", 8)
    T.bfc_sb = P.f32("f0_bfc", 8 * 32)
    T.xnT = P.bf16("f0_xnT", NDC * 2048)
    T.HT = P.bf16("f0_HT", 32 * 512)
    T.wfc_sb = P.bf16("f0_wfc", 4 * 8 * 128)
    T.wpr_sb = P.bf16("f0_wpr", 8 * 512)
    T.gwT_sb = P.bf16("f0_gwT", 8 * 8)
    T.ln_xt = P.bf16("f0_ln_xt", 2 * D)
    T.bpr_sb = stack.enter_context(nc.sbuf_tensor("f0_bpr", [1, 8 * 1024], BF16))
    return T


def emit_ffn0(nc, block, S, T, C, PS, dr, x_src, y_dst, inst=0):
    NSB, NFC = 4, 32
    WFC_RING, WPR_RING = 4, 8
    pm1 = [PS.pf[0], PS.pf[1]]
    pm2 = [PS.pf[2], PS.pf[3], PS.pf[4], PS.pf[5]]
    pg = PS.pf[2][:, 0:8]
    wfc, wpr, bfc, bpr, gwT = dr["wfc"], dr["wpr"], dr["bfc"], dr["bpr"], dr["gwT"]
    xnT, HT = T.xnT, T.HT
    wfc_sb, wpr_sb, gwT_sb, bpr_sb, bfc_sb = (T.wfc_sb, T.wpr_sb, T.gwT_sb,
                                              T.bpr_sb, T.bfc_sb)
    gates, glog, gscr, gstat = T.gates, T.glog, T.gscr, T.gstat
    x_blk = T.x_blk

    S.rebase()

    @block.sync
    def _(sync):
        S.inc(sync.dma_start(out=bfc_sb[:], in_=bfc), "wld", 16)

    @block.gpsimd
    def _(gpsimd):
        S.inc(gpsimd.dma_start(out=bpr_sb[:],
                               in_=bpr.rearrange("e d -> (e d)")[None, :]),
              "wldg", 16)
        S.inc(gpsimd.dma_start(
            out=apof(gwT_sb).rearrange("p (dc e) -> p dc e", dc=8),
            in_=gwT.rearrange("(dc p) e -> p dc e", p=128)), "wldg", 16)

    emit_ln_stream(nc, block, S, T, C, PS, x_src, range(NT), xnT, 2048)

    @block.tensor
    def _(tensor):
        S.wait(tensor, "wldg")

    for t in range(NT):
        @block.tensor
        def _(tensor, t=t):
            S.waitb(tensor, "xnTe", t + 1)
            if t >= 1:
                S.waitb(tensor, "glogE", t)
            for dc in range(NDC):
                i = tensor.matmul(pg,
                                  xnT[:, dc * 2048 + t * 128: dc * 2048 + t * 128 + 128],
                                  gwT_sb[:, dc * 8:(dc + 1) * 8],
                                  start=(dc == 0), stop=(dc == NDC - 1))
            S.inc(i, "gmm")

        @block.vector
        def _(vector, t=t):
            S.waitb(vector, "gmm", t + 1)
            S.inc(vector.tensor_copy(glog[:, t * 8:t * 8 + 8], pg), "glogE")

    st5 = T.ln_stats
    m1, m2, nm1, d2, rdh = (st5[:, 64:65], st5[:, 65:66], st5[:, 66:67],
                            st5[:, 67:68], st5[:, 68:69])
    for t in range(NT):
        @block.vector
        def _(vector, t=t):
            S.waitb(vector, "glogE", NT)
            l = glog[:, t * 8:t * 8 + 8]
            S.step(vector, vector.tensor_reduce(m1, l, AX.X, ALU.max), "vch")
            S.step(vector, vector.tensor_single_scalar(gscr[:], l, m1, ALU.is_ge), "vch")
            S.step(vector, vector.tensor_scalar(gscr[:], gscr[:], -1e30, None, ALU.mult), "vch")
            S.step(vector, vector.tensor_add(gscr[:], gscr[:], l), "vch")
            S.step(vector, vector.tensor_reduce(m2, gscr[:], AX.X, ALU.max), "vch")
            S.step(vector, vector.tensor_scalar(nm1, m1, -1.0, None, ALU.mult), "vch")
            i = vector.tensor_single_scalar(gstat[:], l, m2, ALU.is_ge)
            S.inc(i, "gateF")

        @block.scalar
        def _(scalar, t=t):
            S.waitb(scalar, "gateF", 3 * t + 1)
            scalar.activation(gates[:, t * 8:t * 8 + 8], glog[:, t * 8:t * 8 + 8],
                              AF.Exp, bias=nm1)
            i = scalar.activation(d2, m2, AF.Exp, bias=nm1)
            S.inc(i, "gateF")

        @block.vector
        def _(vector, t=t):
            S.waitb(vector, "gateF", 3 * t + 2)
            S.step(vector, vector.tensor_scalar(rdh, d2, 1.0, None, ALU.add), "vch")
            S.step(vector, vector.reciprocal(rdh, rdh), "vch")
            S.step(vector, vector.tensor_scalar(rdh, rdh, 0.5, None, ALU.mult), "vch")
            i = vector.scalar_tensor_tensor(gates[:, t * 8:t * 8 + 8],
                                            gstat[:], rdh,
                                            gates[:, t * 8:t * 8 + 8],
                                            ALU.mult, ALU.mult)
            S.inc(i, "gateF")

    @block.gpsimd
    def _(gpsimd):
        for sbe in range(NSB * 8):
            e = sbe % 8
            for fc in range(NFC):
                g = sbe * NFC + fc
                S.waitb(gpsimd, "geluE", g - (WFC_RING - 1))
                if g > 0:
                    gpsimd.wait_ge(S.h["wfcL"], S.hb("wfcL", 16 * g))
                slot = g % WFC_RING
                S.inc(gpsimd.dma_start(
                    out=wfc_sb[:, slot * 1024:(slot + 1) * 1024],
                    in_=wfc[e, fc].rearrange("p dc c -> p (dc c)")), "wfcL", 16)
            for dh in range(2):
                for fc in range(NFC):
                    p = sbe * 64 + dh * NFC + fc
                    S.waitb(gpsimd, "wprUse", p - (WPR_RING - 1))
                    if p > 0:
                        gpsimd.wait_ge(S.h["wprL"], S.hb("wprL", 16 * p))
                    slot = p % WPR_RING
                    S.inc(gpsimd.dma_start(
                        out=wpr_sb[:, slot * 512:(slot + 1) * 512],
                        in_=wpr[e, dh, fc]), "wprL", 16)

    for sb in range(NSB):
        par = sb % 2

        @block.sync
        def _(sync, sb=sb, par=par):
            if sb >= 2:
                S.waitb(sync, "yst", 16 * (sb - 1))
            if sb > 0:
                sync.wait_ge(S.h["xbL"], S.hb("xbL", 16 * sb))
            S.inc(sync.dma_start(out=x_blk[:, par * 4096:(par + 1) * 4096],
                                 in_=x_src[:, sb * 4096:(sb + 1) * 4096]), "xbL", 16)

        for e in range(8):
            sbe = sb * 8 + e

            @block.tensor
            def _(tensor, sb=sb, e=e, sbe=sbe):
                if e == 0:
                    S.waitb(tensor, "xnTe", 4 * sb + 4)
                for fc in range(NFC):
                    g = sbe * NFC + fc
                    slot = g % WFC_RING
                    tensor.wait_ge(S.h["wfcL"], S.hb("wfcL", 16 * (g + 1)))
                    if g >= 2:
                        tensor.wait_ge(S.h["geluE"], S.hb("geluE", g - 1))
                    for dc in range(NDC):
                        i = tensor.matmul(
                            pm1[g % 2][:],
                            wfc_sb[:, slot * 1024 + dc * 128: slot * 1024 + (dc + 1) * 128],
                            xnT[:, dc * 2048 + sb * 512: dc * 2048 + (sb + 1) * 512],
                            start=(dc == 0), stop=(dc == NDC - 1))
                    S.inc(i, "mm1d")

            @block.scalar
            def _(scalar, sb=sb, e=e, sbe=sbe):
                if sbe >= 1:
                    S.waitb(scalar, "mm2d", 8 * sbe)
                for fc in range(NFC):
                    g = sbe * NFC + fc
                    scalar.wait_ge(S.h["mm1d"], S.hb("mm1d", g + 1))
                    i = scalar.activation(HT[:, fc * 512:(fc + 1) * 512],
                                          pm1[g % 2][:], AF.Gelu_apprx_tanh,
                                          bias=bfc_sb[:, e * 32 + fc: e * 32 + fc + 1])
                    S.inc(i, "geluE")

            @block.tensor
            def _(tensor, sb=sb, e=e, sbe=sbe):
                tensor.wait_ge(S.h["geluE"], S.hb("geluE", 32 * (sbe + 1)))
                if sbe == 0:
                    tensor.wait_ge(S.h["glogE"], S.hb("glogE", NT))
                for dh in range(2):
                    m_g = sbe * 2 + dh
                    if m_g >= 1:
                        tensor.wait_ge(S.h["evacE"], S.hb("evacE", 4 * m_g))
                    for fc in range(NFC):
                        p = sbe * 64 + dh * NFC + fc
                        slot = p % WPR_RING
                        tensor.wait_ge(S.h["wprL"], S.hb("wprL", 16 * (p + 1)))
                        for sc in range(4):
                            i = tensor.matmul(
                                pm2[sc][:],
                                HT[:, fc * 512 + sc * 128: fc * 512 + (sc + 1) * 128],
                                wpr_sb[:, slot * 512:(slot + 1) * 512],
                                start=(fc == 0), stop=False)
                        S.inc(i, "wprUse")
                    for sc in range(4):
                        i = tensor.matmul(pm2[sc][:], C.ones_row[:],
                                          bpr_sb[0:1, e * 1024 + dh * 512: e * 1024 + (dh + 1) * 512],
                                          start=False, stop=True)
                        S.inc(i, "mm2d")

            @block.vector
            def _(vector, sb=sb, e=e, sbe=sbe, par=par):
                if sbe == 0:
                    S.waitb(vector, "gateF", 3 * NT)
                if e == 0:
                    vector.wait_ge(S.h["xbL"], S.hb("xbL", 16 * (sb + 1)))
                for dh in range(2):
                    m_g = sbe * 2 + dh
                    for sc in range(4):
                        vector.wait_ge(S.h["mm2d"], S.hb("mm2d", 4 * m_g + sc + 1))
                        t = sb * 4 + sc
                        i = vector.scalar_tensor_tensor(
                            x_blk[:, par * 4096 + sc * D + dh * 512:
                                  par * 4096 + sc * D + (dh + 1) * 512],
                            pm2[sc][:], gates[:, t * 8 + e: t * 8 + e + 1],
                            x_blk[:, par * 4096 + sc * D + dh * 512:
                                  par * 4096 + sc * D + (dh + 1) * 512],
                            ALU.mult, ALU.add)
                        S.inc(i, "evacE")

        @block.sync
        def _(sync, sb=sb, par=par):
            S.waitb(sync, "evacE", 64 * (sb + 1))
            S.inc(sync.dma_start(out=y_dst[:, sb * 4096:(sb + 1) * 4096],
                                 in_=x_blk[:, par * 4096:(par + 1) * 4096]), "yst", 16)

    @block.sync
    def _(sync):
        sync.wait_ge(S.h["yst"], S.c["yst"])


def host_ffn0_weights(w_fc, b_fc, w_proj, b_proj, gate_w):
    wfc = np.ascontiguousarray(
        w_fc.transpose(0, 2, 1).reshape(8, 8, 128, 32, 128).transpose(0, 3, 2, 1, 4))
    wpr = np.ascontiguousarray(
        w_proj.transpose(0, 2, 1).reshape(8, 32, 128, 2, 512).transpose(0, 3, 1, 2, 4))
    bfc = np.ascontiguousarray(b_fc.reshape(8, 32, 128).transpose(2, 0, 1)
                               .reshape(128, 256))
    return {"wfc": wfc, "wpr": wpr, "bfc": bfc,
            "bpr": np.ascontiguousarray(b_proj),
            "gwT": np.ascontiguousarray(gate_w.T),
            "identw": np.eye(128, dtype=BBF16)}



def alloc_attn0(nc, stack, P):
    T = NS()
    T.xb = P.f32("a0_xb", 2 * D)
    T.ln_stats = P.f32("a0_ln_stats", 80)
    T.kv_sb = P.f32("a0_kv_sb", 32 * 130)
    T.dcol = P.f32("a0_dcol", 4)
    T.xnT = P.bf16("a0_xnT", NDC * 2048)
    T.qf = P.bf16("a0_qf", NDC * 2048)
    T.kst = P.bf16("a0_kst", 1024)
    T.vst = P.bf16("a0_vst", 1024)
    T.wbufA = P.bf16("a0_wbufA", 8 * 1024)
    T.wbufB = P.bf16("a0_wbufB", 8 * 1024)
    T.kv_bf = P.bf16("a0_kv_bf", 32 * 130)
    T.astage = P.bf16("a0_astage", 1024)
    T.escr = P.f32("a0_escr", 4 * 512)
    T.ln_xt = P.bf16("a0_ln_xt", 2 * D)
    T.bo_sb = stack.enter_context(nc.sbuf_tensor("a0_bo", [1, 1024], BF16))
    return T


def emit_attn0(nc, block, S, T, C, PS, dr, x_src, y_dst):
    AEPS = 1e-6
    pp = [PS.pf[0], PS.pf[1]]
    pkv = PS.pf[2]
    pks = PS.pf[3][:, 0:8]
    pqk = [PS.pf[4][:, 0:130], PS.pf[5][:, 0:130]]
    kv_cc_in, kv_cc_out = dr["kv_cc_in"], dr["kv_cc_out"]
    xnT, qf, kst, vst = T.xnT, T.qf, T.kst, T.vst
    wbufA, wbufB, kv_sb, kv_bf = T.wbufA, T.wbufB, T.kv_sb, T.kv_bf
    astage, escr, dcol, bo_sb, xb = T.astage, T.escr, T.dcol, T.bo_sb, T.xb

    @block.gpsimd
    def _(gpsimd):
        if S.c["evacE"] > 0:
            gpsimd.wait_ge(S.h["evacE"], S.c["evacE"])
        for dc in range(NDC):
            S.inc(gpsimd.dma_start(
                out=wbufA[:, dc * 1024:(dc + 1) * 1024],
                in_=dr["wqT"][dc * 128:(dc + 1) * 128, :]), "wldg", 16)
        S.inc(gpsimd.dma_start(out=bo_sb[:], in_=dr["bo"]), "wldg", 16)

    S.rebase()

    @block.vector
    def _(vector):
        S.wait(vector, "yst")
        i = vector.memset(kv_sb[:], 0.0)
        S.inc(i, "kvE2")

    emit_ln_stream(nc, block, S, T, C, PS, x_src, range(NT), xnT, 2048)

    nq = 0
    for oc in range(NDC):
        for sb4 in range(4):
            nq += 1

            @block.tensor
            def _(tensor, oc=oc, sb4=sb4, nq=nq):
                if nq == 1:
                    S.wait(tensor, "wldg")
                    S.waitb(tensor, "xnTe", NT)
                if nq > 2:
                    S.waitb(tensor, "qfE", nq - 2)
                for dc in range(NDC):
                    i = tensor.matmul(
                        pp[nq % 2][:],
                        wbufA[:, dc * 1024 + oc * 128: dc * 1024 + (oc + 1) * 128],
                        xnT[:, dc * 2048 + sb4 * 512: dc * 2048 + (sb4 + 1) * 512],
                        start=(dc == 0), stop=(dc == NDC - 1))
                S.inc(i, "qmm")

            @block.vector
            def _(vector, nq=nq):
                S.waitb(vector, "qmm", nq)
                i = vector.tensor_scalar_min(
                    escr[:, (nq % 2) * 512:(nq % 2 + 1) * 512], pp[nq % 2][:], 0.0)
                S.inc(i, "qe1")

            @block.scalar
            def _(scalar, nq=nq):
                S.waitb(scalar, "qe1", nq)
                i = scalar.activation(
                    escr[:, (2 + nq % 2) * 512:(3 + nq % 2) * 512],
                    escr[:, (nq % 2) * 512:(nq % 2 + 1) * 512], AF.Exp)
                S.inc(i, "qe2")

            @block.vector
            def _(vector, oc=oc, sb4=sb4, nq=nq):
                S.waitb(vector, "qe2", nq)
                i = vector.scalar_tensor_tensor(
                    qf[:, oc * 2048 + sb4 * 512: oc * 2048 + (sb4 + 1) * 512],
                    pp[nq % 2][:], 1.0,
                    escr[:, (2 + nq % 2) * 512:(3 + nq % 2) * 512],
                    ALU.add, ALU.max)
                S.inc(i, "qfE")

    @block.gpsimd
    def _(gpsimd):
        gpsimd.wait_ge(S.h["qfE"], S.hb("qfE", 32))
        for dc in range(NDC):
            S.inc(gpsimd.dma_start(
                out=wbufA[:, dc * 1024:(dc + 1) * 1024],
                in_=dr["wkT"][dc * 128:(dc + 1) * 128, :]), "wldg", 16)
        for dc in range(NDC):
            S.inc(gpsimd.dma_start(
                out=wbufB[:, dc * 1024:(dc + 1) * 1024],
                in_=dr["wvT"][dc * 128:(dc + 1) * 128, :]), "wldg", 16)

    nkv = 0
    nke = 0
    for t in range(NT):
        b = t // 4
        for proj in ("k", "v"):
            wb = wbufA if proj == "k" else wbufB
            for half in range(2):
                nkv += 1

                @block.tensor
                def _(tensor, t=t, wb=wb, half=half, nkv=nkv):
                    if nkv <= 2:
                        S.wait(tensor, "wldg")
                        S.waitb(tensor, "qfE", 32)
                    else:
                        S.waitb(tensor, "kvE", nkv - 2)
                    for dc in range(NDC):
                        i = tensor.matmul(
                            pp[nkv % 2][:],
                            xnT[:, dc * 2048 + t * 128: dc * 2048 + (t + 1) * 128],
                            wb[:, dc * 1024 + half * 512: dc * 1024 + (half + 1) * 512],
                            start=(dc == 0), stop=(dc == NDC - 1))
                    S.inc(i, "kvmm")

                if proj == "k":
                    nke += 1

                    @block.vector
                    def _(vector, nkv=nkv):
                        S.waitb(vector, "kvmm", nkv)
                        i = vector.tensor_scalar_min(
                            escr[:, (nkv % 2) * 512:(nkv % 2 + 1) * 512],
                            pp[nkv % 2][:], 0.0)
                        S.inc(i, "ke1")

                    @block.scalar
                    def _(scalar, nkv=nkv, nke=nke):
                        S.waitb(scalar, "ke1", nke)
                        i = scalar.activation(
                            escr[:, (2 + nkv % 2) * 512:(3 + nkv % 2) * 512],
                            escr[:, (nkv % 2) * 512:(nkv % 2 + 1) * 512], AF.Exp)
                        S.inc(i, "ke2")

                    @block.vector
                    def _(vector, nkv=nkv, nke=nke, half=half):
                        S.waitb(vector, "ke2", nke)
                        i = vector.scalar_tensor_tensor(
                            kst[:, half * 512:(half + 1) * 512],
                            pp[nkv % 2][:], 1.0,
                            escr[:, (2 + nkv % 2) * 512:(3 + nkv % 2) * 512],
                            ALU.add, ALU.max)
                        S.inc(i, "kvE")
                else:
                    @block.vector
                    def _(vector, nkv=nkv, half=half):
                        S.waitb(vector, "kvmm", nkv)
                        i = vector.tensor_copy(
                            vst[:, half * 512:(half + 1) * 512], pp[nkv % 2][:])
                        S.inc(i, "kvE")

        @block.tensor
        def _(tensor, t=t, b=b):
            S.wait(tensor, "kvE")
            first = (t % 4 == 0)
            last = (t % 4 == 3)
            if first and t > 0:
                S.waitb(tensor, "kvE2", t // 4 + 1)
            for h in range(16):
                pbase = 64 * (h % 2)
                tensor.matmul(pkv[pbase:pbase + 64, (h // 2) * 64:(h // 2) * 64 + 64],
                              kst[:, h * 64:(h + 1) * 64],
                              vst[:, h * 64:(h + 1) * 64],
                              start=(first and h == 0), stop=last)
            for h in range(16):
                pbase = 64 * (h % 2)
                i = tensor.matmul(pks[pbase:pbase + 64, h // 2:h // 2 + 1],
                                  kst[:, h * 64:(h + 1) * 64],
                                  C.ones_col[:],
                                  start=(first and h == 0), stop=last)
            S.inc(i, "kvmmB")

        if t % 4 == 3:
            @block.vector
            def _(vector, b=b):
                S.waitb(vector, "kvmmB", 4 * (b + 1))
                for h in range(16):
                    pbase = 64 * (h % 2)
                    blk = (b * 8 + h // 2) * 130
                    i = vector.tensor_copy(
                        kv_sb[pbase:pbase + 64, blk + 65 * (h % 2): blk + 65 * (h % 2) + 64],
                        pkv[pbase:pbase + 64, (h // 2) * 64:(h // 2) * 64 + 64])
                for par2 in range(2):
                    pbase = 64 * par2
                    i = vector.tensor_copy(
                        apof(kv_sb).rearrange("p (g c) -> p g c", c=130)[
                            pbase:pbase + 64, b * 8:(b + 1) * 8,
                            64 + 65 * par2:65 + 65 * par2],
                        pks[pbase:pbase + 64, 0:8].rearrange("p (g c) -> p g c", c=1))
                S.inc(i, "kvE2")

    @block.sync
    def _(sync):
        S.wait(sync, "kvE2")
        S.inc(sync.dma_start(out=kv_cc_in, in_=kv_sb[:]), "cc", 16)

    @block.gpsimd
    def _(gpsimd):
        gpsimd.wait_ge(S.h["cc"], S.hb("cc", 16))
        i = gpsimd.collective_compute(
            "AllReduce", ALU.add,
            replica_groups=[list(range(N_CORES))],
            ins=[kv_cc_in.opt()], outs=[kv_cc_out.opt()])
        S.inc(i, "ccb")

    @block.sync
    def _(sync):
        S.wait(sync, "ccb")
        S.inc(sync.dma_start(out=kv_sb[:], in_=kv_cc_out), "cc", 16)

    @block.vector
    def _(vector):
        S.waitb(vector, "cc", 32)
        S.inc(vector.tensor_copy(kv_bf[:], kv_sb[:]), "kvbf")

    na = 0
    for t in range(NT):
        b = t // 4
        for oc in range(NDC):
            na += 1

            @block.tensor
            def _(tensor, t=t, b=b, oc=oc, na=na):
                if na == 1:
                    S.wait(tensor, "kvbf")
                if na > 2:
                    S.waitb(tensor, "adiv", 2 * (na - 2))
                blk = (b * 8 + oc) * 130
                i = tensor.matmul(
                    pqk[na % 2],
                    qf[:, oc * 2048 + t * 128: oc * 2048 + (t + 1) * 128],
                    kv_bf[:, blk: blk + 130],
                    start=True, stop=True)
                S.inc(i, "amm")

            @block.vector
            def _(vector, oc=oc, na=na):
                S.waitb(vector, "amm", na)
                for par2 in range(2):
                    S.step(vector, vector.tensor_copy(
                        dcol[:, 2 * par2:2 * par2 + 1],
                        pqk[na % 2][:, 64 + 65 * par2:65 + 65 * par2]), "vch")
                    S.step(vector, vector.tensor_scalar(
                        dcol[:, 2 * par2:2 * par2 + 1],
                        dcol[:, 2 * par2:2 * par2 + 1], AEPS, None, ALU.add), "vch")
                    S.step(vector, vector.reciprocal(
                        dcol[:, 2 * par2 + 1:2 * par2 + 2],
                        dcol[:, 2 * par2:2 * par2 + 1]), "vch")
                    i = vector.tensor_scalar(
                        astage[:, (2 * oc + par2) * 64:(2 * oc + par2 + 1) * 64],
                        pqk[na % 2][:, 65 * par2:65 * par2 + 64],
                        dcol[:, 2 * par2 + 1:2 * par2 + 2], None, ALU.mult)
                    S.inc(i, "adiv")

        @block.tensor
        def _(tensor, t=t):
            S.waitb(tensor, "adiv", 16 * (t + 1))
            if t >= 2:
                S.waitb(tensor, "aTe", t - 1)
            for dc in range(NDC):
                i = tensor.transpose(
                    PS.ptA[:, (t % 2) * 1024 + dc * 128:(t % 2) * 1024 + (dc + 1) * 128],
                    astage[:, dc * 128:(dc + 1) * 128], C.ident[:])
            S.inc(i, "atr")

        @block.vector
        def _(vector, t=t):
            S.waitb(vector, "atr", t + 1)
            for dc in range(NDC):
                i = vector.tensor_copy(
                    xnT[:, dc * 2048 + t * 128: dc * 2048 + t * 128 + 128],
                    PS.ptA[:, (t % 2) * 1024 + dc * 128:(t % 2) * 1024 + (dc + 1) * 128])
            S.inc(i, "aTe")

    @block.gpsimd
    def _(gpsimd):
        gpsimd.wait_ge(S.h["kvbf"], S.c["kvbf"])
        for dc in range(NDC):
            S.inc(gpsimd.dma_start(
                out=wbufB[:, dc * 1024:(dc + 1) * 1024],
                in_=dr["woT"][dc * 128:(dc + 1) * 128, :]), "wldg", 16)

    no = 0
    for t in range(NT):
        par = t % 2

        @block.sync
        def _(sync, t=t, par=par):
            if t == 0:
                S.waitb(sync, "lnTok", NT)
                S.waitb(sync, "lnSq", NT)
            if t >= 2:
                S.waitb(sync, "yst", 16 * (t - 1))
            sync.wait_ge(S.h["xld"], S.hb("xld", 16 * (2 * NT + t)))
            S.inc(sync.dma_start(out=xb[:, par * D:(par + 1) * D],
                                 in_=x_src[:, t * D:(t + 1) * D]), "xld", 16)

        for dh in range(2):
            no += 1

            @block.tensor
            def _(tensor, t=t, dh=dh, no=no):
                if no == 1:
                    S.wait(tensor, "wldg")
                S.waitb(tensor, "aTe", t + 1)
                if no > 2:
                    S.waitb(tensor, "oev", no - 2)
                for dc in range(NDC):
                    tensor.matmul(
                        pp[no % 2][:],
                        xnT[:, dc * 2048 + t * 128: dc * 2048 + (t + 1) * 128],
                        wbufB[:, dc * 1024 + dh * 512: dc * 1024 + (dh + 1) * 512],
                        start=(dc == 0), stop=False)
                i = tensor.matmul(pp[no % 2][:], C.ones_row[:],
                                  bo_sb[0:1, dh * 512:(dh + 1) * 512],
                                  start=False, stop=True)
                S.inc(i, "omm")

            @block.vector
            def _(vector, t=t, dh=dh, no=no, par=par):
                S.waitb(vector, "omm", no)
                vector.wait_ge(S.h["xld"], S.hb("xld", 16 * (2 * NT + t + 1)))
                i = vector.tensor_add(
                    xb[:, par * D + dh * 512: par * D + (dh + 1) * 512],
                    xb[:, par * D + dh * 512: par * D + (dh + 1) * 512],
                    pp[no % 2][:])
                S.inc(i, "oev")

        @block.sync
        def _(sync, t=t, par=par):
            S.waitb(sync, "oev", 2 * (t + 1))
            S.inc(sync.dma_start(out=y_dst[:, t * D:(t + 1) * D],
                                 in_=xb[:, par * D:(par + 1) * D]), "yst", 16)

    @block.sync
    def _(sync):
        sync.wait_ge(S.h["yst"], S.c["yst"])


def host_attn0_weights(wq, wk, wv, wo, bo):
    return {"wqT": np.ascontiguousarray(wq.T), "wkT": np.ascontiguousarray(wk.T),
            "wvT": np.ascontiguousarray(wv.T), "woT": np.ascontiguousarray(wo.T),
            "bo": np.ascontiguousarray(bo[None, :]),
            "identw": np.eye(128, dtype=BBF16)}



def alloc_ffn1(nc, stack, P):
    T = NS()
    T.xb = P.f32("f1_xb", 2 * D)
    T.x_blk = P.f32("f1_x_blk", 4096)
    T.ln_stats = P.f32("f1_ln_stats", 80)
    T.ew_sb = P.f32("f1_ew", 8)
    T.ws = P.f32("f1_ws", 4)
    T.xnT1 = P.bf16("f1_xnT1", NDC * 512)
    T.HT1 = P.bf16("f1_HT1", 64 * 512)
    T.w1_sb = P.bf16("f1_w1", 4 * 8 * 128)
    T.w2_sb = P.bf16("f1_w2", 8 * 512)
    T.ln_xt = P.bf16("f1_ln_xt", 2 * D)
    return T


def emit_ffn1(nc, block, S, T, C, PS, dr, x_src, y_dst, copy_rest):
    NFC = 64
    W1_RING, W2_RING = 4, 8
    pm1 = [PS.pf[0], PS.pf[1]]
    pm2 = [PS.pf[2], PS.pf[3], PS.pf[4], PS.pf[5]]
    w1, w2, ew = dr["w1r"], dr["w2r"], dr["ew"]
    xnT1, HT1, w1_sb, w2_sb = T.xnT1, T.HT1, T.w1_sb, T.w2_sb
    ew_sb, ws, x_blk = T.ew_sb, T.ws, T.x_blk

    S.rebase()

    @block.sync
    def _(sync):
        S.wait(sync, "yst")
        S.inc(sync.dma_start(out=ew_sb[:], in_=ew), "wld", 16)
        S.inc(sync.dma_start(out=x_blk[:], in_=x_src[:, 0:4096]), "xbL", 16)
        if copy_rest:
            S.inc(sync.dma_start(out=y_dst[:, 4096:NT * D],
                                 in_=x_src[:, 4096:NT * D]), "yst", 16)

    @block.vector
    def _(vector):
        vector.wait_ge(S.h["wld"], S.c["wld"])
        ewv = apof(ew_sb).rearrange("p (t c) -> p t c", c=2)
        wsv = apof(ws).rearrange("p (t c) -> p t c", c=1)
        S.step(vector, vector.tensor_add(wsv, ewv[:, :, 0:1], ewv[:, :, 1:2]), "vch")
        i = vector.tensor_scalar(ws[:], ws[:], 0.5, None, ALU.mult)
        S.inc(i, "wsF")

    emit_ln_stream(nc, block, S, T, C, PS, x_src, range(4), xnT1, 512)

    @block.gpsimd
    def _(gpsimd):
        if S.b["evacE"] > 0:
            gpsimd.wait_ge(S.h["evacE"], S.b["evacE"])
        if S.b["scanD"] > 0:
            gpsimd.wait_ge(S.h["scanD"], S.b["scanD"])
        for fc in range(NFC):
            S.waitb(gpsimd, "geluE", fc - (W1_RING - 1))
            if fc > 0:
                gpsimd.wait_ge(S.h["w1L"], S.hb("w1L", 16 * fc))
            slot = fc % W1_RING
            S.inc(gpsimd.dma_start(
                out=w1_sb[:, slot * 1024:(slot + 1) * 1024],
                in_=w1[fc].rearrange("p dc c -> p (dc c)")), "w1L", 16)
        for dh in range(2):
            for p in range(dh * NFC, dh * NFC + NFC):
                fc = p - dh * NFC
                S.waitb(gpsimd, "w2U", p - (W2_RING - 1))
                if p > 0:
                    gpsimd.wait_ge(S.h["w2L"], S.hb("w2L", 16 * p))
                slot = p % W2_RING
                S.inc(gpsimd.dma_start(
                    out=w2_sb[:, slot * 512:(slot + 1) * 512],
                    in_=w2[dh, fc]), "w2L", 16)

    for fc in range(NFC):
        @block.tensor
        def _(tensor, fc=fc):
            if fc == 0:
                S.waitb(tensor, "xnTe", 4)
            slot = fc % W1_RING
            tensor.wait_ge(S.h["w1L"], S.hb("w1L", 16 * (fc + 1)))
            if fc >= 2:
                tensor.wait_ge(S.h["geluE"], S.hb("geluE", fc - 1))
            for dc in range(NDC):
                i = tensor.matmul(
                    pm1[fc % 2][:],
                    w1_sb[:, slot * 1024 + dc * 128: slot * 1024 + (dc + 1) * 128],
                    xnT1[:, dc * 512:(dc + 1) * 512],
                    start=(dc == 0), stop=(dc == NDC - 1))
            S.inc(i, "mm1d")

        @block.scalar
        def _(scalar, fc=fc):
            scalar.wait_ge(S.h["mm1d"], S.hb("mm1d", fc + 1))
            i = scalar.activation(HT1[:, fc * 512:(fc + 1) * 512],
                                  pm1[fc % 2][:], AF.Gelu_apprx_tanh)
            S.inc(i, "geluE")

    for dh in range(2):
        @block.tensor
        def _(tensor, dh=dh):
            if dh == 0:
                tensor.wait_ge(S.h["geluE"], S.hb("geluE", NFC))
            else:
                tensor.wait_ge(S.h["evacE"], S.hb("evacE", 4))
            for fc in range(NFC):
                p = dh * NFC + fc
                slot = p % W2_RING
                tensor.wait_ge(S.h["w2L"], S.hb("w2L", 16 * (p + 1)))
                for sc in range(4):
                    i = tensor.matmul(
                        pm2[sc][:],
                        HT1[:, fc * 512 + sc * 128: fc * 512 + (sc + 1) * 128],
                        w2_sb[:, slot * 512:(slot + 1) * 512],
                        start=(fc == 0), stop=(fc == NFC - 1))
                S.inc(i, "w2U")

        @block.vector
        def _(vector, dh=dh):
            vector.wait_ge(S.h["w2U"], S.hb("w2U", NFC * (dh + 1)))
            S.waitb(vector, "wsF", 1)
            if dh == 0:
                vector.wait_ge(S.h["xbL"], S.hb("xbL", 16))
            for sc in range(4):
                i = vector.scalar_tensor_tensor(
                    x_blk[:, sc * D + dh * 512: sc * D + (dh + 1) * 512],
                    pm2[sc][:], ws[:, sc:sc + 1],
                    x_blk[:, sc * D + dh * 512: sc * D + (dh + 1) * 512],
                    ALU.mult, ALU.add)
                S.inc(i, "evacE")

    @block.sync
    def _(sync):
        S.waitb(sync, "evacE", 8)
        S.inc(sync.dma_start(out=y_dst[:, 0:4096], in_=x_blk[:]), "yst", 16)
        sync.wait_ge(S.h["yst"], S.c["yst"])


def host_ffn1_weights(w1, w2):
    w1r = np.ascontiguousarray(w1.reshape(8, 128, 64, 128).transpose(2, 1, 0, 3))
    w2r = np.ascontiguousarray(w2.reshape(64, 128, 2, 512).transpose(2, 0, 1, 3))
    return {"w1r": w1r, "w2r": w2r, "identw": np.eye(128, dtype=BBF16)}


def host_ffn1_ew(expert_weights, c):
    ews = expert_weights[512 * c:512 * (c + 1), :]
    return np.ascontiguousarray(ews.reshape(4, 128, 2).transpose(1, 0, 2)
                                .reshape(128, 8))



def alloc_attn1(nc, stack, P):
    T = NS()
    T.xb = P.f32("a1_xb", 2 * D)
    T.ln_stats = P.f32("a1_ln_stats", 80)
    T.S_f = P.f32("a1_S_f", 512)
    T.bcol = P.f32("a1_bcol", 32)
    T.glt = P.f32("a1_glt", 16)
    T.gsc = P.f32("a1_gsc", 48)
    T.rmsc = P.f32("a1_rmsc", 8)
    T.itemp3 = P.f32("a1_itemp", 12)
    T.cv_sb = P.f32("a1_cv", 48)
    T.fir_sb = P.f32("a1_fir", 4 * 34)
    T.rmsw_sb = P.f32("a1_rmsw", 2)
    T.xnT = P.bf16("a1_xnT", NDC * 2048)
    T.xnBb = P.bf16("a1_xnBb", NDC * 512)
    T.P_scr = P.bf16("a1_Pscr", 2 * 4128)
    T.q_fm = P.bf16("a1_qfm", 2 * 4096)
    T.k_fm = P.bf16("a1_kfm", 2 * 4096)
    T.v_fm = P.bf16("a1_vfm", 2 * 4128)
    T.kb_fm = P.bf16("a1_kbfm", 2 * 4096)
    T.tokst = P.bf16("a1_tokst", 1024)
    T.lsb = P.bf16("a1_lsb", 2 * 512)
    T.llb = P.bf16("a1_llb", 2 * 512)
    T.wch = P.bf16("a1_wch", 8 * 256)
    T.wgl_sb = P.bf16("a1_wgl", 8 * 16)
    T.wb_sb = P.bf16("a1_wb", 8 * 2)
    T.ln_xt = P.bf16("a1_ln_xt", 2 * D)
    T.MT_t = P.bf16("a1_MT", 32)
    T.M_t = P.bf16("a1_M", 32)
    T.M2T_t = P.bf16("a1_M2T", 32)
    T.M2_t = P.bf16("a1_M2", 32)
    T.M4T_t = P.bf16("a1_M4T", 32)
    T.M4_t = P.bf16("a1_M4", 32)
    T.M8T_t = P.bf16("a1_M8T", 32)
    T.M8_t = P.bf16("a1_M8", 32)
    T.A1T_t = P.bf16("a1_A1T", 32)
    T.A2_t = P.bf16("a1_A2", 32)
    T.A4_t = P.bf16("a1_A4", 32)
    T.A8_t = P.bf16("a1_A8", 32)
    T.A16_t = P.bf16("a1_A16", 32)
    T.R_t = P.bf16("a1_R", 64)
    T.TT_t = P.bf16("a1_TT", 32)
    T.AT_t = P.bf16("a1_AT", 32)
    T.u_sb = P.bf16("a1_u", 256)
    T.z_sb = P.bf16("a1_z", 256)
    T.S_bf = P.bf16("a1_Sbf", 512)
    T.bcol_bf = P.bf16("a1_bcolbf", 32)
    T.ostage = P.bf16("a1_ostage", 256)
    T.ot_t = P.bf16("a1_ot", 1024)
    T.mx_t = P.bf16("a1_mx", 1024)
    T.trowb = stack.enter_context(nc.sbuf_tensor("a1_trowb", [1, 4], BF16))
    T.trow = stack.enter_context(nc.sbuf_tensor("a1_trow", [1, 16], F32))
    T.bgl_sb = stack.enter_context(nc.sbuf_tensor("a1_bgl", [1, 16], BF16))
    T.l2row = stack.enter_context(nc.sbuf_tensor("a1_l2row", [1, 512], F32))
    T.l2rowb = stack.enter_context(nc.sbuf_tensor("a1_l2rowb", [1, 512], BF16))
    return T


def emit_attn1(nc, block, S, T, C, PS, dr, x_src, y_dst):
    NBLK, NCH, NGRP = 8, 2, 32
    REPS, L2EPS = 1e-5, 1e-6
    pp = [PS.pf[0], PS.pf[1]]
    pA = PS.pf[2][:, 0:64]
    pO = PS.pf[3][:, 0:256]
    pS = PS.pf[4]
    pB = PS.pf[5][:, 0:32]
    pZ = PS.pf[0]
    pU = PS.pf[1]
    pt = PS.ptA
    a1_in, a1_out = dr["a1_in"], dr["a1_out"]
    a2_in, a2_out = dr["a2_in"], dr["a2_out"]

    xnT, xnBb, P_scr = T.xnT, T.xnBb, T.P_scr
    q_fm, k_fm, v_fm, kb_fm = T.q_fm, T.k_fm, T.v_fm, T.kb_fm
    k_tok = xnT[:, 0:8192]
    vb_tok = xnT[:, 8192:16384]
    wo_sb = P_scr[:, 0:8192]
    tokst, lsb, llb, wch = T.tokst, T.lsb, T.llb, T.wch
    wgl_sb, wb_sb, cv_sb, fir_sb = T.wgl_sb, T.wb_sb, T.cv_sb, T.fir_sb
    rmsw_sb, itemp3, trow, trowb, bgl_sb = (T.rmsw_sb, T.itemp3, T.trow,
                                            T.trowb, T.bgl_sb)
    S_f, S_bf = T.S_f, T.S_bf
    MT_t, M_t, M2T_t, M2_t = T.MT_t, T.M_t, T.M2T_t, T.M2_t
    M4T_t, M4_t, M8T_t, M8_t = T.M4T_t, T.M4_t, T.M8T_t, T.M8_t
    A1T_t, A2_t, A4_t, A8_t, A16_t = T.A1T_t, T.A2_t, T.A4_t, T.A8_t, T.A16_t
    R_t, TT_t, AT_t = T.R_t, T.TT_t, T.AT_t
    u_sb, z_sb, bcol, bcol_bf = T.u_sb, T.z_sb, T.bcol, T.bcol_bf
    l2row, l2rowb = T.l2row, T.l2rowb
    glt, gsc, rmsc, ostage = T.glt, T.gsc, T.rmsc, T.ostage
    ot_t, mx_t, xb = T.ot_t, T.mx_t, T.xb
    mask_su = C.masks[:, 0:32]
    mask_iu = C.masks[:, 32:64]
    eye4 = C.masks[:, 64:96]

    S.rebase()

    @block.sync
    def _(sync):
        S.inc(sync.dma_start(out=cv_sb[:, 0:16], in_=dr["cvq"]), "wld", 16)
        S.inc(sync.dma_start(out=cv_sb[:, 16:32], in_=dr["cvk"]), "wld", 16)
        S.inc(sync.dma_start(out=cv_sb[:, 32:48], in_=dr["cvv"]), "wld", 16)
        S.inc(sync.dma_start(
            out=apof(fir_sb).rearrange("p (c t) -> p c t", t=34)[:, :, 0:3],
            in_=dr["firs"].rearrange("p (c t) -> p c t", t=3)), "wld", 16)
        S.inc(sync.dma_start(
            out=apof(fir_sb).rearrange("p (c t) -> p c t", t=34)[:, :, 3:34],
            in_=dr["firl"].rearrange("p (c t) -> p c t", t=31)), "wld", 16)
        S.inc(sync.dma_start(out=rmsw_sb[:], in_=dr["rmsw"]), "wld", 16)
        S.inc(sync.dma_start(out=trow[0:1, 0:4], in_=dr["ltemp"]), "wld", 16)

    @block.gpsimd
    def _(gpsimd):
        S.inc(gpsimd.dma_start(out=wb_sb[:], in_=dr["wbc"]), "wldg", 16)
        S.inc(gpsimd.dma_start(out=wgl_sb[:], in_=dr["wgl"]), "wldg", 16)
        S.inc(gpsimd.dma_start(out=bgl_sb[:], in_=dr["bgl"]), "wldg", 16)

    @block.vector
    def _(vector):
        vector.memset(v_fm[:], 0.0)
        i = vector.memset(P_scr[:], 0.0)
        S.inc(i, "cvD")

    emit_ln_stream(nc, block, S, T, C, PS, x_src, range(NT), xnT, 2048)

    @block.sync
    def _(sync):
        S.waitb(sync, "xnTe", NT)
        for d in range(8):
            b = d // 2
            S.inc(sync.dma_start(
                out=a1_in[d],
                in_=apof(xnT).rearrange("p (dc s) -> p dc s", s=2048)[
                    :, :, b * 512:(b + 1) * 512]), "shA", 16)

    @block.gpsimd
    def _(gpsimd):
        gpsimd.wait_ge(S.h["shA"], S.hb("shA", 16 * 8))
        with allow_ici_all_to_all():
            i = gpsimd.collective_compute(
                "AllToAll", ALU.bypass,
                replica_groups=[list(range(N_CORES))],
                ins=[a1_in.opt()], outs=[a1_out.opt()])
        S.inc(i, "ccA")

    ctr = {"gp": 0, "tp": 0}

    def pp_group(tensor):
        ctr["gp"] += 1
        g = ctr["gp"]
        if g > 2:
            tensor.wait_ge(S.h["pev"], S.hb("pev", g - 2))
        return pp[g % 2], g

    def pt_half(tensor):
        ctr["tp"] += 1
        n = ctr["tp"]
        if n > 2:
            tensor.wait_ge(S.h["trE"], S.hb("trE", n - 2))
        return pt[:, (n % 2) * 1024:(n % 2 + 1) * 1024], n

    fmbuf = {"q": q_fm, "k": k_fm, "v": v_fm}
    fmoff = {"q": (0, 4096), "k": (0, 4096), "v": (32, 4128)}

    def emit_scan(ch):
        for g in range(NGRP):
            @block.tensor
            def _(tensor, g=g):
                tensor.wait_ge(S.h["sAe"], S.c["sAe"])
                for j in range(4):
                    sp = (4 * g + j) * 32
                    for cc in range(2):
                        tensor.matmul(
                            pA[32 * j:32 * (j + 1), 0:32],
                            k_fm[:, cc * 4096 + sp: cc * 4096 + sp + 32],
                            kb_fm[:, cc * 4096 + sp: cc * 4096 + sp + 32],
                            start=(cc == 0), stop=(cc == 1),
                            tile_position=(0, 32 * j))
                    for cc in range(2):
                        i = tensor.matmul(
                            pA[32 * j:32 * (j + 1), 32:64],
                            k_fm[:, cc * 4096 + sp: cc * 4096 + sp + 32],
                            q_fm[:, cc * 4096 + sp: cc * 4096 + sp + 32],
                            start=(cc == 0), stop=(cc == 1),
                            tile_position=(0, 32 * j))
                S.inc(i, "sAm")

            @block.vector
            def _(vector, g=g):
                vector.wait_ge(S.h["sAm"], S.c["sAm"])
                S.step(vector, vector.tensor_mul(MT_t[:], pA[:, 0:32], mask_su), "vch")
                S.step(vector, vector.tensor_mul(AT_t[:], pA[:, 32:64], mask_iu), "vch")
                S.step(vector, vector.transpose(M_t[:], MT_t[:]), "vch")
                i = vector.tensor_add(A1T_t[:], MT_t[:], eye4)
                S.inc(i, "sAe")

            stages = [
                (M_t, MT_t, M2T_t, M2_t, A2_t),
                (M2_t, M2T_t, M4T_t, M4_t, A4_t),
                (M4_t, M4T_t, M8T_t, M8_t, A8_t),
            ]
            for lhs_t, rhs_t, outT_t, out_t, outA_t in stages:
                @block.tensor
                def _(tensor, lhs_t=lhs_t, rhs_t=rhs_t):
                    tensor.wait_ge(S.h["sAe"], S.c["sAe"])
                    for j in range(4):
                        i = tensor.matmul(
                            pA[32 * j:32 * (j + 1), 0:32],
                            lhs_t[32 * j:32 * (j + 1), :],
                            rhs_t[32 * j:32 * (j + 1), :],
                            start=True, stop=True,
                            tile_position=(32 * j, 32 * j))
                    S.inc(i, "sAm")

                @block.vector
                def _(vector, outT_t=outT_t, out_t=out_t, outA_t=outA_t):
                    vector.wait_ge(S.h["sAm"], S.c["sAm"])
                    S.step(vector, vector.tensor_copy(outT_t[:], pA[:, 0:32]), "vch")
                    S.step(vector, vector.transpose(out_t[:], outT_t[:]), "vch")
                    i = vector.tensor_add(outA_t[:], out_t[:], eye4)
                    S.inc(i, "sAe")

            @block.tensor
            def _(tensor):
                tensor.wait_ge(S.h["sAe"], S.c["sAe"])
                for j in range(4):
                    i = tensor.matmul(
                        pA[32 * j:32 * (j + 1), 0:32],
                        M8_t[32 * j:32 * (j + 1), :],
                        M8T_t[32 * j:32 * (j + 1), :],
                        start=True, stop=True,
                        tile_position=(32 * j, 32 * j))
                S.inc(i, "sAm")

            @block.vector
            def _(vector):
                vector.wait_ge(S.h["sAm"], S.c["sAm"])
                S.step(vector, vector.tensor_add(R_t[:, 32:64], pA[:, 0:32], eye4), "vch")
                i = vector.transpose(A16_t[:], R_t[:, 32:64])
                S.inc(i, "sAe")

            prods = [(A2_t, A1T_t, 0, 0), (A4_t, None, 0, 32),
                     (A8_t, None, 32, 0), (A16_t, None, 0, -1)]
            for lhs_t, rhsA, rhs_off, outoff in prods:
                @block.tensor
                def _(tensor, lhs_t=lhs_t, rhsA=rhsA, rhs_off=rhs_off):
                    tensor.wait_ge(S.h["sAe"], S.c["sAe"])
                    for j in range(4):
                        rhs = (rhsA[32 * j:32 * (j + 1), :] if rhsA is not None
                               else R_t[32 * j:32 * (j + 1), rhs_off:rhs_off + 32])
                        i = tensor.matmul(
                            pA[32 * j:32 * (j + 1), 0:32],
                            lhs_t[32 * j:32 * (j + 1), :], rhs,
                            start=True, stop=True,
                            tile_position=(32 * j, 32 * j))
                    S.inc(i, "sAm")

                @block.vector
                def _(vector, outoff=outoff):
                    vector.wait_ge(S.h["sAm"], S.c["sAm"])
                    if outoff < 0:
                        i = vector.tensor_copy(TT_t[:], pA[:, 0:32])
                    else:
                        i = vector.tensor_copy(R_t[:, outoff:outoff + 32], pA[:, 0:32])
                    S.inc(i, "sAe")

            for j in range(4):
                cidx = 4 * g + j
                w = cidx // 4
                sp = cidx * 32

                @block.tensor
                def _(tensor, g=g, j=j, w=w, sp=sp):
                    tensor.wait_ge(S.h["scanD"], S.c["scanD"])
                    if j == 0:
                        tensor.wait_ge(S.h["oev"], S.c["oev"])
                    for cc in range(2):
                        i = tensor.matmul(
                            pZ[32 * j:32 * (j + 1), 0:256],
                            kb_fm[:, cc * 4096 + sp: cc * 4096 + sp + 32],
                            S_bf[:, cc * 256:(cc + 1) * 256],
                            start=(cc == 0), stop=(cc == 1),
                            tile_position=(0, 32 * j))
                    S.inc(i, "sZm")

                @block.vector
                def _(vector, j=j, w=w):
                    vector.wait_ge(S.h["sZm"], S.c["sZm"])
                    i = vector.tensor_sub(
                        z_sb[32 * j:32 * (j + 1), :],
                        vb_tok[32 * j:32 * (j + 1), w * 256:(w + 1) * 256],
                        pZ[32 * j:32 * (j + 1), 0:256])
                    S.inc(i, "sZe")

                @block.tensor
                def _(tensor, j=j):
                    tensor.wait_ge(S.h["sZe"], S.c["sZe"])
                    i = tensor.matmul(
                        pU[32 * j:32 * (j + 1), 0:256],
                        TT_t[32 * j:32 * (j + 1), :],
                        z_sb[32 * j:32 * (j + 1), :],
                        start=True, stop=True,
                        tile_position=(32 * j, 32 * j))
                    S.inc(i, "sUm")

                @block.vector
                def _(vector, j=j):
                    vector.wait_ge(S.h["sUm"], S.c["sUm"])
                    i = vector.tensor_copy(u_sb[32 * j:32 * (j + 1), :],
                                           pU[32 * j:32 * (j + 1), 0:256])
                    S.inc(i, "sUe")

                @block.tensor
                def _(tensor, g=g, j=j, w=w, sp=sp):
                    tensor.wait_ge(S.h["sUe"], S.c["sUe"])
                    for cc in range(2):
                        tensor.matmul(
                            pO[32 * j:32 * (j + 1), :],
                            q_fm[:, cc * 4096 + sp: cc * 4096 + sp + 32],
                            S_bf[:, cc * 256:(cc + 1) * 256],
                            start=(cc == 0), stop=False,
                            tile_position=(0, 32 * j))
                    tensor.matmul(
                        pO[32 * j:32 * (j + 1), :],
                        AT_t[32 * j:32 * (j + 1), :],
                        u_sb[32 * j:32 * (j + 1), :],
                        start=False, stop=(j == 3),
                        tile_position=(32 * j, 32 * j))
                    for cc in range(2):
                        i = tensor.matmul(
                            pS[:, cc * 256:(cc + 1) * 256],
                            k_tok[32 * j:32 * (j + 1),
                                  w * 256 + cc * 128: w * 256 + (cc + 1) * 128],
                            u_sb[32 * j:32 * (j + 1), :],
                            start=(cc == 0), stop=(cc == 1),
                            tile_position=(32 * j, 0))
                    S.inc(i, "sSm")

                @block.vector
                def _(vector, j=j):
                    vector.wait_ge(S.h["sSm"], S.c["sSm"])
                    S.step(vector, vector.tensor_add(S_f[:], S_f[:], pS[:]), "vch")
                    i = vector.tensor_copy(S_bf[:], S_f[:])
                    S.inc(i, "scanD")

            @block.vector
            def _(vector, g=g):
                vector.wait_ge(S.h["sSm"], S.c["sSm"])
                if S.c["payD"] > 0:
                    vector.wait_ge(S.h["payD"], S.c["payD"])
                i = vector.tensor_copy(ostage[:], pO[:])
                S.inc(i, "oev")

            @block.sync
            def _(sync, ch=ch, g=g):
                sync.wait_ge(S.h["oev"], S.c["oev"])
                if S.c["payD"] > 0:
                    sync.wait_ge(S.h["payD"], S.c["payD"])
                S.inc(sync.dma_start(
                    out=a2_in[g // 4, 0][:, g % 4, ch * 256:(ch + 1) * 256],
                    in_=ostage[:]), "payD", 16)

    def emit_fir(ch):
        for sb8 in range(NBLK):
            @block.vector
            def _(vector, sb8=sb8, ch=ch):
                if sb8 == 0:
                    vector.wait_ge(S.h["oev"], S.c["oev"])
                for cc in range(2):
                    base = cc * 4128 + 32 + sb8 * 512
                    fw = fir_sb[:, (2 * ch + cc) * 34:(2 * ch + cc) * 34 + 34]
                    i = vector.tensor_scalar(
                        lsb[:, cc * 512:(cc + 1) * 512],
                        v_fm[:, base - 2: base - 2 + 512],
                        fw[:, 0:1], None, ALU.mult)
                    for tpp in range(1, 3):
                        i = vector.scalar_tensor_tensor(
                            lsb[:, cc * 512:(cc + 1) * 512],
                            v_fm[:, base - 2 + tpp: base - 2 + tpp + 512],
                            fw[:, tpp:tpp + 1],
                            lsb[:, cc * 512:(cc + 1) * 512], ALU.mult, ALU.add)
                    i = vector.tensor_scalar(
                        llb[:, cc * 512:(cc + 1) * 512],
                        v_fm[:, base - 30: base - 30 + 512],
                        fw[:, 3:4], None, ALU.mult)
                    for tpp in range(1, 31):
                        i = vector.scalar_tensor_tensor(
                            llb[:, cc * 512:(cc + 1) * 512],
                            v_fm[:, base - 30 + tpp: base - 30 + tpp + 512],
                            fw[:, 3 + tpp:4 + tpp],
                            llb[:, cc * 512:(cc + 1) * 512], ALU.mult, ALU.add)
                S.inc(i, "fir")

            for w4 in range(4):
                @block.tensor
                def _(tensor, sb8=sb8, w4=w4):
                    tensor.wait_ge(S.h["fir"], S.c["fir"])
                    pth, n = pt_half(tensor)
                    for cc in range(2):
                        tensor.transpose(
                            pth[:, cc * 128:(cc + 1) * 128],
                            lsb[:, cc * 512 + w4 * 128: cc * 512 + (w4 + 1) * 128],
                            C.ident[:])
                    for cc in range(2):
                        i = tensor.transpose(
                            pth[:, 256 + cc * 128: 256 + (cc + 1) * 128],
                            llb[:, cc * 512 + w4 * 128: cc * 512 + (w4 + 1) * 128],
                            C.ident[:])
                    S.inc(i, "trP")

                @block.vector
                def _(vector, sb8=sb8, w4=w4):
                    n = ctr["tp"]
                    vector.wait_ge(S.h["trP"], S.c["trP"])
                    vector.wait_ge(S.h["payD"], S.c["payD"])
                    pth = pt[:, (n % 2) * 1024:(n % 2 + 1) * 1024]
                    i = vector.tensor_copy(tokst[:, 512:1024], pth[:, 0:512])
                    S.inc(i, "trE")

                @block.sync
                def _(sync, sb8=sb8, w4=w4, ch=ch):
                    sync.wait_ge(S.h["trE"], S.c["trE"])
                    if S.c["payD"] > 0:
                        sync.wait_ge(S.h["payD"], S.c["payD"])
                    S.inc(sync.dma_start(
                        out=a2_in[sb8, 2][:, w4, ch * 256:(ch + 1) * 256],
                        in_=tokst[:, 512:768]), "payD", 16)
                    sync.wait_ge(S.h["payD"], S.c["payD"])
                    S.inc(sync.dma_start(
                        out=a2_in[sb8, 3][:, w4, ch * 256:(ch + 1) * 256],
                        in_=tokst[:, 768:1024]), "payD", 16)

    for ch in range(NCH):
        for pi, pn in enumerate(("q", "k", "v")):
            @block.gpsimd
            def _(gpsimd, pi=pi, ch=ch):
                gpsimd.wait_ge(S.h["pmm"], S.c["pmm"])
                src = (dr["wqc"], dr["wkc"], dr["wvc"])[pi]
                for dc in range(NDC):
                    if S.c["wldg"] > 0:
                        gpsimd.wait_ge(S.h["wldg"], S.c["wldg"])
                    S.inc(gpsimd.dma_start(
                        out=wch[:, dc * 256:(dc + 1) * 256],
                        in_=src[dc * 128:(dc + 1) * 128,
                                ch * 256:(ch + 1) * 256]), "wldg", 16)

            for blk in range(NBLK):
                nblk_g = (ch * 3 + pi) * NBLK + blk

                @block.sync
                def _(sync, blk=blk, nblk_g=nblk_g):
                    if nblk_g == 0:
                        S.waitb(sync, "ccA", 1)
                    if nblk_g >= 1:
                        sync.wait_ge(S.h["pmm"], S.c["pmm"])
                    if S.c["blkL"] > 0:
                        sync.wait_ge(S.h["blkL"], S.c["blkL"])
                    S.inc(sync.dma_start(out=xnBb[:], in_=a1_out[blk]), "blkL", 16)

                for oc in range(2):
                    @block.tensor
                    def _(tensor, pi=pi, blk=blk, oc=oc, ch=ch):
                        tensor.wait_ge(S.h["blkL"], S.c["blkL"])
                        if blk == 0 and oc == 0:
                            tensor.wait_ge(S.h["wldg"], S.c["wldg"])
                        ppt, g = pp_group(tensor)
                        for dc in range(NDC):
                            i = tensor.matmul(
                                ppt[:],
                                wch[:, dc * 256 + oc * 128: dc * 256 + (oc + 1) * 128],
                                xnBb[:, dc * 512:(dc + 1) * 512],
                                start=(dc == 0), stop=(dc == NDC - 1))
                        S.inc(i, "pmm")

                    @block.vector
                    def _(vector, pn=pn, blk=blk, oc=oc):
                        g = ctr["gp"]
                        vector.wait_ge(S.h["pmm"], S.c["pmm"])
                        i = vector.tensor_copy(
                            P_scr[:, oc * 4128 + 32 + blk * 512:
                                  oc * 4128 + 32 + (blk + 1) * 512],
                            pp[g % 2][:])
                        S.inc(i, "pev")

                if pi == 2:
                    @block.tensor
                    def _(tensor, blk=blk, ch=ch):
                        if blk == 0:
                            S.wait(tensor, "bsig")
                        for sc in range(4):
                            w = blk * 4 + sc
                            for dc in range(NDC):
                                i = tensor.matmul(
                                    pB[:, w:w + 1],
                                    xnBb[:, dc * 512 + sc * 128: dc * 512 + (sc + 1) * 128],
                                    wb_sb[:, dc * 2 + ch: dc * 2 + ch + 1],
                                    start=(dc == 0 and w == 0),
                                    stop=(dc == NDC - 1))
                        S.inc(i, "pmm")

            for sb8 in range(NBLK):
                @block.vector
                def _(vector, pi=pi, sb8=sb8, ch=ch):
                    if sb8 == 0:
                        vector.wait_ge(S.h["pev"], S.c["pev"])
                    for cc in range(2):
                        base = cc * 4128 + 32 + sb8 * 512
                        tap = cv_sb[:, pi * 16 + (2 * ch + cc) * 4:
                                    pi * 16 + (2 * ch + cc) * 4 + 4]
                        i = vector.tensor_scalar(
                            lsb[:, cc * 512:(cc + 1) * 512],
                            P_scr[:, base - 3:base - 3 + 512],
                            tap[:, 0:1], None, ALU.mult)
                        for tpp in range(1, 4):
                            i = vector.scalar_tensor_tensor(
                                lsb[:, cc * 512:(cc + 1) * 512],
                                P_scr[:, base - 3 + tpp: base - 3 + tpp + 512],
                                tap[:, tpp:tpp + 1],
                                lsb[:, cc * 512:(cc + 1) * 512],
                                ALU.mult, ALU.add)
                    S.inc(i, "cvD")

                @block.scalar
                def _(scalar, pn=pn, sb8=sb8):
                    scalar.wait_ge(S.h["cvD"], S.c["cvD"])
                    pre, stride = fmoff[pn]
                    for cc in range(2):
                        i = scalar.activation(
                            fmbuf[pn][:, cc * stride + pre + sb8 * 512:
                                      cc * stride + pre + (sb8 + 1) * 512],
                            lsb[:, cc * 512:(cc + 1) * 512], AF.Silu)
                    S.inc(i, "silu")

                @block.vector
                def _(vector):
                    vector.wait_ge(S.h["silu"], S.c["silu"])

        for li, pn in enumerate(("q", "k")):
            for sb8 in range(NBLK):
                @block.scalar
                def _(scalar, pn=pn, sb8=sb8):
                    for cc in range(2):
                        i = scalar.activation(
                            lsb[:, cc * 512:(cc + 1) * 512],
                            fmbuf[pn][:, cc * 4096 + sb8 * 512: cc * 4096 + (sb8 + 1) * 512],
                            AF.Square)
                    S.inc(i, "l2q")

                @block.tensor
                def _(tensor):
                    tensor.wait_ge(S.h["l2q"], S.c["l2q"])
                    ppt, g = pp_group(tensor)
                    for cc in range(2):
                        i = tensor.matmul(ppt[0:1, :], C.ones_col[:],
                                          lsb[:, cc * 512:(cc + 1) * 512],
                                          start=(cc == 0), stop=(cc == 1))
                    S.inc(i, "pmm")

                @block.vector
                def _(vector):
                    g = ctr["gp"]
                    vector.wait_ge(S.h["pmm"], S.c["pmm"])
                    i = vector.tensor_copy(l2row[0:1, :], pp[g % 2][0:1, :])
                    S.inc(i, "pev")

                @block.scalar
                def _(scalar):
                    S.wait(scalar, "pev", S.c["pev"])
                    S.step(scalar, scalar.activation(
                        l2row[0:1, :], l2row[0:1, :], AF.Ln,
                        bias=C.eps[0:1, 0:1]), "sch")
                    i = scalar.activation(l2rowb[0:1, :], l2row[0:1, :],
                                          AF.Exp, scale=-0.5)
                    S.inc(i, "l2k")

                @block.tensor
                def _(tensor):
                    tensor.wait_ge(S.h["l2k"], S.c["l2k"])
                    ppt, g = pp_group(tensor)
                    i = tensor.matmul(ppt[:], C.ones_row[:], l2rowb[0:1, :],
                                      start=True, stop=True)
                    S.inc(i, "pmm")

                @block.vector
                def _(vector, pn=pn, sb8=sb8):
                    g = ctr["gp"]
                    vector.wait_ge(S.h["pmm"], S.c["pmm"])
                    for cc in range(2):
                        i = vector.tensor_mul(
                            fmbuf[pn][:, cc * 4096 + sb8 * 512: cc * 4096 + (sb8 + 1) * 512],
                            fmbuf[pn][:, cc * 4096 + sb8 * 512: cc * 4096 + (sb8 + 1) * 512],
                            pp[g % 2][:])
                    S.inc(i, "pev")

        @block.vector
        def _(vector):
            vector.wait_ge(S.h["pmm"], S.c["pmm"])
            S.step(vector, vector.tensor_copy(bcol[:], pB), "vch")
            S.inc(nc.vector.engine_nop(), "bsig")

        @block.scalar
        def _(scalar):
            S.wait(scalar, "bsig", S.c["bsig"])
            i = scalar.activation(bcol[:], bcol[:], AF.Exp, scale=-1.0)
            S.inc(i, "bsig")

        @block.vector
        def _(vector):
            S.wait(vector, "bsig", S.c["bsig"])
            S.step(vector, vector.tensor_scalar(bcol[:], bcol[:], 1.0, None, ALU.add), "vch")
            S.step(vector, vector.reciprocal(bcol[:], bcol[:]), "vch")
            i = vector.tensor_copy(bcol_bf[:], bcol[:])
            S.inc(i, "bsig")

        for w in range(32):
            @block.tensor
            def _(tensor, w=w):
                if w == 0:
                    S.wait(tensor, "bsig")
                pth, n = pt_half(tensor)
                for cc in range(2):
                    tensor.transpose(
                        pth[:, cc * 128:(cc + 1) * 128],
                        k_fm[:, cc * 4096 + w * 128: cc * 4096 + (w + 1) * 128],
                        C.ident[:])
                for cc in range(2):
                    i = tensor.transpose(
                        pth[:, 256 + cc * 128: 256 + (cc + 1) * 128],
                        v_fm[:, cc * 4128 + 32 + w * 128: cc * 4128 + 32 + (w + 1) * 128],
                        C.ident[:])
                S.inc(i, "trP")

            @block.vector
            def _(vector, w=w):
                n = ctr["tp"]
                vector.wait_ge(S.h["trP"], S.c["trP"])
                if S.c["payD"] > 0:
                    vector.wait_ge(S.h["payD"], S.c["payD"])
                pth = pt[:, (n % 2) * 1024:(n % 2 + 1) * 1024]
                for cc in range(2):
                    vector.tensor_copy(
                        k_tok[:, w * 256 + cc * 128: w * 256 + (cc + 1) * 128],
                        pth[:, cc * 128:(cc + 1) * 128])
                for cc in range(2):
                    vector.tensor_copy(
                        tokst[:, cc * 128:(cc + 1) * 128],
                        pth[:, 256 + cc * 128: 256 + (cc + 1) * 128])
                i = vector.tensor_scalar(
                    vb_tok[:, w * 256:(w + 1) * 256],
                    tokst[:, 0:256], bcol[:, w:w + 1], None, ALU.mult)
                S.inc(i, "trE")

            @block.sync
            def _(sync, w=w, ch=ch):
                sync.wait_ge(S.h["trE"], S.c["trE"])
                if S.c["payD"] > 0:
                    sync.wait_ge(S.h["payD"], S.c["payD"])
                S.inc(sync.dma_start(
                    out=a2_in[w // 4, 1][:, w % 4, ch * 256:(ch + 1) * 256],
                    in_=tokst[:, 0:256]), "payD", 16)

        for w in range(32):
            @block.vector
            def _(vector, w=w):
                if w == 0:
                    vector.wait_ge(S.h["trE"], S.c["trE"])
                i = vector.tensor_scalar(
                    tokst[:, 256:512], k_tok[:, w * 256:(w + 1) * 256],
                    bcol[:, w:w + 1], None, ALU.mult)
                S.inc(i, "kbE")

            @block.tensor
            def _(tensor, w=w):
                tensor.wait_ge(S.h["kbE"], S.c["kbE"])
                pth, n = pt_half(tensor)
                for cc in range(2):
                    i = tensor.transpose(
                        pth[:, cc * 128:(cc + 1) * 128],
                        tokst[:, 256 + cc * 128: 256 + (cc + 1) * 128],
                        C.ident[:])
                S.inc(i, "trP")

            @block.vector
            def _(vector, w=w):
                n = ctr["tp"]
                vector.wait_ge(S.h["trP"], S.c["trP"])
                pth = pt[:, (n % 2) * 1024:(n % 2 + 1) * 1024]
                for cc in range(2):
                    i = vector.tensor_copy(
                        kb_fm[:, cc * 4096 + w * 128: cc * 4096 + (w + 1) * 128],
                        pth[:, cc * 128:(cc + 1) * 128])
                S.inc(i, "trE")

        @block.vector
        def _(vector):
            vector.wait_ge(S.h["trE"], S.c["trE"])
            vector.memset(S_f[:], 0.0)
            i = vector.memset(S_bf[:], 0.0)
            S.inc(i, "scanD")

        emit_scan(ch)
        emit_fir(ch)

    @block.gpsimd
    def _(gpsimd):
        gpsimd.wait_ge(S.h["payD"], S.c["payD"])
        with allow_ici_all_to_all():
            i = gpsimd.collective_compute(
                "AllToAll", ALU.bypass,
                replica_groups=[list(range(N_CORES))],
                ins=[a2_in.opt()], outs=[a2_out.opt()])
        S.inc(i, "ccB")

    @block.gpsimd
    def _(gpsimd):
        gpsimd.wait_ge(S.h["silu"], S.c["silu"])
        for dc in range(NDC):
            gpsimd.wait_ge(S.h["clg"], S.hb("clg", 16 * dc)) if dc > 0 else None
            S.inc(gpsimd.dma_start(
                out=wo_sb[:, dc * 1024:(dc + 1) * 1024],
                in_=dr["wo_n"][dc * 128:(dc + 1) * 128, :]), "clg", 16)

    @block.vector
    def _(vector):
        vector.wait_ge(S.h["clg"], S.c["clg"])
        for dc in range(NDC):
            i = vector.tensor_scalar(
                wo_sb[:, dc * 1024:(dc + 1) * 1024],
                wo_sb[:, dc * 1024:(dc + 1) * 1024],
                rmsw_sb[:, dc % 2: dc % 2 + 1], None, ALU.mult)
            S.inc(i, "clv")

    @block.scalar
    def _(scalar):
        S.wait(scalar, "wld")
        i = scalar.activation(trow[0:1, 4:8], trow[0:1, 0:4], AF.Exp)
        S.inc(i, "cst")

    @block.vector
    def _(vector):
        S.waitb(vector, "cst", 1)
        S.step(vector, vector.reciprocal(trow[0:1, 8:12], trow[0:1, 4:8]), "vch")
        i = vector.tensor_copy(trowb[0:1, 0:4], trow[0:1, 8:12])
        S.inc(i, "cst")

    @block.tensor
    def _(tensor):
        S.waitb(tensor, "cst", 2)
        tensor.wait_ge(S.h["scanD"], S.c["scanD"])
        tensor.wait_ge(S.h["sUe"], S.c["sUe"])
        ppt, g = pp_group(tensor)
        i = tensor.matmul(ppt[:, 0:4], C.ones_row[:], trowb[0:1, 0:4],
                          start=True, stop=True)
        S.inc(i, "pmm")

    @block.vector
    def _(vector):
        g = ctr["gp"]
        vector.wait_ge(S.h["pmm"], S.c["pmm"])
        i = vector.tensor_copy(itemp3[:, 0:4], pp[g % 2][:, 0:4])
        S.inc(i, "pev")

    for t in range(NT):
        b, ti, par = t // 4, t % 4, t % 2
        ct_od = k_tok[:, par * 4096:par * 4096 + 1024]
        ct_v = k_tok[:, par * 4096 + 1024:par * 4096 + 2048]
        ct_ls = k_tok[:, par * 4096 + 2048:par * 4096 + 3072]
        ct_ll = k_tok[:, par * 4096 + 3072:par * 4096 + 4096]
        ct_xn = vb_tok[:, par * 1024:(par + 1) * 1024]

        @block.sync
        def _(sync, t=t, b=b, ti=ti, par=par, ct_od=ct_od, ct_v=ct_v,
              ct_ls=ct_ls, ct_ll=ct_ll, ct_xn=ct_xn):
            if t == 0:
                S.wait(sync, "ccB")
            if t >= 2:
                sync.wait_ge(S.h["pev"], S.c["pev"])
                S.waitb(sync, "yst", 16 * (t - 1))
            for half in range(2):
                src = 2 * b + half
                for tix, buf in ((0, ct_od), (1, ct_v), (2, ct_ls), (3, ct_ll)):
                    S.inc(sync.dma_start(
                        out=buf[:, half * 512:(half + 1) * 512],
                        in_=a2_out[src, tix][:, ti, :]), "cl", 16)
            S.inc(sync.dma_start(
                out=ct_xn.rearrange("p (dc s) -> p dc s", s=128),
                in_=a1_in[2 * b][:, :, ti * 128:(ti + 1) * 128]), "cl", 16)
            S.inc(sync.dma_start(out=xb[:, par * D:(par + 1) * D],
                                 in_=x_src[:, t * D:(t + 1) * D]), "cl", 16)

        @block.tensor
        def _(tensor, t=t, ct_xn=ct_xn):
            tensor.wait_ge(S.h["cl"], S.c["cl"])
            ppt, g = pp_group(tensor)
            for dc in range(NDC):
                tensor.matmul(ppt[:, 0:16],
                              ct_xn[:, dc * 128:(dc + 1) * 128],
                              wgl_sb[:, dc * 16:(dc + 1) * 16],
                              start=(dc == 0), stop=False)
            i = tensor.matmul(ppt[:, 0:16], C.ones_row[:], bgl_sb[0:1, 0:16],
                              start=False, stop=True)
            S.inc(i, "gmm")

        @block.vector
        def _(vector, t=t):
            g = ctr["gp"]
            vector.wait_ge(S.h["gmm"], S.c["gmm"])
            i = vector.tensor_copy(glt[:], pp[g % 2][:, 0:16])
            S.inc(i, "pev")

        @block.scalar
        def _(scalar, t=t):
            S.wait(scalar, "pev", S.c["pev"])
            i = scalar.activation(gsc[:, 0:4], glt[:, 0:4], AF.Exp, scale=-1.0)
            S.inc(i, "cst")

        @block.vector
        def _(vector, t=t):
            S.wait(vector, "cst", S.c["cst"])
            S.step(vector, vector.tensor_scalar(gsc[:, 0:4], gsc[:, 0:4],
                                                1.0, None, ALU.add), "vch")
            S.step(vector, vector.reciprocal(gsc[:, 0:4], gsc[:, 0:4]), "vch")
            for h in range(4):
                S.step(vector, vector.tensor_scalar(
                    gsc[:, 12 + 3 * h:15 + 3 * h], glt[:, 4 + 3 * h:7 + 3 * h],
                    itemp3[:, h:h + 1], None, ALU.mult), "vch")
            S.step(vector, vector.tensor_reduce(
                gsc[:, 24:28].rearrange("p (h c) -> p h c", c=1),
                gsc[:, 12:24].rearrange("p (h c) -> p h c", c=3),
                AX.X, ALU.max), "vch")
            for j3 in range(3):
                S.step(vector, vector.tensor_copy(
                    gsc[:, 28:40].rearrange("p (h c) -> p h c", c=3)[:, :, j3:j3 + 1],
                    gsc[:, 24:28].rearrange("p (h c) -> p h c", c=1)), "vch")
            i = vector.tensor_sub(gsc[:, 12:24], gsc[:, 12:24], gsc[:, 28:40])
            S.inc(i, "mixv")

        @block.scalar
        def _(scalar, t=t):
            S.wait(scalar, "mixv", S.c["mixv"])
            i = scalar.activation(gsc[:, 28:40], gsc[:, 12:24], AF.Exp)
            S.inc(i, "cst")

        @block.vector
        def _(vector, t=t, ct_od=ct_od, ct_v=ct_v, ct_ls=ct_ls, ct_ll=ct_ll):
            S.wait(vector, "cst", S.c["cst"])
            S.step(vector, vector.tensor_reduce(
                gsc[:, 24:28].rearrange("p (h c) -> p h c", c=1),
                gsc[:, 28:40].rearrange("p (h c) -> p h c", c=3),
                AX.X, ALU.add), "vch")
            S.step(vector, vector.reciprocal(gsc[:, 24:28], gsc[:, 24:28]), "vch")
            for j3 in range(3):
                S.step(vector, vector.tensor_copy(
                    gsc[:, 12:24].rearrange("p (h c) -> p h c", c=3)[:, :, j3:j3 + 1],
                    gsc[:, 24:28].rearrange("p (h c) -> p h c", c=1)), "vch")
            S.step(vector, vector.tensor_mul(gsc[:, 12:24], gsc[:, 12:24],
                                             gsc[:, 28:40]), "vch")
            for h in range(4):
                p0 = gsc[:, 12 + 3 * h:13 + 3 * h]
                p1 = gsc[:, 13 + 3 * h:14 + 3 * h]
                p2 = gsc[:, 14 + 3 * h:15 + 3 * h]
                sl = slice(h * 256, (h + 1) * 256)
                vector.tensor_scalar(mx_t[:, sl], ct_v[:, sl], p0, None, ALU.mult)
                vector.scalar_tensor_tensor(mx_t[:, sl], ct_ls[:, sl], p1,
                                            mx_t[:, sl], ALU.mult, ALU.add)
                vector.scalar_tensor_tensor(mx_t[:, sl], ct_ll[:, sl], p2,
                                            mx_t[:, sl], ALU.mult, ALU.add)
                vector.tensor_sub(ot_t[:, sl], ct_od[:, sl], mx_t[:, sl])
                i = vector.scalar_tensor_tensor(ot_t[:, sl], ot_t[:, sl],
                                                gsc[:, h:h + 1],
                                                mx_t[:, sl], ALU.mult, ALU.add)
            S.inc(i, "mixv")

        @block.scalar
        def _(scalar, t=t, par=par):
            S.wait(scalar, "mixv", S.c["mixv"])
            for h in range(4):
                scalar.activation(T.ln_xt[:, par * D + h * 256: par * D + (h + 1) * 256],
                                  ot_t[:, h * 256:(h + 1) * 256],
                                  AF.Square, accum_out=rmsc[:, h:h + 1])
            S.step(scalar, scalar.activation(rmsc[:, 4:8], rmsc[:, 0:4],
                                             AF.Ln, bias=C.eps[:, 1:2],
                                             scale=1.0 / 256), "sch")
            i = scalar.activation(rmsc[:, 0:4], rmsc[:, 4:8], AF.Exp, scale=-0.5)
            S.inc(i, "cst")

        @block.vector
        def _(vector, t=t):
            S.wait(vector, "cst", S.c["cst"])
            for h in range(4):
                i = vector.tensor_scalar(ot_t[:, h * 256:(h + 1) * 256],
                                         ot_t[:, h * 256:(h + 1) * 256],
                                         rmsc[:, h:h + 1], None, ALU.mult)
            S.inc(i, "otr")

        @block.tensor
        def _(tensor, t=t):
            tensor.wait_ge(S.h["otr"], S.c["otr"])
            pth, n = pt_half(tensor)
            for dc in range(NDC):
                i = tensor.transpose(pth[:, dc * 128:(dc + 1) * 128],
                                     ot_t[:, dc * 128:(dc + 1) * 128], C.ident[:])
            S.inc(i, "trP")

        @block.vector
        def _(vector, t=t):
            n = ctr["tp"]
            vector.wait_ge(S.h["trP"], S.c["trP"])
            vector.wait_ge(S.h["omm"], S.c["omm"])
            pth = pt[:, (n % 2) * 1024:(n % 2 + 1) * 1024]
            for dc in range(NDC):
                i = vector.tensor_copy(mx_t[:, dc * 128:(dc + 1) * 128],
                                       pth[:, dc * 128:(dc + 1) * 128])
            S.inc(i, "trE")

        for dh in range(2):
            @block.tensor
            def _(tensor, t=t, dh=dh):
                tensor.wait_ge(S.h["trE"], S.c["trE"])
                if dh == 0 and t == 0:
                    tensor.wait_ge(S.h["clv"], S.c["clv"])
                ppt, g = pp_group(tensor)
                for dc in range(NDC):
                    i = tensor.matmul(
                        ppt[:],
                        mx_t[:, dc * 128:(dc + 1) * 128],
                        wo_sb[:, dc * 1024 + dh * 512: dc * 1024 + (dh + 1) * 512],
                        start=(dc == 0), stop=(dc == NDC - 1))
                S.inc(i, "omm")

            @block.vector
            def _(vector, t=t, dh=dh, par=par):
                g = ctr["gp"]
                vector.wait_ge(S.h["omm"], S.c["omm"])
                i = vector.tensor_add(
                    xb[:, par * D + dh * 512: par * D + (dh + 1) * 512],
                    xb[:, par * D + dh * 512: par * D + (dh + 1) * 512],
                    pp[g % 2][:])
                S.inc(i, "pev")

        @block.sync
        def _(sync, t=t, par=par):
            sync.wait_ge(S.h["pev"], S.c["pev"])
            S.inc(sync.dma_start(out=y_dst[:, t * D:(t + 1) * D],
                                 in_=xb[:, par * D:(par + 1) * D]), "yst", 16)

    @block.sync
    def _(sync):
        sync.wait_ge(S.h["yst"], S.c["yst"])


def host_attn1_weights(Wq, Wk, Wv, conv_q, conv_k, conv_v, Wb, fir_short,
                       fir_long, Wg, bg, Wl, bl, log_temp, rms_w, Wo):
    masks = np.zeros((128, 96), np.float32)
    su = -np.triu(np.ones((32, 32), np.float32), 1)
    iu = np.triu(np.ones((32, 32), np.float32), 0)
    ey = np.eye(32, dtype=np.float32)
    for j in range(4):
        masks[32 * j:32 * (j + 1), 0:32] = su
        masks[32 * j:32 * (j + 1), 32:64] = iu
        masks[32 * j:32 * (j + 1), 64:96] = ey
    wgl_np = np.concatenate([Wg, Wl], axis=1)
    return {
        "wgl": np.ascontiguousarray(wgl_np.reshape(8, 128, 16).transpose(1, 0, 2)
                                    .reshape(128, 128)),
        "bgl": np.ascontiguousarray(np.concatenate([bg, bl])[None, :]),
        "ltemp": np.ascontiguousarray(log_temp[None, :]),
        "rmsw": np.ascontiguousarray(rms_w.reshape(2, 128).T),
        "wo_n": np.ascontiguousarray(Wo),
        "masks": masks,
        "identw": np.eye(128, dtype=BBF16),
        "wbc_full": Wb, "wq_full": Wq, "wk_full": Wk, "wv_full": Wv,
        "cvq_full": conv_q, "cvk_full": conv_k, "cvv_full": conv_v,
        "firs_full": fir_short, "firl_full": fir_long,
    }


def host_attn1_core_inputs(wd, c):
    h0 = 2 * (c % 2)
    cs, ce = h0 * 256, (h0 + 2) * 256
    out = {k: v for k, v in wd.items() if not k.endswith("_full")}
    out["wqc"] = np.ascontiguousarray(wd["wq_full"][:, cs:ce])
    out["wkc"] = np.ascontiguousarray(wd["wk_full"][:, cs:ce])
    out["wvc"] = np.ascontiguousarray(wd["wv_full"][:, cs:ce])
    for name, key in (("cvq", "cvq_full"), ("cvk", "cvk_full"), ("cvv", "cvv_full")):
        sl = wd[key][cs:ce]
        out[name] = np.ascontiguousarray(sl.reshape(4, 128, 4).transpose(1, 0, 2)
                                         .reshape(128, 16))
    out["wbc"] = np.ascontiguousarray(
        wd["wbc_full"][:, h0:h0 + 2].reshape(8, 128, 2).transpose(1, 0, 2)
        .reshape(128, 16))
    fs = wd["firs_full"][h0:h0 + 2].reshape(512, 3)
    fl = wd["firl_full"][h0:h0 + 2].reshape(512, 31)
    out["firs"] = np.ascontiguousarray(fs.reshape(4, 128, 3).transpose(1, 0, 2)
                                       .reshape(128, 12))
    out["firl"] = np.ascontiguousarray(fl.reshape(4, 128, 31).transpose(1, 0, 2)
                                       .reshape(128, 124))
    return out



def _build_standalone(which):
    nc = bass.Bass("TRN2", target_bir_lowering=False)
    x_in = nc.dram_tensor("x", [128, NT * D], F32, kind="ExternalInput").ap()
    y_out = nc.dram_tensor("y", [128, NT * D], F32, kind="ExternalOutput").ap()
    shapes = {
        "ffn0": [("wfc", [8, 32, 128, 8, 128]), ("wpr", [8, 2, 32, 128, 512]),
                 ("bfc", [128, 256]), ("bpr", [8, 1024]), ("gwT", [1024, 8])],
        "attn0": [("wqT", [1024, 1024]), ("wkT", [1024, 1024]),
                  ("wvT", [1024, 1024]), ("woT", [1024, 1024]), ("bo", [1, 1024])],
        "ffn1": [("w1r", [64, 128, 8, 128]), ("w2r", [2, 64, 128, 512]),
                 ("ew", [128, 8])],
        "attn1": [("wqc", [1024, 512]), ("wkc", [1024, 512]), ("wvc", [1024, 512]),
                  ("cvq", [128, 16]), ("cvk", [128, 16]), ("cvv", [128, 16]),
                  ("wbc", [128, 16]), ("firs", [128, 12]), ("firl", [128, 124]),
                  ("wgl", [128, 128]), ("bgl", [1, 16]), ("ltemp", [1, 4]),
                  ("rmsw", [128, 2]), ("wo_n", [1024, 1024]), ("masks", [128, 96])],
    }
    dr = {n: nc.dram_tensor(n, shp, F32, kind="ExternalInput").ap()
          for n, shp in shapes[which]}
    dr["identw"] = nc.dram_tensor("identw", [128, 128], BF16,
                                  kind="ExternalInput").ap()
    if which == "attn0":
        dr["kv_cc_in"] = nc.dram_tensor("kv_cc_in", [128, 32 * 130], F32).ap()
        dr["kv_cc_out"] = nc.dram_tensor("kv_cc_out", [128, 32 * 130], F32).ap()
    if which == "attn1":
        dr["a1_in"] = nc.dram_tensor("a1_in", [8, 128, 8, 512], BF16).ap()
        dr["a1_out"] = nc.dram_tensor("a1_out", [8, 128, 8, 512], BF16).ap()
        dr["a2_in"] = nc.dram_tensor("a2_in", [8, 4, 128, 4, 512], BF16).ap()
        dr["a2_out"] = nc.dram_tensor("a2_out", [8, 4, 128, 4, 512], BF16).ap()
    alloc = {"ffn0": alloc_ffn0, "attn0": alloc_attn0,
             "ffn1": alloc_ffn1, "attn1": alloc_attn1}[which]
    with contextlib.ExitStack() as stack:
        P = Pool(nc, stack)
        C = alloc_consts(nc, stack)
        PS = alloc_psums(nc, stack)
        T = alloc(nc, stack, P)
        S = Sems(nc, stack, FUSED_SEMS)
        with nc.Block() as block:
            emit_const_init(nc, block, S, C, dr)
            if which == "ffn0":
                emit_ffn0(nc, block, S, T, C, PS, dr, x_in, y_out)
            elif which == "attn0":
                emit_attn0(nc, block, S, T, C, PS, dr, x_in, y_out)
            elif which == "ffn1":
                emit_ffn1(nc, block, S, T, C, PS, dr, x_in, y_out, copy_rest=True)
            else:
                emit_attn1(nc, block, S, T, C, PS, dr, x_in, y_out)
    return nc


def _shard_x(x, c):
    s = x[:, 512 * c:512 * (c + 1), :].reshape(T_LOC, D)
    return np.ascontiguousarray(s.reshape(NT, 128, D).transpose(1, 0, 2)
                                .reshape(128, NT * D))


def _run_spmd(nc, in_maps):
    res = run_bass_kernel_spmd(nc, in_maps, core_ids=list(range(N_CORES)))
    out = np.empty((4, 4096, D), np.float32)
    for c in range(N_CORES):
        y = np.asarray(res.results[c]["y"]).reshape(128, NT, D) \
            .transpose(1, 0, 2).reshape(T_LOC, D)
        out[:, 512 * c:512 * (c + 1), :] = y.reshape(4, 512, D)
    return out


def run_ffn0(x, wd):
    nc = _build_standalone("ffn0")
    return _run_spmd(nc, [{"x": _shard_x(x, c), **wd} for c in range(N_CORES)])


def run_attn0(x, wd):
    nc = _build_standalone("attn0")
    return _run_spmd(nc, [{"x": _shard_x(x, c), **wd} for c in range(N_CORES)])


def run_ffn1(x, wd, ew):
    nc = _build_standalone("ffn1")
    return _run_spmd(nc, [{"x": _shard_x(x, c), "ew": host_ffn1_ew(ew, c), **wd}
                          for c in range(N_CORES)])


def run_attn1(x, wd):
    nc = _build_standalone("attn1")
    return _run_spmd(nc, [{"x": _shard_x(x, c), **host_attn1_core_inputs(wd, c)}
                          for c in range(N_CORES)])



def build_fused():
    nc = bass.Bass("TRN2", target_bir_lowering=False)
    dr = {}
    x_in = nc.dram_tensor("x", [128, NT * D], F32, kind="ExternalInput").ap()
    y_out = nc.dram_tensor("y", [128, NT * D], F32, kind="ExternalOutput").ap()
    xbuf = nc.dram_tensor("xbuf", [128, NT * D], F32).ap()

    for n, shp in [("wfc", [8, 32, 128, 8, 128]), ("wpr", [8, 2, 32, 128, 512]),
                   ("bfc", [128, 256]), ("bpr", [8, 1024]), ("gwT", [1024, 8]),
                   ("wqT", [1024, 1024]), ("wkT", [1024, 1024]),
                   ("wvT", [1024, 1024]), ("woT", [1024, 1024]), ("bo", [1, 1024]),
                   ("w1r", [64, 128, 8, 128]), ("w2r", [2, 64, 128, 512]),
                   ("ew", [128, 8]),
                   ("wqc", [1024, 512]), ("wkc", [1024, 512]), ("wvc", [1024, 512]),
                   ("cvq", [128, 16]), ("cvk", [128, 16]), ("cvv", [128, 16]),
                   ("wbc", [128, 16]), ("firs", [128, 12]), ("firl", [128, 124]),
                   ("wgl", [128, 128]), ("bgl", [1, 16]), ("ltemp", [1, 4]),
                   ("rmsw", [128, 2]), ("wo_n", [1024, 1024]),
                   ("masks", [128, 96])]:
        dr[n] = nc.dram_tensor(n, shp, F32, kind="ExternalInput").ap()
    dr["identw"] = nc.dram_tensor("identw", [128, 128], BF16,
                                  kind="ExternalInput").ap()
    dr["kv_cc_in"] = nc.dram_tensor("kv_cc_in", [128, 32 * 130], F32).ap()
    dr["kv_cc_out"] = nc.dram_tensor("kv_cc_out", [128, 32 * 130], F32).ap()
    dr["a1_in"] = nc.dram_tensor("a1_in", [8, 128, 8, 512], BF16).ap()
    dr["a1_out"] = nc.dram_tensor("a1_out", [8, 128, 8, 512], BF16).ap()
    dr["a2_in"] = nc.dram_tensor("a2_in", [8, 4, 128, 4, 512], BF16).ap()
    dr["a2_out"] = nc.dram_tensor("a2_out", [8, 4, 128, 4, 512], BF16).ap()

    with contextlib.ExitStack() as stack:
        P = Pool(nc, stack, fused=True, f32_cols=10848, bf16_cols=72784)
        C = alloc_consts(nc, stack)
        PS = alloc_psums(nc, stack)
        T_f0 = alloc_ffn0(nc, stack, P)
        P.reset()
        T_a0 = alloc_attn0(nc, stack, P)
        P.reset()
        T_f1 = alloc_ffn1(nc, stack, P)
        P.reset()
        T_a1 = alloc_attn1(nc, stack, P)
        P.reset()
        assert P.hw32 <= 10848 and P.hw16 <= 72784, (P.hw32, P.hw16)
        S = Sems(nc, stack, FUSED_SEMS)
        with nc.Block() as block:
            emit_const_init(nc, block, S, C, dr)
            emit_ffn0(nc, block, S, T_f0, C, PS, dr, x_in, xbuf)
            emit_attn0(nc, block, S, T_a0, C, PS, dr, xbuf, xbuf)
            emit_ffn0(nc, block, S, T_f0, C, PS, dr, xbuf, xbuf)
            emit_ffn1(nc, block, S, T_f1, C, PS, dr, xbuf, xbuf, copy_rest=False)
            emit_attn1(nc, block, S, T_a1, C, PS, dr, xbuf, xbuf)
            emit_ffn1(nc, block, S, T_f1, C, PS, dr, xbuf, y_out, copy_rest=True)

            @block.sync
            def _(sync):
                sync.wait_ge(S.h["yst"], S.c["yst"])
    return nc


_FUSED_CACHE = {}


def kernel(**inputs):
    _ = inputs.get("L1_ffn_scores"), inputs.get("L1_ffn_top_experts")
    f32 = np.float32
    x = np.ascontiguousarray(np.asarray(inputs["L0_attn_x"], f32))

    wd_ffn0 = host_ffn0_weights(
        np.asarray(inputs["L0_ffn_w_fc"], f32), np.asarray(inputs["L0_ffn_b_fc"], f32),
        np.asarray(inputs["L0_ffn_w_proj"], f32), np.asarray(inputs["L0_ffn_b_proj"], f32),
        np.asarray(inputs["L0_ffn_gate_w"], f32))
    wd_attn0 = host_attn0_weights(
        np.asarray(inputs["L0_attn_wq"], f32), np.asarray(inputs["L0_attn_wk"], f32),
        np.asarray(inputs["L0_attn_wv"], f32), np.asarray(inputs["L0_attn_wo"], f32),
        np.asarray(inputs["L0_attn_bo"], f32))
    wd_ffn1 = host_ffn1_weights(
        np.asarray(inputs["L1_ffn_w1"], f32), np.asarray(inputs["L1_ffn_w2"], f32))
    ew = np.asarray(inputs["L1_ffn_expert_weights"], f32)
    wd_attn1 = host_attn1_weights(
        np.asarray(inputs["L1_attn_Wq"], f32), np.asarray(inputs["L1_attn_Wk"], f32),
        np.asarray(inputs["L1_attn_Wv"], f32), np.asarray(inputs["L1_attn_conv_q"], f32),
        np.asarray(inputs["L1_attn_conv_k"], f32), np.asarray(inputs["L1_attn_conv_v"], f32),
        np.asarray(inputs["L1_attn_Wb"], f32), np.asarray(inputs["L1_attn_fir_short"], f32),
        np.asarray(inputs["L1_attn_fir_long"], f32), np.asarray(inputs["L1_attn_Wg"], f32),
        np.asarray(inputs["L1_attn_bg"], f32), np.asarray(inputs["L1_attn_Wl"], f32),
        np.asarray(inputs["L1_attn_bl"], f32), np.asarray(inputs["L1_attn_log_temp"], f32),
        np.asarray(inputs["L1_attn_rms_w"], f32), np.asarray(inputs["L1_attn_Wo"], f32))

    nc = build_fused()
    in_maps = []
    shared = {**{k: v for k, v in wd_ffn0.items()},
              **{k: v for k, v in wd_attn0.items() if k != "identw"},
              **{k: v for k, v in wd_ffn1.items() if k != "identw"}}
    for c in range(N_CORES):
        s = x[:, 512 * c:512 * (c + 1), :].reshape(T_LOC, D)
        xs = np.ascontiguousarray(s.reshape(NT, 128, D).transpose(1, 0, 2)
                                  .reshape(128, NT * D))
        m = {"x": xs, "ew": host_ffn1_ew(ew, c), **shared,
             **host_attn1_core_inputs(wd_attn1, c)}
        in_maps.append(m)
    res = run_bass_kernel_spmd(nc, in_maps, core_ids=list(range(N_CORES)))
    out = np.empty((4, 4096, D), np.float32)
    for c in range(N_CORES):
        y = np.asarray(res.results[c]["y"]).reshape(128, NT, D).transpose(1, 0, 2) \
            .reshape(T_LOC, D)
        out[:, 512 * c:512 * (c + 1), :] = y.reshape(4, 512, D)
    return out


# version 03
# speedup vs baseline: 1.0011x; 1.0011x over previous
import contextlib
import numpy as np
import ml_dtypes
import concourse.bass as bass
import concourse.mybir as mybir
from concourse.bass_utils import run_bass_kernel_spmd
from concourse.aot_env import allow_ici_all_to_all

F32 = mybir.dt.float32
BF16 = mybir.dt.bfloat16
AF = mybir.ActivationFunctionType
ALU = mybir.AluOpType
AX = mybir.AxisListType
BBF16 = ml_dtypes.bfloat16

N_CORES = 8
D = 1024
T_LOC = 2048
NT = 16
NDC = 8
LN_EPS = 1e-5


class NS:
    pass


def apof(t):
    return t.ap() if callable(getattr(t, "ap", None)) else t


class Sems:
    def __init__(self, nc, stack, names):
        self.h = {}
        self.c = {}
        self.b = {}
        for n in names:
            self.h[n] = stack.enter_context(nc.semaphore(n))
            self.c[n] = 0
            self.b[n] = 0

    def inc(self, instr, name, v=1):
        instr.then_inc(self.h[name], v)
        self.c[name] += v
        return self.c[name]

    def wait(self, eng, name, val=None):
        v = self.c[name] if val is None else val
        if v > 0:
            eng.wait_ge(self.h[name], v)

    def step(self, eng, instr, name):
        self.inc(instr, name)
        eng.wait_ge(self.h[name], self.c[name])

    def rebase(self):
        self.b = dict(self.c)

    def waitb(self, eng, name, k):
        if k > 0:
            eng.wait_ge(self.h[name], self.b[name] + k)

    def hb(self, name, k):
        return self.b[name] + k


FUSED_SEMS = [
    "xld", "wld", "wldg", "lnSq", "lnVar", "lnSqrt", "lnRstd", "lnTok", "tr",
    "xnTe", "xbL", "yst",
    "gmm", "glogE", "gateF", "wfcL", "wprL", "wprUse", "mm1d", "geluE",
    "mm2d", "evacE",
    "qmm", "qe1", "qe2", "qfE", "kvmm", "ke1", "ke2", "kvE", "kvmmB", "kvE2",
    "cc", "ccb", "kvbf", "amm", "adiv", "atr", "aTe", "omm", "oev",
    "w1L", "w2L", "w2U", "wsF",
    "shA", "ccA", "blkL", "pmm", "pev", "cvD", "silu", "l2q", "l2k", "bsig",
    "kbE", "scanD", "fir", "payT", "payD", "ccB", "cl", "clg", "clv", "cst",
    "trP", "trE", "sAm", "sAe", "sZm", "sZe", "sUm", "sUe", "sSm",
    "mixv", "otr", "resv", "st", "vch", "sch",
]


class Pool:

    def __init__(self, nc, stack, fused=False, f32_cols=0, bf16_cols=0):
        self.nc, self.stack, self.fused = nc, stack, fused
        if fused:
            self.m32 = stack.enter_context(
                nc.sbuf_tensor("pool_f32", [128, f32_cols], F32))
            self.m16 = stack.enter_context(
                nc.sbuf_tensor("pool_b16", [128, bf16_cols], BF16))
            self.o32 = self.o16 = 0
            self.hw32 = self.hw16 = 0

    def reset(self):
        if self.fused:
            self.hw32 = max(self.hw32, self.o32)
            self.hw16 = max(self.hw16, self.o16)
            self.o32 = self.o16 = 0

    def f32(self, name, cols):
        if not self.fused:
            return self.stack.enter_context(
                self.nc.sbuf_tensor(name, [128, cols], F32))
        v = self.m32[:, self.o32:self.o32 + cols]
        self.o32 += cols
        return v

    def bf16(self, name, cols):
        if not self.fused:
            return self.stack.enter_context(
                self.nc.sbuf_tensor(name, [128, cols], BF16))
        v = self.m16[:, self.o16:self.o16 + cols]
        self.o16 += cols
        return v


def alloc_psums(nc, stack):
    PS = NS()
    PS.ptA = stack.enter_context(nc.psum_tensor("ptA", [128, 2048], BF16))
    PS.pf = [stack.enter_context(nc.psum_tensor(f"pf{i}", [128, 512], F32))
             for i in range(6)]
    return PS


def alloc_consts(nc, stack):
    C = NS()
    C.ident = stack.enter_context(nc.sbuf_tensor("c_ident", [128, 128], BF16))
    C.ones_row = stack.enter_context(nc.sbuf_tensor("c_ones_row", [1, 128], BF16))
    C.ones_col = stack.enter_context(nc.sbuf_tensor("c_ones_col", [128, 1], BF16))
    C.masks = stack.enter_context(nc.sbuf_tensor("c_masks", [128, 96], F32))
    C.eps = stack.enter_context(nc.sbuf_tensor("c_eps", [128, 2], F32))
    return C


def emit_const_init(nc, block, S, C, dr):

    @block.sync
    def _(sync):
        S.inc(sync.dma_start(out=C.ident[:], in_=dr["identw"]), "wld", 16)
        if "masks" in dr:
            S.inc(sync.dma_start(out=C.masks[:], in_=dr["masks"]), "wld", 16)

    @block.vector
    def _(vector):
        vector.memset(C.ones_row[:], 1.0)
        vector.memset(C.ones_col[:], 1.0)
        vector.memset(C.eps[:, 0:1], 1e-6)
        i = vector.memset(C.eps[:, 1:2], 1e-5)
        S.inc(i, "wld", 1)


def emit_phase_entry(block, S):
    v = S.c["yst"]
    if v == 0:
        return

    @block.sync
    def _(sync):
        sync.wait_ge(S.h["yst"], v)

    @block.gpsimd
    def _(gpsimd):
        gpsimd.wait_ge(S.h["yst"], v)

    @block.vector
    def _(vector):
        vector.wait_ge(S.h["yst"], v)

    @block.scalar
    def _(scalar):
        scalar.wait_ge(S.h["yst"], v)

    @block.tensor
    def _(tensor):
        tensor.wait_ge(S.h["yst"], v)



def emit_ln_stream(nc, block, S, T, C, PS, x_src, tiles, xnT, xnT_stride):
    tiles = list(tiles)
    NTL = len(tiles)
    st = T.ln_stats
    sums, sumsq = st[:, 0:16], st[:, 16:32]
    mean, rstd, tmp = st[:, 32:48], st[:, 48:64], st[:, 64:80]
    xb, ln_xt, pt = T.xb, T.ln_xt, PS.ptA

    for j, t in enumerate(tiles):
        @block.sync
        def _(sync, j=j, t=t):
            if j == 0:
                S.wait(sync, "yst")
            if j >= 2:
                S.waitb(sync, "lnSq", j - 1)
                S.waitb(sync, "lnVar", j - 1)
            if j > 0:
                sync.wait_ge(S.h["xld"], S.hb("xld", 16 * j))
            S.inc(sync.dma_start(out=xb[:, (j % 2) * D:(j % 2 + 1) * D],
                                 in_=x_src[:, t * D:(t + 1) * D]), "xld", 16)

        @block.vector
        def _(vector, j=j):
            vector.wait_ge(S.h["xld"], S.hb("xld", 16 * (j + 1)))
            i = vector.tensor_reduce(sums[:, j:j + 1],
                                     xb[:, (j % 2) * D:(j % 2 + 1) * D],
                                     AX.X, ALU.add)
            S.inc(i, "lnVar")

        @block.scalar
        def _(scalar, j=j):
            scalar.wait_ge(S.h["lnVar"], S.hb("lnVar", j + 1))
            i = scalar.activation(ln_xt[:, (j % 2) * D:(j % 2 + 1) * D],
                                  xb[:, (j % 2) * D:(j % 2 + 1) * D],
                                  AF.Square, accum_out=sumsq[:, j:j + 1])
            S.inc(i, "lnSq")

    @block.vector
    def _(vector):
        S.waitb(vector, "lnSq", NTL)
        S.step(vector, vector.tensor_scalar(mean[:, 0:NTL], sums[:, 0:NTL],
                                            1.0 / D, None, ALU.mult), "vch")
        S.step(vector, vector.tensor_scalar(tmp[:, 0:NTL], sumsq[:, 0:NTL],
                                            1.0 / D, None, ALU.mult), "vch")
        S.step(vector, vector.tensor_mul(sumsq[:, 0:NTL], mean[:, 0:NTL],
                                         mean[:, 0:NTL]), "vch")
        S.step(vector, vector.tensor_sub(tmp[:, 0:NTL], tmp[:, 0:NTL],
                                         sumsq[:, 0:NTL]), "vch")
        i = vector.tensor_scalar(tmp[:, 0:NTL], tmp[:, 0:NTL], LN_EPS, None, ALU.add)
        S.inc(i, "lnVar")

    @block.scalar
    def _(scalar):
        S.waitb(scalar, "lnVar", NTL + 1)
        S.inc(scalar.sqrt(tmp[:, 0:NTL], tmp[:, 0:NTL]), "lnSqrt")

    @block.vector
    def _(vector):
        S.waitb(vector, "lnSqrt", 1)
        S.step(vector, vector.reciprocal(rstd[:, 0:NTL], tmp[:, 0:NTL]), "vch")
        S.inc(nc.vector.engine_nop(), "lnRstd")

    for j, t in enumerate(tiles):
        @block.sync
        def _(sync, j=j, t=t):
            if j < 2:
                S.waitb(sync, "lnSq", NTL)
                S.waitb(sync, "lnVar", NTL)
            else:
                S.waitb(sync, "lnTok", j - 1)
            sync.wait_ge(S.h["xld"], S.hb("xld", 16 * (NTL + j)))
            S.inc(sync.dma_start(out=xb[:, (j % 2) * D:(j % 2 + 1) * D],
                                 in_=x_src[:, t * D:(t + 1) * D]), "xld", 16)

        @block.vector
        def _(vector, j=j):
            if j == 0:
                S.waitb(vector, "lnRstd", 1)
            vector.wait_ge(S.h["xld"], S.hb("xld", 16 * (NTL + j + 1)))
            i = vector.tensor_scalar(ln_xt[:, (j % 2) * D:(j % 2 + 1) * D],
                                     xb[:, (j % 2) * D:(j % 2 + 1) * D],
                                     mean[:, j:j + 1], rstd[:, j:j + 1],
                                     ALU.subtract, ALU.mult)
            S.inc(i, "lnTok")

        @block.tensor
        def _(tensor, j=j):
            if j == 0:
                S.wait(tensor, "wld")
            S.waitb(tensor, "lnTok", j + 1)
            if j >= 2:
                S.waitb(tensor, "xnTe", j - 1)
            for dc in range(NDC):
                i = tensor.transpose(
                    pt[:, (j % 2) * 1024 + dc * 128:(j % 2) * 1024 + (dc + 1) * 128],
                    ln_xt[:, (j % 2) * D + dc * 128:(j % 2) * D + (dc + 1) * 128],
                    C.ident[:])
            S.inc(i, "tr")

        @block.vector
        def _(vector, j=j):
            S.waitb(vector, "tr", j + 1)
            for dc in range(NDC):
                i = vector.tensor_copy(
                    xnT[:, dc * xnT_stride + j * 128: dc * xnT_stride + j * 128 + 128],
                    pt[:, (j % 2) * 1024 + dc * 128:(j % 2) * 1024 + (dc + 1) * 128])
            S.inc(i, "xnTe")



def alloc_ffn0(nc, stack, P):
    T = NS()
    T.xb = P.f32("f0_xb", 2 * D)
    T.x_blk = P.f32("f0_x_blk", 2 * 4096)
    T.ln_stats = P.f32("f0_ln_stats", 80)
    T.gates = P.f32("f0_gates", NT * 8)
    T.glog = P.f32("f0_glog", NT * 8)
    T.gscr = P.f32("f0_gscr", 8)
    T.gstat = P.f32("f0_gstat", 8)
    T.bfc_sb = P.f32("f0_bfc", 8 * 32)
    T.xnT = P.bf16("f0_xnT", NDC * 2048)
    T.HT = P.bf16("f0_HT", 32 * 512)
    T.wfc_sb = P.bf16("f0_wfc", 4 * 8 * 128)
    T.wpr_sb = P.bf16("f0_wpr", 8 * 512)
    T.gwT_sb = P.bf16("f0_gwT", 8 * 8)
    T.ln_xt = P.bf16("f0_ln_xt", 2 * D)
    T.bpr_sb = stack.enter_context(nc.sbuf_tensor("f0_bpr", [1, 8 * 1024], BF16))
    return T


def emit_ffn0(nc, block, S, T, C, PS, dr, x_src, y_dst, inst=0):
    NSB, NFC = 4, 32
    WFC_RING, WPR_RING = 4, 8
    pm1 = [PS.pf[0], PS.pf[1]]
    pm2 = [PS.pf[2], PS.pf[3], PS.pf[4], PS.pf[5]]
    pg = PS.pf[2][:, 0:8]
    wfc, wpr, bfc, bpr, gwT = dr["wfc"], dr["wpr"], dr["bfc"], dr["bpr"], dr["gwT"]
    xnT, HT = T.xnT, T.HT
    wfc_sb, wpr_sb, gwT_sb, bpr_sb, bfc_sb = (T.wfc_sb, T.wpr_sb, T.gwT_sb,
                                              T.bpr_sb, T.bfc_sb)
    gates, glog, gscr, gstat = T.gates, T.glog, T.gscr, T.gstat
    x_blk = T.x_blk

    emit_phase_entry(block, S)
    S.rebase()

    @block.sync
    def _(sync):
        S.inc(sync.dma_start(out=bfc_sb[:], in_=bfc), "wld", 16)

    @block.gpsimd
    def _(gpsimd):
        S.inc(gpsimd.dma_start(out=bpr_sb[:],
                               in_=bpr.rearrange("e d -> (e d)")[None, :]),
              "wldg", 16)
        S.inc(gpsimd.dma_start(
            out=apof(gwT_sb).rearrange("p (dc e) -> p dc e", dc=8),
            in_=gwT.rearrange("(dc p) e -> p dc e", p=128)), "wldg", 16)

    emit_ln_stream(nc, block, S, T, C, PS, x_src, range(NT), xnT, 2048)

    @block.tensor
    def _(tensor):
        S.wait(tensor, "wldg")

    for t in range(NT):
        @block.tensor
        def _(tensor, t=t):
            S.waitb(tensor, "xnTe", t + 1)
            if t >= 1:
                S.waitb(tensor, "glogE", t)
            for dc in range(NDC):
                i = tensor.matmul(pg,
                                  xnT[:, dc * 2048 + t * 128: dc * 2048 + t * 128 + 128],
                                  gwT_sb[:, dc * 8:(dc + 1) * 8],
                                  start=(dc == 0), stop=(dc == NDC - 1))
            S.inc(i, "gmm")

        @block.vector
        def _(vector, t=t):
            S.waitb(vector, "gmm", t + 1)
            S.inc(vector.tensor_copy(glog[:, t * 8:t * 8 + 8], pg), "glogE")

    st5 = T.ln_stats
    m1, m2, nm1, d2, rdh = (st5[:, 64:65], st5[:, 65:66], st5[:, 66:67],
                            st5[:, 67:68], st5[:, 68:69])
    for t in range(NT):
        @block.vector
        def _(vector, t=t):
            S.waitb(vector, "glogE", NT)
            l = glog[:, t * 8:t * 8 + 8]
            S.step(vector, vector.tensor_reduce(m1, l, AX.X, ALU.max), "vch")
            S.step(vector, vector.tensor_single_scalar(gscr[:], l, m1, ALU.is_ge), "vch")
            S.step(vector, vector.tensor_scalar(gscr[:], gscr[:], -1e30, None, ALU.mult), "vch")
            S.step(vector, vector.tensor_add(gscr[:], gscr[:], l), "vch")
            S.step(vector, vector.tensor_reduce(m2, gscr[:], AX.X, ALU.max), "vch")
            S.step(vector, vector.tensor_scalar(nm1, m1, -1.0, None, ALU.mult), "vch")
            i = vector.tensor_single_scalar(gstat[:], l, m2, ALU.is_ge)
            S.inc(i, "gateF")

        @block.scalar
        def _(scalar, t=t):
            S.waitb(scalar, "gateF", 3 * t + 1)
            scalar.activation(gates[:, t * 8:t * 8 + 8], glog[:, t * 8:t * 8 + 8],
                              AF.Exp, bias=nm1)
            i = scalar.activation(d2, m2, AF.Exp, bias=nm1)
            S.inc(i, "gateF")

        @block.vector
        def _(vector, t=t):
            S.waitb(vector, "gateF", 3 * t + 2)
            S.step(vector, vector.tensor_scalar(rdh, d2, 1.0, None, ALU.add), "vch")
            S.step(vector, vector.reciprocal(rdh, rdh), "vch")
            S.step(vector, vector.tensor_scalar(rdh, rdh, 0.5, None, ALU.mult), "vch")
            i = vector.scalar_tensor_tensor(gates[:, t * 8:t * 8 + 8],
                                            gstat[:], rdh,
                                            gates[:, t * 8:t * 8 + 8],
                                            ALU.mult, ALU.mult)
            S.inc(i, "gateF")

    @block.gpsimd
    def _(gpsimd):
        for sbe in range(NSB * 8):
            e = sbe % 8
            for fc in range(NFC):
                g = sbe * NFC + fc
                S.waitb(gpsimd, "geluE", g - (WFC_RING - 1))
                if g > 0:
                    gpsimd.wait_ge(S.h["wfcL"], S.hb("wfcL", 16 * g))
                slot = g % WFC_RING
                S.inc(gpsimd.dma_start(
                    out=wfc_sb[:, slot * 1024:(slot + 1) * 1024],
                    in_=wfc[e, fc].rearrange("p dc c -> p (dc c)")), "wfcL", 16)
            for dh in range(2):
                for fc in range(NFC):
                    p = sbe * 64 + dh * NFC + fc
                    S.waitb(gpsimd, "wprUse", p - (WPR_RING - 1))
                    if p > 0:
                        gpsimd.wait_ge(S.h["wprL"], S.hb("wprL", 16 * p))
                    slot = p % WPR_RING
                    S.inc(gpsimd.dma_start(
                        out=wpr_sb[:, slot * 512:(slot + 1) * 512],
                        in_=wpr[e, dh, fc]), "wprL", 16)

    for sb in range(NSB):
        par = sb % 2

        @block.sync
        def _(sync, sb=sb, par=par):
            if sb >= 2:
                S.waitb(sync, "yst", 16 * (sb - 1))
            if sb > 0:
                sync.wait_ge(S.h["xbL"], S.hb("xbL", 16 * sb))
            S.inc(sync.dma_start(out=x_blk[:, par * 4096:(par + 1) * 4096],
                                 in_=x_src[:, sb * 4096:(sb + 1) * 4096]), "xbL", 16)

        for e in range(8):
            sbe = sb * 8 + e

            @block.tensor
            def _(tensor, sb=sb, e=e, sbe=sbe):
                if e == 0:
                    S.waitb(tensor, "xnTe", 4 * sb + 4)
                for fc in range(NFC):
                    g = sbe * NFC + fc
                    slot = g % WFC_RING
                    tensor.wait_ge(S.h["wfcL"], S.hb("wfcL", 16 * (g + 1)))
                    if g >= 2:
                        tensor.wait_ge(S.h["geluE"], S.hb("geluE", g - 1))
                    for dc in range(NDC):
                        i = tensor.matmul(
                            pm1[g % 2][:],
                            wfc_sb[:, slot * 1024 + dc * 128: slot * 1024 + (dc + 1) * 128],
                            xnT[:, dc * 2048 + sb * 512: dc * 2048 + (sb + 1) * 512],
                            start=(dc == 0), stop=(dc == NDC - 1))
                    S.inc(i, "mm1d")

            @block.scalar
            def _(scalar, sb=sb, e=e, sbe=sbe):
                if sbe >= 1:
                    S.waitb(scalar, "mm2d", 8 * sbe)
                for fc in range(NFC):
                    g = sbe * NFC + fc
                    scalar.wait_ge(S.h["mm1d"], S.hb("mm1d", g + 1))
                    i = scalar.activation(HT[:, fc * 512:(fc + 1) * 512],
                                          pm1[g % 2][:], AF.Gelu_apprx_tanh,
                                          bias=bfc_sb[:, e * 32 + fc: e * 32 + fc + 1])
                    S.inc(i, "geluE")

            @block.tensor
            def _(tensor, sb=sb, e=e, sbe=sbe):
                tensor.wait_ge(S.h["geluE"], S.hb("geluE", 32 * (sbe + 1)))
                if sbe == 0:
                    tensor.wait_ge(S.h["glogE"], S.hb("glogE", NT))
                for dh in range(2):
                    m_g = sbe * 2 + dh
                    if m_g >= 1:
                        tensor.wait_ge(S.h["evacE"], S.hb("evacE", 4 * m_g))
                    for fc in range(NFC):
                        p = sbe * 64 + dh * NFC + fc
                        slot = p % WPR_RING
                        tensor.wait_ge(S.h["wprL"], S.hb("wprL", 16 * (p + 1)))
                        for sc in range(4):
                            i = tensor.matmul(
                                pm2[sc][:],
                                HT[:, fc * 512 + sc * 128: fc * 512 + (sc + 1) * 128],
                                wpr_sb[:, slot * 512:(slot + 1) * 512],
                                start=(fc == 0), stop=False)
                        S.inc(i, "wprUse")
                    for sc in range(4):
                        i = tensor.matmul(pm2[sc][:], C.ones_row[:],
                                          bpr_sb[0:1, e * 1024 + dh * 512: e * 1024 + (dh + 1) * 512],
                                          start=False, stop=True)
                        S.inc(i, "mm2d")

            @block.vector
            def _(vector, sb=sb, e=e, sbe=sbe, par=par):
                if sbe == 0:
                    S.waitb(vector, "gateF", 3 * NT)
                if e == 0:
                    vector.wait_ge(S.h["xbL"], S.hb("xbL", 16 * (sb + 1)))
                for dh in range(2):
                    m_g = sbe * 2 + dh
                    for sc in range(4):
                        vector.wait_ge(S.h["mm2d"], S.hb("mm2d", 4 * m_g + sc + 1))
                        t = sb * 4 + sc
                        i = vector.scalar_tensor_tensor(
                            x_blk[:, par * 4096 + sc * D + dh * 512:
                                  par * 4096 + sc * D + (dh + 1) * 512],
                            pm2[sc][:], gates[:, t * 8 + e: t * 8 + e + 1],
                            x_blk[:, par * 4096 + sc * D + dh * 512:
                                  par * 4096 + sc * D + (dh + 1) * 512],
                            ALU.mult, ALU.add)
                        S.inc(i, "evacE")

        @block.sync
        def _(sync, sb=sb, par=par):
            S.waitb(sync, "evacE", 64 * (sb + 1))
            S.inc(sync.dma_start(out=y_dst[:, sb * 4096:(sb + 1) * 4096],
                                 in_=x_blk[:, par * 4096:(par + 1) * 4096]), "yst", 16)

    @block.sync
    def _(sync):
        sync.wait_ge(S.h["yst"], S.c["yst"])


def host_ffn0_weights(w_fc, b_fc, w_proj, b_proj, gate_w):
    wfc = np.ascontiguousarray(
        w_fc.transpose(0, 2, 1).reshape(8, 8, 128, 32, 128).transpose(0, 3, 2, 1, 4))
    wpr = np.ascontiguousarray(
        w_proj.transpose(0, 2, 1).reshape(8, 32, 128, 2, 512).transpose(0, 3, 1, 2, 4))
    bfc = np.ascontiguousarray(b_fc.reshape(8, 32, 128).transpose(2, 0, 1)
                               .reshape(128, 256))
    return {"wfc": wfc, "wpr": wpr, "bfc": bfc,
            "bpr": np.ascontiguousarray(b_proj),
            "gwT": np.ascontiguousarray(gate_w.T),
            "identw": np.eye(128, dtype=BBF16)}



def alloc_attn0(nc, stack, P):
    T = NS()
    T.xb = P.f32("a0_xb", 2 * D)
    T.ln_stats = P.f32("a0_ln_stats", 80)
    T.kv_sb = P.f32("a0_kv_sb", 32 * 130)
    T.dcol = P.f32("a0_dcol", 4)
    T.xnT = P.bf16("a0_xnT", NDC * 2048)
    T.qf = P.bf16("a0_qf", NDC * 2048)
    T.kst = P.bf16("a0_kst", 1024)
    T.vst = P.bf16("a0_vst", 1024)
    T.wbufA = P.bf16("a0_wbufA", 8 * 1024)
    T.wbufB = P.bf16("a0_wbufB", 8 * 1024)
    T.kv_bf = P.bf16("a0_kv_bf", 32 * 130)
    T.astage = P.bf16("a0_astage", 1024)
    T.escr = P.f32("a0_escr", 4 * 512)
    T.ln_xt = P.bf16("a0_ln_xt", 2 * D)
    T.bo_sb = stack.enter_context(nc.sbuf_tensor("a0_bo", [1, 1024], BF16))
    return T


def emit_attn0(nc, block, S, T, C, PS, dr, x_src, y_dst):
    AEPS = 1e-6
    pp = [PS.pf[0], PS.pf[1]]
    pkv = PS.pf[2]
    pks = PS.pf[3][:, 0:8]
    pqk = [PS.pf[4][:, 0:130], PS.pf[5][:, 0:130]]
    kv_cc_in, kv_cc_out = dr["kv_cc_in"], dr["kv_cc_out"]
    xnT, qf, kst, vst = T.xnT, T.qf, T.kst, T.vst
    wbufA, wbufB, kv_sb, kv_bf = T.wbufA, T.wbufB, T.kv_sb, T.kv_bf
    astage, escr, dcol, bo_sb, xb = T.astage, T.escr, T.dcol, T.bo_sb, T.xb

    emit_phase_entry(block, S)

    @block.gpsimd
    def _(gpsimd):
        for dc in range(NDC):
            S.inc(gpsimd.dma_start(
                out=wbufA[:, dc * 1024:(dc + 1) * 1024],
                in_=dr["wqT"][dc * 128:(dc + 1) * 128, :]), "wldg", 16)
        S.inc(gpsimd.dma_start(out=bo_sb[:], in_=dr["bo"]), "wldg", 16)

    S.rebase()

    @block.vector
    def _(vector):
        S.wait(vector, "yst")
        i = vector.memset(kv_sb[:], 0.0)
        S.inc(i, "kvE2")

    emit_ln_stream(nc, block, S, T, C, PS, x_src, range(NT), xnT, 2048)

    nq = 0
    for oc in range(NDC):
        for sb4 in range(4):
            nq += 1

            @block.tensor
            def _(tensor, oc=oc, sb4=sb4, nq=nq):
                if nq == 1:
                    S.wait(tensor, "wldg")
                    S.waitb(tensor, "xnTe", NT)
                if nq > 2:
                    S.waitb(tensor, "qfE", nq - 2)
                for dc in range(NDC):
                    i = tensor.matmul(
                        pp[nq % 2][:],
                        wbufA[:, dc * 1024 + oc * 128: dc * 1024 + (oc + 1) * 128],
                        xnT[:, dc * 2048 + sb4 * 512: dc * 2048 + (sb4 + 1) * 512],
                        start=(dc == 0), stop=(dc == NDC - 1))
                S.inc(i, "qmm")

            @block.vector
            def _(vector, nq=nq):
                S.waitb(vector, "qmm", nq)
                i = vector.tensor_scalar_min(
                    escr[:, (nq % 2) * 512:(nq % 2 + 1) * 512], pp[nq % 2][:], 0.0)
                S.inc(i, "qe1")

            @block.scalar
            def _(scalar, nq=nq):
                S.waitb(scalar, "qe1", nq)
                i = scalar.activation(
                    escr[:, (2 + nq % 2) * 512:(3 + nq % 2) * 512],
                    escr[:, (nq % 2) * 512:(nq % 2 + 1) * 512], AF.Exp)
                S.inc(i, "qe2")

            @block.vector
            def _(vector, oc=oc, sb4=sb4, nq=nq):
                S.waitb(vector, "qe2", nq)
                i = vector.scalar_tensor_tensor(
                    qf[:, oc * 2048 + sb4 * 512: oc * 2048 + (sb4 + 1) * 512],
                    pp[nq % 2][:], 1.0,
                    escr[:, (2 + nq % 2) * 512:(3 + nq % 2) * 512],
                    ALU.add, ALU.max)
                S.inc(i, "qfE")

    @block.gpsimd
    def _(gpsimd):
        gpsimd.wait_ge(S.h["qfE"], S.hb("qfE", 32))
        for dc in range(NDC):
            S.inc(gpsimd.dma_start(
                out=wbufA[:, dc * 1024:(dc + 1) * 1024],
                in_=dr["wkT"][dc * 128:(dc + 1) * 128, :]), "wldg", 16)
        for dc in range(NDC):
            S.inc(gpsimd.dma_start(
                out=wbufB[:, dc * 1024:(dc + 1) * 1024],
                in_=dr["wvT"][dc * 128:(dc + 1) * 128, :]), "wldg", 16)

    nkv = 0
    nke = 0
    for t in range(NT):
        b = t // 4
        for proj in ("k", "v"):
            wb = wbufA if proj == "k" else wbufB
            for half in range(2):
                nkv += 1

                @block.tensor
                def _(tensor, t=t, wb=wb, half=half, nkv=nkv):
                    if nkv <= 2:
                        S.wait(tensor, "wldg")
                        S.waitb(tensor, "qfE", 32)
                    else:
                        S.waitb(tensor, "kvE", nkv - 2)
                    for dc in range(NDC):
                        i = tensor.matmul(
                            pp[nkv % 2][:],
                            xnT[:, dc * 2048 + t * 128: dc * 2048 + (t + 1) * 128],
                            wb[:, dc * 1024 + half * 512: dc * 1024 + (half + 1) * 512],
                            start=(dc == 0), stop=(dc == NDC - 1))
                    S.inc(i, "kvmm")

                if proj == "k":
                    nke += 1

                    @block.vector
                    def _(vector, nkv=nkv):
                        S.waitb(vector, "kvmm", nkv)
                        i = vector.tensor_scalar_min(
                            escr[:, (nkv % 2) * 512:(nkv % 2 + 1) * 512],
                            pp[nkv % 2][:], 0.0)
                        S.inc(i, "ke1")

                    @block.scalar
                    def _(scalar, nkv=nkv, nke=nke):
                        S.waitb(scalar, "ke1", nke)
                        i = scalar.activation(
                            escr[:, (2 + nkv % 2) * 512:(3 + nkv % 2) * 512],
                            escr[:, (nkv % 2) * 512:(nkv % 2 + 1) * 512], AF.Exp)
                        S.inc(i, "ke2")

                    @block.vector
                    def _(vector, nkv=nkv, nke=nke, half=half):
                        S.waitb(vector, "ke2", nke)
                        i = vector.scalar_tensor_tensor(
                            kst[:, half * 512:(half + 1) * 512],
                            pp[nkv % 2][:], 1.0,
                            escr[:, (2 + nkv % 2) * 512:(3 + nkv % 2) * 512],
                            ALU.add, ALU.max)
                        S.inc(i, "kvE")
                else:
                    @block.vector
                    def _(vector, nkv=nkv, half=half):
                        S.waitb(vector, "kvmm", nkv)
                        i = vector.tensor_copy(
                            vst[:, half * 512:(half + 1) * 512], pp[nkv % 2][:])
                        S.inc(i, "kvE")

        @block.tensor
        def _(tensor, t=t, b=b):
            S.wait(tensor, "kvE")
            first = (t % 4 == 0)
            last = (t % 4 == 3)
            if first and t > 0:
                S.waitb(tensor, "kvE2", t // 4 + 1)
            for h in range(16):
                pbase = 64 * (h % 2)
                tensor.matmul(pkv[pbase:pbase + 64, (h // 2) * 64:(h // 2) * 64 + 64],
                              kst[:, h * 64:(h + 1) * 64],
                              vst[:, h * 64:(h + 1) * 64],
                              start=(first and h == 0), stop=last)
            for h in range(16):
                pbase = 64 * (h % 2)
                i = tensor.matmul(pks[pbase:pbase + 64, h // 2:h // 2 + 1],
                                  kst[:, h * 64:(h + 1) * 64],
                                  C.ones_col[:],
                                  start=(first and h == 0), stop=last)
            S.inc(i, "kvmmB")

        if t % 4 == 3:
            @block.vector
            def _(vector, b=b):
                S.waitb(vector, "kvmmB", 4 * (b + 1))
                for h in range(16):
                    pbase = 64 * (h % 2)
                    blk = (b * 8 + h // 2) * 130
                    i = vector.tensor_copy(
                        kv_sb[pbase:pbase + 64, blk + 65 * (h % 2): blk + 65 * (h % 2) + 64],
                        pkv[pbase:pbase + 64, (h // 2) * 64:(h // 2) * 64 + 64])
                for par2 in range(2):
                    pbase = 64 * par2
                    i = vector.tensor_copy(
                        apof(kv_sb).rearrange("p (g c) -> p g c", c=130)[
                            pbase:pbase + 64, b * 8:(b + 1) * 8,
                            64 + 65 * par2:65 + 65 * par2],
                        pks[pbase:pbase + 64, 0:8].rearrange("p (g c) -> p g c", c=1))
                S.inc(i, "kvE2")

    @block.sync
    def _(sync):
        S.wait(sync, "kvE2")
        S.inc(sync.dma_start(out=kv_cc_in, in_=kv_sb[:]), "cc", 16)

    @block.gpsimd
    def _(gpsimd):
        gpsimd.wait_ge(S.h["cc"], S.hb("cc", 16))
        i = gpsimd.collective_compute(
            "AllReduce", ALU.add,
            replica_groups=[list(range(N_CORES))],
            ins=[kv_cc_in.opt()], outs=[kv_cc_out.opt()])
        S.inc(i, "ccb")

    @block.sync
    def _(sync):
        S.wait(sync, "ccb")
        S.inc(sync.dma_start(out=kv_sb[:], in_=kv_cc_out), "cc", 16)

    @block.vector
    def _(vector):
        S.waitb(vector, "cc", 32)
        S.inc(vector.tensor_copy(kv_bf[:], kv_sb[:]), "kvbf")

    na = 0
    for t in range(NT):
        b = t // 4
        for oc in range(NDC):
            na += 1

            @block.tensor
            def _(tensor, t=t, b=b, oc=oc, na=na):
                if na == 1:
                    S.wait(tensor, "kvbf")
                if na > 2:
                    S.waitb(tensor, "adiv", 2 * (na - 2))
                blk = (b * 8 + oc) * 130
                i = tensor.matmul(
                    pqk[na % 2],
                    qf[:, oc * 2048 + t * 128: oc * 2048 + (t + 1) * 128],
                    kv_bf[:, blk: blk + 130],
                    start=True, stop=True)
                S.inc(i, "amm")

            @block.vector
            def _(vector, oc=oc, na=na):
                S.waitb(vector, "amm", na)
                for par2 in range(2):
                    S.step(vector, vector.tensor_copy(
                        dcol[:, 2 * par2:2 * par2 + 1],
                        pqk[na % 2][:, 64 + 65 * par2:65 + 65 * par2]), "vch")
                    S.step(vector, vector.tensor_scalar(
                        dcol[:, 2 * par2:2 * par2 + 1],
                        dcol[:, 2 * par2:2 * par2 + 1], AEPS, None, ALU.add), "vch")
                    S.step(vector, vector.reciprocal(
                        dcol[:, 2 * par2 + 1:2 * par2 + 2],
                        dcol[:, 2 * par2:2 * par2 + 1]), "vch")
                    i = vector.tensor_scalar(
                        astage[:, (2 * oc + par2) * 64:(2 * oc + par2 + 1) * 64],
                        pqk[na % 2][:, 65 * par2:65 * par2 + 64],
                        dcol[:, 2 * par2 + 1:2 * par2 + 2], None, ALU.mult)
                    S.inc(i, "adiv")

        @block.tensor
        def _(tensor, t=t):
            S.waitb(tensor, "adiv", 16 * (t + 1))
            if t >= 2:
                S.waitb(tensor, "aTe", t - 1)
            for dc in range(NDC):
                i = tensor.transpose(
                    PS.ptA[:, (t % 2) * 1024 + dc * 128:(t % 2) * 1024 + (dc + 1) * 128],
                    astage[:, dc * 128:(dc + 1) * 128], C.ident[:])
            S.inc(i, "atr")

        @block.vector
        def _(vector, t=t):
            S.waitb(vector, "atr", t + 1)
            for dc in range(NDC):
                i = vector.tensor_copy(
                    xnT[:, dc * 2048 + t * 128: dc * 2048 + t * 128 + 128],
                    PS.ptA[:, (t % 2) * 1024 + dc * 128:(t % 2) * 1024 + (dc + 1) * 128])
            S.inc(i, "aTe")

    @block.gpsimd
    def _(gpsimd):
        gpsimd.wait_ge(S.h["kvbf"], S.c["kvbf"])
        for dc in range(NDC):
            S.inc(gpsimd.dma_start(
                out=wbufB[:, dc * 1024:(dc + 1) * 1024],
                in_=dr["woT"][dc * 128:(dc + 1) * 128, :]), "wldg", 16)

    no = 0
    for t in range(NT):
        par = t % 2

        @block.sync
        def _(sync, t=t, par=par):
            if t == 0:
                S.waitb(sync, "lnTok", NT)
                S.waitb(sync, "lnSq", NT)
            if t >= 2:
                S.waitb(sync, "yst", 16 * (t - 1))
            sync.wait_ge(S.h["xld"], S.hb("xld", 16 * (2 * NT + t)))
            S.inc(sync.dma_start(out=xb[:, par * D:(par + 1) * D],
                                 in_=x_src[:, t * D:(t + 1) * D]), "xld", 16)

        for dh in range(2):
            no += 1

            @block.tensor
            def _(tensor, t=t, dh=dh, no=no):
                if no == 1:
                    S.wait(tensor, "wldg")
                S.waitb(tensor, "aTe", t + 1)
                if no > 2:
                    S.waitb(tensor, "oev", no - 2)
                for dc in range(NDC):
                    tensor.matmul(
                        pp[no % 2][:],
                        xnT[:, dc * 2048 + t * 128: dc * 2048 + (t + 1) * 128],
                        wbufB[:, dc * 1024 + dh * 512: dc * 1024 + (dh + 1) * 512],
                        start=(dc == 0), stop=False)
                i = tensor.matmul(pp[no % 2][:], C.ones_row[:],
                                  bo_sb[0:1, dh * 512:(dh + 1) * 512],
                                  start=False, stop=True)
                S.inc(i, "omm")

            @block.vector
            def _(vector, t=t, dh=dh, no=no, par=par):
                S.waitb(vector, "omm", no)
                vector.wait_ge(S.h["xld"], S.hb("xld", 16 * (2 * NT + t + 1)))
                i = vector.tensor_add(
                    xb[:, par * D + dh * 512: par * D + (dh + 1) * 512],
                    xb[:, par * D + dh * 512: par * D + (dh + 1) * 512],
                    pp[no % 2][:])
                S.inc(i, "oev")

        @block.sync
        def _(sync, t=t, par=par):
            S.waitb(sync, "oev", 2 * (t + 1))
            S.inc(sync.dma_start(out=y_dst[:, t * D:(t + 1) * D],
                                 in_=xb[:, par * D:(par + 1) * D]), "yst", 16)

    @block.sync
    def _(sync):
        sync.wait_ge(S.h["yst"], S.c["yst"])


def host_attn0_weights(wq, wk, wv, wo, bo):
    return {"wqT": np.ascontiguousarray(wq.T), "wkT": np.ascontiguousarray(wk.T),
            "wvT": np.ascontiguousarray(wv.T), "woT": np.ascontiguousarray(wo.T),
            "bo": np.ascontiguousarray(bo[None, :]),
            "identw": np.eye(128, dtype=BBF16)}



def alloc_ffn1(nc, stack, P):
    T = NS()
    T.xb = P.f32("f1_xb", 2 * D)
    T.x_blk = P.f32("f1_x_blk", 4096)
    T.ln_stats = P.f32("f1_ln_stats", 80)
    T.ew_sb = P.f32("f1_ew", 8)
    T.ws = P.f32("f1_ws", 4)
    T.xnT1 = P.bf16("f1_xnT1", NDC * 512)
    T.HT1 = P.bf16("f1_HT1", 64 * 512)
    T.w1_sb = P.bf16("f1_w1", 4 * 8 * 128)
    T.w2_sb = P.bf16("f1_w2", 8 * 512)
    T.ln_xt = P.bf16("f1_ln_xt", 2 * D)
    return T


def emit_ffn1(nc, block, S, T, C, PS, dr, x_src, y_dst, copy_rest):
    NFC = 64
    W1_RING, W2_RING = 4, 8
    pm1 = [PS.pf[0], PS.pf[1]]
    pm2 = [PS.pf[2], PS.pf[3], PS.pf[4], PS.pf[5]]
    w1, w2, ew = dr["w1r"], dr["w2r"], dr["ew"]
    xnT1, HT1, w1_sb, w2_sb = T.xnT1, T.HT1, T.w1_sb, T.w2_sb
    ew_sb, ws, x_blk = T.ew_sb, T.ws, T.x_blk

    emit_phase_entry(block, S)
    S.rebase()

    @block.sync
    def _(sync):
        S.wait(sync, "yst")
        S.inc(sync.dma_start(out=ew_sb[:], in_=ew), "wld", 16)
        S.inc(sync.dma_start(out=x_blk[:], in_=x_src[:, 0:4096]), "xbL", 16)
        if copy_rest:
            S.inc(sync.dma_start(out=y_dst[:, 4096:NT * D],
                                 in_=x_src[:, 4096:NT * D]), "yst", 16)

    @block.vector
    def _(vector):
        vector.wait_ge(S.h["wld"], S.c["wld"])
        ewv = apof(ew_sb).rearrange("p (t c) -> p t c", c=2)
        wsv = apof(ws).rearrange("p (t c) -> p t c", c=1)
        S.step(vector, vector.tensor_add(wsv, ewv[:, :, 0:1], ewv[:, :, 1:2]), "vch")
        i = vector.tensor_scalar(ws[:], ws[:], 0.5, None, ALU.mult)
        S.inc(i, "wsF")

    emit_ln_stream(nc, block, S, T, C, PS, x_src, range(4), xnT1, 512)

    @block.gpsimd
    def _(gpsimd):
        for fc in range(NFC):
            S.waitb(gpsimd, "geluE", fc - (W1_RING - 1))
            if fc > 0:
                gpsimd.wait_ge(S.h["w1L"], S.hb("w1L", 16 * fc))
            slot = fc % W1_RING
            S.inc(gpsimd.dma_start(
                out=w1_sb[:, slot * 1024:(slot + 1) * 1024],
                in_=w1[fc].rearrange("p dc c -> p (dc c)")), "w1L", 16)
        for dh in range(2):
            for p in range(dh * NFC, dh * NFC + NFC):
                fc = p - dh * NFC
                S.waitb(gpsimd, "w2U", p - (W2_RING - 1))
                if p > 0:
                    gpsimd.wait_ge(S.h["w2L"], S.hb("w2L", 16 * p))
                slot = p % W2_RING
                S.inc(gpsimd.dma_start(
                    out=w2_sb[:, slot * 512:(slot + 1) * 512],
                    in_=w2[dh, fc]), "w2L", 16)

    for fc in range(NFC):
        @block.tensor
        def _(tensor, fc=fc):
            if fc == 0:
                S.waitb(tensor, "xnTe", 4)
            slot = fc % W1_RING
            tensor.wait_ge(S.h["w1L"], S.hb("w1L", 16 * (fc + 1)))
            if fc >= 2:
                tensor.wait_ge(S.h["geluE"], S.hb("geluE", fc - 1))
            for dc in range(NDC):
                i = tensor.matmul(
                    pm1[fc % 2][:],
                    w1_sb[:, slot * 1024 + dc * 128: slot * 1024 + (dc + 1) * 128],
                    xnT1[:, dc * 512:(dc + 1) * 512],
                    start=(dc == 0), stop=(dc == NDC - 1))
            S.inc(i, "mm1d")

        @block.scalar
        def _(scalar, fc=fc):
            scalar.wait_ge(S.h["mm1d"], S.hb("mm1d", fc + 1))
            i = scalar.activation(HT1[:, fc * 512:(fc + 1) * 512],
                                  pm1[fc % 2][:], AF.Gelu_apprx_tanh)
            S.inc(i, "geluE")

    for dh in range(2):
        @block.tensor
        def _(tensor, dh=dh):
            if dh == 0:
                tensor.wait_ge(S.h["geluE"], S.hb("geluE", NFC))
            else:
                tensor.wait_ge(S.h["evacE"], S.hb("evacE", 4))
            for fc in range(NFC):
                p = dh * NFC + fc
                slot = p % W2_RING
                tensor.wait_ge(S.h["w2L"], S.hb("w2L", 16 * (p + 1)))
                for sc in range(4):
                    i = tensor.matmul(
                        pm2[sc][:],
                        HT1[:, fc * 512 + sc * 128: fc * 512 + (sc + 1) * 128],
                        w2_sb[:, slot * 512:(slot + 1) * 512],
                        start=(fc == 0), stop=(fc == NFC - 1))
                S.inc(i, "w2U")

        @block.vector
        def _(vector, dh=dh):
            vector.wait_ge(S.h["w2U"], S.hb("w2U", NFC * (dh + 1)))
            S.waitb(vector, "wsF", 1)
            if dh == 0:
                vector.wait_ge(S.h["xbL"], S.hb("xbL", 16))
            for sc in range(4):
                i = vector.scalar_tensor_tensor(
                    x_blk[:, sc * D + dh * 512: sc * D + (dh + 1) * 512],
                    pm2[sc][:], ws[:, sc:sc + 1],
                    x_blk[:, sc * D + dh * 512: sc * D + (dh + 1) * 512],
                    ALU.mult, ALU.add)
                S.inc(i, "evacE")

    @block.sync
    def _(sync):
        S.waitb(sync, "evacE", 8)
        S.inc(sync.dma_start(out=y_dst[:, 0:4096], in_=x_blk[:]), "yst", 16)
        sync.wait_ge(S.h["yst"], S.c["yst"])


def host_ffn1_weights(w1, w2):
    w1r = np.ascontiguousarray(w1.reshape(8, 128, 64, 128).transpose(2, 1, 0, 3))
    w2r = np.ascontiguousarray(w2.reshape(64, 128, 2, 512).transpose(2, 0, 1, 3))
    return {"w1r": w1r, "w2r": w2r, "identw": np.eye(128, dtype=BBF16)}


def host_ffn1_ew(expert_weights, c):
    ews = expert_weights[512 * c:512 * (c + 1), :]
    return np.ascontiguousarray(ews.reshape(4, 128, 2).transpose(1, 0, 2)
                                .reshape(128, 8))



def alloc_attn1(nc, stack, P):
    T = NS()
    T.xb = P.f32("a1_xb", 2 * D)
    T.ln_stats = P.f32("a1_ln_stats", 80)
    T.S_f = P.f32("a1_S_f", 512)
    T.bcol = P.f32("a1_bcol", 32)
    T.glt = P.f32("a1_glt", 16)
    T.gsc = P.f32("a1_gsc", 48)
    T.rmsc = P.f32("a1_rmsc", 8)
    T.itemp3 = P.f32("a1_itemp", 12)
    T.cv_sb = P.f32("a1_cv", 48)
    T.fir_sb = P.f32("a1_fir", 4 * 34)
    T.rmsw_sb = P.f32("a1_rmsw", 2)
    T.xnT = P.bf16("a1_xnT", NDC * 2048)
    T.xnBb = P.bf16("a1_xnBb", NDC * 512)
    T.P_scr = P.bf16("a1_Pscr", 2 * 4128)
    T.q_fm = P.bf16("a1_qfm", 2 * 4096)
    T.k_fm = P.bf16("a1_kfm", 2 * 4096)
    T.v_fm = P.bf16("a1_vfm", 2 * 4128)
    T.kb_fm = P.bf16("a1_kbfm", 2 * 4096)
    T.tokst = P.bf16("a1_tokst", 1024)
    T.lsb = P.bf16("a1_lsb", 2 * 512)
    T.llb = P.bf16("a1_llb", 2 * 512)
    T.wch = P.bf16("a1_wch", 8 * 256)
    T.wgl_sb = P.bf16("a1_wgl", 8 * 16)
    T.wb_sb = P.bf16("a1_wb", 8 * 2)
    T.ln_xt = P.bf16("a1_ln_xt", 2 * D)
    T.MT_t = P.bf16("a1_MT", 32)
    T.M_t = P.bf16("a1_M", 32)
    T.M2T_t = P.bf16("a1_M2T", 32)
    T.M2_t = P.bf16("a1_M2", 32)
    T.M4T_t = P.bf16("a1_M4T", 32)
    T.M4_t = P.bf16("a1_M4", 32)
    T.M8T_t = P.bf16("a1_M8T", 32)
    T.M8_t = P.bf16("a1_M8", 32)
    T.A1T_t = P.bf16("a1_A1T", 32)
    T.A2_t = P.bf16("a1_A2", 32)
    T.A4_t = P.bf16("a1_A4", 32)
    T.A8_t = P.bf16("a1_A8", 32)
    T.A16_t = P.bf16("a1_A16", 32)
    T.R_t = P.bf16("a1_R", 64)
    T.TT_t = P.bf16("a1_TT", 32)
    T.AT_t = P.bf16("a1_AT", 32)
    T.u_sb = P.bf16("a1_u", 256)
    T.z_sb = P.bf16("a1_z", 256)
    T.S_bf = P.bf16("a1_Sbf", 512)
    T.bcol_bf = P.bf16("a1_bcolbf", 32)
    T.ostage = P.bf16("a1_ostage", 256)
    T.ot_t = P.bf16("a1_ot", 1024)
    T.mx_t = P.bf16("a1_mx", 1024)
    T.trowb = stack.enter_context(nc.sbuf_tensor("a1_trowb", [1, 4], BF16))
    T.trow = stack.enter_context(nc.sbuf_tensor("a1_trow", [1, 16], F32))
    T.bgl_sb = stack.enter_context(nc.sbuf_tensor("a1_bgl", [1, 16], BF16))
    T.l2row = stack.enter_context(nc.sbuf_tensor("a1_l2row", [1, 512], F32))
    T.l2rowb = stack.enter_context(nc.sbuf_tensor("a1_l2rowb", [1, 512], BF16))
    return T


def emit_attn1(nc, block, S, T, C, PS, dr, x_src, y_dst):
    NBLK, NCH, NGRP = 8, 2, 32
    REPS, L2EPS = 1e-5, 1e-6
    pp = [PS.pf[0], PS.pf[1]]
    pA = PS.pf[2][:, 0:64]
    pO = PS.pf[3][:, 0:256]
    pS = PS.pf[4]
    pB = PS.pf[5][:, 0:32]
    pZ = PS.pf[0]
    pU = PS.pf[1]
    pt = PS.ptA
    a1_in, a1_out = dr["a1_in"], dr["a1_out"]
    a2_in, a2_out = dr["a2_in"], dr["a2_out"]

    xnT, xnBb, P_scr = T.xnT, T.xnBb, T.P_scr
    q_fm, k_fm, v_fm, kb_fm = T.q_fm, T.k_fm, T.v_fm, T.kb_fm
    k_tok = xnT[:, 0:8192]
    vb_tok = xnT[:, 8192:16384]
    wo_sb = P_scr[:, 0:8192]
    tokst, lsb, llb, wch = T.tokst, T.lsb, T.llb, T.wch
    wgl_sb, wb_sb, cv_sb, fir_sb = T.wgl_sb, T.wb_sb, T.cv_sb, T.fir_sb
    rmsw_sb, itemp3, trow, trowb, bgl_sb = (T.rmsw_sb, T.itemp3, T.trow,
                                            T.trowb, T.bgl_sb)
    S_f, S_bf = T.S_f, T.S_bf
    MT_t, M_t, M2T_t, M2_t = T.MT_t, T.M_t, T.M2T_t, T.M2_t
    M4T_t, M4_t, M8T_t, M8_t = T.M4T_t, T.M4_t, T.M8T_t, T.M8_t
    A1T_t, A2_t, A4_t, A8_t, A16_t = T.A1T_t, T.A2_t, T.A4_t, T.A8_t, T.A16_t
    R_t, TT_t, AT_t = T.R_t, T.TT_t, T.AT_t
    u_sb, z_sb, bcol, bcol_bf = T.u_sb, T.z_sb, T.bcol, T.bcol_bf
    l2row, l2rowb = T.l2row, T.l2rowb
    glt, gsc, rmsc, ostage = T.glt, T.gsc, T.rmsc, T.ostage
    ot_t, mx_t, xb = T.ot_t, T.mx_t, T.xb
    mask_su = C.masks[:, 0:32]
    mask_iu = C.masks[:, 32:64]
    eye4 = C.masks[:, 64:96]

    emit_phase_entry(block, S)
    S.rebase()

    @block.sync
    def _(sync):
        S.inc(sync.dma_start(out=cv_sb[:, 0:16], in_=dr["cvq"]), "wld", 16)
        S.inc(sync.dma_start(out=cv_sb[:, 16:32], in_=dr["cvk"]), "wld", 16)
        S.inc(sync.dma_start(out=cv_sb[:, 32:48], in_=dr["cvv"]), "wld", 16)
        S.inc(sync.dma_start(
            out=apof(fir_sb).rearrange("p (c t) -> p c t", t=34)[:, :, 0:3],
            in_=dr["firs"].rearrange("p (c t) -> p c t", t=3)), "wld", 16)
        S.inc(sync.dma_start(
            out=apof(fir_sb).rearrange("p (c t) -> p c t", t=34)[:, :, 3:34],
            in_=dr["firl"].rearrange("p (c t) -> p c t", t=31)), "wld", 16)
        S.inc(sync.dma_start(out=rmsw_sb[:], in_=dr["rmsw"]), "wld", 16)
        S.inc(sync.dma_start(out=trow[0:1, 0:4], in_=dr["ltemp"]), "wld", 16)

    @block.gpsimd
    def _(gpsimd):
        S.inc(gpsimd.dma_start(out=wb_sb[:], in_=dr["wbc"]), "wldg", 16)
        S.inc(gpsimd.dma_start(out=wgl_sb[:], in_=dr["wgl"]), "wldg", 16)
        S.inc(gpsimd.dma_start(out=bgl_sb[:], in_=dr["bgl"]), "wldg", 16)

    @block.vector
    def _(vector):
        vector.memset(v_fm[:], 0.0)
        i = vector.memset(P_scr[:], 0.0)
        S.inc(i, "cvD")

    emit_ln_stream(nc, block, S, T, C, PS, x_src, range(NT), xnT, 2048)

    @block.sync
    def _(sync):
        S.waitb(sync, "xnTe", NT)
        for d in range(8):
            b = d // 2
            S.inc(sync.dma_start(
                out=a1_in[d],
                in_=apof(xnT).rearrange("p (dc s) -> p dc s", s=2048)[
                    :, :, b * 512:(b + 1) * 512]), "shA", 16)

    @block.gpsimd
    def _(gpsimd):
        gpsimd.wait_ge(S.h["shA"], S.hb("shA", 16 * 8))
        with allow_ici_all_to_all():
            i = gpsimd.collective_compute(
                "AllToAll", ALU.bypass,
                replica_groups=[list(range(N_CORES))],
                ins=[a1_in.opt()], outs=[a1_out.opt()])
        S.inc(i, "ccA")

    ctr = {"gp": 0, "tp": 0}

    def pp_group(tensor):
        ctr["gp"] += 1
        g = ctr["gp"]
        if g > 2:
            tensor.wait_ge(S.h["pev"], S.hb("pev", g - 2))
        return pp[g % 2], g

    def pt_half(tensor):
        ctr["tp"] += 1
        n = ctr["tp"]
        if n > 2:
            tensor.wait_ge(S.h["trE"], S.hb("trE", n - 2))
        return pt[:, (n % 2) * 1024:(n % 2 + 1) * 1024], n

    fmbuf = {"q": q_fm, "k": k_fm, "v": v_fm}
    fmoff = {"q": (0, 4096), "k": (0, 4096), "v": (32, 4128)}

    def emit_scan(ch):
        for g in range(NGRP):
            @block.tensor
            def _(tensor, g=g):
                tensor.wait_ge(S.h["sAe"], S.c["sAe"])
                for j in range(4):
                    sp = (4 * g + j) * 32
                    for cc in range(2):
                        tensor.matmul(
                            pA[32 * j:32 * (j + 1), 0:32],
                            k_fm[:, cc * 4096 + sp: cc * 4096 + sp + 32],
                            kb_fm[:, cc * 4096 + sp: cc * 4096 + sp + 32],
                            start=(cc == 0), stop=(cc == 1),
                            tile_position=(0, 32 * j))
                    for cc in range(2):
                        i = tensor.matmul(
                            pA[32 * j:32 * (j + 1), 32:64],
                            k_fm[:, cc * 4096 + sp: cc * 4096 + sp + 32],
                            q_fm[:, cc * 4096 + sp: cc * 4096 + sp + 32],
                            start=(cc == 0), stop=(cc == 1),
                            tile_position=(0, 32 * j))
                S.inc(i, "sAm")

            @block.vector
            def _(vector, g=g):
                vector.wait_ge(S.h["sAm"], S.c["sAm"])
                S.step(vector, vector.tensor_mul(MT_t[:], pA[:, 0:32], mask_su), "vch")
                S.step(vector, vector.tensor_mul(AT_t[:], pA[:, 32:64], mask_iu), "vch")
                S.step(vector, vector.transpose(M_t[:], MT_t[:]), "vch")
                i = vector.tensor_add(A1T_t[:], MT_t[:], eye4)
                S.inc(i, "sAe")

            stages = [
                (M_t, MT_t, M2T_t, M2_t, A2_t),
                (M2_t, M2T_t, M4T_t, M4_t, A4_t),
                (M4_t, M4T_t, M8T_t, M8_t, A8_t),
            ]
            for lhs_t, rhs_t, outT_t, out_t, outA_t in stages:
                @block.tensor
                def _(tensor, lhs_t=lhs_t, rhs_t=rhs_t):
                    tensor.wait_ge(S.h["sAe"], S.c["sAe"])
                    for j in range(4):
                        i = tensor.matmul(
                            pA[32 * j:32 * (j + 1), 0:32],
                            lhs_t[32 * j:32 * (j + 1), :],
                            rhs_t[32 * j:32 * (j + 1), :],
                            start=True, stop=True,
                            tile_position=(32 * j, 32 * j))
                    S.inc(i, "sAm")

                @block.vector
                def _(vector, outT_t=outT_t, out_t=out_t, outA_t=outA_t):
                    vector.wait_ge(S.h["sAm"], S.c["sAm"])
                    S.step(vector, vector.tensor_copy(outT_t[:], pA[:, 0:32]), "vch")
                    S.step(vector, vector.transpose(out_t[:], outT_t[:]), "vch")
                    i = vector.tensor_add(outA_t[:], out_t[:], eye4)
                    S.inc(i, "sAe")

            @block.tensor
            def _(tensor):
                tensor.wait_ge(S.h["sAe"], S.c["sAe"])
                for j in range(4):
                    i = tensor.matmul(
                        pA[32 * j:32 * (j + 1), 0:32],
                        M8_t[32 * j:32 * (j + 1), :],
                        M8T_t[32 * j:32 * (j + 1), :],
                        start=True, stop=True,
                        tile_position=(32 * j, 32 * j))
                S.inc(i, "sAm")

            @block.vector
            def _(vector):
                vector.wait_ge(S.h["sAm"], S.c["sAm"])
                S.step(vector, vector.tensor_add(R_t[:, 32:64], pA[:, 0:32], eye4), "vch")
                i = vector.transpose(A16_t[:], R_t[:, 32:64])
                S.inc(i, "sAe")

            prods = [(A2_t, A1T_t, 0, 0), (A4_t, None, 0, 32),
                     (A8_t, None, 32, 0), (A16_t, None, 0, -1)]
            for lhs_t, rhsA, rhs_off, outoff in prods:
                @block.tensor
                def _(tensor, lhs_t=lhs_t, rhsA=rhsA, rhs_off=rhs_off):
                    tensor.wait_ge(S.h["sAe"], S.c["sAe"])
                    for j in range(4):
                        rhs = (rhsA[32 * j:32 * (j + 1), :] if rhsA is not None
                               else R_t[32 * j:32 * (j + 1), rhs_off:rhs_off + 32])
                        i = tensor.matmul(
                            pA[32 * j:32 * (j + 1), 0:32],
                            lhs_t[32 * j:32 * (j + 1), :], rhs,
                            start=True, stop=True,
                            tile_position=(32 * j, 32 * j))
                    S.inc(i, "sAm")

                @block.vector
                def _(vector, outoff=outoff):
                    vector.wait_ge(S.h["sAm"], S.c["sAm"])
                    if outoff < 0:
                        i = vector.tensor_copy(TT_t[:], pA[:, 0:32])
                    else:
                        i = vector.tensor_copy(R_t[:, outoff:outoff + 32], pA[:, 0:32])
                    S.inc(i, "sAe")

            for j in range(4):
                cidx = 4 * g + j
                w = cidx // 4
                sp = cidx * 32

                @block.tensor
                def _(tensor, g=g, j=j, w=w, sp=sp):
                    tensor.wait_ge(S.h["scanD"], S.c["scanD"])
                    if j == 0:
                        tensor.wait_ge(S.h["oev"], S.c["oev"])
                    for cc in range(2):
                        i = tensor.matmul(
                            pZ[32 * j:32 * (j + 1), 0:256],
                            kb_fm[:, cc * 4096 + sp: cc * 4096 + sp + 32],
                            S_bf[:, cc * 256:(cc + 1) * 256],
                            start=(cc == 0), stop=(cc == 1),
                            tile_position=(0, 32 * j))
                    S.inc(i, "sZm")

                @block.vector
                def _(vector, j=j, w=w):
                    vector.wait_ge(S.h["sZm"], S.c["sZm"])
                    i = vector.tensor_sub(
                        z_sb[32 * j:32 * (j + 1), :],
                        vb_tok[32 * j:32 * (j + 1), w * 256:(w + 1) * 256],
                        pZ[32 * j:32 * (j + 1), 0:256])
                    S.inc(i, "sZe")

                @block.tensor
                def _(tensor, j=j):
                    tensor.wait_ge(S.h["sZe"], S.c["sZe"])
                    i = tensor.matmul(
                        pU[32 * j:32 * (j + 1), 0:256],
                        TT_t[32 * j:32 * (j + 1), :],
                        z_sb[32 * j:32 * (j + 1), :],
                        start=True, stop=True,
                        tile_position=(32 * j, 32 * j))
                    S.inc(i, "sUm")

                @block.vector
                def _(vector, j=j):
                    vector.wait_ge(S.h["sUm"], S.c["sUm"])
                    i = vector.tensor_copy(u_sb[32 * j:32 * (j + 1), :],
                                           pU[32 * j:32 * (j + 1), 0:256])
                    S.inc(i, "sUe")

                @block.tensor
                def _(tensor, g=g, j=j, w=w, sp=sp):
                    tensor.wait_ge(S.h["sUe"], S.c["sUe"])
                    for cc in range(2):
                        tensor.matmul(
                            pO[32 * j:32 * (j + 1), :],
                            q_fm[:, cc * 4096 + sp: cc * 4096 + sp + 32],
                            S_bf[:, cc * 256:(cc + 1) * 256],
                            start=(cc == 0), stop=False,
                            tile_position=(0, 32 * j))
                    tensor.matmul(
                        pO[32 * j:32 * (j + 1), :],
                        AT_t[32 * j:32 * (j + 1), :],
                        u_sb[32 * j:32 * (j + 1), :],
                        start=False, stop=(j == 3),
                        tile_position=(32 * j, 32 * j))
                    for cc in range(2):
                        i = tensor.matmul(
                            pS[:, cc * 256:(cc + 1) * 256],
                            k_tok[32 * j:32 * (j + 1),
                                  w * 256 + cc * 128: w * 256 + (cc + 1) * 128],
                            u_sb[32 * j:32 * (j + 1), :],
                            start=(cc == 0), stop=(cc == 1),
                            tile_position=(32 * j, 0))
                    S.inc(i, "sSm")

                @block.vector
                def _(vector, j=j):
                    vector.wait_ge(S.h["sSm"], S.c["sSm"])
                    S.step(vector, vector.tensor_add(S_f[:], S_f[:], pS[:]), "vch")
                    i = vector.tensor_copy(S_bf[:], S_f[:])
                    S.inc(i, "scanD")

            @block.vector
            def _(vector, g=g):
                vector.wait_ge(S.h["sSm"], S.c["sSm"])
                if S.c["payD"] > 0:
                    vector.wait_ge(S.h["payD"], S.c["payD"])
                i = vector.tensor_copy(ostage[:], pO[:])
                S.inc(i, "oev")

            @block.sync
            def _(sync, ch=ch, g=g):
                sync.wait_ge(S.h["oev"], S.c["oev"])
                if S.c["payD"] > 0:
                    sync.wait_ge(S.h["payD"], S.c["payD"])
                S.inc(sync.dma_start(
                    out=a2_in[g // 4, 0][:, g % 4, ch * 256:(ch + 1) * 256],
                    in_=ostage[:]), "payD", 16)

    def emit_fir(ch):
        for sb8 in range(NBLK):
            @block.vector
            def _(vector, sb8=sb8, ch=ch):
                if sb8 == 0:
                    vector.wait_ge(S.h["oev"], S.c["oev"])
                for cc in range(2):
                    base = cc * 4128 + 32 + sb8 * 512
                    fw = fir_sb[:, (2 * ch + cc) * 34:(2 * ch + cc) * 34 + 34]
                    i = vector.tensor_scalar(
                        lsb[:, cc * 512:(cc + 1) * 512],
                        v_fm[:, base - 2: base - 2 + 512],
                        fw[:, 0:1], None, ALU.mult)
                    for tpp in range(1, 3):
                        i = vector.scalar_tensor_tensor(
                            lsb[:, cc * 512:(cc + 1) * 512],
                            v_fm[:, base - 2 + tpp: base - 2 + tpp + 512],
                            fw[:, tpp:tpp + 1],
                            lsb[:, cc * 512:(cc + 1) * 512], ALU.mult, ALU.add)
                    i = vector.tensor_scalar(
                        llb[:, cc * 512:(cc + 1) * 512],
                        v_fm[:, base - 30: base - 30 + 512],
                        fw[:, 3:4], None, ALU.mult)
                    for tpp in range(1, 31):
                        i = vector.scalar_tensor_tensor(
                            llb[:, cc * 512:(cc + 1) * 512],
                            v_fm[:, base - 30 + tpp: base - 30 + tpp + 512],
                            fw[:, 3 + tpp:4 + tpp],
                            llb[:, cc * 512:(cc + 1) * 512], ALU.mult, ALU.add)
                S.inc(i, "fir")

            for w4 in range(4):
                @block.tensor
                def _(tensor, sb8=sb8, w4=w4):
                    tensor.wait_ge(S.h["fir"], S.c["fir"])
                    pth, n = pt_half(tensor)
                    for cc in range(2):
                        tensor.transpose(
                            pth[:, cc * 128:(cc + 1) * 128],
                            lsb[:, cc * 512 + w4 * 128: cc * 512 + (w4 + 1) * 128],
                            C.ident[:])
                    for cc in range(2):
                        i = tensor.transpose(
                            pth[:, 256 + cc * 128: 256 + (cc + 1) * 128],
                            llb[:, cc * 512 + w4 * 128: cc * 512 + (w4 + 1) * 128],
                            C.ident[:])
                    S.inc(i, "trP")

                @block.vector
                def _(vector, sb8=sb8, w4=w4):
                    n = ctr["tp"]
                    vector.wait_ge(S.h["trP"], S.c["trP"])
                    vector.wait_ge(S.h["payD"], S.c["payD"])
                    pth = pt[:, (n % 2) * 1024:(n % 2 + 1) * 1024]
                    i = vector.tensor_copy(tokst[:, 512:1024], pth[:, 0:512])
                    S.inc(i, "trE")

                @block.sync
                def _(sync, sb8=sb8, w4=w4, ch=ch):
                    sync.wait_ge(S.h["trE"], S.c["trE"])
                    if S.c["payD"] > 0:
                        sync.wait_ge(S.h["payD"], S.c["payD"])
                    S.inc(sync.dma_start(
                        out=a2_in[sb8, 2][:, w4, ch * 256:(ch + 1) * 256],
                        in_=tokst[:, 512:768]), "payD", 16)
                    sync.wait_ge(S.h["payD"], S.c["payD"])
                    S.inc(sync.dma_start(
                        out=a2_in[sb8, 3][:, w4, ch * 256:(ch + 1) * 256],
                        in_=tokst[:, 768:1024]), "payD", 16)

    for ch in range(NCH):
        for pi, pn in enumerate(("q", "k", "v")):
            @block.gpsimd
            def _(gpsimd, pi=pi, ch=ch):
                gpsimd.wait_ge(S.h["pmm"], S.c["pmm"])
                src = (dr["wqc"], dr["wkc"], dr["wvc"])[pi]
                for dc in range(NDC):
                    if S.c["wldg"] > 0:
                        gpsimd.wait_ge(S.h["wldg"], S.c["wldg"])
                    S.inc(gpsimd.dma_start(
                        out=wch[:, dc * 256:(dc + 1) * 256],
                        in_=src[dc * 128:(dc + 1) * 128,
                                ch * 256:(ch + 1) * 256]), "wldg", 16)

            for blk in range(NBLK):
                nblk_g = (ch * 3 + pi) * NBLK + blk

                @block.sync
                def _(sync, blk=blk, nblk_g=nblk_g):
                    if nblk_g == 0:
                        S.waitb(sync, "ccA", 1)
                    if nblk_g >= 1:
                        sync.wait_ge(S.h["pmm"], S.c["pmm"])
                    if S.c["blkL"] > 0:
                        sync.wait_ge(S.h["blkL"], S.c["blkL"])
                    S.inc(sync.dma_start(out=xnBb[:], in_=a1_out[blk]), "blkL", 16)

                for oc in range(2):
                    @block.tensor
                    def _(tensor, pi=pi, blk=blk, oc=oc, ch=ch):
                        tensor.wait_ge(S.h["blkL"], S.c["blkL"])
                        if blk == 0 and oc == 0:
                            tensor.wait_ge(S.h["wldg"], S.c["wldg"])
                        ppt, g = pp_group(tensor)
                        for dc in range(NDC):
                            i = tensor.matmul(
                                ppt[:],
                                wch[:, dc * 256 + oc * 128: dc * 256 + (oc + 1) * 128],
                                xnBb[:, dc * 512:(dc + 1) * 512],
                                start=(dc == 0), stop=(dc == NDC - 1))
                        S.inc(i, "pmm")

                    @block.vector
                    def _(vector, pn=pn, blk=blk, oc=oc):
                        g = ctr["gp"]
                        vector.wait_ge(S.h["pmm"], S.c["pmm"])
                        i = vector.tensor_copy(
                            P_scr[:, oc * 4128 + 32 + blk * 512:
                                  oc * 4128 + 32 + (blk + 1) * 512],
                            pp[g % 2][:])
                        S.inc(i, "pev")

                if pi == 2:
                    @block.tensor
                    def _(tensor, blk=blk, ch=ch):
                        if blk == 0:
                            S.wait(tensor, "bsig")
                        for sc in range(4):
                            w = blk * 4 + sc
                            for dc in range(NDC):
                                i = tensor.matmul(
                                    pB[:, w:w + 1],
                                    xnBb[:, dc * 512 + sc * 128: dc * 512 + (sc + 1) * 128],
                                    wb_sb[:, dc * 2 + ch: dc * 2 + ch + 1],
                                    start=(dc == 0 and w == 0),
                                    stop=(dc == NDC - 1))
                        S.inc(i, "pmm")

            for sb8 in range(NBLK):
                @block.vector
                def _(vector, pi=pi, sb8=sb8, ch=ch):
                    if sb8 == 0:
                        vector.wait_ge(S.h["pev"], S.c["pev"])
                    for cc in range(2):
                        base = cc * 4128 + 32 + sb8 * 512
                        tap = cv_sb[:, pi * 16 + (2 * ch + cc) * 4:
                                    pi * 16 + (2 * ch + cc) * 4 + 4]
                        i = vector.tensor_scalar(
                            lsb[:, cc * 512:(cc + 1) * 512],
                            P_scr[:, base - 3:base - 3 + 512],
                            tap[:, 0:1], None, ALU.mult)
                        for tpp in range(1, 4):
                            i = vector.scalar_tensor_tensor(
                                lsb[:, cc * 512:(cc + 1) * 512],
                                P_scr[:, base - 3 + tpp: base - 3 + tpp + 512],
                                tap[:, tpp:tpp + 1],
                                lsb[:, cc * 512:(cc + 1) * 512],
                                ALU.mult, ALU.add)
                    S.inc(i, "cvD")

                @block.scalar
                def _(scalar, pn=pn, sb8=sb8):
                    scalar.wait_ge(S.h["cvD"], S.c["cvD"])
                    pre, stride = fmoff[pn]
                    for cc in range(2):
                        i = scalar.activation(
                            fmbuf[pn][:, cc * stride + pre + sb8 * 512:
                                      cc * stride + pre + (sb8 + 1) * 512],
                            lsb[:, cc * 512:(cc + 1) * 512], AF.Silu)
                    S.inc(i, "silu")

                @block.vector
                def _(vector):
                    vector.wait_ge(S.h["silu"], S.c["silu"])

        for li, pn in enumerate(("q", "k")):
            for sb8 in range(NBLK):
                @block.scalar
                def _(scalar, pn=pn, sb8=sb8):
                    for cc in range(2):
                        i = scalar.activation(
                            lsb[:, cc * 512:(cc + 1) * 512],
                            fmbuf[pn][:, cc * 4096 + sb8 * 512: cc * 4096 + (sb8 + 1) * 512],
                            AF.Square)
                    S.inc(i, "l2q")

                @block.tensor
                def _(tensor):
                    tensor.wait_ge(S.h["l2q"], S.c["l2q"])
                    ppt, g = pp_group(tensor)
                    for cc in range(2):
                        i = tensor.matmul(ppt[0:1, :], C.ones_col[:],
                                          lsb[:, cc * 512:(cc + 1) * 512],
                                          start=(cc == 0), stop=(cc == 1))
                    S.inc(i, "pmm")

                @block.vector
                def _(vector):
                    g = ctr["gp"]
                    vector.wait_ge(S.h["pmm"], S.c["pmm"])
                    i = vector.tensor_copy(l2row[0:1, :], pp[g % 2][0:1, :])
                    S.inc(i, "pev")

                @block.scalar
                def _(scalar):
                    S.wait(scalar, "pev", S.c["pev"])
                    S.step(scalar, scalar.activation(
                        l2row[0:1, :], l2row[0:1, :], AF.Ln,
                        bias=C.eps[0:1, 0:1]), "sch")
                    i = scalar.activation(l2rowb[0:1, :], l2row[0:1, :],
                                          AF.Exp, scale=-0.5)
                    S.inc(i, "l2k")

                @block.tensor
                def _(tensor):
                    tensor.wait_ge(S.h["l2k"], S.c["l2k"])
                    ppt, g = pp_group(tensor)
                    i = tensor.matmul(ppt[:], C.ones_row[:], l2rowb[0:1, :],
                                      start=True, stop=True)
                    S.inc(i, "pmm")

                @block.vector
                def _(vector, pn=pn, sb8=sb8):
                    g = ctr["gp"]
                    vector.wait_ge(S.h["pmm"], S.c["pmm"])
                    for cc in range(2):
                        i = vector.tensor_mul(
                            fmbuf[pn][:, cc * 4096 + sb8 * 512: cc * 4096 + (sb8 + 1) * 512],
                            fmbuf[pn][:, cc * 4096 + sb8 * 512: cc * 4096 + (sb8 + 1) * 512],
                            pp[g % 2][:])
                    S.inc(i, "pev")

        @block.vector
        def _(vector):
            vector.wait_ge(S.h["pmm"], S.c["pmm"])
            S.step(vector, vector.tensor_copy(bcol[:], pB), "vch")
            S.inc(nc.vector.engine_nop(), "bsig")

        @block.scalar
        def _(scalar):
            S.wait(scalar, "bsig", S.c["bsig"])
            i = scalar.activation(bcol[:], bcol[:], AF.Exp, scale=-1.0)
            S.inc(i, "bsig")

        @block.vector
        def _(vector):
            S.wait(vector, "bsig", S.c["bsig"])
            S.step(vector, vector.tensor_scalar(bcol[:], bcol[:], 1.0, None, ALU.add), "vch")
            S.step(vector, vector.reciprocal(bcol[:], bcol[:]), "vch")
            i = vector.tensor_copy(bcol_bf[:], bcol[:])
            S.inc(i, "bsig")

        for w in range(32):
            @block.tensor
            def _(tensor, w=w):
                if w == 0:
                    S.wait(tensor, "bsig")
                pth, n = pt_half(tensor)
                for cc in range(2):
                    tensor.transpose(
                        pth[:, cc * 128:(cc + 1) * 128],
                        k_fm[:, cc * 4096 + w * 128: cc * 4096 + (w + 1) * 128],
                        C.ident[:])
                for cc in range(2):
                    i = tensor.transpose(
                        pth[:, 256 + cc * 128: 256 + (cc + 1) * 128],
                        v_fm[:, cc * 4128 + 32 + w * 128: cc * 4128 + 32 + (w + 1) * 128],
                        C.ident[:])
                S.inc(i, "trP")

            @block.vector
            def _(vector, w=w):
                n = ctr["tp"]
                vector.wait_ge(S.h["trP"], S.c["trP"])
                if S.c["payD"] > 0:
                    vector.wait_ge(S.h["payD"], S.c["payD"])
                pth = pt[:, (n % 2) * 1024:(n % 2 + 1) * 1024]
                for cc in range(2):
                    vector.tensor_copy(
                        k_tok[:, w * 256 + cc * 128: w * 256 + (cc + 1) * 128],
                        pth[:, cc * 128:(cc + 1) * 128])
                for cc in range(2):
                    vector.tensor_copy(
                        tokst[:, cc * 128:(cc + 1) * 128],
                        pth[:, 256 + cc * 128: 256 + (cc + 1) * 128])
                i = vector.tensor_scalar(
                    vb_tok[:, w * 256:(w + 1) * 256],
                    tokst[:, 0:256], bcol[:, w:w + 1], None, ALU.mult)
                S.inc(i, "trE")

            @block.sync
            def _(sync, w=w, ch=ch):
                sync.wait_ge(S.h["trE"], S.c["trE"])
                if S.c["payD"] > 0:
                    sync.wait_ge(S.h["payD"], S.c["payD"])
                S.inc(sync.dma_start(
                    out=a2_in[w // 4, 1][:, w % 4, ch * 256:(ch + 1) * 256],
                    in_=tokst[:, 0:256]), "payD", 16)

        for w in range(32):
            @block.vector
            def _(vector, w=w):
                if w == 0:
                    vector.wait_ge(S.h["trE"], S.c["trE"])
                i = vector.tensor_scalar(
                    tokst[:, 256:512], k_tok[:, w * 256:(w + 1) * 256],
                    bcol[:, w:w + 1], None, ALU.mult)
                S.inc(i, "kbE")

            @block.tensor
            def _(tensor, w=w):
                tensor.wait_ge(S.h["kbE"], S.c["kbE"])
                pth, n = pt_half(tensor)
                for cc in range(2):
                    i = tensor.transpose(
                        pth[:, cc * 128:(cc + 1) * 128],
                        tokst[:, 256 + cc * 128: 256 + (cc + 1) * 128],
                        C.ident[:])
                S.inc(i, "trP")

            @block.vector
            def _(vector, w=w):
                n = ctr["tp"]
                vector.wait_ge(S.h["trP"], S.c["trP"])
                pth = pt[:, (n % 2) * 1024:(n % 2 + 1) * 1024]
                for cc in range(2):
                    i = vector.tensor_copy(
                        kb_fm[:, cc * 4096 + w * 128: cc * 4096 + (w + 1) * 128],
                        pth[:, cc * 128:(cc + 1) * 128])
                S.inc(i, "trE")

        @block.vector
        def _(vector):
            vector.wait_ge(S.h["trE"], S.c["trE"])
            vector.memset(S_f[:], 0.0)
            i = vector.memset(S_bf[:], 0.0)
            S.inc(i, "scanD")

        emit_scan(ch)
        emit_fir(ch)

    @block.gpsimd
    def _(gpsimd):
        gpsimd.wait_ge(S.h["payD"], S.c["payD"])
        with allow_ici_all_to_all():
            i = gpsimd.collective_compute(
                "AllToAll", ALU.bypass,
                replica_groups=[list(range(N_CORES))],
                ins=[a2_in.opt()], outs=[a2_out.opt()])
        S.inc(i, "ccB")

    @block.gpsimd
    def _(gpsimd):
        gpsimd.wait_ge(S.h["silu"], S.c["silu"])
        for dc in range(NDC):
            gpsimd.wait_ge(S.h["clg"], S.hb("clg", 16 * dc)) if dc > 0 else None
            S.inc(gpsimd.dma_start(
                out=wo_sb[:, dc * 1024:(dc + 1) * 1024],
                in_=dr["wo_n"][dc * 128:(dc + 1) * 128, :]), "clg", 16)

    @block.vector
    def _(vector):
        vector.wait_ge(S.h["clg"], S.c["clg"])
        for dc in range(NDC):
            i = vector.tensor_scalar(
                wo_sb[:, dc * 1024:(dc + 1) * 1024],
                wo_sb[:, dc * 1024:(dc + 1) * 1024],
                rmsw_sb[:, dc % 2: dc % 2 + 1], None, ALU.mult)
            S.inc(i, "clv")

    @block.scalar
    def _(scalar):
        S.wait(scalar, "wld")
        i = scalar.activation(trow[0:1, 4:8], trow[0:1, 0:4], AF.Exp)
        S.inc(i, "cst")

    @block.vector
    def _(vector):
        S.waitb(vector, "cst", 1)
        S.step(vector, vector.reciprocal(trow[0:1, 8:12], trow[0:1, 4:8]), "vch")
        i = vector.tensor_copy(trowb[0:1, 0:4], trow[0:1, 8:12])
        S.inc(i, "cst")

    @block.tensor
    def _(tensor):
        S.waitb(tensor, "cst", 2)
        tensor.wait_ge(S.h["scanD"], S.c["scanD"])
        tensor.wait_ge(S.h["sUe"], S.c["sUe"])
        ppt, g = pp_group(tensor)
        i = tensor.matmul(ppt[:, 0:4], C.ones_row[:], trowb[0:1, 0:4],
                          start=True, stop=True)
        S.inc(i, "pmm")

    @block.vector
    def _(vector):
        g = ctr["gp"]
        vector.wait_ge(S.h["pmm"], S.c["pmm"])
        i = vector.tensor_copy(itemp3[:, 0:4], pp[g % 2][:, 0:4])
        S.inc(i, "pev")

    for t in range(NT):
        b, ti, par = t // 4, t % 4, t % 2
        ct_od = k_tok[:, par * 4096:par * 4096 + 1024]
        ct_v = k_tok[:, par * 4096 + 1024:par * 4096 + 2048]
        ct_ls = k_tok[:, par * 4096 + 2048:par * 4096 + 3072]
        ct_ll = k_tok[:, par * 4096 + 3072:par * 4096 + 4096]
        ct_xn = vb_tok[:, par * 1024:(par + 1) * 1024]

        @block.sync
        def _(sync, t=t, b=b, ti=ti, par=par, ct_od=ct_od, ct_v=ct_v,
              ct_ls=ct_ls, ct_ll=ct_ll, ct_xn=ct_xn):
            if t == 0:
                S.wait(sync, "ccB")
            if t >= 2:
                sync.wait_ge(S.h["pev"], S.c["pev"])
                S.waitb(sync, "yst", 16 * (t - 1))
            for half in range(2):
                src = 2 * b + half
                for tix, buf in ((0, ct_od), (1, ct_v), (2, ct_ls), (3, ct_ll)):
                    S.inc(sync.dma_start(
                        out=buf[:, half * 512:(half + 1) * 512],
                        in_=a2_out[src, tix][:, ti, :]), "cl", 16)
            S.inc(sync.dma_start(
                out=ct_xn.rearrange("p (dc s) -> p dc s", s=128),
                in_=a1_in[2 * b][:, :, ti * 128:(ti + 1) * 128]), "cl", 16)
            S.inc(sync.dma_start(out=xb[:, par * D:(par + 1) * D],
                                 in_=x_src[:, t * D:(t + 1) * D]), "cl", 16)

        @block.tensor
        def _(tensor, t=t, ct_xn=ct_xn):
            tensor.wait_ge(S.h["cl"], S.c["cl"])
            ppt, g = pp_group(tensor)
            for dc in range(NDC):
                tensor.matmul(ppt[:, 0:16],
                              ct_xn[:, dc * 128:(dc + 1) * 128],
                              wgl_sb[:, dc * 16:(dc + 1) * 16],
                              start=(dc == 0), stop=False)
            i = tensor.matmul(ppt[:, 0:16], C.ones_row[:], bgl_sb[0:1, 0:16],
                              start=False, stop=True)
            S.inc(i, "gmm")

        @block.vector
        def _(vector, t=t):
            g = ctr["gp"]
            vector.wait_ge(S.h["gmm"], S.c["gmm"])
            i = vector.tensor_copy(glt[:], pp[g % 2][:, 0:16])
            S.inc(i, "pev")

        @block.scalar
        def _(scalar, t=t):
            S.wait(scalar, "pev", S.c["pev"])
            i = scalar.activation(gsc[:, 0:4], glt[:, 0:4], AF.Exp, scale=-1.0)
            S.inc(i, "cst")

        @block.vector
        def _(vector, t=t):
            S.wait(vector, "cst", S.c["cst"])
            S.step(vector, vector.tensor_scalar(gsc[:, 0:4], gsc[:, 0:4],
                                                1.0, None, ALU.add), "vch")
            S.step(vector, vector.reciprocal(gsc[:, 0:4], gsc[:, 0:4]), "vch")
            for h in range(4):
                S.step(vector, vector.tensor_scalar(
                    gsc[:, 12 + 3 * h:15 + 3 * h], glt[:, 4 + 3 * h:7 + 3 * h],
                    itemp3[:, h:h + 1], None, ALU.mult), "vch")
            S.step(vector, vector.tensor_reduce(
                gsc[:, 24:28].rearrange("p (h c) -> p h c", c=1),
                gsc[:, 12:24].rearrange("p (h c) -> p h c", c=3),
                AX.X, ALU.max), "vch")
            for j3 in range(3):
                S.step(vector, vector.tensor_copy(
                    gsc[:, 28:40].rearrange("p (h c) -> p h c", c=3)[:, :, j3:j3 + 1],
                    gsc[:, 24:28].rearrange("p (h c) -> p h c", c=1)), "vch")
            i = vector.tensor_sub(gsc[:, 12:24], gsc[:, 12:24], gsc[:, 28:40])
            S.inc(i, "mixv")

        @block.scalar
        def _(scalar, t=t):
            S.wait(scalar, "mixv", S.c["mixv"])
            i = scalar.activation(gsc[:, 28:40], gsc[:, 12:24], AF.Exp)
            S.inc(i, "cst")

        @block.vector
        def _(vector, t=t, ct_od=ct_od, ct_v=ct_v, ct_ls=ct_ls, ct_ll=ct_ll):
            S.wait(vector, "cst", S.c["cst"])
            S.step(vector, vector.tensor_reduce(
                gsc[:, 24:28].rearrange("p (h c) -> p h c", c=1),
                gsc[:, 28:40].rearrange("p (h c) -> p h c", c=3),
                AX.X, ALU.add), "vch")
            S.step(vector, vector.reciprocal(gsc[:, 24:28], gsc[:, 24:28]), "vch")
            for j3 in range(3):
                S.step(vector, vector.tensor_copy(
                    gsc[:, 12:24].rearrange("p (h c) -> p h c", c=3)[:, :, j3:j3 + 1],
                    gsc[:, 24:28].rearrange("p (h c) -> p h c", c=1)), "vch")
            S.step(vector, vector.tensor_mul(gsc[:, 12:24], gsc[:, 12:24],
                                             gsc[:, 28:40]), "vch")
            for h in range(4):
                p0 = gsc[:, 12 + 3 * h:13 + 3 * h]
                p1 = gsc[:, 13 + 3 * h:14 + 3 * h]
                p2 = gsc[:, 14 + 3 * h:15 + 3 * h]
                sl = slice(h * 256, (h + 1) * 256)
                vector.tensor_scalar(mx_t[:, sl], ct_v[:, sl], p0, None, ALU.mult)
                vector.scalar_tensor_tensor(mx_t[:, sl], ct_ls[:, sl], p1,
                                            mx_t[:, sl], ALU.mult, ALU.add)
                vector.scalar_tensor_tensor(mx_t[:, sl], ct_ll[:, sl], p2,
                                            mx_t[:, sl], ALU.mult, ALU.add)
                vector.tensor_sub(ot_t[:, sl], ct_od[:, sl], mx_t[:, sl])
                i = vector.scalar_tensor_tensor(ot_t[:, sl], ot_t[:, sl],
                                                gsc[:, h:h + 1],
                                                mx_t[:, sl], ALU.mult, ALU.add)
            S.inc(i, "mixv")

        @block.scalar
        def _(scalar, t=t, par=par):
            S.wait(scalar, "mixv", S.c["mixv"])
            for h in range(4):
                scalar.activation(T.ln_xt[:, par * D + h * 256: par * D + (h + 1) * 256],
                                  ot_t[:, h * 256:(h + 1) * 256],
                                  AF.Square, accum_out=rmsc[:, h:h + 1])
            S.step(scalar, scalar.activation(rmsc[:, 4:8], rmsc[:, 0:4],
                                             AF.Ln, bias=C.eps[:, 1:2],
                                             scale=1.0 / 256), "sch")
            i = scalar.activation(rmsc[:, 0:4], rmsc[:, 4:8], AF.Exp, scale=-0.5)
            S.inc(i, "cst")

        @block.vector
        def _(vector, t=t):
            S.wait(vector, "cst", S.c["cst"])
            for h in range(4):
                i = vector.tensor_scalar(ot_t[:, h * 256:(h + 1) * 256],
                                         ot_t[:, h * 256:(h + 1) * 256],
                                         rmsc[:, h:h + 1], None, ALU.mult)
            S.inc(i, "otr")

        @block.tensor
        def _(tensor, t=t):
            tensor.wait_ge(S.h["otr"], S.c["otr"])
            pth, n = pt_half(tensor)
            for dc in range(NDC):
                i = tensor.transpose(pth[:, dc * 128:(dc + 1) * 128],
                                     ot_t[:, dc * 128:(dc + 1) * 128], C.ident[:])
            S.inc(i, "trP")

        @block.vector
        def _(vector, t=t):
            n = ctr["tp"]
            vector.wait_ge(S.h["trP"], S.c["trP"])
            vector.wait_ge(S.h["omm"], S.c["omm"])
            pth = pt[:, (n % 2) * 1024:(n % 2 + 1) * 1024]
            for dc in range(NDC):
                i = vector.tensor_copy(mx_t[:, dc * 128:(dc + 1) * 128],
                                       pth[:, dc * 128:(dc + 1) * 128])
            S.inc(i, "trE")

        for dh in range(2):
            @block.tensor
            def _(tensor, t=t, dh=dh):
                tensor.wait_ge(S.h["trE"], S.c["trE"])
                if dh == 0 and t == 0:
                    tensor.wait_ge(S.h["clv"], S.c["clv"])
                ppt, g = pp_group(tensor)
                for dc in range(NDC):
                    i = tensor.matmul(
                        ppt[:],
                        mx_t[:, dc * 128:(dc + 1) * 128],
                        wo_sb[:, dc * 1024 + dh * 512: dc * 1024 + (dh + 1) * 512],
                        start=(dc == 0), stop=(dc == NDC - 1))
                S.inc(i, "omm")

            @block.vector
            def _(vector, t=t, dh=dh, par=par):
                g = ctr["gp"]
                vector.wait_ge(S.h["omm"], S.c["omm"])
                i = vector.tensor_add(
                    xb[:, par * D + dh * 512: par * D + (dh + 1) * 512],
                    xb[:, par * D + dh * 512: par * D + (dh + 1) * 512],
                    pp[g % 2][:])
                S.inc(i, "pev")

        @block.sync
        def _(sync, t=t, par=par):
            sync.wait_ge(S.h["pev"], S.c["pev"])
            S.inc(sync.dma_start(out=y_dst[:, t * D:(t + 1) * D],
                                 in_=xb[:, par * D:(par + 1) * D]), "yst", 16)

    @block.sync
    def _(sync):
        sync.wait_ge(S.h["yst"], S.c["yst"])


def host_attn1_weights(Wq, Wk, Wv, conv_q, conv_k, conv_v, Wb, fir_short,
                       fir_long, Wg, bg, Wl, bl, log_temp, rms_w, Wo):
    masks = np.zeros((128, 96), np.float32)
    su = -np.triu(np.ones((32, 32), np.float32), 1)
    iu = np.triu(np.ones((32, 32), np.float32), 0)
    ey = np.eye(32, dtype=np.float32)
    for j in range(4):
        masks[32 * j:32 * (j + 1), 0:32] = su
        masks[32 * j:32 * (j + 1), 32:64] = iu
        masks[32 * j:32 * (j + 1), 64:96] = ey
    wgl_np = np.concatenate([Wg, Wl], axis=1)
    return {
        "wgl": np.ascontiguousarray(wgl_np.reshape(8, 128, 16).transpose(1, 0, 2)
                                    .reshape(128, 128)),
        "bgl": np.ascontiguousarray(np.concatenate([bg, bl])[None, :]),
        "ltemp": np.ascontiguousarray(log_temp[None, :]),
        "rmsw": np.ascontiguousarray(rms_w.reshape(2, 128).T),
        "wo_n": np.ascontiguousarray(Wo),
        "masks": masks,
        "identw": np.eye(128, dtype=BBF16),
        "wbc_full": Wb, "wq_full": Wq, "wk_full": Wk, "wv_full": Wv,
        "cvq_full": conv_q, "cvk_full": conv_k, "cvv_full": conv_v,
        "firs_full": fir_short, "firl_full": fir_long,
    }


def host_attn1_core_inputs(wd, c):
    h0 = 2 * (c % 2)
    cs, ce = h0 * 256, (h0 + 2) * 256
    out = {k: v for k, v in wd.items() if not k.endswith("_full")}
    out["wqc"] = np.ascontiguousarray(wd["wq_full"][:, cs:ce])
    out["wkc"] = np.ascontiguousarray(wd["wk_full"][:, cs:ce])
    out["wvc"] = np.ascontiguousarray(wd["wv_full"][:, cs:ce])
    for name, key in (("cvq", "cvq_full"), ("cvk", "cvk_full"), ("cvv", "cvv_full")):
        sl = wd[key][cs:ce]
        out[name] = np.ascontiguousarray(sl.reshape(4, 128, 4).transpose(1, 0, 2)
                                         .reshape(128, 16))
    out["wbc"] = np.ascontiguousarray(
        wd["wbc_full"][:, h0:h0 + 2].reshape(8, 128, 2).transpose(1, 0, 2)
        .reshape(128, 16))
    fs = wd["firs_full"][h0:h0 + 2].reshape(512, 3)
    fl = wd["firl_full"][h0:h0 + 2].reshape(512, 31)
    out["firs"] = np.ascontiguousarray(fs.reshape(4, 128, 3).transpose(1, 0, 2)
                                       .reshape(128, 12))
    out["firl"] = np.ascontiguousarray(fl.reshape(4, 128, 31).transpose(1, 0, 2)
                                       .reshape(128, 124))
    return out



def _build_standalone(which):
    nc = bass.Bass("TRN2", target_bir_lowering=False)
    x_in = nc.dram_tensor("x", [128, NT * D], F32, kind="ExternalInput").ap()
    y_out = nc.dram_tensor("y", [128, NT * D], F32, kind="ExternalOutput").ap()
    shapes = {
        "ffn0": [("wfc", [8, 32, 128, 8, 128]), ("wpr", [8, 2, 32, 128, 512]),
                 ("bfc", [128, 256]), ("bpr", [8, 1024]), ("gwT", [1024, 8])],
        "attn0": [("wqT", [1024, 1024]), ("wkT", [1024, 1024]),
                  ("wvT", [1024, 1024]), ("woT", [1024, 1024]), ("bo", [1, 1024])],
        "ffn1": [("w1r", [64, 128, 8, 128]), ("w2r", [2, 64, 128, 512]),
                 ("ew", [128, 8])],
        "attn1": [("wqc", [1024, 512]), ("wkc", [1024, 512]), ("wvc", [1024, 512]),
                  ("cvq", [128, 16]), ("cvk", [128, 16]), ("cvv", [128, 16]),
                  ("wbc", [128, 16]), ("firs", [128, 12]), ("firl", [128, 124]),
                  ("wgl", [128, 128]), ("bgl", [1, 16]), ("ltemp", [1, 4]),
                  ("rmsw", [128, 2]), ("wo_n", [1024, 1024]), ("masks", [128, 96])],
    }
    dr = {n: nc.dram_tensor(n, shp, F32, kind="ExternalInput").ap()
          for n, shp in shapes[which]}
    dr["identw"] = nc.dram_tensor("identw", [128, 128], BF16,
                                  kind="ExternalInput").ap()
    if which == "attn0":
        dr["kv_cc_in"] = nc.dram_tensor("kv_cc_in", [128, 32 * 130], F32).ap()
        dr["kv_cc_out"] = nc.dram_tensor("kv_cc_out", [128, 32 * 130], F32).ap()
    if which == "attn1":
        dr["a1_in"] = nc.dram_tensor("a1_in", [8, 128, 8, 512], BF16).ap()
        dr["a1_out"] = nc.dram_tensor("a1_out", [8, 128, 8, 512], BF16).ap()
        dr["a2_in"] = nc.dram_tensor("a2_in", [8, 4, 128, 4, 512], BF16).ap()
        dr["a2_out"] = nc.dram_tensor("a2_out", [8, 4, 128, 4, 512], BF16).ap()
    alloc = {"ffn0": alloc_ffn0, "attn0": alloc_attn0,
             "ffn1": alloc_ffn1, "attn1": alloc_attn1}[which]
    with contextlib.ExitStack() as stack:
        P = Pool(nc, stack)
        C = alloc_consts(nc, stack)
        PS = alloc_psums(nc, stack)
        T = alloc(nc, stack, P)
        S = Sems(nc, stack, FUSED_SEMS)
        with nc.Block() as block:
            emit_const_init(nc, block, S, C, dr)
            if which == "ffn0":
                emit_ffn0(nc, block, S, T, C, PS, dr, x_in, y_out)
            elif which == "attn0":
                emit_attn0(nc, block, S, T, C, PS, dr, x_in, y_out)
            elif which == "ffn1":
                emit_ffn1(nc, block, S, T, C, PS, dr, x_in, y_out, copy_rest=True)
            else:
                emit_attn1(nc, block, S, T, C, PS, dr, x_in, y_out)
    return nc


def _shard_x(x, c):
    s = x[:, 512 * c:512 * (c + 1), :].reshape(T_LOC, D)
    return np.ascontiguousarray(s.reshape(NT, 128, D).transpose(1, 0, 2)
                                .reshape(128, NT * D))


def _run_spmd(nc, in_maps):
    res = run_bass_kernel_spmd(nc, in_maps, core_ids=list(range(N_CORES)))
    out = np.empty((4, 4096, D), np.float32)
    for c in range(N_CORES):
        y = np.asarray(res.results[c]["y"]).reshape(128, NT, D) \
            .transpose(1, 0, 2).reshape(T_LOC, D)
        out[:, 512 * c:512 * (c + 1), :] = y.reshape(4, 512, D)
    return out


def run_ffn0(x, wd):
    nc = _build_standalone("ffn0")
    return _run_spmd(nc, [{"x": _shard_x(x, c), **wd} for c in range(N_CORES)])


def run_attn0(x, wd):
    nc = _build_standalone("attn0")
    return _run_spmd(nc, [{"x": _shard_x(x, c), **wd} for c in range(N_CORES)])


def run_ffn1(x, wd, ew):
    nc = _build_standalone("ffn1")
    return _run_spmd(nc, [{"x": _shard_x(x, c), "ew": host_ffn1_ew(ew, c), **wd}
                          for c in range(N_CORES)])


def run_attn1(x, wd):
    nc = _build_standalone("attn1")
    return _run_spmd(nc, [{"x": _shard_x(x, c), **host_attn1_core_inputs(wd, c)}
                          for c in range(N_CORES)])



def build_fused(phases=("f0", "a0", "f0", "f1", "a1", "f1")):
    nc = bass.Bass("TRN2", target_bir_lowering=False)
    dr = {}
    x_in = nc.dram_tensor("x", [128, NT * D], F32, kind="ExternalInput").ap()
    y_out = nc.dram_tensor("y", [128, NT * D], F32, kind="ExternalOutput").ap()
    xbuf = nc.dram_tensor("xbuf", [128, NT * D], F32).ap()

    for n, shp in [("wfc", [8, 32, 128, 8, 128]), ("wpr", [8, 2, 32, 128, 512]),
                   ("bfc", [128, 256]), ("bpr", [8, 1024]), ("gwT", [1024, 8]),
                   ("wqT", [1024, 1024]), ("wkT", [1024, 1024]),
                   ("wvT", [1024, 1024]), ("woT", [1024, 1024]), ("bo", [1, 1024]),
                   ("w1r", [64, 128, 8, 128]), ("w2r", [2, 64, 128, 512]),
                   ("ew", [128, 8]),
                   ("wqc", [1024, 512]), ("wkc", [1024, 512]), ("wvc", [1024, 512]),
                   ("cvq", [128, 16]), ("cvk", [128, 16]), ("cvv", [128, 16]),
                   ("wbc", [128, 16]), ("firs", [128, 12]), ("firl", [128, 124]),
                   ("wgl", [128, 128]), ("bgl", [1, 16]), ("ltemp", [1, 4]),
                   ("rmsw", [128, 2]), ("wo_n", [1024, 1024]),
                   ("masks", [128, 96])]:
        dr[n] = nc.dram_tensor(n, shp, F32, kind="ExternalInput").ap()
    dr["identw"] = nc.dram_tensor("identw", [128, 128], BF16,
                                  kind="ExternalInput").ap()
    dr["kv_cc_in"] = nc.dram_tensor("kv_cc_in", [128, 32 * 130], F32).ap()
    dr["kv_cc_out"] = nc.dram_tensor("kv_cc_out", [128, 32 * 130], F32).ap()
    dr["a1_in"] = nc.dram_tensor("a1_in", [8, 128, 8, 512], BF16).ap()
    dr["a1_out"] = nc.dram_tensor("a1_out", [8, 128, 8, 512], BF16).ap()
    dr["a2_in"] = nc.dram_tensor("a2_in", [8, 4, 128, 4, 512], BF16).ap()
    dr["a2_out"] = nc.dram_tensor("a2_out", [8, 4, 128, 4, 512], BF16).ap()

    with contextlib.ExitStack() as stack:
        P = Pool(nc, stack, fused=True, f32_cols=10848, bf16_cols=72784)
        C = alloc_consts(nc, stack)
        PS = alloc_psums(nc, stack)
        T_f0 = alloc_ffn0(nc, stack, P)
        P.reset()
        T_a0 = alloc_attn0(nc, stack, P)
        P.reset()
        T_f1 = alloc_ffn1(nc, stack, P)
        P.reset()
        T_a1 = alloc_attn1(nc, stack, P)
        P.reset()
        assert P.hw32 <= 10848 and P.hw16 <= 72784, (P.hw32, P.hw16)
        S = Sems(nc, stack, FUSED_SEMS)
        with nc.Block() as block:
            emit_const_init(nc, block, S, C, dr)
            last = len(phases) - 1
            src = x_in
            for idx, ph in enumerate(phases):
                dst = y_out if idx == last else xbuf
                if ph == "f0":
                    emit_ffn0(nc, block, S, T_f0, C, PS, dr, src, dst)
                elif ph == "a0":
                    emit_attn0(nc, block, S, T_a0, C, PS, dr, src, dst)
                elif ph == "f1":
                    emit_ffn1(nc, block, S, T_f1, C, PS, dr, src, dst,
                              copy_rest=(idx == last))
                elif ph == "a1":
                    emit_attn1(nc, block, S, T_a1, C, PS, dr, src, dst)
                src = dst

            @block.sync
            def _(sync):
                sync.wait_ge(S.h["yst"], S.c["yst"])
    return nc


_FUSED_CACHE = {}


def _fused_inmaps(inputs):
    f32 = np.float32
    x = np.ascontiguousarray(np.asarray(inputs["L0_attn_x"], f32))

    wd_ffn0 = host_ffn0_weights(
        np.asarray(inputs["L0_ffn_w_fc"], f32), np.asarray(inputs["L0_ffn_b_fc"], f32),
        np.asarray(inputs["L0_ffn_w_proj"], f32), np.asarray(inputs["L0_ffn_b_proj"], f32),
        np.asarray(inputs["L0_ffn_gate_w"], f32))
    wd_attn0 = host_attn0_weights(
        np.asarray(inputs["L0_attn_wq"], f32), np.asarray(inputs["L0_attn_wk"], f32),
        np.asarray(inputs["L0_attn_wv"], f32), np.asarray(inputs["L0_attn_wo"], f32),
        np.asarray(inputs["L0_attn_bo"], f32))
    wd_ffn1 = host_ffn1_weights(
        np.asarray(inputs["L1_ffn_w1"], f32), np.asarray(inputs["L1_ffn_w2"], f32))
    ew = np.asarray(inputs["L1_ffn_expert_weights"], f32)
    wd_attn1 = host_attn1_weights(
        np.asarray(inputs["L1_attn_Wq"], f32), np.asarray(inputs["L1_attn_Wk"], f32),
        np.asarray(inputs["L1_attn_Wv"], f32), np.asarray(inputs["L1_attn_conv_q"], f32),
        np.asarray(inputs["L1_attn_conv_k"], f32), np.asarray(inputs["L1_attn_conv_v"], f32),
        np.asarray(inputs["L1_attn_Wb"], f32), np.asarray(inputs["L1_attn_fir_short"], f32),
        np.asarray(inputs["L1_attn_fir_long"], f32), np.asarray(inputs["L1_attn_Wg"], f32),
        np.asarray(inputs["L1_attn_bg"], f32), np.asarray(inputs["L1_attn_Wl"], f32),
        np.asarray(inputs["L1_attn_bl"], f32), np.asarray(inputs["L1_attn_log_temp"], f32),
        np.asarray(inputs["L1_attn_rms_w"], f32), np.asarray(inputs["L1_attn_Wo"], f32))

    in_maps = []
    shared = {**{k: v for k, v in wd_ffn0.items()},
              **{k: v for k, v in wd_attn0.items() if k != "identw"},
              **{k: v for k, v in wd_ffn1.items() if k != "identw"}}
    for c in range(N_CORES):
        m = {"x": _shard_x(x, c), "ew": host_ffn1_ew(ew, c), **shared,
             **host_attn1_core_inputs(wd_attn1, c)}
        in_maps.append(m)
    return in_maps


def kernel(**inputs):
    _ = inputs.get("L1_ffn_scores"), inputs.get("L1_ffn_top_experts")
    nc = build_fused()
    return _run_spmd(nc, _fused_inmaps(inputs))


# version 04
# speedup vs baseline: 1.6657x; 1.6639x over previous
import contextlib
import numpy as np
import ml_dtypes
import concourse.bass as bass
import concourse.mybir as mybir
from concourse.bass_utils import run_bass_kernel_spmd
from concourse.aot_env import allow_ici_all_to_all

F32 = mybir.dt.float32
BF16 = mybir.dt.bfloat16
AF = mybir.ActivationFunctionType
ALU = mybir.AluOpType
AX = mybir.AxisListType
BBF16 = ml_dtypes.bfloat16

N_CORES = 8
D = 1024
T_LOC = 2048
NT = 16
NDC = 8
LN_EPS = 1e-5


class NS:
    pass


def apof(t):
    return t.ap() if callable(getattr(t, "ap", None)) else t


class Sems:
    def __init__(self, nc, stack, names):
        self.h = {}
        self.c = {}
        self.b = {}
        for n in names:
            self.h[n] = stack.enter_context(nc.semaphore(n))
            self.c[n] = 0
            self.b[n] = 0

    def inc(self, instr, name, v=1):
        instr.then_inc(self.h[name], v)
        self.c[name] += v
        return self.c[name]

    def wait(self, eng, name, val=None):
        v = self.c[name] if val is None else val
        if v > 0:
            eng.wait_ge(self.h[name], v)

    def step(self, eng, instr, name):
        self.inc(instr, name)
        eng.wait_ge(self.h[name], self.c[name])

    def rebase(self):
        self.b = dict(self.c)

    def waitb(self, eng, name, k):
        if k > 0:
            eng.wait_ge(self.h[name], self.b[name] + k)

    def hb(self, name, k):
        return self.b[name] + k


FUSED_SEMS = [
    "xld", "wld", "wldg", "lnSq", "lnVar", "lnSqrt", "lnRstd", "lnTok", "tr",
    "xnTe", "xbL", "yst",
    "gmm", "glogE", "gateF", "wfcL", "wprL", "wprUse", "mm1d", "geluE",
    "mm2d", "evacE", "wcv", "xbG", "ystM",
    "wfcH0", "wfcH1", "wfcH2", "wfcH3",
    "wprH0", "wprH1", "wprH2", "wprH3", "wprH4", "wprH5", "wprH6", "wprH7",
    "qmm", "qe1", "qe2", "qfE", "kvmm", "ke1", "ke2", "kvE", "kvmmB", "kvE2",
    "cc", "ccb", "kvbf", "amm", "adiv", "atr", "aTe", "omm", "oev",
    "w1L", "w2L", "w2U", "wsF",
    "shA", "ccA", "blkL", "pmm", "pev", "cvD", "silu", "l2q", "l2k", "bsig",
    "kbE", "scanD", "fir", "payT", "payD", "ccB", "cl", "clg", "clv", "cst",
    "trP", "trE", "sAm", "sAe", "sZm", "sZe", "sUm", "sUe", "sSm",
    "mixv", "otr", "resv", "st", "vch", "sch",
]


class Pool:

    def __init__(self, nc, stack, fused=False, f32_cols=0, bf16_cols=0):
        self.nc, self.stack, self.fused = nc, stack, fused
        if fused:
            self.m32 = stack.enter_context(
                nc.sbuf_tensor("pool_f32", [128, f32_cols], F32))
            self.m16 = stack.enter_context(
                nc.sbuf_tensor("pool_b16", [128, bf16_cols], BF16))
            self.o32 = self.o16 = 0
            self.hw32 = self.hw16 = 0

    def reset(self):
        if self.fused:
            self.hw32 = max(self.hw32, self.o32)
            self.hw16 = max(self.hw16, self.o16)
            self.o32 = self.o16 = 0

    def f32(self, name, cols):
        if not self.fused:
            return self.stack.enter_context(
                self.nc.sbuf_tensor(name, [128, cols], F32))
        v = self.m32[:, self.o32:self.o32 + cols]
        self.o32 += cols
        return v

    def bf16(self, name, cols):
        if not self.fused:
            return self.stack.enter_context(
                self.nc.sbuf_tensor(name, [128, cols], BF16))
        v = self.m16[:, self.o16:self.o16 + cols]
        self.o16 += cols
        return v


def alloc_psums(nc, stack):
    PS = NS()
    PS.ptA = stack.enter_context(nc.psum_tensor("ptA", [128, 2048], BF16))
    PS.pf = [stack.enter_context(nc.psum_tensor(f"pf{i}", [128, 512], F32))
             for i in range(6)]
    return PS


def alloc_consts(nc, stack):
    C = NS()
    C.ident = stack.enter_context(nc.sbuf_tensor("c_ident", [128, 128], BF16))
    C.ones_row = stack.enter_context(nc.sbuf_tensor("c_ones_row", [1, 128], BF16))
    C.ones_col = stack.enter_context(nc.sbuf_tensor("c_ones_col", [128, 1], BF16))
    C.masks = stack.enter_context(nc.sbuf_tensor("c_masks", [128, 96], F32))
    C.eps = stack.enter_context(nc.sbuf_tensor("c_eps", [128, 2], F32))
    return C


def emit_const_init(nc, block, S, C, dr):

    @block.sync
    def _(sync):
        S.inc(sync.dma_start(out=C.ident[:], in_=dr["identw"]), "wld", 16)
        if "masks" in dr:
            S.inc(sync.dma_start(out=C.masks[:], in_=dr["masks"]), "wld", 16)

    @block.vector
    def _(vector):
        vector.memset(C.ones_row[:], 1.0)
        vector.memset(C.ones_col[:], 1.0)
        vector.memset(C.eps[:, 0:1], 1e-6)
        i = vector.memset(C.eps[:, 1:2], 1e-5)
        S.inc(i, "wld", 1)


def emit_phase_entry(block, S):
    v = S.c["yst"]
    if v == 0:
        return

    @block.sync
    def _(sync):
        sync.wait_ge(S.h["yst"], v)

    @block.gpsimd
    def _(gpsimd):
        gpsimd.wait_ge(S.h["yst"], v)

    @block.vector
    def _(vector):
        vector.wait_ge(S.h["yst"], v)

    @block.scalar
    def _(scalar):
        scalar.wait_ge(S.h["yst"], v)

    @block.tensor
    def _(tensor):
        tensor.wait_ge(S.h["yst"], v)



def emit_ln_stream(nc, block, S, T, C, PS, x_src, tiles, xnT, xnT_stride):
    tiles = list(tiles)
    NTL = len(tiles)
    st = T.ln_stats
    sums, sumsq = st[:, 0:16], st[:, 16:32]
    mean, rstd, tmp = st[:, 32:48], st[:, 48:64], st[:, 64:80]
    xb, ln_xt, pt = T.xb, T.ln_xt, PS.ptA

    for j, t in enumerate(tiles):
        @block.sync
        def _(sync, j=j, t=t):
            if j == 0:
                S.wait(sync, "yst")
            if j >= 2:
                S.waitb(sync, "lnSq", j - 1)
                S.waitb(sync, "lnVar", j - 1)
            if j > 0:
                sync.wait_ge(S.h["xld"], S.hb("xld", 16 * j))
            S.inc(sync.dma_start(out=xb[:, (j % 2) * D:(j % 2 + 1) * D],
                                 in_=x_src[:, t * D:(t + 1) * D]), "xld", 16)

        @block.vector
        def _(vector, j=j):
            vector.wait_ge(S.h["xld"], S.hb("xld", 16 * (j + 1)))
            i = vector.tensor_reduce(sums[:, j:j + 1],
                                     xb[:, (j % 2) * D:(j % 2 + 1) * D],
                                     AX.X, ALU.add)
            S.inc(i, "lnVar")

        @block.scalar
        def _(scalar, j=j):
            scalar.wait_ge(S.h["lnVar"], S.hb("lnVar", j + 1))
            i = scalar.activation(ln_xt[:, (j % 2) * D:(j % 2 + 1) * D],
                                  xb[:, (j % 2) * D:(j % 2 + 1) * D],
                                  AF.Square, accum_out=sumsq[:, j:j + 1])
            S.inc(i, "lnSq")

    @block.vector
    def _(vector):
        S.waitb(vector, "lnSq", NTL)
        S.step(vector, vector.tensor_scalar(mean[:, 0:NTL], sums[:, 0:NTL],
                                            1.0 / D, None, ALU.mult), "vch")
        S.step(vector, vector.tensor_scalar(tmp[:, 0:NTL], sumsq[:, 0:NTL],
                                            1.0 / D, None, ALU.mult), "vch")
        S.step(vector, vector.tensor_mul(sumsq[:, 0:NTL], mean[:, 0:NTL],
                                         mean[:, 0:NTL]), "vch")
        S.step(vector, vector.tensor_sub(tmp[:, 0:NTL], tmp[:, 0:NTL],
                                         sumsq[:, 0:NTL]), "vch")
        i = vector.tensor_scalar(tmp[:, 0:NTL], tmp[:, 0:NTL], LN_EPS, None, ALU.add)
        S.inc(i, "lnVar")

    @block.scalar
    def _(scalar):
        S.waitb(scalar, "lnVar", NTL + 1)
        S.inc(scalar.sqrt(tmp[:, 0:NTL], tmp[:, 0:NTL]), "lnSqrt")

    @block.vector
    def _(vector):
        S.waitb(vector, "lnSqrt", 1)
        S.step(vector, vector.reciprocal(rstd[:, 0:NTL], tmp[:, 0:NTL]), "vch")
        S.inc(nc.vector.engine_nop(), "lnRstd")

    for j, t in enumerate(tiles):
        @block.sync
        def _(sync, j=j, t=t):
            if j < 2:
                S.waitb(sync, "lnSq", NTL)
                S.waitb(sync, "lnVar", NTL)
            else:
                S.waitb(sync, "lnTok", j - 1)
            sync.wait_ge(S.h["xld"], S.hb("xld", 16 * (NTL + j)))
            S.inc(sync.dma_start(out=xb[:, (j % 2) * D:(j % 2 + 1) * D],
                                 in_=x_src[:, t * D:(t + 1) * D]), "xld", 16)

        @block.vector
        def _(vector, j=j):
            if j == 0:
                S.waitb(vector, "lnRstd", 1)
            vector.wait_ge(S.h["xld"], S.hb("xld", 16 * (NTL + j + 1)))
            i = vector.tensor_scalar(ln_xt[:, (j % 2) * D:(j % 2 + 1) * D],
                                     xb[:, (j % 2) * D:(j % 2 + 1) * D],
                                     mean[:, j:j + 1], rstd[:, j:j + 1],
                                     ALU.subtract, ALU.mult)
            S.inc(i, "lnTok")

        @block.tensor
        def _(tensor, j=j):
            if j == 0:
                S.wait(tensor, "wld")
            S.waitb(tensor, "lnTok", j + 1)
            if j >= 2:
                S.waitb(tensor, "xnTe", j - 1)
            for dc in range(NDC):
                i = tensor.transpose(
                    pt[:, (j % 2) * 1024 + dc * 128:(j % 2) * 1024 + (dc + 1) * 128],
                    ln_xt[:, (j % 2) * D + dc * 128:(j % 2) * D + (dc + 1) * 128],
                    C.ident[:])
            S.inc(i, "tr")

        @block.vector
        def _(vector, j=j):
            S.waitb(vector, "tr", j + 1)
            for dc in range(NDC):
                i = vector.tensor_copy(
                    xnT[:, dc * xnT_stride + j * 128: dc * xnT_stride + j * 128 + 128],
                    pt[:, (j % 2) * 1024 + dc * 128:(j % 2) * 1024 + (dc + 1) * 128])
            S.inc(i, "xnTe")



def alloc_ffn0(nc, stack, P):
    T = NS()
    T.xb = P.f32("f0_xb", 2 * D)
    T.x_blk = P.f32("f0_x_blk", 2 * 4096)
    T.ln_stats = P.f32("f0_ln_stats", 80)
    T.gates = P.f32("f0_gates", NT * 8)
    T.glog = P.f32("f0_glog", NT * 8)
    T.gscr = P.f32("f0_gscr", 8)
    T.gstat = P.f32("f0_gstat", 8)
    T.bfc_sb = P.f32("f0_bfc", 8 * 32)
    T.xnT = P.bf16("f0_xnT", NDC * 2048)
    T.HT = P.bf16("f0_HT", 32 * 512)
    T.wfc_sb = P.bf16("f0_wfc", 4 * 8 * 128)
    T.wpr_sb = P.bf16("f0_wpr", 8 * 512)
    T.gwT_sb = P.bf16("f0_gwT", 8 * 8)
    T.ln_xt = P.bf16("f0_ln_xt", 2 * D)
    T.bpr_sb = stack.enter_context(nc.sbuf_tensor("f0_bpr", [1, 8 * 1024], BF16))
    return T


def emit_ffn0(nc, block, S, T, C, PS, dr, x_src, y_dst, conv=True):
    NSB, NFC = 4, 32
    WFC_RING, WPR_RING = 4, 8
    pm1 = [PS.pf[0], PS.pf[1]]
    pm2 = [PS.pf[2], PS.pf[3], PS.pf[4], PS.pf[5]]
    pg = PS.pf[2][:, 0:8]
    wfc, wpr, bfc, bpr, gwT = dr["wfc"], dr["wpr"], dr["bfc"], dr["bpr"], dr["gwT"]
    wfc_bf, wpr_bf = dr["wfc_bf"], dr["wpr_bf"]
    xnT, HT = T.xnT, T.HT
    wfc_sb, wpr_sb, gwT_sb, bpr_sb, bfc_sb = (T.wfc_sb, T.wpr_sb, T.gwT_sb,
                                              T.bpr_sb, T.bfc_sb)
    gates, glog, gscr, gstat = T.gates, T.glog, T.gscr, T.gstat
    x_blk = T.x_blk

    emit_phase_entry(block, S)
    S.rebase()

    @block.sync
    def _(sync):
        S.inc(sync.dma_start(out=bfc_sb[:], in_=bfc), "wld", 16)

    @block.gpsimd
    def _(gpsimd):
        S.inc(gpsimd.dma_start(out=bpr_sb[:],
                               in_=bpr.rearrange("e d -> (e d)")[None, :]),
              "wldg", 16)
        S.inc(gpsimd.dma_start(
            out=apof(gwT_sb).rearrange("p (dc e) -> p dc e", dc=8),
            in_=gwT.rearrange("(dc p) e -> p dc e", p=128)), "wldg", 16)

    emit_ln_stream(nc, block, S, T, C, PS, x_src, range(NT), xnT, 2048)

    @block.tensor
    def _(tensor):
        S.wait(tensor, "wldg")

    for t in range(NT):
        @block.tensor
        def _(tensor, t=t):
            S.waitb(tensor, "xnTe", t + 1)
            if t >= 1:
                S.waitb(tensor, "glogE", t)
            for dc in range(NDC):
                i = tensor.matmul(pg,
                                  xnT[:, dc * 2048 + t * 128: dc * 2048 + t * 128 + 128],
                                  gwT_sb[:, dc * 8:(dc + 1) * 8],
                                  start=(dc == 0), stop=(dc == NDC - 1))
            S.inc(i, "gmm")

        @block.vector
        def _(vector, t=t):
            S.waitb(vector, "gmm", t + 1)
            S.inc(vector.tensor_copy(glog[:, t * 8:t * 8 + 8], pg), "glogE")

    st5 = T.ln_stats
    m1, m2, nm1, d2, rdh = (st5[:, 64:65], st5[:, 65:66], st5[:, 66:67],
                            st5[:, 67:68], st5[:, 68:69])
    for t in range(NT):
        @block.vector
        def _(vector, t=t):
            S.waitb(vector, "glogE", NT)
            l = glog[:, t * 8:t * 8 + 8]
            S.step(vector, vector.tensor_reduce(m1, l, AX.X, ALU.max), "vch")
            S.step(vector, vector.tensor_single_scalar(gscr[:], l, m1, ALU.is_ge), "vch")
            S.step(vector, vector.tensor_scalar(gscr[:], gscr[:], -1e30, None, ALU.mult), "vch")
            S.step(vector, vector.tensor_add(gscr[:], gscr[:], l), "vch")
            S.step(vector, vector.tensor_reduce(m2, gscr[:], AX.X, ALU.max), "vch")
            S.step(vector, vector.tensor_scalar(nm1, m1, -1.0, None, ALU.mult), "vch")
            i = vector.tensor_single_scalar(gstat[:], l, m2, ALU.is_ge)
            S.inc(i, "gateF")

        @block.scalar
        def _(scalar, t=t):
            S.waitb(scalar, "gateF", 3 * t + 1)
            scalar.activation(gates[:, t * 8:t * 8 + 8], glog[:, t * 8:t * 8 + 8],
                              AF.Exp, bias=nm1)
            i = scalar.activation(d2, m2, AF.Exp, bias=nm1)
            S.inc(i, "gateF")

        @block.vector
        def _(vector, t=t):
            S.waitb(vector, "gateF", 3 * t + 2)
            S.step(vector, vector.tensor_scalar(rdh, d2, 1.0, None, ALU.add), "vch")
            S.step(vector, vector.reciprocal(rdh, rdh), "vch")
            S.step(vector, vector.tensor_scalar(rdh, rdh, 0.5, None, ALU.mult), "vch")
            i = vector.scalar_tensor_tensor(gates[:, t * 8:t * 8 + 8],
                                            gstat[:], rdh,
                                            gates[:, t * 8:t * 8 + 8],
                                            ALU.mult, ALU.mult)
            S.inc(i, "gateF")

    if conv:
        T.wcv_base = S.c["wcv"]

    def emit_conv_chunk(e):
        @block.gpsimd
        def _(gpsimd, e=e):
            for fc in range(NFC):
                S.wait(gpsimd, "wcv")
                S.inc(gpsimd.dma_start(
                    out=wfc_bf[e, fc],
                    in_=wfc[e, fc].rearrange("p dc c -> p (dc c)")), "wcv", 16)
            for dh in range(2):
                for fc in range(NFC):
                    S.wait(gpsimd, "wcv")
                    S.inc(gpsimd.dma_start(
                        out=wpr_bf[e, dh, fc], in_=wpr[e, dh, fc]), "wcv", 16)

    def emit_wchunk(sbe):
        if sbe >= NSB * 8:
            return

        @block.sync
        def _(sync, sbe=sbe):
            e = sbe % 8
            for fc in range(NFC):
                g = sbe * NFC + fc
                S.waitb(sync, "geluE", g - (WFC_RING - 1))
                if conv:
                    sync.wait_ge(S.h["wcv"],
                                 T.wcv_base + 16 * (e * 96 + fc + 1))
                slot = g % WFC_RING
                S.inc(sync.dma_start(
                    out=wfc_sb[:, slot * 1024:(slot + 1) * 1024],
                    in_=wfc_bf[e, fc]), "wfcH%d" % slot, 16)
            for dh in range(2):
                for fc in range(NFC):
                    p = sbe * 64 + dh * NFC + fc
                    S.waitb(sync, "wprUse", p - (WPR_RING - 1))
                    if conv:
                        sync.wait_ge(S.h["wcv"],
                                     T.wcv_base + 16 * (e * 96 + 32 + dh * 32 + fc + 1))
                    slot = p % WPR_RING
                    S.inc(sync.dma_start(
                        out=wpr_sb[:, slot * 512:(slot + 1) * 512],
                        in_=wpr_bf[e, dh, fc]), "wprH%d" % slot, 16)

    for sb in range(NSB):
        par = sb % 2

        @block.gpsimd
        def _(gpsimd, sb=sb, par=par):
            if sb >= 2:
                gpsimd.wait_ge(S.h["ystM"], S.hb("ystM", 16 * (sb - 1)))
            if sb > 0:
                gpsimd.wait_ge(S.h["xbG"], S.hb("xbG", 16 * sb))
            S.inc(gpsimd.dma_start(out=x_blk[:, par * 4096:(par + 1) * 4096],
                                   in_=x_src[:, sb * 4096:(sb + 1) * 4096]),
                  "xbG", 16)

        if sb == 0:
            if conv:
                for e8 in range(8):
                    emit_conv_chunk(e8)
            emit_wchunk(0)

        for e in range(8):
            sbe = sb * 8 + e
            emit_wchunk(sbe + 1)

            @block.tensor
            def _(tensor, sb=sb, e=e, sbe=sbe):
                if e == 0:
                    S.waitb(tensor, "xnTe", 4 * sb + 4)
                for fc in range(NFC):
                    g = sbe * NFC + fc
                    slot = g % WFC_RING
                    tensor.wait_ge(S.h["wfcH%d" % slot],
                                   S.hb("wfcH%d" % slot, 16 * (g // WFC_RING + 1)))
                    if g >= 2:
                        tensor.wait_ge(S.h["geluE"], S.hb("geluE", g - 1))
                    for dc in range(NDC):
                        i = tensor.matmul(
                            pm1[g % 2][:],
                            wfc_sb[:, slot * 1024 + dc * 128: slot * 1024 + (dc + 1) * 128],
                            xnT[:, dc * 2048 + sb * 512: dc * 2048 + (sb + 1) * 512],
                            start=(dc == 0), stop=(dc == NDC - 1))
                    S.inc(i, "mm1d")

            @block.scalar
            def _(scalar, sb=sb, e=e, sbe=sbe):
                if sbe >= 1:
                    S.waitb(scalar, "mm2d", 8 * sbe)
                for fc in range(NFC):
                    g = sbe * NFC + fc
                    scalar.wait_ge(S.h["mm1d"], S.hb("mm1d", g + 1))
                    i = scalar.activation(HT[:, fc * 512:(fc + 1) * 512],
                                          pm1[g % 2][:], AF.Gelu_apprx_tanh,
                                          bias=bfc_sb[:, e * 32 + fc: e * 32 + fc + 1])
                    S.inc(i, "geluE")

            @block.tensor
            def _(tensor, sb=sb, e=e, sbe=sbe):
                tensor.wait_ge(S.h["geluE"], S.hb("geluE", 32 * (sbe + 1)))
                if sbe == 0:
                    tensor.wait_ge(S.h["glogE"], S.hb("glogE", NT))
                for dh in range(2):
                    m_g = sbe * 2 + dh
                    if m_g >= 1:
                        tensor.wait_ge(S.h["evacE"], S.hb("evacE", 4 * m_g))
                    for fc in range(NFC):
                        p = sbe * 64 + dh * NFC + fc
                        slot = p % WPR_RING
                        tensor.wait_ge(S.h["wprH%d" % slot],
                                       S.hb("wprH%d" % slot, 16 * (p // WPR_RING + 1)))
                        for sc in range(4):
                            i = tensor.matmul(
                                pm2[sc][:],
                                HT[:, fc * 512 + sc * 128: fc * 512 + (sc + 1) * 128],
                                wpr_sb[:, slot * 512:(slot + 1) * 512],
                                start=(fc == 0), stop=False)
                        S.inc(i, "wprUse")
                    for sc in range(4):
                        i = tensor.matmul(pm2[sc][:], C.ones_row[:],
                                          bpr_sb[0:1, e * 1024 + dh * 512: e * 1024 + (dh + 1) * 512],
                                          start=False, stop=True)
                        S.inc(i, "mm2d")

            @block.vector
            def _(vector, sb=sb, e=e, sbe=sbe, par=par):
                if sbe == 0:
                    S.waitb(vector, "gateF", 3 * NT)
                if e == 0:
                    vector.wait_ge(S.h["xbG"], S.hb("xbG", 16 * (sb + 1)))
                for dh in range(2):
                    m_g = sbe * 2 + dh
                    for sc in range(4):
                        vector.wait_ge(S.h["mm2d"], S.hb("mm2d", 4 * m_g + sc + 1))
                        t = sb * 4 + sc
                        i = vector.scalar_tensor_tensor(
                            x_blk[:, par * 4096 + sc * D + dh * 512:
                                  par * 4096 + sc * D + (dh + 1) * 512],
                            pm2[sc][:], gates[:, t * 8 + e: t * 8 + e + 1],
                            x_blk[:, par * 4096 + sc * D + dh * 512:
                                  par * 4096 + sc * D + (dh + 1) * 512],
                            ALU.mult, ALU.add)
                        S.inc(i, "evacE")

        @block.sync
        def _(sync, sb=sb, par=par):
            S.waitb(sync, "evacE", 64 * (sb + 1))
            S.inc(sync.dma_start(out=y_dst[:, sb * 4096:(sb + 1) * 4096],
                                 in_=x_blk[:, par * 4096:(par + 1) * 4096]), "yst", 16)
            if sb < 2:
                S.waitb(sync, "yst", 16 * (sb + 1))
                S.inc(sync.dma_start(out=dr["f0scr"], in_=C.eps[0:1, 0:2]),
                      "ystM", 16)

    @block.sync
    def _(sync):
        sync.wait_ge(S.h["yst"], S.c["yst"])


def host_ffn0_weights(w_fc, b_fc, w_proj, b_proj, gate_w):
    wfc = np.ascontiguousarray(
        w_fc.transpose(0, 2, 1).reshape(8, 8, 128, 32, 128).transpose(0, 3, 2, 1, 4))
    wpr = np.ascontiguousarray(
        w_proj.transpose(0, 2, 1).reshape(8, 32, 128, 2, 512).transpose(0, 3, 1, 2, 4))
    bfc = np.ascontiguousarray(b_fc.reshape(8, 32, 128).transpose(2, 0, 1)
                               .reshape(128, 256))
    return {"wfc": wfc, "wpr": wpr, "bfc": bfc,
            "bpr": np.ascontiguousarray(b_proj),
            "gwT": np.ascontiguousarray(gate_w.T),
            "identw": np.eye(128, dtype=BBF16)}



def alloc_attn0(nc, stack, P):
    T = NS()
    T.xb = P.f32("a0_xb", 2 * D)
    T.ln_stats = P.f32("a0_ln_stats", 80)
    T.kv_sb = P.f32("a0_kv_sb", 32 * 130)
    T.dcol = P.f32("a0_dcol", 4)
    T.xnT = P.bf16("a0_xnT", NDC * 2048)
    T.qf = P.bf16("a0_qf", NDC * 2048)
    T.kst = P.bf16("a0_kst", 1024)
    T.vst = P.bf16("a0_vst", 1024)
    T.wbufA = P.bf16("a0_wbufA", 8 * 1024)
    T.wbufB = P.bf16("a0_wbufB", 8 * 1024)
    T.kv_bf = P.bf16("a0_kv_bf", 32 * 130)
    T.astage = P.bf16("a0_astage", 1024)
    T.escr = P.f32("a0_escr", 4 * 512)
    T.ln_xt = P.bf16("a0_ln_xt", 2 * D)
    T.bo_sb = stack.enter_context(nc.sbuf_tensor("a0_bo", [1, 1024], BF16))
    return T


def emit_attn0(nc, block, S, T, C, PS, dr, x_src, y_dst):
    AEPS = 1e-6
    pp = [PS.pf[0], PS.pf[1]]
    pkv = PS.pf[2]
    pks = PS.pf[3][:, 0:8]
    pqk = [PS.pf[4][:, 0:130], PS.pf[5][:, 0:130]]
    kv_cc_in, kv_cc_out = dr["kv_cc_in"], dr["kv_cc_out"]
    xnT, qf, kst, vst = T.xnT, T.qf, T.kst, T.vst
    wbufA, wbufB, kv_sb, kv_bf = T.wbufA, T.wbufB, T.kv_sb, T.kv_bf
    astage, escr, dcol, bo_sb, xb = T.astage, T.escr, T.dcol, T.bo_sb, T.xb

    emit_phase_entry(block, S)

    @block.gpsimd
    def _(gpsimd):
        for dc in range(NDC):
            S.inc(gpsimd.dma_start(
                out=wbufA[:, dc * 1024:(dc + 1) * 1024],
                in_=dr["wqT"][dc * 128:(dc + 1) * 128, :]), "wldg", 16)
        S.inc(gpsimd.dma_start(out=bo_sb[:], in_=dr["bo"]), "wldg", 16)

    S.rebase()

    @block.vector
    def _(vector):
        S.wait(vector, "yst")
        i = vector.memset(kv_sb[:], 0.0)
        S.inc(i, "kvE2")

    emit_ln_stream(nc, block, S, T, C, PS, x_src, range(NT), xnT, 2048)

    nq = 0
    for oc in range(NDC):
        for sb4 in range(4):
            nq += 1

            @block.tensor
            def _(tensor, oc=oc, sb4=sb4, nq=nq):
                if nq == 1:
                    S.wait(tensor, "wldg")
                    S.waitb(tensor, "xnTe", NT)
                if nq > 2:
                    S.waitb(tensor, "qfE", nq - 2)
                for dc in range(NDC):
                    i = tensor.matmul(
                        pp[nq % 2][:],
                        wbufA[:, dc * 1024 + oc * 128: dc * 1024 + (oc + 1) * 128],
                        xnT[:, dc * 2048 + sb4 * 512: dc * 2048 + (sb4 + 1) * 512],
                        start=(dc == 0), stop=(dc == NDC - 1))
                S.inc(i, "qmm")

            @block.vector
            def _(vector, nq=nq):
                S.waitb(vector, "qmm", nq)
                i = vector.tensor_scalar_min(
                    escr[:, (nq % 2) * 512:(nq % 2 + 1) * 512], pp[nq % 2][:], 0.0)
                S.inc(i, "qe1")

            @block.scalar
            def _(scalar, nq=nq):
                S.waitb(scalar, "qe1", nq)
                i = scalar.activation(
                    escr[:, (2 + nq % 2) * 512:(3 + nq % 2) * 512],
                    escr[:, (nq % 2) * 512:(nq % 2 + 1) * 512], AF.Exp)
                S.inc(i, "qe2")

            @block.vector
            def _(vector, oc=oc, sb4=sb4, nq=nq):
                S.waitb(vector, "qe2", nq)
                i = vector.scalar_tensor_tensor(
                    qf[:, oc * 2048 + sb4 * 512: oc * 2048 + (sb4 + 1) * 512],
                    pp[nq % 2][:], 1.0,
                    escr[:, (2 + nq % 2) * 512:(3 + nq % 2) * 512],
                    ALU.add, ALU.max)
                S.inc(i, "qfE")

    @block.gpsimd
    def _(gpsimd):
        gpsimd.wait_ge(S.h["qfE"], S.hb("qfE", 32))
        for dc in range(NDC):
            S.inc(gpsimd.dma_start(
                out=wbufA[:, dc * 1024:(dc + 1) * 1024],
                in_=dr["wkT"][dc * 128:(dc + 1) * 128, :]), "wldg", 16)
        for dc in range(NDC):
            S.inc(gpsimd.dma_start(
                out=wbufB[:, dc * 1024:(dc + 1) * 1024],
                in_=dr["wvT"][dc * 128:(dc + 1) * 128, :]), "wldg", 16)

    nkv = 0
    nke = 0
    for t in range(NT):
        b = t // 4
        for proj in ("k", "v"):
            wb = wbufA if proj == "k" else wbufB
            for half in range(2):
                nkv += 1

                @block.tensor
                def _(tensor, t=t, wb=wb, half=half, nkv=nkv):
                    if nkv <= 2:
                        S.wait(tensor, "wldg")
                        S.waitb(tensor, "qfE", 32)
                    else:
                        S.waitb(tensor, "kvE", nkv - 2)
                    for dc in range(NDC):
                        i = tensor.matmul(
                            pp[nkv % 2][:],
                            xnT[:, dc * 2048 + t * 128: dc * 2048 + (t + 1) * 128],
                            wb[:, dc * 1024 + half * 512: dc * 1024 + (half + 1) * 512],
                            start=(dc == 0), stop=(dc == NDC - 1))
                    S.inc(i, "kvmm")

                if proj == "k":
                    nke += 1

                    @block.vector
                    def _(vector, nkv=nkv):
                        S.waitb(vector, "kvmm", nkv)
                        i = vector.tensor_scalar_min(
                            escr[:, (nkv % 2) * 512:(nkv % 2 + 1) * 512],
                            pp[nkv % 2][:], 0.0)
                        S.inc(i, "ke1")

                    @block.scalar
                    def _(scalar, nkv=nkv, nke=nke):
                        S.waitb(scalar, "ke1", nke)
                        i = scalar.activation(
                            escr[:, (2 + nkv % 2) * 512:(3 + nkv % 2) * 512],
                            escr[:, (nkv % 2) * 512:(nkv % 2 + 1) * 512], AF.Exp)
                        S.inc(i, "ke2")

                    @block.vector
                    def _(vector, nkv=nkv, nke=nke, half=half):
                        S.waitb(vector, "ke2", nke)
                        i = vector.scalar_tensor_tensor(
                            kst[:, half * 512:(half + 1) * 512],
                            pp[nkv % 2][:], 1.0,
                            escr[:, (2 + nkv % 2) * 512:(3 + nkv % 2) * 512],
                            ALU.add, ALU.max)
                        S.inc(i, "kvE")
                else:
                    @block.vector
                    def _(vector, nkv=nkv, half=half):
                        S.waitb(vector, "kvmm", nkv)
                        i = vector.tensor_copy(
                            vst[:, half * 512:(half + 1) * 512], pp[nkv % 2][:])
                        S.inc(i, "kvE")

        @block.tensor
        def _(tensor, t=t, b=b):
            S.wait(tensor, "kvE")
            first = (t % 4 == 0)
            last = (t % 4 == 3)
            if first and t > 0:
                S.waitb(tensor, "kvE2", t // 4 + 1)
            for h in range(16):
                pbase = 64 * (h % 2)
                tensor.matmul(pkv[pbase:pbase + 64, (h // 2) * 64:(h // 2) * 64 + 64],
                              kst[:, h * 64:(h + 1) * 64],
                              vst[:, h * 64:(h + 1) * 64],
                              start=(first and h == 0), stop=last)
            for h in range(16):
                pbase = 64 * (h % 2)
                i = tensor.matmul(pks[pbase:pbase + 64, h // 2:h // 2 + 1],
                                  kst[:, h * 64:(h + 1) * 64],
                                  C.ones_col[:],
                                  start=(first and h == 0), stop=last)
            S.inc(i, "kvmmB")

        if t % 4 == 3:
            @block.vector
            def _(vector, b=b):
                S.waitb(vector, "kvmmB", 4 * (b + 1))
                for h in range(16):
                    pbase = 64 * (h % 2)
                    blk = (b * 8 + h // 2) * 130
                    i = vector.tensor_copy(
                        kv_sb[pbase:pbase + 64, blk + 65 * (h % 2): blk + 65 * (h % 2) + 64],
                        pkv[pbase:pbase + 64, (h // 2) * 64:(h // 2) * 64 + 64])
                for par2 in range(2):
                    pbase = 64 * par2
                    i = vector.tensor_copy(
                        apof(kv_sb).rearrange("p (g c) -> p g c", c=130)[
                            pbase:pbase + 64, b * 8:(b + 1) * 8,
                            64 + 65 * par2:65 + 65 * par2],
                        pks[pbase:pbase + 64, 0:8].rearrange("p (g c) -> p g c", c=1))
                S.inc(i, "kvE2")

    @block.sync
    def _(sync):
        S.wait(sync, "kvE2")
        S.inc(sync.dma_start(out=kv_cc_in, in_=kv_sb[:]), "cc", 16)

    @block.gpsimd
    def _(gpsimd):
        gpsimd.wait_ge(S.h["cc"], S.hb("cc", 16))
        i = gpsimd.collective_compute(
            "AllReduce", ALU.add,
            replica_groups=[list(range(N_CORES))],
            ins=[kv_cc_in.opt()], outs=[kv_cc_out.opt()])
        S.inc(i, "ccb")

    @block.sync
    def _(sync):
        S.wait(sync, "ccb")
        S.inc(sync.dma_start(out=kv_sb[:], in_=kv_cc_out), "cc", 16)

    @block.vector
    def _(vector):
        S.waitb(vector, "cc", 32)
        S.inc(vector.tensor_copy(kv_bf[:], kv_sb[:]), "kvbf")

    na = 0
    for t in range(NT):
        b = t // 4
        for oc in range(NDC):
            na += 1

            @block.tensor
            def _(tensor, t=t, b=b, oc=oc, na=na):
                if na == 1:
                    S.wait(tensor, "kvbf")
                if na > 2:
                    S.waitb(tensor, "adiv", 2 * (na - 2))
                blk = (b * 8 + oc) * 130
                i = tensor.matmul(
                    pqk[na % 2],
                    qf[:, oc * 2048 + t * 128: oc * 2048 + (t + 1) * 128],
                    kv_bf[:, blk: blk + 130],
                    start=True, stop=True)
                S.inc(i, "amm")

            @block.vector
            def _(vector, oc=oc, na=na):
                S.waitb(vector, "amm", na)
                for par2 in range(2):
                    S.step(vector, vector.tensor_copy(
                        dcol[:, 2 * par2:2 * par2 + 1],
                        pqk[na % 2][:, 64 + 65 * par2:65 + 65 * par2]), "vch")
                    S.step(vector, vector.tensor_scalar(
                        dcol[:, 2 * par2:2 * par2 + 1],
                        dcol[:, 2 * par2:2 * par2 + 1], AEPS, None, ALU.add), "vch")
                    S.step(vector, vector.reciprocal(
                        dcol[:, 2 * par2 + 1:2 * par2 + 2],
                        dcol[:, 2 * par2:2 * par2 + 1]), "vch")
                    i = vector.tensor_scalar(
                        astage[:, (2 * oc + par2) * 64:(2 * oc + par2 + 1) * 64],
                        pqk[na % 2][:, 65 * par2:65 * par2 + 64],
                        dcol[:, 2 * par2 + 1:2 * par2 + 2], None, ALU.mult)
                    S.inc(i, "adiv")

        @block.tensor
        def _(tensor, t=t):
            S.waitb(tensor, "adiv", 16 * (t + 1))
            if t >= 2:
                S.waitb(tensor, "aTe", t - 1)
            for dc in range(NDC):
                i = tensor.transpose(
                    PS.ptA[:, (t % 2) * 1024 + dc * 128:(t % 2) * 1024 + (dc + 1) * 128],
                    astage[:, dc * 128:(dc + 1) * 128], C.ident[:])
            S.inc(i, "atr")

        @block.vector
        def _(vector, t=t):
            S.waitb(vector, "atr", t + 1)
            for dc in range(NDC):
                i = vector.tensor_copy(
                    xnT[:, dc * 2048 + t * 128: dc * 2048 + t * 128 + 128],
                    PS.ptA[:, (t % 2) * 1024 + dc * 128:(t % 2) * 1024 + (dc + 1) * 128])
            S.inc(i, "aTe")

    @block.gpsimd
    def _(gpsimd):
        gpsimd.wait_ge(S.h["kvbf"], S.c["kvbf"])
        for dc in range(NDC):
            S.inc(gpsimd.dma_start(
                out=wbufB[:, dc * 1024:(dc + 1) * 1024],
                in_=dr["woT"][dc * 128:(dc + 1) * 128, :]), "wldg", 16)

    no = 0
    for t in range(NT):
        par = t % 2

        @block.sync
        def _(sync, t=t, par=par):
            if t == 0:
                S.waitb(sync, "lnTok", NT)
                S.waitb(sync, "lnSq", NT)
            if t >= 2:
                S.wait(sync, "yst")
            sync.wait_ge(S.h["xld"], S.hb("xld", 16 * (2 * NT + t)))
            S.inc(sync.dma_start(out=xb[:, par * D:(par + 1) * D],
                                 in_=x_src[:, t * D:(t + 1) * D]), "xld", 16)

        for dh in range(2):
            no += 1

            @block.tensor
            def _(tensor, t=t, dh=dh, no=no):
                if no == 1:
                    S.wait(tensor, "wldg")
                S.waitb(tensor, "aTe", t + 1)
                if no > 2:
                    S.waitb(tensor, "oev", no - 2)
                for dc in range(NDC):
                    tensor.matmul(
                        pp[no % 2][:],
                        xnT[:, dc * 2048 + t * 128: dc * 2048 + (t + 1) * 128],
                        wbufB[:, dc * 1024 + dh * 512: dc * 1024 + (dh + 1) * 512],
                        start=(dc == 0), stop=False)
                i = tensor.matmul(pp[no % 2][:], C.ones_row[:],
                                  bo_sb[0:1, dh * 512:(dh + 1) * 512],
                                  start=False, stop=True)
                S.inc(i, "omm")

            @block.vector
            def _(vector, t=t, dh=dh, no=no, par=par):
                S.waitb(vector, "omm", no)
                vector.wait_ge(S.h["xld"], S.hb("xld", 16 * (2 * NT + t + 1)))
                i = vector.tensor_add(
                    xb[:, par * D + dh * 512: par * D + (dh + 1) * 512],
                    xb[:, par * D + dh * 512: par * D + (dh + 1) * 512],
                    pp[no % 2][:])
                S.inc(i, "oev")

        @block.sync
        def _(sync, t=t, par=par):
            S.waitb(sync, "oev", 2 * (t + 1))
            S.inc(sync.dma_start(out=y_dst[:, t * D:(t + 1) * D],
                                 in_=xb[:, par * D:(par + 1) * D]), "yst", 16)

    @block.sync
    def _(sync):
        sync.wait_ge(S.h["yst"], S.c["yst"])


def host_attn0_weights(wq, wk, wv, wo, bo):
    return {"wqT": np.ascontiguousarray(wq.T), "wkT": np.ascontiguousarray(wk.T),
            "wvT": np.ascontiguousarray(wv.T), "woT": np.ascontiguousarray(wo.T),
            "bo": np.ascontiguousarray(bo[None, :]),
            "identw": np.eye(128, dtype=BBF16)}



def alloc_ffn1(nc, stack, P):
    T = NS()
    T.xb = P.f32("f1_xb", 2 * D)
    T.x_blk = P.f32("f1_x_blk", 4096)
    T.ln_stats = P.f32("f1_ln_stats", 80)
    T.ew_sb = P.f32("f1_ew", 8)
    T.ws = P.f32("f1_ws", 4)
    T.xnT1 = P.bf16("f1_xnT1", NDC * 512)
    T.HT1 = P.bf16("f1_HT1", 64 * 512)
    T.w1_sb = P.bf16("f1_w1", 4 * 8 * 128)
    T.w2_sb = P.bf16("f1_w2", 8 * 512)
    T.ln_xt = P.bf16("f1_ln_xt", 2 * D)
    return T


def emit_ffn1(nc, block, S, T, C, PS, dr, x_src, y_dst, copy_rest):
    NFC = 64
    W1_RING, W2_RING = 4, 8
    pm1 = [PS.pf[0], PS.pf[1]]
    pm2 = [PS.pf[2], PS.pf[3], PS.pf[4], PS.pf[5]]
    w1, w2, ew = dr["w1r"], dr["w2r"], dr["ew"]
    xnT1, HT1, w1_sb, w2_sb = T.xnT1, T.HT1, T.w1_sb, T.w2_sb
    ew_sb, ws, x_blk = T.ew_sb, T.ws, T.x_blk

    emit_phase_entry(block, S)
    S.rebase()

    @block.sync
    def _(sync):
        S.wait(sync, "yst")
        S.inc(sync.dma_start(out=ew_sb[:], in_=ew), "wld", 16)
        S.inc(sync.dma_start(out=x_blk[:], in_=x_src[:, 0:4096]), "xbL", 16)
        if copy_rest:
            S.inc(sync.dma_start(out=y_dst[:, 4096:NT * D],
                                 in_=x_src[:, 4096:NT * D]), "yst", 16)

    @block.vector
    def _(vector):
        vector.wait_ge(S.h["wld"], S.c["wld"])
        ewv = apof(ew_sb).rearrange("p (t c) -> p t c", c=2)
        wsv = apof(ws).rearrange("p (t c) -> p t c", c=1)
        S.step(vector, vector.tensor_add(wsv, ewv[:, :, 0:1], ewv[:, :, 1:2]), "vch")
        i = vector.tensor_scalar(ws[:], ws[:], 0.5, None, ALU.mult)
        S.inc(i, "wsF")

    emit_ln_stream(nc, block, S, T, C, PS, x_src, range(4), xnT1, 512)

    @block.gpsimd
    def _(gpsimd):
        for fc in range(NFC):
            S.waitb(gpsimd, "geluE", fc - (W1_RING - 1))
            if fc > 0:
                gpsimd.wait_ge(S.h["w1L"], S.hb("w1L", 16 * fc))
            slot = fc % W1_RING
            S.inc(gpsimd.dma_start(
                out=w1_sb[:, slot * 1024:(slot + 1) * 1024],
                in_=w1[fc].rearrange("p dc c -> p (dc c)")), "w1L", 16)
        for dh in range(2):
            for p in range(dh * NFC, dh * NFC + NFC):
                fc = p - dh * NFC
                S.waitb(gpsimd, "w2U", p - (W2_RING - 1))
                if p > 0:
                    gpsimd.wait_ge(S.h["w2L"], S.hb("w2L", 16 * p))
                slot = p % W2_RING
                S.inc(gpsimd.dma_start(
                    out=w2_sb[:, slot * 512:(slot + 1) * 512],
                    in_=w2[dh, fc]), "w2L", 16)

    for fc in range(NFC):
        @block.tensor
        def _(tensor, fc=fc):
            if fc == 0:
                S.waitb(tensor, "xnTe", 4)
            slot = fc % W1_RING
            tensor.wait_ge(S.h["w1L"], S.hb("w1L", 16 * (fc + 1)))
            if fc >= 2:
                tensor.wait_ge(S.h["geluE"], S.hb("geluE", fc - 1))
            for dc in range(NDC):
                i = tensor.matmul(
                    pm1[fc % 2][:],
                    w1_sb[:, slot * 1024 + dc * 128: slot * 1024 + (dc + 1) * 128],
                    xnT1[:, dc * 512:(dc + 1) * 512],
                    start=(dc == 0), stop=(dc == NDC - 1))
            S.inc(i, "mm1d")

        @block.scalar
        def _(scalar, fc=fc):
            scalar.wait_ge(S.h["mm1d"], S.hb("mm1d", fc + 1))
            i = scalar.activation(HT1[:, fc * 512:(fc + 1) * 512],
                                  pm1[fc % 2][:], AF.Gelu_apprx_tanh)
            S.inc(i, "geluE")

    for dh in range(2):
        @block.tensor
        def _(tensor, dh=dh):
            if dh == 0:
                tensor.wait_ge(S.h["geluE"], S.hb("geluE", NFC))
            else:
                tensor.wait_ge(S.h["evacE"], S.hb("evacE", 4))
            for fc in range(NFC):
                p = dh * NFC + fc
                slot = p % W2_RING
                tensor.wait_ge(S.h["w2L"], S.hb("w2L", 16 * (p + 1)))
                for sc in range(4):
                    i = tensor.matmul(
                        pm2[sc][:],
                        HT1[:, fc * 512 + sc * 128: fc * 512 + (sc + 1) * 128],
                        w2_sb[:, slot * 512:(slot + 1) * 512],
                        start=(fc == 0), stop=(fc == NFC - 1))
                S.inc(i, "w2U")

        @block.vector
        def _(vector, dh=dh):
            vector.wait_ge(S.h["w2U"], S.hb("w2U", NFC * (dh + 1)))
            S.waitb(vector, "wsF", 1)
            if dh == 0:
                vector.wait_ge(S.h["xbL"], S.hb("xbL", 16))
            for sc in range(4):
                i = vector.scalar_tensor_tensor(
                    x_blk[:, sc * D + dh * 512: sc * D + (dh + 1) * 512],
                    pm2[sc][:], ws[:, sc:sc + 1],
                    x_blk[:, sc * D + dh * 512: sc * D + (dh + 1) * 512],
                    ALU.mult, ALU.add)
                S.inc(i, "evacE")

    @block.sync
    def _(sync):
        S.waitb(sync, "evacE", 8)
        S.inc(sync.dma_start(out=y_dst[:, 0:4096], in_=x_blk[:]), "yst", 16)
        sync.wait_ge(S.h["yst"], S.c["yst"])


def host_ffn1_weights(w1, w2):
    w1r = np.ascontiguousarray(w1.reshape(8, 128, 64, 128).transpose(2, 1, 0, 3))
    w2r = np.ascontiguousarray(w2.reshape(64, 128, 2, 512).transpose(2, 0, 1, 3))
    return {"w1r": w1r, "w2r": w2r, "identw": np.eye(128, dtype=BBF16)}


def host_ffn1_ew(expert_weights, c):
    ews = expert_weights[512 * c:512 * (c + 1), :]
    return np.ascontiguousarray(ews.reshape(4, 128, 2).transpose(1, 0, 2)
                                .reshape(128, 8))



def alloc_attn1(nc, stack, P):
    T = NS()
    T.xb = P.f32("a1_xb", 2 * D)
    T.ln_stats = P.f32("a1_ln_stats", 80)
    T.S_f = P.f32("a1_S_f", 512)
    T.bcol = P.f32("a1_bcol", 32)
    T.glt = P.f32("a1_glt", 16)
    T.gsc = P.f32("a1_gsc", 48)
    T.rmsc = P.f32("a1_rmsc", 8)
    T.itemp3 = P.f32("a1_itemp", 12)
    T.cv_sb = P.f32("a1_cv", 48)
    T.fir_sb = P.f32("a1_fir", 4 * 34)
    T.rmsw_sb = P.f32("a1_rmsw", 2)
    T.xnT = P.bf16("a1_xnT", NDC * 2048)
    T.xnBb = P.bf16("a1_xnBb", NDC * 512)
    T.P_scr = P.bf16("a1_Pscr", 2 * 4128)
    T.q_fm = P.bf16("a1_qfm", 2 * 4096)
    T.k_fm = P.bf16("a1_kfm", 2 * 4096)
    T.v_fm = P.bf16("a1_vfm", 2 * 4128)
    T.kb_fm = P.bf16("a1_kbfm", 2 * 4096)
    T.tokst = P.bf16("a1_tokst", 1024)
    T.lsb = P.bf16("a1_lsb", 2 * 512)
    T.llb = P.bf16("a1_llb", 2 * 512)
    T.wch = P.bf16("a1_wch", 8 * 256)
    T.wgl_sb = P.bf16("a1_wgl", 8 * 16)
    T.wb_sb = P.bf16("a1_wb", 8 * 2)
    T.ln_xt = P.bf16("a1_ln_xt", 2 * D)
    T.MT_t = P.bf16("a1_MT", 32)
    T.M_t = P.bf16("a1_M", 32)
    T.M2T_t = P.bf16("a1_M2T", 32)
    T.M2_t = P.bf16("a1_M2", 32)
    T.M4T_t = P.bf16("a1_M4T", 32)
    T.M4_t = P.bf16("a1_M4", 32)
    T.M8T_t = P.bf16("a1_M8T", 32)
    T.M8_t = P.bf16("a1_M8", 32)
    T.A1T_t = P.bf16("a1_A1T", 32)
    T.A2_t = P.bf16("a1_A2", 32)
    T.A4_t = P.bf16("a1_A4", 32)
    T.A8_t = P.bf16("a1_A8", 32)
    T.A16_t = P.bf16("a1_A16", 32)
    T.R_t = P.bf16("a1_R", 64)
    T.TT_t = P.bf16("a1_TT", 32)
    T.AT_t = P.bf16("a1_AT", 32)
    T.u_sb = P.bf16("a1_u", 256)
    T.z_sb = P.bf16("a1_z", 256)
    T.S_bf = P.bf16("a1_Sbf", 512)
    T.bcol_bf = P.bf16("a1_bcolbf", 32)
    T.ostage = P.bf16("a1_ostage", 256)
    T.ot_t = P.bf16("a1_ot", 1024)
    T.mx_t = P.bf16("a1_mx", 1024)
    T.trowb = stack.enter_context(nc.sbuf_tensor("a1_trowb", [1, 4], BF16))
    T.trow = stack.enter_context(nc.sbuf_tensor("a1_trow", [1, 16], F32))
    T.bgl_sb = stack.enter_context(nc.sbuf_tensor("a1_bgl", [1, 16], BF16))
    T.l2row = stack.enter_context(nc.sbuf_tensor("a1_l2row", [1, 512], F32))
    T.l2rowb = stack.enter_context(nc.sbuf_tensor("a1_l2rowb", [1, 512], BF16))
    return T


def emit_attn1(nc, block, S, T, C, PS, dr, x_src, y_dst):
    NBLK, NCH, NGRP = 8, 2, 32
    REPS, L2EPS = 1e-5, 1e-6
    pp = [PS.pf[0], PS.pf[1]]
    pA = PS.pf[2][:, 0:64]
    pO = PS.pf[3][:, 0:256]
    pS = PS.pf[4]
    pB = PS.pf[5][:, 0:32]
    pZ = PS.pf[0]
    pU = PS.pf[1]
    pt = PS.ptA
    a1_in, a1_out = dr["a1_in"], dr["a1_out"]
    a2_in, a2_out = dr["a2_in"], dr["a2_out"]

    xnT, xnBb, P_scr = T.xnT, T.xnBb, T.P_scr
    q_fm, k_fm, v_fm, kb_fm = T.q_fm, T.k_fm, T.v_fm, T.kb_fm
    k_tok = xnT[:, 0:8192]
    vb_tok = xnT[:, 8192:16384]
    wo_sb = P_scr[:, 0:8192]
    tokst, lsb, llb, wch = T.tokst, T.lsb, T.llb, T.wch
    wgl_sb, wb_sb, cv_sb, fir_sb = T.wgl_sb, T.wb_sb, T.cv_sb, T.fir_sb
    rmsw_sb, itemp3, trow, trowb, bgl_sb = (T.rmsw_sb, T.itemp3, T.trow,
                                            T.trowb, T.bgl_sb)
    S_f, S_bf = T.S_f, T.S_bf
    MT_t, M_t, M2T_t, M2_t = T.MT_t, T.M_t, T.M2T_t, T.M2_t
    M4T_t, M4_t, M8T_t, M8_t = T.M4T_t, T.M4_t, T.M8T_t, T.M8_t
    A1T_t, A2_t, A4_t, A8_t, A16_t = T.A1T_t, T.A2_t, T.A4_t, T.A8_t, T.A16_t
    R_t, TT_t, AT_t = T.R_t, T.TT_t, T.AT_t
    u_sb, z_sb, bcol, bcol_bf = T.u_sb, T.z_sb, T.bcol, T.bcol_bf
    l2row, l2rowb = T.l2row, T.l2rowb
    glt, gsc, rmsc, ostage = T.glt, T.gsc, T.rmsc, T.ostage
    ot_t, mx_t, xb = T.ot_t, T.mx_t, T.xb
    mask_su = C.masks[:, 0:32]
    mask_iu = C.masks[:, 32:64]
    eye4 = C.masks[:, 64:96]

    emit_phase_entry(block, S)
    S.rebase()

    @block.sync
    def _(sync):
        S.inc(sync.dma_start(out=cv_sb[:, 0:16], in_=dr["cvq"]), "wld", 16)
        S.inc(sync.dma_start(out=cv_sb[:, 16:32], in_=dr["cvk"]), "wld", 16)
        S.inc(sync.dma_start(out=cv_sb[:, 32:48], in_=dr["cvv"]), "wld", 16)
        S.inc(sync.dma_start(
            out=apof(fir_sb).rearrange("p (c t) -> p c t", t=34)[:, :, 0:3],
            in_=dr["firs"].rearrange("p (c t) -> p c t", t=3)), "wld", 16)
        S.inc(sync.dma_start(
            out=apof(fir_sb).rearrange("p (c t) -> p c t", t=34)[:, :, 3:34],
            in_=dr["firl"].rearrange("p (c t) -> p c t", t=31)), "wld", 16)
        S.inc(sync.dma_start(out=rmsw_sb[:], in_=dr["rmsw"]), "wld", 16)
        S.inc(sync.dma_start(out=trow[0:1, 0:4], in_=dr["ltemp"]), "wld", 16)

    @block.gpsimd
    def _(gpsimd):
        S.inc(gpsimd.dma_start(out=wb_sb[:], in_=dr["wbc"]), "wldg", 16)
        S.inc(gpsimd.dma_start(out=wgl_sb[:], in_=dr["wgl"]), "wldg", 16)
        S.inc(gpsimd.dma_start(out=bgl_sb[:], in_=dr["bgl"]), "wldg", 16)

    @block.vector
    def _(vector):
        vector.memset(v_fm[:], 0.0)
        i = vector.memset(P_scr[:], 0.0)
        S.inc(i, "cvD")

    emit_ln_stream(nc, block, S, T, C, PS, x_src, range(NT), xnT, 2048)

    @block.sync
    def _(sync):
        S.waitb(sync, "xnTe", NT)
        for d in range(8):
            b = d // 2
            S.inc(sync.dma_start(
                out=a1_in[d],
                in_=apof(xnT).rearrange("p (dc s) -> p dc s", s=2048)[
                    :, :, b * 512:(b + 1) * 512]), "shA", 16)

    @block.gpsimd
    def _(gpsimd):
        gpsimd.wait_ge(S.h["shA"], S.hb("shA", 16 * 8))
        with allow_ici_all_to_all():
            i = gpsimd.collective_compute(
                "AllToAll", ALU.bypass,
                replica_groups=[list(range(N_CORES))],
                ins=[a1_in.opt()], outs=[a1_out.opt()])
        S.inc(i, "ccA")

    ctr = {"gp": 0, "tp": 0}

    def pp_group(tensor):
        ctr["gp"] += 1
        g = ctr["gp"]
        if g > 2:
            tensor.wait_ge(S.h["pev"], S.hb("pev", g - 2))
        return pp[g % 2], g

    def pt_half(tensor):
        ctr["tp"] += 1
        n = ctr["tp"]
        if n > 2:
            tensor.wait_ge(S.h["trE"], S.hb("trE", n - 2))
        return pt[:, (n % 2) * 1024:(n % 2 + 1) * 1024], n

    fmbuf = {"q": q_fm, "k": k_fm, "v": v_fm}
    fmoff = {"q": (0, 4096), "k": (0, 4096), "v": (32, 4128)}

    def emit_scan(ch):
        for g in range(NGRP):
            @block.tensor
            def _(tensor, g=g):
                tensor.wait_ge(S.h["sAe"], S.c["sAe"])
                for j in range(4):
                    sp = (4 * g + j) * 32
                    for cc in range(2):
                        tensor.matmul(
                            pA[32 * j:32 * (j + 1), 0:32],
                            k_fm[:, cc * 4096 + sp: cc * 4096 + sp + 32],
                            kb_fm[:, cc * 4096 + sp: cc * 4096 + sp + 32],
                            start=(cc == 0), stop=(cc == 1),
                            tile_position=(0, 32 * j))
                    for cc in range(2):
                        i = tensor.matmul(
                            pA[32 * j:32 * (j + 1), 32:64],
                            k_fm[:, cc * 4096 + sp: cc * 4096 + sp + 32],
                            q_fm[:, cc * 4096 + sp: cc * 4096 + sp + 32],
                            start=(cc == 0), stop=(cc == 1),
                            tile_position=(0, 32 * j))
                S.inc(i, "sAm")

            @block.vector
            def _(vector, g=g):
                vector.wait_ge(S.h["sAm"], S.c["sAm"])
                S.step(vector, vector.tensor_mul(MT_t[:], pA[:, 0:32], mask_su), "vch")
                S.step(vector, vector.tensor_mul(AT_t[:], pA[:, 32:64], mask_iu), "vch")
                S.step(vector, vector.transpose(M_t[:], MT_t[:]), "vch")
                i = vector.tensor_add(A1T_t[:], MT_t[:], eye4)
                S.inc(i, "sAe")

            stages = [
                (M_t, MT_t, M2T_t, M2_t, A2_t),
                (M2_t, M2T_t, M4T_t, M4_t, A4_t),
                (M4_t, M4T_t, M8T_t, M8_t, A8_t),
            ]
            for lhs_t, rhs_t, outT_t, out_t, outA_t in stages:
                @block.tensor
                def _(tensor, lhs_t=lhs_t, rhs_t=rhs_t):
                    tensor.wait_ge(S.h["sAe"], S.c["sAe"])
                    for j in range(4):
                        i = tensor.matmul(
                            pA[32 * j:32 * (j + 1), 0:32],
                            lhs_t[32 * j:32 * (j + 1), :],
                            rhs_t[32 * j:32 * (j + 1), :],
                            start=True, stop=True,
                            tile_position=(32 * j, 32 * j))
                    S.inc(i, "sAm")

                @block.vector
                def _(vector, outT_t=outT_t, out_t=out_t, outA_t=outA_t):
                    vector.wait_ge(S.h["sAm"], S.c["sAm"])
                    S.step(vector, vector.tensor_copy(outT_t[:], pA[:, 0:32]), "vch")
                    S.step(vector, vector.transpose(out_t[:], outT_t[:]), "vch")
                    i = vector.tensor_add(outA_t[:], out_t[:], eye4)
                    S.inc(i, "sAe")

            @block.tensor
            def _(tensor):
                tensor.wait_ge(S.h["sAe"], S.c["sAe"])
                for j in range(4):
                    i = tensor.matmul(
                        pA[32 * j:32 * (j + 1), 0:32],
                        M8_t[32 * j:32 * (j + 1), :],
                        M8T_t[32 * j:32 * (j + 1), :],
                        start=True, stop=True,
                        tile_position=(32 * j, 32 * j))
                S.inc(i, "sAm")

            @block.vector
            def _(vector):
                vector.wait_ge(S.h["sAm"], S.c["sAm"])
                S.step(vector, vector.tensor_add(R_t[:, 32:64], pA[:, 0:32], eye4), "vch")
                i = vector.transpose(A16_t[:], R_t[:, 32:64])
                S.inc(i, "sAe")

            prods = [(A2_t, A1T_t, 0, 0), (A4_t, None, 0, 32),
                     (A8_t, None, 32, 0), (A16_t, None, 0, -1)]
            for lhs_t, rhsA, rhs_off, outoff in prods:
                @block.tensor
                def _(tensor, lhs_t=lhs_t, rhsA=rhsA, rhs_off=rhs_off):
                    tensor.wait_ge(S.h["sAe"], S.c["sAe"])
                    for j in range(4):
                        rhs = (rhsA[32 * j:32 * (j + 1), :] if rhsA is not None
                               else R_t[32 * j:32 * (j + 1), rhs_off:rhs_off + 32])
                        i = tensor.matmul(
                            pA[32 * j:32 * (j + 1), 0:32],
                            lhs_t[32 * j:32 * (j + 1), :], rhs,
                            start=True, stop=True,
                            tile_position=(32 * j, 32 * j))
                    S.inc(i, "sAm")

                @block.vector
                def _(vector, outoff=outoff):
                    vector.wait_ge(S.h["sAm"], S.c["sAm"])
                    if outoff < 0:
                        i = vector.tensor_copy(TT_t[:], pA[:, 0:32])
                    else:
                        i = vector.tensor_copy(R_t[:, outoff:outoff + 32], pA[:, 0:32])
                    S.inc(i, "sAe")

            for j in range(4):
                cidx = 4 * g + j
                w = cidx // 4
                sp = cidx * 32

                @block.tensor
                def _(tensor, g=g, j=j, w=w, sp=sp):
                    tensor.wait_ge(S.h["scanD"], S.c["scanD"])
                    if j == 0:
                        tensor.wait_ge(S.h["oev"], S.c["oev"])
                    for cc in range(2):
                        i = tensor.matmul(
                            pZ[32 * j:32 * (j + 1), 0:256],
                            kb_fm[:, cc * 4096 + sp: cc * 4096 + sp + 32],
                            S_bf[:, cc * 256:(cc + 1) * 256],
                            start=(cc == 0), stop=(cc == 1),
                            tile_position=(0, 32 * j))
                    S.inc(i, "sZm")

                @block.vector
                def _(vector, j=j, w=w):
                    vector.wait_ge(S.h["sZm"], S.c["sZm"])
                    i = vector.tensor_sub(
                        z_sb[32 * j:32 * (j + 1), :],
                        vb_tok[32 * j:32 * (j + 1), w * 256:(w + 1) * 256],
                        pZ[32 * j:32 * (j + 1), 0:256])
                    S.inc(i, "sZe")

                @block.tensor
                def _(tensor, j=j):
                    tensor.wait_ge(S.h["sZe"], S.c["sZe"])
                    i = tensor.matmul(
                        pU[32 * j:32 * (j + 1), 0:256],
                        TT_t[32 * j:32 * (j + 1), :],
                        z_sb[32 * j:32 * (j + 1), :],
                        start=True, stop=True,
                        tile_position=(32 * j, 32 * j))
                    S.inc(i, "sUm")

                @block.vector
                def _(vector, j=j):
                    vector.wait_ge(S.h["sUm"], S.c["sUm"])
                    i = vector.tensor_copy(u_sb[32 * j:32 * (j + 1), :],
                                           pU[32 * j:32 * (j + 1), 0:256])
                    S.inc(i, "sUe")

                @block.tensor
                def _(tensor, g=g, j=j, w=w, sp=sp):
                    tensor.wait_ge(S.h["sUe"], S.c["sUe"])
                    for cc in range(2):
                        tensor.matmul(
                            pO[32 * j:32 * (j + 1), :],
                            q_fm[:, cc * 4096 + sp: cc * 4096 + sp + 32],
                            S_bf[:, cc * 256:(cc + 1) * 256],
                            start=(cc == 0), stop=False,
                            tile_position=(0, 32 * j))
                    tensor.matmul(
                        pO[32 * j:32 * (j + 1), :],
                        AT_t[32 * j:32 * (j + 1), :],
                        u_sb[32 * j:32 * (j + 1), :],
                        start=False, stop=(j == 3),
                        tile_position=(32 * j, 32 * j))
                    for cc in range(2):
                        i = tensor.matmul(
                            pS[:, cc * 256:(cc + 1) * 256],
                            k_tok[32 * j:32 * (j + 1),
                                  w * 256 + cc * 128: w * 256 + (cc + 1) * 128],
                            u_sb[32 * j:32 * (j + 1), :],
                            start=(cc == 0), stop=(cc == 1),
                            tile_position=(32 * j, 0))
                    S.inc(i, "sSm")

                @block.vector
                def _(vector, j=j):
                    vector.wait_ge(S.h["sSm"], S.c["sSm"])
                    S.step(vector, vector.tensor_add(S_f[:], S_f[:], pS[:]), "vch")
                    i = vector.tensor_copy(S_bf[:], S_f[:])
                    S.inc(i, "scanD")

            @block.vector
            def _(vector, g=g):
                vector.wait_ge(S.h["sSm"], S.c["sSm"])
                if S.c["payD"] > 0:
                    vector.wait_ge(S.h["payD"], S.c["payD"])
                i = vector.tensor_copy(ostage[:], pO[:])
                S.inc(i, "oev")

            @block.sync
            def _(sync, ch=ch, g=g):
                sync.wait_ge(S.h["oev"], S.c["oev"])
                if S.c["payD"] > 0:
                    sync.wait_ge(S.h["payD"], S.c["payD"])
                S.inc(sync.dma_start(
                    out=a2_in[g // 4, 0][:, g % 4, ch * 256:(ch + 1) * 256],
                    in_=ostage[:]), "payD", 16)

    def emit_fir(ch):
        for sb8 in range(NBLK):
            @block.vector
            def _(vector, sb8=sb8, ch=ch):
                if sb8 == 0:
                    vector.wait_ge(S.h["oev"], S.c["oev"])
                for cc in range(2):
                    base = cc * 4128 + 32 + sb8 * 512
                    fw = fir_sb[:, (2 * ch + cc) * 34:(2 * ch + cc) * 34 + 34]
                    i = vector.tensor_scalar(
                        lsb[:, cc * 512:(cc + 1) * 512],
                        v_fm[:, base - 2: base - 2 + 512],
                        fw[:, 0:1], None, ALU.mult)
                    for tpp in range(1, 3):
                        i = vector.scalar_tensor_tensor(
                            lsb[:, cc * 512:(cc + 1) * 512],
                            v_fm[:, base - 2 + tpp: base - 2 + tpp + 512],
                            fw[:, tpp:tpp + 1],
                            lsb[:, cc * 512:(cc + 1) * 512], ALU.mult, ALU.add)
                    i = vector.tensor_scalar(
                        llb[:, cc * 512:(cc + 1) * 512],
                        v_fm[:, base - 30: base - 30 + 512],
                        fw[:, 3:4], None, ALU.mult)
                    for tpp in range(1, 31):
                        i = vector.scalar_tensor_tensor(
                            llb[:, cc * 512:(cc + 1) * 512],
                            v_fm[:, base - 30 + tpp: base - 30 + tpp + 512],
                            fw[:, 3 + tpp:4 + tpp],
                            llb[:, cc * 512:(cc + 1) * 512], ALU.mult, ALU.add)
                S.inc(i, "fir")

            for w4 in range(4):
                @block.tensor
                def _(tensor, sb8=sb8, w4=w4):
                    tensor.wait_ge(S.h["fir"], S.c["fir"])
                    pth, n = pt_half(tensor)
                    for cc in range(2):
                        tensor.transpose(
                            pth[:, cc * 128:(cc + 1) * 128],
                            lsb[:, cc * 512 + w4 * 128: cc * 512 + (w4 + 1) * 128],
                            C.ident[:])
                    for cc in range(2):
                        i = tensor.transpose(
                            pth[:, 256 + cc * 128: 256 + (cc + 1) * 128],
                            llb[:, cc * 512 + w4 * 128: cc * 512 + (w4 + 1) * 128],
                            C.ident[:])
                    S.inc(i, "trP")

                @block.vector
                def _(vector, sb8=sb8, w4=w4):
                    n = ctr["tp"]
                    vector.wait_ge(S.h["trP"], S.c["trP"])
                    vector.wait_ge(S.h["payD"], S.c["payD"])
                    pth = pt[:, (n % 2) * 1024:(n % 2 + 1) * 1024]
                    i = vector.tensor_copy(tokst[:, 512:1024], pth[:, 0:512])
                    S.inc(i, "trE")

                @block.sync
                def _(sync, sb8=sb8, w4=w4, ch=ch):
                    sync.wait_ge(S.h["trE"], S.c["trE"])
                    if S.c["payD"] > 0:
                        sync.wait_ge(S.h["payD"], S.c["payD"])
                    S.inc(sync.dma_start(
                        out=a2_in[sb8, 2][:, w4, ch * 256:(ch + 1) * 256],
                        in_=tokst[:, 512:768]), "payD", 16)
                    sync.wait_ge(S.h["payD"], S.c["payD"])
                    S.inc(sync.dma_start(
                        out=a2_in[sb8, 3][:, w4, ch * 256:(ch + 1) * 256],
                        in_=tokst[:, 768:1024]), "payD", 16)

    for ch in range(NCH):
        for pi, pn in enumerate(("q", "k", "v")):
            @block.gpsimd
            def _(gpsimd, pi=pi, ch=ch):
                gpsimd.wait_ge(S.h["pmm"], S.c["pmm"])
                src = (dr["wqc"], dr["wkc"], dr["wvc"])[pi]
                for dc in range(NDC):
                    if S.c["wldg"] > 0:
                        gpsimd.wait_ge(S.h["wldg"], S.c["wldg"])
                    S.inc(gpsimd.dma_start(
                        out=wch[:, dc * 256:(dc + 1) * 256],
                        in_=src[dc * 128:(dc + 1) * 128,
                                ch * 256:(ch + 1) * 256]), "wldg", 16)

            for blk in range(NBLK):
                nblk_g = (ch * 3 + pi) * NBLK + blk

                @block.sync
                def _(sync, blk=blk, nblk_g=nblk_g):
                    if nblk_g == 0:
                        S.waitb(sync, "ccA", 1)
                    if nblk_g >= 1:
                        sync.wait_ge(S.h["pmm"], S.c["pmm"])
                    if S.c["blkL"] > 0:
                        sync.wait_ge(S.h["blkL"], S.c["blkL"])
                    S.inc(sync.dma_start(out=xnBb[:], in_=a1_out[blk]), "blkL", 16)

                for oc in range(2):
                    @block.tensor
                    def _(tensor, pi=pi, blk=blk, oc=oc, ch=ch):
                        tensor.wait_ge(S.h["blkL"], S.c["blkL"])
                        if blk == 0 and oc == 0:
                            tensor.wait_ge(S.h["wldg"], S.c["wldg"])
                        ppt, g = pp_group(tensor)
                        for dc in range(NDC):
                            i = tensor.matmul(
                                ppt[:],
                                wch[:, dc * 256 + oc * 128: dc * 256 + (oc + 1) * 128],
                                xnBb[:, dc * 512:(dc + 1) * 512],
                                start=(dc == 0), stop=(dc == NDC - 1))
                        S.inc(i, "pmm")

                    @block.vector
                    def _(vector, pn=pn, blk=blk, oc=oc):
                        g = ctr["gp"]
                        vector.wait_ge(S.h["pmm"], S.c["pmm"])
                        i = vector.tensor_copy(
                            P_scr[:, oc * 4128 + 32 + blk * 512:
                                  oc * 4128 + 32 + (blk + 1) * 512],
                            pp[g % 2][:])
                        S.inc(i, "pev")

                if pi == 2:
                    @block.tensor
                    def _(tensor, blk=blk, ch=ch):
                        if blk == 0:
                            S.wait(tensor, "bsig")
                        for sc in range(4):
                            w = blk * 4 + sc
                            for dc in range(NDC):
                                i = tensor.matmul(
                                    pB[:, w:w + 1],
                                    xnBb[:, dc * 512 + sc * 128: dc * 512 + (sc + 1) * 128],
                                    wb_sb[:, dc * 2 + ch: dc * 2 + ch + 1],
                                    start=(dc == 0 and w == 0),
                                    stop=(dc == NDC - 1))
                        S.inc(i, "pmm")

            for sb8 in range(NBLK):
                @block.vector
                def _(vector, pi=pi, sb8=sb8, ch=ch):
                    if sb8 == 0:
                        vector.wait_ge(S.h["pev"], S.c["pev"])
                    for cc in range(2):
                        base = cc * 4128 + 32 + sb8 * 512
                        tap = cv_sb[:, pi * 16 + (2 * ch + cc) * 4:
                                    pi * 16 + (2 * ch + cc) * 4 + 4]
                        i = vector.tensor_scalar(
                            lsb[:, cc * 512:(cc + 1) * 512],
                            P_scr[:, base - 3:base - 3 + 512],
                            tap[:, 0:1], None, ALU.mult)
                        for tpp in range(1, 4):
                            i = vector.scalar_tensor_tensor(
                                lsb[:, cc * 512:(cc + 1) * 512],
                                P_scr[:, base - 3 + tpp: base - 3 + tpp + 512],
                                tap[:, tpp:tpp + 1],
                                lsb[:, cc * 512:(cc + 1) * 512],
                                ALU.mult, ALU.add)
                    S.inc(i, "cvD")

                @block.scalar
                def _(scalar, pn=pn, sb8=sb8):
                    scalar.wait_ge(S.h["cvD"], S.c["cvD"])
                    pre, stride = fmoff[pn]
                    for cc in range(2):
                        i = scalar.activation(
                            fmbuf[pn][:, cc * stride + pre + sb8 * 512:
                                      cc * stride + pre + (sb8 + 1) * 512],
                            lsb[:, cc * 512:(cc + 1) * 512], AF.Silu)
                    S.inc(i, "silu")

                @block.vector
                def _(vector):
                    vector.wait_ge(S.h["silu"], S.c["silu"])

        for li, pn in enumerate(("q", "k")):
            for sb8 in range(NBLK):
                @block.scalar
                def _(scalar, pn=pn, sb8=sb8):
                    for cc in range(2):
                        i = scalar.activation(
                            lsb[:, cc * 512:(cc + 1) * 512],
                            fmbuf[pn][:, cc * 4096 + sb8 * 512: cc * 4096 + (sb8 + 1) * 512],
                            AF.Square)
                    S.inc(i, "l2q")

                @block.tensor
                def _(tensor):
                    tensor.wait_ge(S.h["l2q"], S.c["l2q"])
                    ppt, g = pp_group(tensor)
                    for cc in range(2):
                        i = tensor.matmul(ppt[0:1, :], C.ones_col[:],
                                          lsb[:, cc * 512:(cc + 1) * 512],
                                          start=(cc == 0), stop=(cc == 1))
                    S.inc(i, "pmm")

                @block.vector
                def _(vector):
                    g = ctr["gp"]
                    vector.wait_ge(S.h["pmm"], S.c["pmm"])
                    i = vector.tensor_copy(l2row[0:1, :], pp[g % 2][0:1, :])
                    S.inc(i, "pev")

                @block.scalar
                def _(scalar):
                    S.wait(scalar, "pev", S.c["pev"])
                    S.step(scalar, scalar.activation(
                        l2row[0:1, :], l2row[0:1, :], AF.Ln,
                        bias=C.eps[0:1, 0:1]), "sch")
                    i = scalar.activation(l2rowb[0:1, :], l2row[0:1, :],
                                          AF.Exp, scale=-0.5)
                    S.inc(i, "l2k")

                @block.tensor
                def _(tensor):
                    tensor.wait_ge(S.h["l2k"], S.c["l2k"])
                    ppt, g = pp_group(tensor)
                    i = tensor.matmul(ppt[:], C.ones_row[:], l2rowb[0:1, :],
                                      start=True, stop=True)
                    S.inc(i, "pmm")

                @block.vector
                def _(vector, pn=pn, sb8=sb8):
                    g = ctr["gp"]
                    vector.wait_ge(S.h["pmm"], S.c["pmm"])
                    for cc in range(2):
                        i = vector.tensor_mul(
                            fmbuf[pn][:, cc * 4096 + sb8 * 512: cc * 4096 + (sb8 + 1) * 512],
                            fmbuf[pn][:, cc * 4096 + sb8 * 512: cc * 4096 + (sb8 + 1) * 512],
                            pp[g % 2][:])
                    S.inc(i, "pev")

        @block.vector
        def _(vector):
            vector.wait_ge(S.h["pmm"], S.c["pmm"])
            S.step(vector, vector.tensor_copy(bcol[:], pB), "vch")
            S.inc(nc.vector.engine_nop(), "bsig")

        @block.scalar
        def _(scalar):
            S.wait(scalar, "bsig", S.c["bsig"])
            i = scalar.activation(bcol[:], bcol[:], AF.Exp, scale=-1.0)
            S.inc(i, "bsig")

        @block.vector
        def _(vector):
            S.wait(vector, "bsig", S.c["bsig"])
            S.step(vector, vector.tensor_scalar(bcol[:], bcol[:], 1.0, None, ALU.add), "vch")
            S.step(vector, vector.reciprocal(bcol[:], bcol[:]), "vch")
            i = vector.tensor_copy(bcol_bf[:], bcol[:])
            S.inc(i, "bsig")

        for w in range(32):
            @block.tensor
            def _(tensor, w=w):
                if w == 0:
                    S.wait(tensor, "bsig")
                pth, n = pt_half(tensor)
                for cc in range(2):
                    tensor.transpose(
                        pth[:, cc * 128:(cc + 1) * 128],
                        k_fm[:, cc * 4096 + w * 128: cc * 4096 + (w + 1) * 128],
                        C.ident[:])
                for cc in range(2):
                    i = tensor.transpose(
                        pth[:, 256 + cc * 128: 256 + (cc + 1) * 128],
                        v_fm[:, cc * 4128 + 32 + w * 128: cc * 4128 + 32 + (w + 1) * 128],
                        C.ident[:])
                S.inc(i, "trP")

            @block.vector
            def _(vector, w=w):
                n = ctr["tp"]
                vector.wait_ge(S.h["trP"], S.c["trP"])
                if S.c["payD"] > 0:
                    vector.wait_ge(S.h["payD"], S.c["payD"])
                pth = pt[:, (n % 2) * 1024:(n % 2 + 1) * 1024]
                for cc in range(2):
                    vector.tensor_copy(
                        k_tok[:, w * 256 + cc * 128: w * 256 + (cc + 1) * 128],
                        pth[:, cc * 128:(cc + 1) * 128])
                for cc in range(2):
                    vector.tensor_copy(
                        tokst[:, cc * 128:(cc + 1) * 128],
                        pth[:, 256 + cc * 128: 256 + (cc + 1) * 128])
                i = vector.tensor_scalar(
                    vb_tok[:, w * 256:(w + 1) * 256],
                    tokst[:, 0:256], bcol[:, w:w + 1], None, ALU.mult)
                S.inc(i, "trE")

            @block.sync
            def _(sync, w=w, ch=ch):
                sync.wait_ge(S.h["trE"], S.c["trE"])
                if S.c["payD"] > 0:
                    sync.wait_ge(S.h["payD"], S.c["payD"])
                S.inc(sync.dma_start(
                    out=a2_in[w // 4, 1][:, w % 4, ch * 256:(ch + 1) * 256],
                    in_=tokst[:, 0:256]), "payD", 16)

        for w in range(32):
            @block.vector
            def _(vector, w=w):
                if w == 0:
                    vector.wait_ge(S.h["trE"], S.c["trE"])
                i = vector.tensor_scalar(
                    tokst[:, 256:512], k_tok[:, w * 256:(w + 1) * 256],
                    bcol[:, w:w + 1], None, ALU.mult)
                S.inc(i, "kbE")

            @block.tensor
            def _(tensor, w=w):
                tensor.wait_ge(S.h["kbE"], S.c["kbE"])
                pth, n = pt_half(tensor)
                for cc in range(2):
                    i = tensor.transpose(
                        pth[:, cc * 128:(cc + 1) * 128],
                        tokst[:, 256 + cc * 128: 256 + (cc + 1) * 128],
                        C.ident[:])
                S.inc(i, "trP")

            @block.vector
            def _(vector, w=w):
                n = ctr["tp"]
                vector.wait_ge(S.h["trP"], S.c["trP"])
                pth = pt[:, (n % 2) * 1024:(n % 2 + 1) * 1024]
                for cc in range(2):
                    i = vector.tensor_copy(
                        kb_fm[:, cc * 4096 + w * 128: cc * 4096 + (w + 1) * 128],
                        pth[:, cc * 128:(cc + 1) * 128])
                S.inc(i, "trE")

        @block.vector
        def _(vector):
            vector.wait_ge(S.h["trE"], S.c["trE"])
            vector.memset(S_f[:], 0.0)
            i = vector.memset(S_bf[:], 0.0)
            S.inc(i, "scanD")

        emit_scan(ch)
        emit_fir(ch)

    @block.gpsimd
    def _(gpsimd):
        gpsimd.wait_ge(S.h["payD"], S.c["payD"])
        with allow_ici_all_to_all():
            i = gpsimd.collective_compute(
                "AllToAll", ALU.bypass,
                replica_groups=[list(range(N_CORES))],
                ins=[a2_in.opt()], outs=[a2_out.opt()])
        S.inc(i, "ccB")

    @block.gpsimd
    def _(gpsimd):
        gpsimd.wait_ge(S.h["silu"], S.c["silu"])
        for dc in range(NDC):
            gpsimd.wait_ge(S.h["clg"], S.hb("clg", 16 * dc)) if dc > 0 else None
            S.inc(gpsimd.dma_start(
                out=wo_sb[:, dc * 1024:(dc + 1) * 1024],
                in_=dr["wo_n"][dc * 128:(dc + 1) * 128, :]), "clg", 16)

    @block.vector
    def _(vector):
        vector.wait_ge(S.h["clg"], S.c["clg"])
        for dc in range(NDC):
            i = vector.tensor_scalar(
                wo_sb[:, dc * 1024:(dc + 1) * 1024],
                wo_sb[:, dc * 1024:(dc + 1) * 1024],
                rmsw_sb[:, dc % 2: dc % 2 + 1], None, ALU.mult)
            S.inc(i, "clv")

    @block.scalar
    def _(scalar):
        S.wait(scalar, "wld")
        i = scalar.activation(trow[0:1, 4:8], trow[0:1, 0:4], AF.Exp)
        S.inc(i, "cst")

    @block.vector
    def _(vector):
        S.waitb(vector, "cst", 1)
        S.step(vector, vector.reciprocal(trow[0:1, 8:12], trow[0:1, 4:8]), "vch")
        i = vector.tensor_copy(trowb[0:1, 0:4], trow[0:1, 8:12])
        S.inc(i, "cst")

    @block.tensor
    def _(tensor):
        S.waitb(tensor, "cst", 2)
        tensor.wait_ge(S.h["scanD"], S.c["scanD"])
        tensor.wait_ge(S.h["sUe"], S.c["sUe"])
        ppt, g = pp_group(tensor)
        i = tensor.matmul(ppt[:, 0:4], C.ones_row[:], trowb[0:1, 0:4],
                          start=True, stop=True)
        S.inc(i, "pmm")

    @block.vector
    def _(vector):
        g = ctr["gp"]
        vector.wait_ge(S.h["pmm"], S.c["pmm"])
        i = vector.tensor_copy(itemp3[:, 0:4], pp[g % 2][:, 0:4])
        S.inc(i, "pev")

    for t in range(NT):
        b, ti, par = t // 4, t % 4, t % 2
        ct_od = k_tok[:, par * 4096:par * 4096 + 1024]
        ct_v = k_tok[:, par * 4096 + 1024:par * 4096 + 2048]
        ct_ls = k_tok[:, par * 4096 + 2048:par * 4096 + 3072]
        ct_ll = k_tok[:, par * 4096 + 3072:par * 4096 + 4096]
        ct_xn = vb_tok[:, par * 1024:(par + 1) * 1024]

        @block.sync
        def _(sync, t=t, b=b, ti=ti, par=par, ct_od=ct_od, ct_v=ct_v,
              ct_ls=ct_ls, ct_ll=ct_ll, ct_xn=ct_xn):
            if t == 0:
                S.wait(sync, "ccB")
            if t >= 2:
                sync.wait_ge(S.h["pev"], S.c["pev"])
                S.wait(sync, "yst")
            for half in range(2):
                src = 2 * b + half
                for tix, buf in ((0, ct_od), (1, ct_v), (2, ct_ls), (3, ct_ll)):
                    S.inc(sync.dma_start(
                        out=buf[:, half * 512:(half + 1) * 512],
                        in_=a2_out[src, tix][:, ti, :]), "cl", 16)
            S.inc(sync.dma_start(
                out=ct_xn.rearrange("p (dc s) -> p dc s", s=128),
                in_=a1_in[2 * b][:, :, ti * 128:(ti + 1) * 128]), "cl", 16)
            S.inc(sync.dma_start(out=xb[:, par * D:(par + 1) * D],
                                 in_=x_src[:, t * D:(t + 1) * D]), "cl", 16)

        @block.tensor
        def _(tensor, t=t, ct_xn=ct_xn):
            tensor.wait_ge(S.h["cl"], S.c["cl"])
            ppt, g = pp_group(tensor)
            for dc in range(NDC):
                tensor.matmul(ppt[:, 0:16],
                              ct_xn[:, dc * 128:(dc + 1) * 128],
                              wgl_sb[:, dc * 16:(dc + 1) * 16],
                              start=(dc == 0), stop=False)
            i = tensor.matmul(ppt[:, 0:16], C.ones_row[:], bgl_sb[0:1, 0:16],
                              start=False, stop=True)
            S.inc(i, "gmm")

        @block.vector
        def _(vector, t=t):
            g = ctr["gp"]
            vector.wait_ge(S.h["gmm"], S.c["gmm"])
            i = vector.tensor_copy(glt[:], pp[g % 2][:, 0:16])
            S.inc(i, "pev")

        @block.scalar
        def _(scalar, t=t):
            S.wait(scalar, "pev", S.c["pev"])
            i = scalar.activation(gsc[:, 0:4], glt[:, 0:4], AF.Exp, scale=-1.0)
            S.inc(i, "cst")

        @block.vector
        def _(vector, t=t):
            S.wait(vector, "cst", S.c["cst"])
            S.step(vector, vector.tensor_scalar(gsc[:, 0:4], gsc[:, 0:4],
                                                1.0, None, ALU.add), "vch")
            S.step(vector, vector.reciprocal(gsc[:, 0:4], gsc[:, 0:4]), "vch")
            for h in range(4):
                S.step(vector, vector.tensor_scalar(
                    gsc[:, 12 + 3 * h:15 + 3 * h], glt[:, 4 + 3 * h:7 + 3 * h],
                    itemp3[:, h:h + 1], None, ALU.mult), "vch")
            S.step(vector, vector.tensor_reduce(
                gsc[:, 24:28].rearrange("p (h c) -> p h c", c=1),
                gsc[:, 12:24].rearrange("p (h c) -> p h c", c=3),
                AX.X, ALU.max), "vch")
            for j3 in range(3):
                S.step(vector, vector.tensor_copy(
                    gsc[:, 28:40].rearrange("p (h c) -> p h c", c=3)[:, :, j3:j3 + 1],
                    gsc[:, 24:28].rearrange("p (h c) -> p h c", c=1)), "vch")
            i = vector.tensor_sub(gsc[:, 12:24], gsc[:, 12:24], gsc[:, 28:40])
            S.inc(i, "mixv")

        @block.scalar
        def _(scalar, t=t):
            S.wait(scalar, "mixv", S.c["mixv"])
            i = scalar.activation(gsc[:, 28:40], gsc[:, 12:24], AF.Exp)
            S.inc(i, "cst")

        @block.vector
        def _(vector, t=t, ct_od=ct_od, ct_v=ct_v, ct_ls=ct_ls, ct_ll=ct_ll):
            S.wait(vector, "cst", S.c["cst"])
            S.step(vector, vector.tensor_reduce(
                gsc[:, 24:28].rearrange("p (h c) -> p h c", c=1),
                gsc[:, 28:40].rearrange("p (h c) -> p h c", c=3),
                AX.X, ALU.add), "vch")
            S.step(vector, vector.reciprocal(gsc[:, 24:28], gsc[:, 24:28]), "vch")
            for j3 in range(3):
                S.step(vector, vector.tensor_copy(
                    gsc[:, 12:24].rearrange("p (h c) -> p h c", c=3)[:, :, j3:j3 + 1],
                    gsc[:, 24:28].rearrange("p (h c) -> p h c", c=1)), "vch")
            S.step(vector, vector.tensor_mul(gsc[:, 12:24], gsc[:, 12:24],
                                             gsc[:, 28:40]), "vch")
            for h in range(4):
                p0 = gsc[:, 12 + 3 * h:13 + 3 * h]
                p1 = gsc[:, 13 + 3 * h:14 + 3 * h]
                p2 = gsc[:, 14 + 3 * h:15 + 3 * h]
                sl = slice(h * 256, (h + 1) * 256)
                vector.tensor_scalar(mx_t[:, sl], ct_v[:, sl], p0, None, ALU.mult)
                vector.scalar_tensor_tensor(mx_t[:, sl], ct_ls[:, sl], p1,
                                            mx_t[:, sl], ALU.mult, ALU.add)
                vector.scalar_tensor_tensor(mx_t[:, sl], ct_ll[:, sl], p2,
                                            mx_t[:, sl], ALU.mult, ALU.add)
                vector.tensor_sub(ot_t[:, sl], ct_od[:, sl], mx_t[:, sl])
                i = vector.scalar_tensor_tensor(ot_t[:, sl], ot_t[:, sl],
                                                gsc[:, h:h + 1],
                                                mx_t[:, sl], ALU.mult, ALU.add)
            S.inc(i, "mixv")

        @block.scalar
        def _(scalar, t=t, par=par):
            S.wait(scalar, "mixv", S.c["mixv"])
            for h in range(4):
                scalar.activation(T.ln_xt[:, par * D + h * 256: par * D + (h + 1) * 256],
                                  ot_t[:, h * 256:(h + 1) * 256],
                                  AF.Square, accum_out=rmsc[:, h:h + 1])
            S.step(scalar, scalar.activation(rmsc[:, 4:8], rmsc[:, 0:4],
                                             AF.Ln, bias=C.eps[:, 1:2],
                                             scale=1.0 / 256), "sch")
            i = scalar.activation(rmsc[:, 0:4], rmsc[:, 4:8], AF.Exp, scale=-0.5)
            S.inc(i, "cst")

        @block.vector
        def _(vector, t=t):
            S.wait(vector, "cst", S.c["cst"])
            for h in range(4):
                i = vector.tensor_scalar(ot_t[:, h * 256:(h + 1) * 256],
                                         ot_t[:, h * 256:(h + 1) * 256],
                                         rmsc[:, h:h + 1], None, ALU.mult)
            S.inc(i, "otr")

        @block.tensor
        def _(tensor, t=t):
            tensor.wait_ge(S.h["otr"], S.c["otr"])
            pth, n = pt_half(tensor)
            for dc in range(NDC):
                i = tensor.transpose(pth[:, dc * 128:(dc + 1) * 128],
                                     ot_t[:, dc * 128:(dc + 1) * 128], C.ident[:])
            S.inc(i, "trP")

        @block.vector
        def _(vector, t=t):
            n = ctr["tp"]
            vector.wait_ge(S.h["trP"], S.c["trP"])
            vector.wait_ge(S.h["omm"], S.c["omm"])
            pth = pt[:, (n % 2) * 1024:(n % 2 + 1) * 1024]
            for dc in range(NDC):
                i = vector.tensor_copy(mx_t[:, dc * 128:(dc + 1) * 128],
                                       pth[:, dc * 128:(dc + 1) * 128])
            S.inc(i, "trE")

        for dh in range(2):
            @block.tensor
            def _(tensor, t=t, dh=dh):
                tensor.wait_ge(S.h["trE"], S.c["trE"])
                if dh == 0 and t == 0:
                    tensor.wait_ge(S.h["clv"], S.c["clv"])
                ppt, g = pp_group(tensor)
                for dc in range(NDC):
                    i = tensor.matmul(
                        ppt[:],
                        mx_t[:, dc * 128:(dc + 1) * 128],
                        wo_sb[:, dc * 1024 + dh * 512: dc * 1024 + (dh + 1) * 512],
                        start=(dc == 0), stop=(dc == NDC - 1))
                S.inc(i, "omm")

            @block.vector
            def _(vector, t=t, dh=dh, par=par):
                g = ctr["gp"]
                vector.wait_ge(S.h["omm"], S.c["omm"])
                i = vector.tensor_add(
                    xb[:, par * D + dh * 512: par * D + (dh + 1) * 512],
                    xb[:, par * D + dh * 512: par * D + (dh + 1) * 512],
                    pp[g % 2][:])
                S.inc(i, "pev")

        @block.sync
        def _(sync, t=t, par=par):
            sync.wait_ge(S.h["pev"], S.c["pev"])
            S.inc(sync.dma_start(out=y_dst[:, t * D:(t + 1) * D],
                                 in_=xb[:, par * D:(par + 1) * D]), "yst", 16)

    @block.sync
    def _(sync):
        sync.wait_ge(S.h["yst"], S.c["yst"])


def host_attn1_weights(Wq, Wk, Wv, conv_q, conv_k, conv_v, Wb, fir_short,
                       fir_long, Wg, bg, Wl, bl, log_temp, rms_w, Wo):
    masks = np.zeros((128, 96), np.float32)
    su = -np.triu(np.ones((32, 32), np.float32), 1)
    iu = np.triu(np.ones((32, 32), np.float32), 0)
    ey = np.eye(32, dtype=np.float32)
    for j in range(4):
        masks[32 * j:32 * (j + 1), 0:32] = su
        masks[32 * j:32 * (j + 1), 32:64] = iu
        masks[32 * j:32 * (j + 1), 64:96] = ey
    wgl_np = np.concatenate([Wg, Wl], axis=1)
    return {
        "wgl": np.ascontiguousarray(wgl_np.reshape(8, 128, 16).transpose(1, 0, 2)
                                    .reshape(128, 128)),
        "bgl": np.ascontiguousarray(np.concatenate([bg, bl])[None, :]),
        "ltemp": np.ascontiguousarray(log_temp[None, :]),
        "rmsw": np.ascontiguousarray(rms_w.reshape(2, 128).T),
        "wo_n": np.ascontiguousarray(Wo),
        "masks": masks,
        "identw": np.eye(128, dtype=BBF16),
        "wbc_full": Wb, "wq_full": Wq, "wk_full": Wk, "wv_full": Wv,
        "cvq_full": conv_q, "cvk_full": conv_k, "cvv_full": conv_v,
        "firs_full": fir_short, "firl_full": fir_long,
    }


def host_attn1_core_inputs(wd, c):
    h0 = 2 * (c % 2)
    cs, ce = h0 * 256, (h0 + 2) * 256
    out = {k: v for k, v in wd.items() if not k.endswith("_full")}
    out["wqc"] = np.ascontiguousarray(wd["wq_full"][:, cs:ce])
    out["wkc"] = np.ascontiguousarray(wd["wk_full"][:, cs:ce])
    out["wvc"] = np.ascontiguousarray(wd["wv_full"][:, cs:ce])
    for name, key in (("cvq", "cvq_full"), ("cvk", "cvk_full"), ("cvv", "cvv_full")):
        sl = wd[key][cs:ce]
        out[name] = np.ascontiguousarray(sl.reshape(4, 128, 4).transpose(1, 0, 2)
                                         .reshape(128, 16))
    out["wbc"] = np.ascontiguousarray(
        wd["wbc_full"][:, h0:h0 + 2].reshape(8, 128, 2).transpose(1, 0, 2)
        .reshape(128, 16))
    fs = wd["firs_full"][h0:h0 + 2].reshape(512, 3)
    fl = wd["firl_full"][h0:h0 + 2].reshape(512, 31)
    out["firs"] = np.ascontiguousarray(fs.reshape(4, 128, 3).transpose(1, 0, 2)
                                       .reshape(128, 12))
    out["firl"] = np.ascontiguousarray(fl.reshape(4, 128, 31).transpose(1, 0, 2)
                                       .reshape(128, 124))
    return out



def _build_standalone(which):
    nc = bass.Bass("TRN2", target_bir_lowering=False)
    x_in = nc.dram_tensor("x", [128, NT * D], F32, kind="ExternalInput").ap()
    y_out = nc.dram_tensor("y", [128, NT * D], F32, kind="ExternalOutput").ap()
    shapes = {
        "ffn0": [("wfc", [8, 32, 128, 8, 128]), ("wpr", [8, 2, 32, 128, 512]),
                 ("bfc", [128, 256]), ("bpr", [8, 1024]), ("gwT", [1024, 8])],
        "attn0": [("wqT", [1024, 1024]), ("wkT", [1024, 1024]),
                  ("wvT", [1024, 1024]), ("woT", [1024, 1024]), ("bo", [1, 1024])],
        "ffn1": [("w1r", [64, 128, 8, 128]), ("w2r", [2, 64, 128, 512]),
                 ("ew", [128, 8])],
        "attn1": [("wqc", [1024, 512]), ("wkc", [1024, 512]), ("wvc", [1024, 512]),
                  ("cvq", [128, 16]), ("cvk", [128, 16]), ("cvv", [128, 16]),
                  ("wbc", [128, 16]), ("firs", [128, 12]), ("firl", [128, 124]),
                  ("wgl", [128, 128]), ("bgl", [1, 16]), ("ltemp", [1, 4]),
                  ("rmsw", [128, 2]), ("wo_n", [1024, 1024]), ("masks", [128, 96])],
    }
    dr = {n: nc.dram_tensor(n, shp, F32, kind="ExternalInput").ap()
          for n, shp in shapes[which]}
    dr["identw"] = nc.dram_tensor("identw", [128, 128], BF16,
                                  kind="ExternalInput").ap()
    if which == "ffn0":
        dr["wfc_bf"] = nc.dram_tensor("wfc_bf", [8, 32, 128, 1024], BF16).ap()
        dr["wpr_bf"] = nc.dram_tensor("wpr_bf", [8, 2, 32, 128, 512], BF16).ap()
        dr["f0scr"] = nc.dram_tensor("f0scr", [1, 2], F32).ap()
    if which == "attn0":
        dr["kv_cc_in"] = nc.dram_tensor("kv_cc_in", [128, 32 * 130], F32).ap()
        dr["kv_cc_out"] = nc.dram_tensor("kv_cc_out", [128, 32 * 130], F32).ap()
    if which == "attn1":
        dr["a1_in"] = nc.dram_tensor("a1_in", [8, 128, 8, 512], BF16).ap()
        dr["a1_out"] = nc.dram_tensor("a1_out", [8, 128, 8, 512], BF16).ap()
        dr["a2_in"] = nc.dram_tensor("a2_in", [8, 4, 128, 4, 512], BF16).ap()
        dr["a2_out"] = nc.dram_tensor("a2_out", [8, 4, 128, 4, 512], BF16).ap()
    alloc = {"ffn0": alloc_ffn0, "attn0": alloc_attn0,
             "ffn1": alloc_ffn1, "attn1": alloc_attn1}[which]
    with contextlib.ExitStack() as stack:
        P = Pool(nc, stack)
        C = alloc_consts(nc, stack)
        PS = alloc_psums(nc, stack)
        T = alloc(nc, stack, P)
        S = Sems(nc, stack, FUSED_SEMS)
        with nc.Block() as block:
            emit_const_init(nc, block, S, C, dr)
            if which == "ffn0":
                emit_ffn0(nc, block, S, T, C, PS, dr, x_in, y_out)
            elif which == "attn0":
                emit_attn0(nc, block, S, T, C, PS, dr, x_in, y_out)
            elif which == "ffn1":
                emit_ffn1(nc, block, S, T, C, PS, dr, x_in, y_out, copy_rest=True)
            else:
                emit_attn1(nc, block, S, T, C, PS, dr, x_in, y_out)
    return nc


def _shard_x(x, c):
    s = x[:, 512 * c:512 * (c + 1), :].reshape(T_LOC, D)
    return np.ascontiguousarray(s.reshape(NT, 128, D).transpose(1, 0, 2)
                                .reshape(128, NT * D))


def _run_spmd(nc, in_maps):
    res = run_bass_kernel_spmd(nc, in_maps, core_ids=list(range(N_CORES)))
    out = np.empty((4, 4096, D), np.float32)
    for c in range(N_CORES):
        y = np.asarray(res.results[c]["y"]).reshape(128, NT, D) \
            .transpose(1, 0, 2).reshape(T_LOC, D)
        out[:, 512 * c:512 * (c + 1), :] = y.reshape(4, 512, D)
    return out


def run_ffn0(x, wd):
    nc = _build_standalone("ffn0")
    return _run_spmd(nc, [{"x": _shard_x(x, c), **wd} for c in range(N_CORES)])


def run_attn0(x, wd):
    nc = _build_standalone("attn0")
    return _run_spmd(nc, [{"x": _shard_x(x, c), **wd} for c in range(N_CORES)])


def run_ffn1(x, wd, ew):
    nc = _build_standalone("ffn1")
    return _run_spmd(nc, [{"x": _shard_x(x, c), "ew": host_ffn1_ew(ew, c), **wd}
                          for c in range(N_CORES)])


def run_attn1(x, wd):
    nc = _build_standalone("attn1")
    return _run_spmd(nc, [{"x": _shard_x(x, c), **host_attn1_core_inputs(wd, c)}
                          for c in range(N_CORES)])



def build_fused(phases=("f0", "a0", "f0", "f1", "a1", "f1")):
    nc = bass.Bass("TRN2", target_bir_lowering=False)
    dr = {}
    x_in = nc.dram_tensor("x", [128, NT * D], F32, kind="ExternalInput").ap()
    y_out = nc.dram_tensor("y", [128, NT * D], F32, kind="ExternalOutput").ap()
    xbuf = nc.dram_tensor("xbuf", [128, NT * D], F32).ap()

    for n, shp in [("wfc", [8, 32, 128, 8, 128]), ("wpr", [8, 2, 32, 128, 512]),
                   ("bfc", [128, 256]), ("bpr", [8, 1024]), ("gwT", [1024, 8]),
                   ("wqT", [1024, 1024]), ("wkT", [1024, 1024]),
                   ("wvT", [1024, 1024]), ("woT", [1024, 1024]), ("bo", [1, 1024]),
                   ("w1r", [64, 128, 8, 128]), ("w2r", [2, 64, 128, 512]),
                   ("ew", [128, 8]),
                   ("wqc", [1024, 512]), ("wkc", [1024, 512]), ("wvc", [1024, 512]),
                   ("cvq", [128, 16]), ("cvk", [128, 16]), ("cvv", [128, 16]),
                   ("wbc", [128, 16]), ("firs", [128, 12]), ("firl", [128, 124]),
                   ("wgl", [128, 128]), ("bgl", [1, 16]), ("ltemp", [1, 4]),
                   ("rmsw", [128, 2]), ("wo_n", [1024, 1024]),
                   ("masks", [128, 96])]:
        dr[n] = nc.dram_tensor(n, shp, F32, kind="ExternalInput").ap()
    dr["identw"] = nc.dram_tensor("identw", [128, 128], BF16,
                                  kind="ExternalInput").ap()
    dr["wfc_bf"] = nc.dram_tensor("wfc_bf", [8, 32, 128, 1024], BF16).ap()
    dr["wpr_bf"] = nc.dram_tensor("wpr_bf", [8, 2, 32, 128, 512], BF16).ap()
    dr["f0scr"] = nc.dram_tensor("f0scr", [1, 2], F32).ap()
    dr["kv_cc_in"] = nc.dram_tensor("kv_cc_in", [128, 32 * 130], F32).ap()
    dr["kv_cc_out"] = nc.dram_tensor("kv_cc_out", [128, 32 * 130], F32).ap()
    dr["a1_in"] = nc.dram_tensor("a1_in", [8, 128, 8, 512], BF16).ap()
    dr["a1_out"] = nc.dram_tensor("a1_out", [8, 128, 8, 512], BF16).ap()
    dr["a2_in"] = nc.dram_tensor("a2_in", [8, 4, 128, 4, 512], BF16).ap()
    dr["a2_out"] = nc.dram_tensor("a2_out", [8, 4, 128, 4, 512], BF16).ap()

    with contextlib.ExitStack() as stack:
        P = Pool(nc, stack, fused=True, f32_cols=10848, bf16_cols=72784)
        C = alloc_consts(nc, stack)
        PS = alloc_psums(nc, stack)
        T_f0 = alloc_ffn0(nc, stack, P)
        P.reset()
        T_a0 = alloc_attn0(nc, stack, P)
        P.reset()
        T_f1 = alloc_ffn1(nc, stack, P)
        P.reset()
        T_a1 = alloc_attn1(nc, stack, P)
        P.reset()
        assert P.hw32 <= 10848 and P.hw16 <= 72784, (P.hw32, P.hw16)
        S = Sems(nc, stack, FUSED_SEMS)
        with nc.Block() as block:
            emit_const_init(nc, block, S, C, dr)
            last = len(phases) - 1
            src = x_in
            f0_seen = False
            for idx, ph in enumerate(phases):
                dst = y_out if idx == last else xbuf
                if ph == "f0":
                    emit_ffn0(nc, block, S, T_f0, C, PS, dr, src, dst,
                              conv=not f0_seen)
                    f0_seen = True
                elif ph == "a0":
                    emit_attn0(nc, block, S, T_a0, C, PS, dr, src, dst)
                elif ph == "f1":
                    emit_ffn1(nc, block, S, T_f1, C, PS, dr, src, dst,
                              copy_rest=(idx == last))
                elif ph == "a1":
                    emit_attn1(nc, block, S, T_a1, C, PS, dr, src, dst)
                src = dst

            @block.sync
            def _(sync):
                sync.wait_ge(S.h["yst"], S.c["yst"])
    return nc


_FUSED_CACHE = {}


def _fused_inmaps(inputs):
    f32 = np.float32
    x = np.ascontiguousarray(np.asarray(inputs["L0_attn_x"], f32))

    wd_ffn0 = host_ffn0_weights(
        np.asarray(inputs["L0_ffn_w_fc"], f32), np.asarray(inputs["L0_ffn_b_fc"], f32),
        np.asarray(inputs["L0_ffn_w_proj"], f32), np.asarray(inputs["L0_ffn_b_proj"], f32),
        np.asarray(inputs["L0_ffn_gate_w"], f32))
    wd_attn0 = host_attn0_weights(
        np.asarray(inputs["L0_attn_wq"], f32), np.asarray(inputs["L0_attn_wk"], f32),
        np.asarray(inputs["L0_attn_wv"], f32), np.asarray(inputs["L0_attn_wo"], f32),
        np.asarray(inputs["L0_attn_bo"], f32))
    wd_ffn1 = host_ffn1_weights(
        np.asarray(inputs["L1_ffn_w1"], f32), np.asarray(inputs["L1_ffn_w2"], f32))
    ew = np.asarray(inputs["L1_ffn_expert_weights"], f32)
    wd_attn1 = host_attn1_weights(
        np.asarray(inputs["L1_attn_Wq"], f32), np.asarray(inputs["L1_attn_Wk"], f32),
        np.asarray(inputs["L1_attn_Wv"], f32), np.asarray(inputs["L1_attn_conv_q"], f32),
        np.asarray(inputs["L1_attn_conv_k"], f32), np.asarray(inputs["L1_attn_conv_v"], f32),
        np.asarray(inputs["L1_attn_Wb"], f32), np.asarray(inputs["L1_attn_fir_short"], f32),
        np.asarray(inputs["L1_attn_fir_long"], f32), np.asarray(inputs["L1_attn_Wg"], f32),
        np.asarray(inputs["L1_attn_bg"], f32), np.asarray(inputs["L1_attn_Wl"], f32),
        np.asarray(inputs["L1_attn_bl"], f32), np.asarray(inputs["L1_attn_log_temp"], f32),
        np.asarray(inputs["L1_attn_rms_w"], f32), np.asarray(inputs["L1_attn_Wo"], f32))

    in_maps = []
    shared = {**{k: v for k, v in wd_ffn0.items()},
              **{k: v for k, v in wd_attn0.items() if k != "identw"},
              **{k: v for k, v in wd_ffn1.items() if k != "identw"}}
    for c in range(N_CORES):
        m = {"x": _shard_x(x, c), "ew": host_ffn1_ew(ew, c), **shared,
             **host_attn1_core_inputs(wd_attn1, c)}
        in_maps.append(m)
    return in_maps


def kernel(**inputs):
    _ = inputs.get("L1_ffn_scores"), inputs.get("L1_ffn_top_experts")
    nc = build_fused()
    return _run_spmd(nc, _fused_inmaps(inputs))


# version 05 (final)
# speedup vs baseline: 1.7015x; 1.0215x over previous
import contextlib
import numpy as np
import ml_dtypes
import concourse.bass as bass
import concourse.mybir as mybir
from concourse.bass_utils import run_bass_kernel_spmd
from concourse.aot_env import allow_ici_all_to_all

F32 = mybir.dt.float32
BF16 = mybir.dt.bfloat16
AF = mybir.ActivationFunctionType
ALU = mybir.AluOpType
AX = mybir.AxisListType
BBF16 = ml_dtypes.bfloat16

N_CORES = 8
D = 1024
T_LOC = 2048
NT = 16
NDC = 8
LN_EPS = 1e-5


class NS:
    pass


def apof(t):
    return t.ap() if callable(getattr(t, "ap", None)) else t


class Sems:
    def __init__(self, nc, stack, names):
        self.h = {}
        self.c = {}
        self.b = {}
        for n in names:
            self.h[n] = stack.enter_context(nc.semaphore(n))
            self.c[n] = 0
            self.b[n] = 0

    def inc(self, instr, name, v=1):
        instr.then_inc(self.h[name], v)
        self.c[name] += v
        return self.c[name]

    def wait(self, eng, name, val=None):
        v = self.c[name] if val is None else val
        if v > 0:
            eng.wait_ge(self.h[name], v)

    def step(self, eng, instr, name):
        self.inc(instr, name)
        eng.wait_ge(self.h[name], self.c[name])

    def rebase(self):
        self.b = dict(self.c)

    def waitb(self, eng, name, k):
        if k > 0:
            eng.wait_ge(self.h[name], self.b[name] + k)

    def hb(self, name, k):
        return self.b[name] + k


FUSED_SEMS = [
    "xld", "wld", "wldg", "lnSq", "lnVar", "lnSqrt", "lnRstd", "lnTok", "tr",
    "xnTe", "xbL", "yst",
    "gmm", "glogE", "gateF", "wfcL", "wprL", "wprUse", "mm1d", "geluE",
    "mm2d", "evacE", "wcv", "xbG", "ystM",
    "wfcH0", "wfcH1", "wfcH2", "wfcH3",
    "wprH0", "wprH1", "wprH2", "wprH3", "wprH4", "wprH5", "wprH6", "wprH7",
    "qmm", "qe1", "qe2", "qfE", "kvmm", "ke1", "ke2", "kvE", "kvmmB", "kvE2",
    "cc", "ccb", "kvbf", "amm", "adiv", "atr", "aTe", "omm", "oev",
    "w1L", "w2L", "w2U", "wsF",
    "shA", "ccA", "blkL", "pmm", "pev", "cvD", "silu", "l2q", "l2k", "bsig",
    "kbE", "scanD", "fir", "payT", "payD", "ccB", "cl", "clg", "clv", "cst",
    "trP", "trE", "sAm", "sAe", "sZm", "sZe", "sUm", "sUe", "sSm",
    "mixv", "otr", "resv", "st", "vch", "sch",
]


class Pool:

    def __init__(self, nc, stack, fused=False, f32_cols=0, bf16_cols=0):
        self.nc, self.stack, self.fused = nc, stack, fused
        if fused:
            self.m32 = stack.enter_context(
                nc.sbuf_tensor("pool_f32", [128, f32_cols], F32))
            self.m16 = stack.enter_context(
                nc.sbuf_tensor("pool_b16", [128, bf16_cols], BF16))
            self.o32 = self.o16 = 0
            self.hw32 = self.hw16 = 0

    def reset(self):
        if self.fused:
            self.hw32 = max(self.hw32, self.o32)
            self.hw16 = max(self.hw16, self.o16)
            self.o32 = self.o16 = 0

    def f32(self, name, cols):
        if not self.fused:
            return self.stack.enter_context(
                self.nc.sbuf_tensor(name, [128, cols], F32))
        v = self.m32[:, self.o32:self.o32 + cols]
        self.o32 += cols
        return v

    def bf16(self, name, cols):
        if not self.fused:
            return self.stack.enter_context(
                self.nc.sbuf_tensor(name, [128, cols], BF16))
        v = self.m16[:, self.o16:self.o16 + cols]
        self.o16 += cols
        return v


def alloc_psums(nc, stack):
    PS = NS()
    PS.ptA = stack.enter_context(nc.psum_tensor("ptA", [128, 2048], BF16))
    PS.pf = [stack.enter_context(nc.psum_tensor(f"pf{i}", [128, 512], F32))
             for i in range(6)]
    return PS


def alloc_consts(nc, stack):
    C = NS()
    C.ident = stack.enter_context(nc.sbuf_tensor("c_ident", [128, 128], BF16))
    C.ones_row = stack.enter_context(nc.sbuf_tensor("c_ones_row", [1, 128], BF16))
    C.ones_col = stack.enter_context(nc.sbuf_tensor("c_ones_col", [128, 1], BF16))
    C.masks = stack.enter_context(nc.sbuf_tensor("c_masks", [128, 96], F32))
    C.eps = stack.enter_context(nc.sbuf_tensor("c_eps", [128, 2], F32))
    return C


def emit_const_init(nc, block, S, C, dr):

    @block.sync
    def _(sync):
        S.inc(sync.dma_start(out=C.ident[:], in_=dr["identw"]), "wld", 16)
        if "masks" in dr:
            S.inc(sync.dma_start(out=C.masks[:], in_=dr["masks"]), "wld", 16)

    @block.vector
    def _(vector):
        vector.memset(C.ones_row[:], 1.0)
        vector.memset(C.ones_col[:], 1.0)
        vector.memset(C.eps[:, 0:1], 1e-6)
        i = vector.memset(C.eps[:, 1:2], 1e-5)
        S.inc(i, "wld", 1)


def emit_phase_entry(block, S):
    v = S.c["yst"]
    if v == 0:
        return

    @block.sync
    def _(sync):
        sync.wait_ge(S.h["yst"], v)

    @block.gpsimd
    def _(gpsimd):
        gpsimd.wait_ge(S.h["yst"], v)

    @block.vector
    def _(vector):
        vector.wait_ge(S.h["yst"], v)

    @block.scalar
    def _(scalar):
        scalar.wait_ge(S.h["yst"], v)

    @block.tensor
    def _(tensor):
        tensor.wait_ge(S.h["yst"], v)



def emit_ln_stream(nc, block, S, T, C, PS, x_src, tiles, xnT, xnT_stride):
    tiles = list(tiles)
    NTL = len(tiles)
    st = T.ln_stats
    sums, sumsq = st[:, 0:16], st[:, 16:32]
    mean, rstd, tmp = st[:, 32:48], st[:, 48:64], st[:, 64:80]
    xb, ln_xt, pt = T.xb, T.ln_xt, PS.ptA

    for j, t in enumerate(tiles):
        @block.sync
        def _(sync, j=j, t=t):
            if j == 0:
                S.wait(sync, "yst")
            if j >= 2:
                S.waitb(sync, "lnSq", j - 1)
                S.waitb(sync, "lnVar", j - 1)
            if j > 0:
                sync.wait_ge(S.h["xld"], S.hb("xld", 16 * j))
            S.inc(sync.dma_start(out=xb[:, (j % 2) * D:(j % 2 + 1) * D],
                                 in_=x_src[:, t * D:(t + 1) * D]), "xld", 16)

        @block.vector
        def _(vector, j=j):
            vector.wait_ge(S.h["xld"], S.hb("xld", 16 * (j + 1)))
            i = vector.tensor_reduce(sums[:, j:j + 1],
                                     xb[:, (j % 2) * D:(j % 2 + 1) * D],
                                     AX.X, ALU.add)
            S.inc(i, "lnVar")

        @block.scalar
        def _(scalar, j=j):
            scalar.wait_ge(S.h["lnVar"], S.hb("lnVar", j + 1))
            i = scalar.activation(ln_xt[:, (j % 2) * D:(j % 2 + 1) * D],
                                  xb[:, (j % 2) * D:(j % 2 + 1) * D],
                                  AF.Square, accum_out=sumsq[:, j:j + 1])
            S.inc(i, "lnSq")

    @block.vector
    def _(vector):
        S.waitb(vector, "lnSq", NTL)
        S.step(vector, vector.tensor_scalar(mean[:, 0:NTL], sums[:, 0:NTL],
                                            1.0 / D, None, ALU.mult), "vch")
        S.step(vector, vector.tensor_scalar(tmp[:, 0:NTL], sumsq[:, 0:NTL],
                                            1.0 / D, None, ALU.mult), "vch")
        S.step(vector, vector.tensor_mul(sumsq[:, 0:NTL], mean[:, 0:NTL],
                                         mean[:, 0:NTL]), "vch")
        S.step(vector, vector.tensor_sub(tmp[:, 0:NTL], tmp[:, 0:NTL],
                                         sumsq[:, 0:NTL]), "vch")
        i = vector.tensor_scalar(tmp[:, 0:NTL], tmp[:, 0:NTL], LN_EPS, None, ALU.add)
        S.inc(i, "lnVar")

    @block.scalar
    def _(scalar):
        S.waitb(scalar, "lnVar", NTL + 1)
        S.inc(scalar.sqrt(tmp[:, 0:NTL], tmp[:, 0:NTL]), "lnSqrt")

    @block.vector
    def _(vector):
        S.waitb(vector, "lnSqrt", 1)
        S.step(vector, vector.reciprocal(rstd[:, 0:NTL], tmp[:, 0:NTL]), "vch")
        S.inc(nc.vector.engine_nop(), "lnRstd")

    for j, t in enumerate(tiles):
        @block.sync
        def _(sync, j=j, t=t):
            if j < 2:
                S.waitb(sync, "lnSq", NTL)
                S.waitb(sync, "lnVar", NTL)
            else:
                S.waitb(sync, "lnTok", j - 1)
            sync.wait_ge(S.h["xld"], S.hb("xld", 16 * (NTL + j)))
            S.inc(sync.dma_start(out=xb[:, (j % 2) * D:(j % 2 + 1) * D],
                                 in_=x_src[:, t * D:(t + 1) * D]), "xld", 16)

        @block.vector
        def _(vector, j=j):
            if j == 0:
                S.waitb(vector, "lnRstd", 1)
            vector.wait_ge(S.h["xld"], S.hb("xld", 16 * (NTL + j + 1)))
            i = vector.tensor_scalar(ln_xt[:, (j % 2) * D:(j % 2 + 1) * D],
                                     xb[:, (j % 2) * D:(j % 2 + 1) * D],
                                     mean[:, j:j + 1], rstd[:, j:j + 1],
                                     ALU.subtract, ALU.mult)
            S.inc(i, "lnTok")

        @block.tensor
        def _(tensor, j=j):
            if j == 0:
                S.wait(tensor, "wld")
            S.waitb(tensor, "lnTok", j + 1)
            if j >= 2:
                S.waitb(tensor, "xnTe", j - 1)
            for dc in range(NDC):
                i = tensor.transpose(
                    pt[:, (j % 2) * 1024 + dc * 128:(j % 2) * 1024 + (dc + 1) * 128],
                    ln_xt[:, (j % 2) * D + dc * 128:(j % 2) * D + (dc + 1) * 128],
                    C.ident[:])
            S.inc(i, "tr")

        @block.vector
        def _(vector, j=j):
            S.waitb(vector, "tr", j + 1)
            for dc in range(NDC):
                i = vector.tensor_copy(
                    xnT[:, dc * xnT_stride + j * 128: dc * xnT_stride + j * 128 + 128],
                    pt[:, (j % 2) * 1024 + dc * 128:(j % 2) * 1024 + (dc + 1) * 128])
            S.inc(i, "xnTe")



def alloc_ffn0(nc, stack, P):
    T = NS()
    T.xb = P.f32("f0_xb", 2 * D)
    T.x_blk = P.f32("f0_x_blk", 2 * 4096)
    T.ln_stats = P.f32("f0_ln_stats", 80)
    T.gates = P.f32("f0_gates", NT * 8)
    T.glog = P.f32("f0_glog", NT * 8)
    T.gscr = P.f32("f0_gscr", 8)
    T.gstat = P.f32("f0_gstat", 8)
    T.bfc_sb = P.f32("f0_bfc", 8 * 32)
    T.xnT = P.bf16("f0_xnT", NDC * 2048)
    T.HT = P.bf16("f0_HT", 32 * 512)
    T.wfc_sb = P.bf16("f0_wfc", 4 * 8 * 128)
    T.wpr_sb = P.bf16("f0_wpr", 8 * 512)
    T.gwT_sb = P.bf16("f0_gwT", 8 * 8)
    T.ln_xt = P.bf16("f0_ln_xt", 2 * D)
    T.bpr_sb = stack.enter_context(nc.sbuf_tensor("f0_bpr", [1, 8 * 1024], BF16))
    return T


def emit_ffn0(nc, block, S, T, C, PS, dr, x_src, y_dst, conv=True):
    NSB, NFC = 4, 32
    WFC_RING, WPR_RING = 4, 8
    pm1 = [PS.pf[0], PS.pf[1]]
    pm2 = [PS.pf[2], PS.pf[3], PS.pf[4], PS.pf[5]]
    pg = PS.pf[2][:, 0:8]
    wfc, wpr, bfc, bpr, gwT = dr["wfc"], dr["wpr"], dr["bfc"], dr["bpr"], dr["gwT"]
    wfc_bf, wpr_bf = dr["wfc_bf"], dr["wpr_bf"]
    xnT, HT = T.xnT, T.HT
    wfc_sb, wpr_sb, gwT_sb, bpr_sb, bfc_sb = (T.wfc_sb, T.wpr_sb, T.gwT_sb,
                                              T.bpr_sb, T.bfc_sb)
    gates, glog, gscr, gstat = T.gates, T.glog, T.gscr, T.gstat
    x_blk = T.x_blk

    emit_phase_entry(block, S)
    S.rebase()

    @block.sync
    def _(sync):
        S.inc(sync.dma_start(out=bfc_sb[:], in_=bfc), "wld", 16)

    @block.gpsimd
    def _(gpsimd):
        S.inc(gpsimd.dma_start(out=bpr_sb[:],
                               in_=bpr.rearrange("e d -> (e d)")[None, :]),
              "wldg", 16)
        S.inc(gpsimd.dma_start(
            out=apof(gwT_sb).rearrange("p (dc e) -> p dc e", dc=8),
            in_=gwT.rearrange("(dc p) e -> p dc e", p=128)), "wldg", 16)

    emit_ln_stream(nc, block, S, T, C, PS, x_src, range(NT), xnT, 2048)

    @block.tensor
    def _(tensor):
        S.wait(tensor, "wldg")

    for t in range(NT):
        @block.tensor
        def _(tensor, t=t):
            S.waitb(tensor, "xnTe", t + 1)
            if t >= 1:
                S.waitb(tensor, "glogE", t)
            for dc in range(NDC):
                i = tensor.matmul(pg,
                                  xnT[:, dc * 2048 + t * 128: dc * 2048 + t * 128 + 128],
                                  gwT_sb[:, dc * 8:(dc + 1) * 8],
                                  start=(dc == 0), stop=(dc == NDC - 1))
            S.inc(i, "gmm")

        @block.vector
        def _(vector, t=t):
            S.waitb(vector, "gmm", t + 1)
            S.inc(vector.tensor_copy(glog[:, t * 8:t * 8 + 8], pg), "glogE")

    st5 = T.ln_stats
    m1, m2, nm1, d2, rdh = (st5[:, 64:65], st5[:, 65:66], st5[:, 66:67],
                            st5[:, 67:68], st5[:, 68:69])
    for t in range(NT):
        @block.vector
        def _(vector, t=t):
            S.waitb(vector, "glogE", NT)
            l = glog[:, t * 8:t * 8 + 8]
            S.step(vector, vector.tensor_reduce(m1, l, AX.X, ALU.max), "vch")
            S.step(vector, vector.tensor_single_scalar(gscr[:], l, m1, ALU.is_ge), "vch")
            S.step(vector, vector.tensor_scalar(gscr[:], gscr[:], -1e30, None, ALU.mult), "vch")
            S.step(vector, vector.tensor_add(gscr[:], gscr[:], l), "vch")
            S.step(vector, vector.tensor_reduce(m2, gscr[:], AX.X, ALU.max), "vch")
            S.step(vector, vector.tensor_scalar(nm1, m1, -1.0, None, ALU.mult), "vch")
            i = vector.tensor_single_scalar(gstat[:], l, m2, ALU.is_ge)
            S.inc(i, "gateF")

        @block.scalar
        def _(scalar, t=t):
            S.waitb(scalar, "gateF", 3 * t + 1)
            scalar.activation(gates[:, t * 8:t * 8 + 8], glog[:, t * 8:t * 8 + 8],
                              AF.Exp, bias=nm1)
            i = scalar.activation(d2, m2, AF.Exp, bias=nm1)
            S.inc(i, "gateF")

        @block.vector
        def _(vector, t=t):
            S.waitb(vector, "gateF", 3 * t + 2)
            S.step(vector, vector.tensor_scalar(rdh, d2, 1.0, None, ALU.add), "vch")
            S.step(vector, vector.reciprocal(rdh, rdh), "vch")
            S.step(vector, vector.tensor_scalar(rdh, rdh, 0.5, None, ALU.mult), "vch")
            i = vector.scalar_tensor_tensor(gates[:, t * 8:t * 8 + 8],
                                            gstat[:], rdh,
                                            gates[:, t * 8:t * 8 + 8],
                                            ALU.mult, ALU.mult)
            S.inc(i, "gateF")

    if conv:
        T.wcv_base = S.c["wcv"]

    def emit_conv_chunk(e):
        @block.gpsimd
        def _(gpsimd, e=e):
            for fc in range(NFC):
                S.wait(gpsimd, "wcv")
                S.inc(gpsimd.dma_start(
                    out=wfc_bf[e, fc],
                    in_=wfc[e, fc].rearrange("p dc c -> p (dc c)")), "wcv", 16)
            for dh in range(2):
                for fc in range(NFC):
                    S.wait(gpsimd, "wcv")
                    S.inc(gpsimd.dma_start(
                        out=wpr_bf[e, dh, fc], in_=wpr[e, dh, fc]), "wcv", 16)

    def emit_wchunk(sbe):
        if sbe >= NSB * 8:
            return

        @block.sync
        def _(sync, sbe=sbe):
            e = sbe % 8
            for fc in range(NFC):
                g = sbe * NFC + fc
                S.waitb(sync, "geluE", g - (WFC_RING - 1))
                if conv:
                    sync.wait_ge(S.h["wcv"],
                                 T.wcv_base + 16 * (e * 96 + fc + 1))
                slot = g % WFC_RING
                S.inc(sync.dma_start(
                    out=wfc_sb[:, slot * 1024:(slot + 1) * 1024],
                    in_=wfc_bf[e, fc]), "wfcH%d" % slot, 16)
            for dh in range(2):
                for fc in range(NFC):
                    p = sbe * 64 + dh * NFC + fc
                    S.waitb(sync, "wprUse", p - (WPR_RING - 1))
                    if conv:
                        sync.wait_ge(S.h["wcv"],
                                     T.wcv_base + 16 * (e * 96 + 32 + dh * 32 + fc + 1))
                    slot = p % WPR_RING
                    S.inc(sync.dma_start(
                        out=wpr_sb[:, slot * 512:(slot + 1) * 512],
                        in_=wpr_bf[e, dh, fc]), "wprH%d" % slot, 16)

    for sb in range(NSB):
        par = sb % 2

        @block.gpsimd
        def _(gpsimd, sb=sb, par=par):
            if sb >= 2:
                gpsimd.wait_ge(S.h["ystM"], S.hb("ystM", 16 * (sb - 1)))
            if sb > 0:
                gpsimd.wait_ge(S.h["xbG"], S.hb("xbG", 16 * sb))
            S.inc(gpsimd.dma_start(out=x_blk[:, par * 4096:(par + 1) * 4096],
                                   in_=x_src[:, sb * 4096:(sb + 1) * 4096]),
                  "xbG", 16)

        if sb == 0:
            if conv:
                for e8 in range(8):
                    emit_conv_chunk(e8)
            emit_wchunk(0)

        for e in range(8):
            sbe = sb * 8 + e
            emit_wchunk(sbe + 1)

            @block.tensor
            def _(tensor, sb=sb, e=e, sbe=sbe):
                if e == 0:
                    S.waitb(tensor, "xnTe", 4 * sb + 4)
                for fc in range(NFC):
                    g = sbe * NFC + fc
                    slot = g % WFC_RING
                    tensor.wait_ge(S.h["wfcH%d" % slot],
                                   S.hb("wfcH%d" % slot, 16 * (g // WFC_RING + 1)))
                    if g >= 2:
                        tensor.wait_ge(S.h["geluE"], S.hb("geluE", g - 1))
                    for dc in range(NDC):
                        i = tensor.matmul(
                            pm1[g % 2][:],
                            wfc_sb[:, slot * 1024 + dc * 128: slot * 1024 + (dc + 1) * 128],
                            xnT[:, dc * 2048 + sb * 512: dc * 2048 + (sb + 1) * 512],
                            start=(dc == 0), stop=(dc == NDC - 1))
                    S.inc(i, "mm1d")

            @block.scalar
            def _(scalar, sb=sb, e=e, sbe=sbe):
                if sbe >= 1:
                    S.waitb(scalar, "mm2d", 8 * sbe)
                for fc in range(NFC):
                    g = sbe * NFC + fc
                    scalar.wait_ge(S.h["mm1d"], S.hb("mm1d", g + 1))
                    i = scalar.activation(HT[:, fc * 512:(fc + 1) * 512],
                                          pm1[g % 2][:], AF.Gelu_apprx_tanh,
                                          bias=bfc_sb[:, e * 32 + fc: e * 32 + fc + 1])
                    S.inc(i, "geluE")

            @block.tensor
            def _(tensor, sb=sb, e=e, sbe=sbe):
                tensor.wait_ge(S.h["geluE"], S.hb("geluE", 32 * (sbe + 1)))
                if sbe == 0:
                    tensor.wait_ge(S.h["glogE"], S.hb("glogE", NT))
                for dh in range(2):
                    m_g = sbe * 2 + dh
                    if m_g >= 1:
                        tensor.wait_ge(S.h["evacE"], S.hb("evacE", 4 * m_g))
                    for fc in range(NFC):
                        p = sbe * 64 + dh * NFC + fc
                        slot = p % WPR_RING
                        tensor.wait_ge(S.h["wprH%d" % slot],
                                       S.hb("wprH%d" % slot, 16 * (p // WPR_RING + 1)))
                        for sc in range(4):
                            i = tensor.matmul(
                                pm2[sc][:],
                                HT[:, fc * 512 + sc * 128: fc * 512 + (sc + 1) * 128],
                                wpr_sb[:, slot * 512:(slot + 1) * 512],
                                start=(fc == 0), stop=False)
                        S.inc(i, "wprUse")
                    for sc in range(4):
                        i = tensor.matmul(pm2[sc][:], C.ones_row[:],
                                          bpr_sb[0:1, e * 1024 + dh * 512: e * 1024 + (dh + 1) * 512],
                                          start=False, stop=True)
                        S.inc(i, "mm2d")

            @block.vector
            def _(vector, sb=sb, e=e, sbe=sbe, par=par):
                if sbe == 0:
                    S.waitb(vector, "gateF", 3 * NT)
                if e == 0:
                    vector.wait_ge(S.h["xbG"], S.hb("xbG", 16 * (sb + 1)))
                for dh in range(2):
                    m_g = sbe * 2 + dh
                    for sc in range(4):
                        vector.wait_ge(S.h["mm2d"], S.hb("mm2d", 4 * m_g + sc + 1))
                        t = sb * 4 + sc
                        i = vector.scalar_tensor_tensor(
                            x_blk[:, par * 4096 + sc * D + dh * 512:
                                  par * 4096 + sc * D + (dh + 1) * 512],
                            pm2[sc][:], gates[:, t * 8 + e: t * 8 + e + 1],
                            x_blk[:, par * 4096 + sc * D + dh * 512:
                                  par * 4096 + sc * D + (dh + 1) * 512],
                            ALU.mult, ALU.add)
                        S.inc(i, "evacE")

        @block.sync
        def _(sync, sb=sb, par=par):
            S.waitb(sync, "evacE", 64 * (sb + 1))
            S.inc(sync.dma_start(out=y_dst[:, sb * 4096:(sb + 1) * 4096],
                                 in_=x_blk[:, par * 4096:(par + 1) * 4096]), "yst", 16)
            if sb < 2:
                S.waitb(sync, "yst", 16 * (sb + 1))
                S.inc(sync.dma_start(out=dr["f0scr"], in_=C.eps[0:1, 0:2]),
                      "ystM", 16)

    @block.sync
    def _(sync):
        sync.wait_ge(S.h["yst"], S.c["yst"])


def host_ffn0_weights(w_fc, b_fc, w_proj, b_proj, gate_w):
    wfc = np.ascontiguousarray(
        w_fc.transpose(0, 2, 1).reshape(8, 8, 128, 32, 128).transpose(0, 3, 2, 1, 4))
    wpr = np.ascontiguousarray(
        w_proj.transpose(0, 2, 1).reshape(8, 32, 128, 2, 512).transpose(0, 3, 1, 2, 4))
    bfc = np.ascontiguousarray(b_fc.reshape(8, 32, 128).transpose(2, 0, 1)
                               .reshape(128, 256))
    return {"wfc": wfc, "wpr": wpr, "bfc": bfc,
            "bpr": np.ascontiguousarray(b_proj),
            "gwT": np.ascontiguousarray(gate_w.T),
            "identw": np.eye(128, dtype=BBF16)}



def alloc_attn0(nc, stack, P):
    T = NS()
    T.xb = P.f32("a0_xb", 2 * D)
    T.ln_stats = P.f32("a0_ln_stats", 80)
    T.kv_sb = P.f32("a0_kv_sb", 32 * 130)
    T.dcol = P.f32("a0_dcol", 4)
    T.xnT = P.bf16("a0_xnT", NDC * 2048)
    T.qf = P.bf16("a0_qf", NDC * 2048)
    T.kst = P.bf16("a0_kst", 1024)
    T.vst = P.bf16("a0_vst", 1024)
    T.wbufA = P.bf16("a0_wbufA", 8 * 1024)
    T.wbufB = P.bf16("a0_wbufB", 8 * 1024)
    T.kv_bf = P.bf16("a0_kv_bf", 32 * 130)
    T.astage = P.bf16("a0_astage", 1024)
    T.escr = P.f32("a0_escr", 4 * 512)
    T.ln_xt = P.bf16("a0_ln_xt", 2 * D)
    T.bo_sb = stack.enter_context(nc.sbuf_tensor("a0_bo", [1, 1024], BF16))
    return T


def emit_attn0(nc, block, S, T, C, PS, dr, x_src, y_dst):
    AEPS = 1e-6
    pp = [PS.pf[0], PS.pf[1]]
    pkv = PS.pf[2]
    pks = PS.pf[3][:, 0:8]
    pqk = [PS.pf[4][:, 0:130], PS.pf[5][:, 0:130]]
    kv_cc_in, kv_cc_out = dr["kv_cc_in"], dr["kv_cc_out"]
    xnT, qf, kst, vst = T.xnT, T.qf, T.kst, T.vst
    wbufA, wbufB, kv_sb, kv_bf = T.wbufA, T.wbufB, T.kv_sb, T.kv_bf
    astage, escr, dcol, bo_sb, xb = T.astage, T.escr, T.dcol, T.bo_sb, T.xb

    emit_phase_entry(block, S)

    @block.gpsimd
    def _(gpsimd):
        for dc in range(NDC):
            S.inc(gpsimd.dma_start(
                out=wbufA[:, dc * 1024:(dc + 1) * 1024],
                in_=dr["wqT"][dc * 128:(dc + 1) * 128, :]), "wldg", 16)
        S.inc(gpsimd.dma_start(out=bo_sb[:], in_=dr["bo"]), "wldg", 16)

    S.rebase()

    @block.vector
    def _(vector):
        S.wait(vector, "yst")
        i = vector.memset(kv_sb[:], 0.0)
        S.inc(i, "kvE2")

    emit_ln_stream(nc, block, S, T, C, PS, x_src, range(NT), xnT, 2048)

    nq = 0
    for oc in range(NDC):
        for sb4 in range(4):
            nq += 1

            @block.tensor
            def _(tensor, oc=oc, sb4=sb4, nq=nq):
                if nq == 1:
                    S.wait(tensor, "wldg")
                    S.waitb(tensor, "xnTe", NT)
                if nq > 2:
                    S.waitb(tensor, "qfE", nq - 2)
                for dc in range(NDC):
                    i = tensor.matmul(
                        pp[nq % 2][:],
                        wbufA[:, dc * 1024 + oc * 128: dc * 1024 + (oc + 1) * 128],
                        xnT[:, dc * 2048 + sb4 * 512: dc * 2048 + (sb4 + 1) * 512],
                        start=(dc == 0), stop=(dc == NDC - 1))
                S.inc(i, "qmm")

            @block.vector
            def _(vector, nq=nq):
                S.waitb(vector, "qmm", nq)
                i = vector.tensor_scalar_min(
                    escr[:, (nq % 2) * 512:(nq % 2 + 1) * 512], pp[nq % 2][:], 0.0)
                S.inc(i, "qe1")

            @block.scalar
            def _(scalar, nq=nq):
                S.waitb(scalar, "qe1", nq)
                i = scalar.activation(
                    escr[:, (2 + nq % 2) * 512:(3 + nq % 2) * 512],
                    escr[:, (nq % 2) * 512:(nq % 2 + 1) * 512], AF.Exp)
                S.inc(i, "qe2")

            @block.vector
            def _(vector, oc=oc, sb4=sb4, nq=nq):
                S.waitb(vector, "qe2", nq)
                i = vector.scalar_tensor_tensor(
                    qf[:, oc * 2048 + sb4 * 512: oc * 2048 + (sb4 + 1) * 512],
                    pp[nq % 2][:], 1.0,
                    escr[:, (2 + nq % 2) * 512:(3 + nq % 2) * 512],
                    ALU.add, ALU.max)
                S.inc(i, "qfE")

    @block.gpsimd
    def _(gpsimd):
        gpsimd.wait_ge(S.h["qfE"], S.hb("qfE", 32))
        for dc in range(NDC):
            S.inc(gpsimd.dma_start(
                out=wbufA[:, dc * 1024:(dc + 1) * 1024],
                in_=dr["wkT"][dc * 128:(dc + 1) * 128, :]), "wldg", 16)
        for dc in range(NDC):
            S.inc(gpsimd.dma_start(
                out=wbufB[:, dc * 1024:(dc + 1) * 1024],
                in_=dr["wvT"][dc * 128:(dc + 1) * 128, :]), "wldg", 16)

    nkv = 0
    nke = 0
    for t in range(NT):
        b = t // 4
        for proj in ("k", "v"):
            wb = wbufA if proj == "k" else wbufB
            for half in range(2):
                nkv += 1

                @block.tensor
                def _(tensor, t=t, wb=wb, half=half, nkv=nkv):
                    if nkv <= 2:
                        S.wait(tensor, "wldg")
                        S.waitb(tensor, "qfE", 32)
                    else:
                        S.waitb(tensor, "kvE", nkv - 2)
                    for dc in range(NDC):
                        i = tensor.matmul(
                            pp[nkv % 2][:],
                            xnT[:, dc * 2048 + t * 128: dc * 2048 + (t + 1) * 128],
                            wb[:, dc * 1024 + half * 512: dc * 1024 + (half + 1) * 512],
                            start=(dc == 0), stop=(dc == NDC - 1))
                    S.inc(i, "kvmm")

                if proj == "k":
                    nke += 1

                    @block.vector
                    def _(vector, nkv=nkv):
                        S.waitb(vector, "kvmm", nkv)
                        i = vector.tensor_scalar_min(
                            escr[:, (nkv % 2) * 512:(nkv % 2 + 1) * 512],
                            pp[nkv % 2][:], 0.0)
                        S.inc(i, "ke1")

                    @block.scalar
                    def _(scalar, nkv=nkv, nke=nke):
                        S.waitb(scalar, "ke1", nke)
                        i = scalar.activation(
                            escr[:, (2 + nkv % 2) * 512:(3 + nkv % 2) * 512],
                            escr[:, (nkv % 2) * 512:(nkv % 2 + 1) * 512], AF.Exp)
                        S.inc(i, "ke2")

                    @block.vector
                    def _(vector, nkv=nkv, nke=nke, half=half):
                        S.waitb(vector, "ke2", nke)
                        i = vector.scalar_tensor_tensor(
                            kst[:, half * 512:(half + 1) * 512],
                            pp[nkv % 2][:], 1.0,
                            escr[:, (2 + nkv % 2) * 512:(3 + nkv % 2) * 512],
                            ALU.add, ALU.max)
                        S.inc(i, "kvE")
                else:
                    @block.vector
                    def _(vector, nkv=nkv, half=half):
                        S.waitb(vector, "kvmm", nkv)
                        i = vector.tensor_copy(
                            vst[:, half * 512:(half + 1) * 512], pp[nkv % 2][:])
                        S.inc(i, "kvE")

        @block.tensor
        def _(tensor, t=t, b=b):
            S.wait(tensor, "kvE")
            first = (t % 4 == 0)
            last = (t % 4 == 3)
            if first and t > 0:
                S.waitb(tensor, "kvE2", t // 4 + 1)
            for h in range(16):
                pbase = 64 * (h % 2)
                tensor.matmul(pkv[pbase:pbase + 64, (h // 2) * 64:(h // 2) * 64 + 64],
                              kst[:, h * 64:(h + 1) * 64],
                              vst[:, h * 64:(h + 1) * 64],
                              start=(first and h == 0), stop=last)
            for h in range(16):
                pbase = 64 * (h % 2)
                i = tensor.matmul(pks[pbase:pbase + 64, h // 2:h // 2 + 1],
                                  kst[:, h * 64:(h + 1) * 64],
                                  C.ones_col[:],
                                  start=(first and h == 0), stop=last)
            S.inc(i, "kvmmB")

        if t % 4 == 3:
            @block.vector
            def _(vector, b=b):
                S.waitb(vector, "kvmmB", 4 * (b + 1))
                for h in range(16):
                    pbase = 64 * (h % 2)
                    blk = (b * 8 + h // 2) * 130
                    i = vector.tensor_copy(
                        kv_sb[pbase:pbase + 64, blk + 65 * (h % 2): blk + 65 * (h % 2) + 64],
                        pkv[pbase:pbase + 64, (h // 2) * 64:(h // 2) * 64 + 64])
                for par2 in range(2):
                    pbase = 64 * par2
                    i = vector.tensor_copy(
                        apof(kv_sb).rearrange("p (g c) -> p g c", c=130)[
                            pbase:pbase + 64, b * 8:(b + 1) * 8,
                            64 + 65 * par2:65 + 65 * par2],
                        pks[pbase:pbase + 64, 0:8].rearrange("p (g c) -> p g c", c=1))
                S.inc(i, "kvE2")

    @block.sync
    def _(sync):
        S.wait(sync, "kvE2")
        S.inc(sync.dma_start(out=kv_cc_in, in_=kv_sb[:]), "cc", 16)

    @block.gpsimd
    def _(gpsimd):
        gpsimd.wait_ge(S.h["cc"], S.hb("cc", 16))
        i = gpsimd.collective_compute(
            "AllReduce", ALU.add,
            replica_groups=[list(range(N_CORES))],
            ins=[kv_cc_in.opt()], outs=[kv_cc_out.opt()])
        S.inc(i, "ccb")

    @block.sync
    def _(sync):
        S.wait(sync, "ccb")
        S.inc(sync.dma_start(out=kv_sb[:], in_=kv_cc_out), "cc", 16)

    @block.vector
    def _(vector):
        S.waitb(vector, "cc", 32)
        S.inc(vector.tensor_copy(kv_bf[:], kv_sb[:]), "kvbf")

    na = 0
    for t in range(NT):
        b = t // 4
        for oc in range(NDC):
            na += 1

            @block.tensor
            def _(tensor, t=t, b=b, oc=oc, na=na):
                if na == 1:
                    S.wait(tensor, "kvbf")
                if na > 2:
                    S.waitb(tensor, "adiv", 2 * (na - 2))
                blk = (b * 8 + oc) * 130
                i = tensor.matmul(
                    pqk[na % 2],
                    qf[:, oc * 2048 + t * 128: oc * 2048 + (t + 1) * 128],
                    kv_bf[:, blk: blk + 130],
                    start=True, stop=True)
                S.inc(i, "amm")

            @block.vector
            def _(vector, oc=oc, na=na):
                S.waitb(vector, "amm", na)
                for par2 in range(2):
                    S.step(vector, vector.tensor_copy(
                        dcol[:, 2 * par2:2 * par2 + 1],
                        pqk[na % 2][:, 64 + 65 * par2:65 + 65 * par2]), "vch")
                    S.step(vector, vector.tensor_scalar(
                        dcol[:, 2 * par2:2 * par2 + 1],
                        dcol[:, 2 * par2:2 * par2 + 1], AEPS, None, ALU.add), "vch")
                    S.step(vector, vector.reciprocal(
                        dcol[:, 2 * par2 + 1:2 * par2 + 2],
                        dcol[:, 2 * par2:2 * par2 + 1]), "vch")
                    i = vector.tensor_scalar(
                        astage[:, (2 * oc + par2) * 64:(2 * oc + par2 + 1) * 64],
                        pqk[na % 2][:, 65 * par2:65 * par2 + 64],
                        dcol[:, 2 * par2 + 1:2 * par2 + 2], None, ALU.mult)
                    S.inc(i, "adiv")

        @block.tensor
        def _(tensor, t=t):
            S.waitb(tensor, "adiv", 16 * (t + 1))
            if t >= 2:
                S.waitb(tensor, "aTe", t - 1)
            for dc in range(NDC):
                i = tensor.transpose(
                    PS.ptA[:, (t % 2) * 1024 + dc * 128:(t % 2) * 1024 + (dc + 1) * 128],
                    astage[:, dc * 128:(dc + 1) * 128], C.ident[:])
            S.inc(i, "atr")

        @block.vector
        def _(vector, t=t):
            S.waitb(vector, "atr", t + 1)
            for dc in range(NDC):
                i = vector.tensor_copy(
                    xnT[:, dc * 2048 + t * 128: dc * 2048 + t * 128 + 128],
                    PS.ptA[:, (t % 2) * 1024 + dc * 128:(t % 2) * 1024 + (dc + 1) * 128])
            S.inc(i, "aTe")

    @block.gpsimd
    def _(gpsimd):
        gpsimd.wait_ge(S.h["kvbf"], S.c["kvbf"])
        for dc in range(NDC):
            S.inc(gpsimd.dma_start(
                out=wbufB[:, dc * 1024:(dc + 1) * 1024],
                in_=dr["woT"][dc * 128:(dc + 1) * 128, :]), "wldg", 16)

    no = 0
    for t in range(NT):
        par = t % 2

        @block.sync
        def _(sync, t=t, par=par):
            if t == 0:
                S.waitb(sync, "lnTok", NT)
                S.waitb(sync, "lnSq", NT)
            if t >= 2:
                S.wait(sync, "yst")
            sync.wait_ge(S.h["xld"], S.hb("xld", 16 * (2 * NT + t)))
            S.inc(sync.dma_start(out=xb[:, par * D:(par + 1) * D],
                                 in_=x_src[:, t * D:(t + 1) * D]), "xld", 16)

        for dh in range(2):
            no += 1

            @block.tensor
            def _(tensor, t=t, dh=dh, no=no):
                if no == 1:
                    S.wait(tensor, "wldg")
                S.waitb(tensor, "aTe", t + 1)
                if no > 2:
                    S.waitb(tensor, "oev", no - 2)
                for dc in range(NDC):
                    tensor.matmul(
                        pp[no % 2][:],
                        xnT[:, dc * 2048 + t * 128: dc * 2048 + (t + 1) * 128],
                        wbufB[:, dc * 1024 + dh * 512: dc * 1024 + (dh + 1) * 512],
                        start=(dc == 0), stop=False)
                i = tensor.matmul(pp[no % 2][:], C.ones_row[:],
                                  bo_sb[0:1, dh * 512:(dh + 1) * 512],
                                  start=False, stop=True)
                S.inc(i, "omm")

            @block.vector
            def _(vector, t=t, dh=dh, no=no, par=par):
                S.waitb(vector, "omm", no)
                vector.wait_ge(S.h["xld"], S.hb("xld", 16 * (2 * NT + t + 1)))
                i = vector.tensor_add(
                    xb[:, par * D + dh * 512: par * D + (dh + 1) * 512],
                    xb[:, par * D + dh * 512: par * D + (dh + 1) * 512],
                    pp[no % 2][:])
                S.inc(i, "oev")

        @block.sync
        def _(sync, t=t, par=par):
            S.waitb(sync, "oev", 2 * (t + 1))
            S.inc(sync.dma_start(out=y_dst[:, t * D:(t + 1) * D],
                                 in_=xb[:, par * D:(par + 1) * D]), "yst", 16)

    @block.sync
    def _(sync):
        sync.wait_ge(S.h["yst"], S.c["yst"])


def host_attn0_weights(wq, wk, wv, wo, bo):
    return {"wqT": np.ascontiguousarray(wq.T), "wkT": np.ascontiguousarray(wk.T),
            "wvT": np.ascontiguousarray(wv.T), "woT": np.ascontiguousarray(wo.T),
            "bo": np.ascontiguousarray(bo[None, :]),
            "identw": np.eye(128, dtype=BBF16)}



def alloc_ffn1(nc, stack, P):
    T = NS()
    T.xb = P.f32("f1_xb", 2 * D)
    T.x_blk = P.f32("f1_x_blk", 4096)
    T.ln_stats = P.f32("f1_ln_stats", 80)
    T.ew_sb = P.f32("f1_ew", 8)
    T.ws = P.f32("f1_ws", 4)
    T.xnT1 = P.bf16("f1_xnT1", NDC * 512)
    T.HT1 = P.bf16("f1_HT1", 64 * 512)
    T.w1_sb = P.bf16("f1_w1", 4 * 8 * 128)
    T.w2_sb = P.bf16("f1_w2", 8 * 512)
    T.ln_xt = P.bf16("f1_ln_xt", 2 * D)
    return T


def emit_ffn1_conv(nc, block, S, T, dr):
    T.cv_base = S.c["wcv"]

    @block.gpsimd
    def _(gpsimd):
        for fc in range(64):
            S.wait(gpsimd, "wcv")
            S.inc(gpsimd.dma_start(
                out=dr["w1_bf"][fc],
                in_=dr["w1r"][fc].rearrange("p dc c -> p (dc c)")), "wcv", 16)
        for dh in range(2):
            for fc in range(64):
                S.wait(gpsimd, "wcv")
                S.inc(gpsimd.dma_start(
                    out=dr["w2_bf"][dh, fc], in_=dr["w2r"][dh, fc]), "wcv", 16)


def emit_ffn1(nc, block, S, T, C, PS, dr, x_src, y_dst, copy_rest, src_bf=False):
    NFC = 64
    W1_RING, W2_RING = 4, 8
    pm1 = [PS.pf[0], PS.pf[1]]
    pm2 = [PS.pf[2], PS.pf[3], PS.pf[4], PS.pf[5]]
    w1, w2, ew = dr["w1r"], dr["w2r"], dr["ew"]
    xnT1, HT1, w1_sb, w2_sb = T.xnT1, T.HT1, T.w1_sb, T.w2_sb
    ew_sb, ws, x_blk = T.ew_sb, T.ws, T.x_blk

    emit_phase_entry(block, S)
    S.rebase()

    @block.sync
    def _(sync):
        S.wait(sync, "yst")
        S.inc(sync.dma_start(out=ew_sb[:], in_=ew), "wld", 16)
        S.inc(sync.dma_start(out=x_blk[:], in_=x_src[:, 0:4096]), "xbL", 16)
        if copy_rest:
            S.inc(sync.dma_start(out=y_dst[:, 4096:NT * D],
                                 in_=x_src[:, 4096:NT * D]), "yst", 16)

    @block.vector
    def _(vector):
        vector.wait_ge(S.h["wld"], S.c["wld"])
        ewv = apof(ew_sb).rearrange("p (t c) -> p t c", c=2)
        wsv = apof(ws).rearrange("p (t c) -> p t c", c=1)
        S.step(vector, vector.tensor_add(wsv, ewv[:, :, 0:1], ewv[:, :, 1:2]), "vch")
        i = vector.tensor_scalar(ws[:], ws[:], 0.5, None, ALU.mult)
        S.inc(i, "wsF")

    emit_ln_stream(nc, block, S, T, C, PS, x_src, range(4), xnT1, 512)

    if src_bf:
        @block.sync
        def _(sync):
            for fc in range(NFC):
                S.waitb(sync, "geluE", fc - (W1_RING - 1))
                sync.wait_ge(S.h["wcv"], T.cv_base + 16 * (fc + 1))
                slot = fc % W1_RING
                S.inc(sync.dma_start(
                    out=w1_sb[:, slot * 1024:(slot + 1) * 1024],
                    in_=dr["w1_bf"][fc]), "wfcH%d" % slot, 16)
            for dh in range(2):
                for p in range(dh * NFC, dh * NFC + NFC):
                    fc = p - dh * NFC
                    S.waitb(sync, "w2U", p - (W2_RING - 1))
                    sync.wait_ge(S.h["wcv"],
                                 T.cv_base + 16 * (64 + dh * 64 + fc + 1))
                    slot = p % W2_RING
                    S.inc(sync.dma_start(
                        out=w2_sb[:, slot * 512:(slot + 1) * 512],
                        in_=dr["w2_bf"][dh, fc]), "wprH%d" % slot, 16)
    else:
        @block.gpsimd
        def _(gpsimd):
            for fc in range(NFC):
                S.waitb(gpsimd, "geluE", fc - (W1_RING - 1))
                if fc > 0:
                    gpsimd.wait_ge(S.h["w1L"], S.hb("w1L", 16 * fc))
                slot = fc % W1_RING
                S.inc(gpsimd.dma_start(
                    out=w1_sb[:, slot * 1024:(slot + 1) * 1024],
                    in_=w1[fc].rearrange("p dc c -> p (dc c)")), "w1L", 16)
            for dh in range(2):
                for p in range(dh * NFC, dh * NFC + NFC):
                    fc = p - dh * NFC
                    S.waitb(gpsimd, "w2U", p - (W2_RING - 1))
                    if p > 0:
                        gpsimd.wait_ge(S.h["w2L"], S.hb("w2L", 16 * p))
                    slot = p % W2_RING
                    S.inc(gpsimd.dma_start(
                        out=w2_sb[:, slot * 512:(slot + 1) * 512],
                        in_=w2[dh, fc]), "w2L", 16)

    for fc in range(NFC):
        @block.tensor
        def _(tensor, fc=fc):
            if fc == 0:
                S.waitb(tensor, "xnTe", 4)
            slot = fc % W1_RING
            if src_bf:
                tensor.wait_ge(S.h["wfcH%d" % slot],
                               S.hb("wfcH%d" % slot, 16 * (fc // W1_RING + 1)))
            else:
                tensor.wait_ge(S.h["w1L"], S.hb("w1L", 16 * (fc + 1)))
            if fc >= 2:
                tensor.wait_ge(S.h["geluE"], S.hb("geluE", fc - 1))
            for dc in range(NDC):
                i = tensor.matmul(
                    pm1[fc % 2][:],
                    w1_sb[:, slot * 1024 + dc * 128: slot * 1024 + (dc + 1) * 128],
                    xnT1[:, dc * 512:(dc + 1) * 512],
                    start=(dc == 0), stop=(dc == NDC - 1))
            S.inc(i, "mm1d")

        @block.scalar
        def _(scalar, fc=fc):
            scalar.wait_ge(S.h["mm1d"], S.hb("mm1d", fc + 1))
            i = scalar.activation(HT1[:, fc * 512:(fc + 1) * 512],
                                  pm1[fc % 2][:], AF.Gelu_apprx_tanh)
            S.inc(i, "geluE")

    for dh in range(2):
        @block.tensor
        def _(tensor, dh=dh):
            if dh == 0:
                tensor.wait_ge(S.h["geluE"], S.hb("geluE", NFC))
            else:
                tensor.wait_ge(S.h["evacE"], S.hb("evacE", 4))
            for fc in range(NFC):
                p = dh * NFC + fc
                slot = p % W2_RING
                if src_bf:
                    tensor.wait_ge(S.h["wprH%d" % slot],
                                   S.hb("wprH%d" % slot, 16 * (p // W2_RING + 1)))
                else:
                    tensor.wait_ge(S.h["w2L"], S.hb("w2L", 16 * (p + 1)))
                for sc in range(4):
                    i = tensor.matmul(
                        pm2[sc][:],
                        HT1[:, fc * 512 + sc * 128: fc * 512 + (sc + 1) * 128],
                        w2_sb[:, slot * 512:(slot + 1) * 512],
                        start=(fc == 0), stop=(fc == NFC - 1))
                S.inc(i, "w2U")

        @block.vector
        def _(vector, dh=dh):
            vector.wait_ge(S.h["w2U"], S.hb("w2U", NFC * (dh + 1)))
            S.waitb(vector, "wsF", 1)
            if dh == 0:
                vector.wait_ge(S.h["xbL"], S.hb("xbL", 16))
            for sc in range(4):
                i = vector.scalar_tensor_tensor(
                    x_blk[:, sc * D + dh * 512: sc * D + (dh + 1) * 512],
                    pm2[sc][:], ws[:, sc:sc + 1],
                    x_blk[:, sc * D + dh * 512: sc * D + (dh + 1) * 512],
                    ALU.mult, ALU.add)
                S.inc(i, "evacE")

    @block.sync
    def _(sync):
        S.waitb(sync, "evacE", 8)
        S.inc(sync.dma_start(out=y_dst[:, 0:4096], in_=x_blk[:]), "yst", 16)
        sync.wait_ge(S.h["yst"], S.c["yst"])


def host_ffn1_weights(w1, w2):
    w1r = np.ascontiguousarray(w1.reshape(8, 128, 64, 128).transpose(2, 1, 0, 3))
    w2r = np.ascontiguousarray(w2.reshape(64, 128, 2, 512).transpose(2, 0, 1, 3))
    return {"w1r": w1r, "w2r": w2r, "identw": np.eye(128, dtype=BBF16)}


def host_ffn1_ew(expert_weights, c):
    ews = expert_weights[512 * c:512 * (c + 1), :]
    return np.ascontiguousarray(ews.reshape(4, 128, 2).transpose(1, 0, 2)
                                .reshape(128, 8))



def alloc_attn1(nc, stack, P):
    T = NS()
    T.xb = P.f32("a1_xb", 2 * D)
    T.ln_stats = P.f32("a1_ln_stats", 80)
    T.S_f = P.f32("a1_S_f", 512)
    T.bcol = P.f32("a1_bcol", 32)
    T.glt = P.f32("a1_glt", 16)
    T.gsc = P.f32("a1_gsc", 48)
    T.rmsc = P.f32("a1_rmsc", 8)
    T.itemp3 = P.f32("a1_itemp", 12)
    T.cv_sb = P.f32("a1_cv", 48)
    T.fir_sb = P.f32("a1_fir", 4 * 34)
    T.rmsw_sb = P.f32("a1_rmsw", 2)
    T.xnT = P.bf16("a1_xnT", NDC * 2048)
    T.xnBb = P.bf16("a1_xnBb", NDC * 512)
    T.P_scr = P.bf16("a1_Pscr", 2 * 4128)
    T.q_fm = P.bf16("a1_qfm", 2 * 4096)
    T.k_fm = P.bf16("a1_kfm", 2 * 4096)
    T.v_fm = P.bf16("a1_vfm", 2 * 4128)
    T.kb_fm = P.bf16("a1_kbfm", 2 * 4096)
    T.tokst = P.bf16("a1_tokst", 1024)
    T.lsb = P.bf16("a1_lsb", 2 * 512)
    T.llb = P.bf16("a1_llb", 2 * 512)
    T.wch = P.bf16("a1_wch", 8 * 256)
    T.wgl_sb = P.bf16("a1_wgl", 8 * 16)
    T.wb_sb = P.bf16("a1_wb", 8 * 2)
    T.ln_xt = P.bf16("a1_ln_xt", 2 * D)
    T.MT_t = P.bf16("a1_MT", 32)
    T.M_t = P.bf16("a1_M", 32)
    T.M2T_t = P.bf16("a1_M2T", 32)
    T.M2_t = P.bf16("a1_M2", 32)
    T.M4T_t = P.bf16("a1_M4T", 32)
    T.M4_t = P.bf16("a1_M4", 32)
    T.M8T_t = P.bf16("a1_M8T", 32)
    T.M8_t = P.bf16("a1_M8", 32)
    T.A1T_t = P.bf16("a1_A1T", 32)
    T.A2_t = P.bf16("a1_A2", 32)
    T.A4_t = P.bf16("a1_A4", 32)
    T.A8_t = P.bf16("a1_A8", 32)
    T.A16_t = P.bf16("a1_A16", 32)
    T.R_t = P.bf16("a1_R", 64)
    T.TT_t = P.bf16("a1_TT", 32)
    T.AT_t = P.bf16("a1_AT", 32)
    T.u_sb = P.bf16("a1_u", 256)
    T.z_sb = P.bf16("a1_z", 256)
    T.S_bf = P.bf16("a1_Sbf", 512)
    T.bcol_bf = P.bf16("a1_bcolbf", 32)
    T.ostage = P.bf16("a1_ostage", 256)
    T.ot_t = P.bf16("a1_ot", 1024)
    T.mx_t = P.bf16("a1_mx", 1024)
    T.trowb = stack.enter_context(nc.sbuf_tensor("a1_trowb", [1, 4], BF16))
    T.trow = stack.enter_context(nc.sbuf_tensor("a1_trow", [1, 16], F32))
    T.bgl_sb = stack.enter_context(nc.sbuf_tensor("a1_bgl", [1, 16], BF16))
    T.l2row = stack.enter_context(nc.sbuf_tensor("a1_l2row", [1, 512], F32))
    T.l2rowb = stack.enter_context(nc.sbuf_tensor("a1_l2rowb", [1, 512], BF16))
    return T


def emit_attn1(nc, block, S, T, C, PS, dr, x_src, y_dst):
    NBLK, NCH, NGRP = 8, 2, 32
    REPS, L2EPS = 1e-5, 1e-6
    pp = [PS.pf[0], PS.pf[1]]
    pA = PS.pf[2][:, 0:64]
    pO = PS.pf[3][:, 0:256]
    pS = PS.pf[4]
    pB = PS.pf[5][:, 0:32]
    pZ = PS.pf[0]
    pU = PS.pf[1]
    pt = PS.ptA
    a1_in, a1_out = dr["a1_in"], dr["a1_out"]
    a2_in, a2_out = dr["a2_in"], dr["a2_out"]

    xnT, xnBb, P_scr = T.xnT, T.xnBb, T.P_scr
    q_fm, k_fm, v_fm, kb_fm = T.q_fm, T.k_fm, T.v_fm, T.kb_fm
    k_tok = xnT[:, 0:8192]
    vb_tok = xnT[:, 8192:16384]
    wo_sb = P_scr[:, 0:8192]
    tokst, lsb, llb, wch = T.tokst, T.lsb, T.llb, T.wch
    wgl_sb, wb_sb, cv_sb, fir_sb = T.wgl_sb, T.wb_sb, T.cv_sb, T.fir_sb
    rmsw_sb, itemp3, trow, trowb, bgl_sb = (T.rmsw_sb, T.itemp3, T.trow,
                                            T.trowb, T.bgl_sb)
    S_f, S_bf = T.S_f, T.S_bf
    MT_t, M_t, M2T_t, M2_t = T.MT_t, T.M_t, T.M2T_t, T.M2_t
    M4T_t, M4_t, M8T_t, M8_t = T.M4T_t, T.M4_t, T.M8T_t, T.M8_t
    A1T_t, A2_t, A4_t, A8_t, A16_t = T.A1T_t, T.A2_t, T.A4_t, T.A8_t, T.A16_t
    R_t, TT_t, AT_t = T.R_t, T.TT_t, T.AT_t
    u_sb, z_sb, bcol, bcol_bf = T.u_sb, T.z_sb, T.bcol, T.bcol_bf
    l2row, l2rowb = T.l2row, T.l2rowb
    glt, gsc, rmsc, ostage = T.glt, T.gsc, T.rmsc, T.ostage
    ot_t, mx_t, xb = T.ot_t, T.mx_t, T.xb
    mask_su = C.masks[:, 0:32]
    mask_iu = C.masks[:, 32:64]
    eye4 = C.masks[:, 64:96]

    emit_phase_entry(block, S)
    S.rebase()

    @block.sync
    def _(sync):
        S.inc(sync.dma_start(out=cv_sb[:, 0:16], in_=dr["cvq"]), "wld", 16)
        S.inc(sync.dma_start(out=cv_sb[:, 16:32], in_=dr["cvk"]), "wld", 16)
        S.inc(sync.dma_start(out=cv_sb[:, 32:48], in_=dr["cvv"]), "wld", 16)
        S.inc(sync.dma_start(
            out=apof(fir_sb).rearrange("p (c t) -> p c t", t=34)[:, :, 0:3],
            in_=dr["firs"].rearrange("p (c t) -> p c t", t=3)), "wld", 16)
        S.inc(sync.dma_start(
            out=apof(fir_sb).rearrange("p (c t) -> p c t", t=34)[:, :, 3:34],
            in_=dr["firl"].rearrange("p (c t) -> p c t", t=31)), "wld", 16)
        S.inc(sync.dma_start(out=rmsw_sb[:], in_=dr["rmsw"]), "wld", 16)
        S.inc(sync.dma_start(out=trow[0:1, 0:4], in_=dr["ltemp"]), "wld", 16)

    @block.gpsimd
    def _(gpsimd):
        S.inc(gpsimd.dma_start(out=wb_sb[:], in_=dr["wbc"]), "wldg", 16)
        S.inc(gpsimd.dma_start(out=wgl_sb[:], in_=dr["wgl"]), "wldg", 16)
        S.inc(gpsimd.dma_start(out=bgl_sb[:], in_=dr["bgl"]), "wldg", 16)

    @block.vector
    def _(vector):
        vector.memset(v_fm[:], 0.0)
        i = vector.memset(P_scr[:], 0.0)
        S.inc(i, "cvD")

    emit_ln_stream(nc, block, S, T, C, PS, x_src, range(NT), xnT, 2048)

    @block.sync
    def _(sync):
        S.waitb(sync, "xnTe", NT)
        for d in range(8):
            b = d // 2
            S.inc(sync.dma_start(
                out=a1_in[d],
                in_=apof(xnT).rearrange("p (dc s) -> p dc s", s=2048)[
                    :, :, b * 512:(b + 1) * 512]), "shA", 16)

    @block.gpsimd
    def _(gpsimd):
        gpsimd.wait_ge(S.h["shA"], S.hb("shA", 16 * 8))
        with allow_ici_all_to_all():
            i = gpsimd.collective_compute(
                "AllToAll", ALU.bypass,
                replica_groups=[list(range(N_CORES))],
                ins=[a1_in.opt()], outs=[a1_out.opt()])
        S.inc(i, "ccA")

    ctr = {"gp": 0, "tp": 0}

    def pp_group(tensor):
        ctr["gp"] += 1
        g = ctr["gp"]
        if g > 2:
            tensor.wait_ge(S.h["pev"], S.hb("pev", g - 2))
        return pp[g % 2], g

    def pt_half(tensor):
        ctr["tp"] += 1
        n = ctr["tp"]
        if n > 2:
            tensor.wait_ge(S.h["trE"], S.hb("trE", n - 2))
        return pt[:, (n % 2) * 1024:(n % 2 + 1) * 1024], n

    fmbuf = {"q": q_fm, "k": k_fm, "v": v_fm}
    fmoff = {"q": (0, 4096), "k": (0, 4096), "v": (32, 4128)}

    def emit_scan(ch):
        for g in range(NGRP):
            @block.tensor
            def _(tensor, g=g):
                tensor.wait_ge(S.h["sAe"], S.c["sAe"])
                for j in range(4):
                    sp = (4 * g + j) * 32
                    for cc in range(2):
                        tensor.matmul(
                            pA[32 * j:32 * (j + 1), 0:32],
                            k_fm[:, cc * 4096 + sp: cc * 4096 + sp + 32],
                            kb_fm[:, cc * 4096 + sp: cc * 4096 + sp + 32],
                            start=(cc == 0), stop=(cc == 1),
                            tile_position=(0, 32 * j))
                    for cc in range(2):
                        i = tensor.matmul(
                            pA[32 * j:32 * (j + 1), 32:64],
                            k_fm[:, cc * 4096 + sp: cc * 4096 + sp + 32],
                            q_fm[:, cc * 4096 + sp: cc * 4096 + sp + 32],
                            start=(cc == 0), stop=(cc == 1),
                            tile_position=(0, 32 * j))
                S.inc(i, "sAm")

            @block.vector
            def _(vector, g=g):
                vector.wait_ge(S.h["sAm"], S.c["sAm"])
                S.step(vector, vector.tensor_mul(MT_t[:], pA[:, 0:32], mask_su), "vch")
                S.step(vector, vector.tensor_mul(AT_t[:], pA[:, 32:64], mask_iu), "vch")
                S.step(vector, vector.transpose(M_t[:], MT_t[:]), "vch")
                i = vector.tensor_add(A1T_t[:], MT_t[:], eye4)
                S.inc(i, "sAe")

            stages = [
                (M_t, MT_t, M2T_t, M2_t, A2_t),
                (M2_t, M2T_t, M4T_t, M4_t, A4_t),
                (M4_t, M4T_t, M8T_t, M8_t, A8_t),
            ]
            for lhs_t, rhs_t, outT_t, out_t, outA_t in stages:
                @block.tensor
                def _(tensor, lhs_t=lhs_t, rhs_t=rhs_t):
                    tensor.wait_ge(S.h["sAe"], S.c["sAe"])
                    for j in range(4):
                        i = tensor.matmul(
                            pA[32 * j:32 * (j + 1), 0:32],
                            lhs_t[32 * j:32 * (j + 1), :],
                            rhs_t[32 * j:32 * (j + 1), :],
                            start=True, stop=True,
                            tile_position=(32 * j, 32 * j))
                    S.inc(i, "sAm")

                @block.vector
                def _(vector, outT_t=outT_t, out_t=out_t, outA_t=outA_t):
                    vector.wait_ge(S.h["sAm"], S.c["sAm"])
                    S.step(vector, vector.tensor_copy(outT_t[:], pA[:, 0:32]), "vch")
                    S.step(vector, vector.transpose(out_t[:], outT_t[:]), "vch")
                    i = vector.tensor_add(outA_t[:], out_t[:], eye4)
                    S.inc(i, "sAe")

            @block.tensor
            def _(tensor):
                tensor.wait_ge(S.h["sAe"], S.c["sAe"])
                for j in range(4):
                    i = tensor.matmul(
                        pA[32 * j:32 * (j + 1), 0:32],
                        M8_t[32 * j:32 * (j + 1), :],
                        M8T_t[32 * j:32 * (j + 1), :],
                        start=True, stop=True,
                        tile_position=(32 * j, 32 * j))
                S.inc(i, "sAm")

            @block.vector
            def _(vector):
                vector.wait_ge(S.h["sAm"], S.c["sAm"])
                S.step(vector, vector.tensor_add(R_t[:, 32:64], pA[:, 0:32], eye4), "vch")
                i = vector.transpose(A16_t[:], R_t[:, 32:64])
                S.inc(i, "sAe")

            prods = [(A2_t, A1T_t, 0, 0), (A4_t, None, 0, 32),
                     (A8_t, None, 32, 0), (A16_t, None, 0, -1)]
            for lhs_t, rhsA, rhs_off, outoff in prods:
                @block.tensor
                def _(tensor, lhs_t=lhs_t, rhsA=rhsA, rhs_off=rhs_off):
                    tensor.wait_ge(S.h["sAe"], S.c["sAe"])
                    for j in range(4):
                        rhs = (rhsA[32 * j:32 * (j + 1), :] if rhsA is not None
                               else R_t[32 * j:32 * (j + 1), rhs_off:rhs_off + 32])
                        i = tensor.matmul(
                            pA[32 * j:32 * (j + 1), 0:32],
                            lhs_t[32 * j:32 * (j + 1), :], rhs,
                            start=True, stop=True,
                            tile_position=(32 * j, 32 * j))
                    S.inc(i, "sAm")

                @block.vector
                def _(vector, outoff=outoff):
                    vector.wait_ge(S.h["sAm"], S.c["sAm"])
                    if outoff < 0:
                        i = vector.tensor_copy(TT_t[:], pA[:, 0:32])
                    else:
                        i = vector.tensor_copy(R_t[:, outoff:outoff + 32], pA[:, 0:32])
                    S.inc(i, "sAe")

            for j in range(4):
                cidx = 4 * g + j
                w = cidx // 4
                sp = cidx * 32

                @block.tensor
                def _(tensor, g=g, j=j, w=w, sp=sp):
                    tensor.wait_ge(S.h["scanD"], S.c["scanD"])
                    if j == 0:
                        tensor.wait_ge(S.h["oev"], S.c["oev"])
                    for cc in range(2):
                        i = tensor.matmul(
                            pZ[32 * j:32 * (j + 1), 0:256],
                            kb_fm[:, cc * 4096 + sp: cc * 4096 + sp + 32],
                            S_bf[:, cc * 256:(cc + 1) * 256],
                            start=(cc == 0), stop=(cc == 1),
                            tile_position=(0, 32 * j))
                    S.inc(i, "sZm")

                @block.vector
                def _(vector, j=j, w=w):
                    vector.wait_ge(S.h["sZm"], S.c["sZm"])
                    i = vector.tensor_sub(
                        z_sb[32 * j:32 * (j + 1), :],
                        vb_tok[32 * j:32 * (j + 1), w * 256:(w + 1) * 256],
                        pZ[32 * j:32 * (j + 1), 0:256])
                    S.inc(i, "sZe")

                @block.tensor
                def _(tensor, j=j):
                    tensor.wait_ge(S.h["sZe"], S.c["sZe"])
                    i = tensor.matmul(
                        pU[32 * j:32 * (j + 1), 0:256],
                        TT_t[32 * j:32 * (j + 1), :],
                        z_sb[32 * j:32 * (j + 1), :],
                        start=True, stop=True,
                        tile_position=(32 * j, 32 * j))
                    S.inc(i, "sUm")

                @block.vector
                def _(vector, j=j):
                    vector.wait_ge(S.h["sUm"], S.c["sUm"])
                    i = vector.tensor_copy(u_sb[32 * j:32 * (j + 1), :],
                                           pU[32 * j:32 * (j + 1), 0:256])
                    S.inc(i, "sUe")

                @block.tensor
                def _(tensor, g=g, j=j, w=w, sp=sp):
                    tensor.wait_ge(S.h["sUe"], S.c["sUe"])
                    for cc in range(2):
                        tensor.matmul(
                            pO[32 * j:32 * (j + 1), :],
                            q_fm[:, cc * 4096 + sp: cc * 4096 + sp + 32],
                            S_bf[:, cc * 256:(cc + 1) * 256],
                            start=(cc == 0), stop=False,
                            tile_position=(0, 32 * j))
                    tensor.matmul(
                        pO[32 * j:32 * (j + 1), :],
                        AT_t[32 * j:32 * (j + 1), :],
                        u_sb[32 * j:32 * (j + 1), :],
                        start=False, stop=(j == 3),
                        tile_position=(32 * j, 32 * j))
                    for cc in range(2):
                        i = tensor.matmul(
                            pS[:, cc * 256:(cc + 1) * 256],
                            k_tok[32 * j:32 * (j + 1),
                                  w * 256 + cc * 128: w * 256 + (cc + 1) * 128],
                            u_sb[32 * j:32 * (j + 1), :],
                            start=(cc == 0), stop=(cc == 1),
                            tile_position=(32 * j, 0))
                    S.inc(i, "sSm")

                @block.vector
                def _(vector, j=j):
                    vector.wait_ge(S.h["sSm"], S.c["sSm"])
                    S.step(vector, vector.tensor_add(S_f[:], S_f[:], pS[:]), "vch")
                    i = vector.tensor_copy(S_bf[:], S_f[:])
                    S.inc(i, "scanD")

            @block.vector
            def _(vector, g=g):
                vector.wait_ge(S.h["sSm"], S.c["sSm"])
                if S.c["payD"] > 0:
                    vector.wait_ge(S.h["payD"], S.c["payD"])
                i = vector.tensor_copy(ostage[:], pO[:])
                S.inc(i, "oev")

            @block.sync
            def _(sync, ch=ch, g=g):
                sync.wait_ge(S.h["oev"], S.c["oev"])
                if S.c["payD"] > 0:
                    sync.wait_ge(S.h["payD"], S.c["payD"])
                S.inc(sync.dma_start(
                    out=a2_in[g // 4, 0][:, g % 4, ch * 256:(ch + 1) * 256],
                    in_=ostage[:]), "payD", 16)

    def emit_fir(ch):
        for sb8 in range(NBLK):
            @block.vector
            def _(vector, sb8=sb8, ch=ch):
                if sb8 == 0:
                    vector.wait_ge(S.h["oev"], S.c["oev"])
                for cc in range(2):
                    base = cc * 4128 + 32 + sb8 * 512
                    fw = fir_sb[:, (2 * ch + cc) * 34:(2 * ch + cc) * 34 + 34]
                    i = vector.tensor_scalar(
                        lsb[:, cc * 512:(cc + 1) * 512],
                        v_fm[:, base - 2: base - 2 + 512],
                        fw[:, 0:1], None, ALU.mult)
                    for tpp in range(1, 3):
                        i = vector.scalar_tensor_tensor(
                            lsb[:, cc * 512:(cc + 1) * 512],
                            v_fm[:, base - 2 + tpp: base - 2 + tpp + 512],
                            fw[:, tpp:tpp + 1],
                            lsb[:, cc * 512:(cc + 1) * 512], ALU.mult, ALU.add)
                    i = vector.tensor_scalar(
                        llb[:, cc * 512:(cc + 1) * 512],
                        v_fm[:, base - 30: base - 30 + 512],
                        fw[:, 3:4], None, ALU.mult)
                    for tpp in range(1, 31):
                        i = vector.scalar_tensor_tensor(
                            llb[:, cc * 512:(cc + 1) * 512],
                            v_fm[:, base - 30 + tpp: base - 30 + tpp + 512],
                            fw[:, 3 + tpp:4 + tpp],
                            llb[:, cc * 512:(cc + 1) * 512], ALU.mult, ALU.add)
                S.inc(i, "fir")

            for w4 in range(4):
                @block.tensor
                def _(tensor, sb8=sb8, w4=w4):
                    tensor.wait_ge(S.h["fir"], S.c["fir"])
                    pth, n = pt_half(tensor)
                    for cc in range(2):
                        tensor.transpose(
                            pth[:, cc * 128:(cc + 1) * 128],
                            lsb[:, cc * 512 + w4 * 128: cc * 512 + (w4 + 1) * 128],
                            C.ident[:])
                    for cc in range(2):
                        i = tensor.transpose(
                            pth[:, 256 + cc * 128: 256 + (cc + 1) * 128],
                            llb[:, cc * 512 + w4 * 128: cc * 512 + (w4 + 1) * 128],
                            C.ident[:])
                    S.inc(i, "trP")

                @block.vector
                def _(vector, sb8=sb8, w4=w4):
                    n = ctr["tp"]
                    vector.wait_ge(S.h["trP"], S.c["trP"])
                    vector.wait_ge(S.h["payD"], S.c["payD"])
                    pth = pt[:, (n % 2) * 1024:(n % 2 + 1) * 1024]
                    i = vector.tensor_copy(tokst[:, 512:1024], pth[:, 0:512])
                    S.inc(i, "trE")

                @block.sync
                def _(sync, sb8=sb8, w4=w4, ch=ch):
                    sync.wait_ge(S.h["trE"], S.c["trE"])
                    if S.c["payD"] > 0:
                        sync.wait_ge(S.h["payD"], S.c["payD"])
                    S.inc(sync.dma_start(
                        out=a2_in[sb8, 2][:, w4, ch * 256:(ch + 1) * 256],
                        in_=tokst[:, 512:768]), "payD", 16)
                    sync.wait_ge(S.h["payD"], S.c["payD"])
                    S.inc(sync.dma_start(
                        out=a2_in[sb8, 3][:, w4, ch * 256:(ch + 1) * 256],
                        in_=tokst[:, 768:1024]), "payD", 16)

    for ch in range(NCH):
        for pi, pn in enumerate(("q", "k", "v")):
            @block.gpsimd
            def _(gpsimd, pi=pi, ch=ch):
                gpsimd.wait_ge(S.h["pmm"], S.c["pmm"])
                src = (dr["wqc"], dr["wkc"], dr["wvc"])[pi]
                for dc in range(NDC):
                    if S.c["wldg"] > 0:
                        gpsimd.wait_ge(S.h["wldg"], S.c["wldg"])
                    S.inc(gpsimd.dma_start(
                        out=wch[:, dc * 256:(dc + 1) * 256],
                        in_=src[dc * 128:(dc + 1) * 128,
                                ch * 256:(ch + 1) * 256]), "wldg", 16)

            for blk in range(NBLK):
                nblk_g = (ch * 3 + pi) * NBLK + blk

                @block.sync
                def _(sync, blk=blk, nblk_g=nblk_g):
                    if nblk_g == 0:
                        S.waitb(sync, "ccA", 1)
                    if nblk_g >= 1:
                        sync.wait_ge(S.h["pmm"], S.c["pmm"])
                    if S.c["blkL"] > 0:
                        sync.wait_ge(S.h["blkL"], S.c["blkL"])
                    S.inc(sync.dma_start(out=xnBb[:], in_=a1_out[blk]), "blkL", 16)

                for oc in range(2):
                    @block.tensor
                    def _(tensor, pi=pi, blk=blk, oc=oc, ch=ch):
                        tensor.wait_ge(S.h["blkL"], S.c["blkL"])
                        if blk == 0 and oc == 0:
                            tensor.wait_ge(S.h["wldg"], S.c["wldg"])
                        ppt, g = pp_group(tensor)
                        for dc in range(NDC):
                            i = tensor.matmul(
                                ppt[:],
                                wch[:, dc * 256 + oc * 128: dc * 256 + (oc + 1) * 128],
                                xnBb[:, dc * 512:(dc + 1) * 512],
                                start=(dc == 0), stop=(dc == NDC - 1))
                        S.inc(i, "pmm")

                    @block.vector
                    def _(vector, pn=pn, blk=blk, oc=oc):
                        g = ctr["gp"]
                        vector.wait_ge(S.h["pmm"], S.c["pmm"])
                        i = vector.tensor_copy(
                            P_scr[:, oc * 4128 + 32 + blk * 512:
                                  oc * 4128 + 32 + (blk + 1) * 512],
                            pp[g % 2][:])
                        S.inc(i, "pev")

                if pi == 2:
                    @block.tensor
                    def _(tensor, blk=blk, ch=ch):
                        if blk == 0:
                            S.wait(tensor, "bsig")
                        for sc in range(4):
                            w = blk * 4 + sc
                            for dc in range(NDC):
                                i = tensor.matmul(
                                    pB[:, w:w + 1],
                                    xnBb[:, dc * 512 + sc * 128: dc * 512 + (sc + 1) * 128],
                                    wb_sb[:, dc * 2 + ch: dc * 2 + ch + 1],
                                    start=(dc == 0 and w == 0),
                                    stop=(dc == NDC - 1))
                        S.inc(i, "pmm")

            for sb8 in range(NBLK):
                @block.vector
                def _(vector, pi=pi, sb8=sb8, ch=ch):
                    if sb8 == 0:
                        vector.wait_ge(S.h["pev"], S.c["pev"])
                    for cc in range(2):
                        base = cc * 4128 + 32 + sb8 * 512
                        tap = cv_sb[:, pi * 16 + (2 * ch + cc) * 4:
                                    pi * 16 + (2 * ch + cc) * 4 + 4]
                        i = vector.tensor_scalar(
                            lsb[:, cc * 512:(cc + 1) * 512],
                            P_scr[:, base - 3:base - 3 + 512],
                            tap[:, 0:1], None, ALU.mult)
                        for tpp in range(1, 4):
                            i = vector.scalar_tensor_tensor(
                                lsb[:, cc * 512:(cc + 1) * 512],
                                P_scr[:, base - 3 + tpp: base - 3 + tpp + 512],
                                tap[:, tpp:tpp + 1],
                                lsb[:, cc * 512:(cc + 1) * 512],
                                ALU.mult, ALU.add)
                    S.inc(i, "cvD")

                @block.scalar
                def _(scalar, pn=pn, sb8=sb8):
                    scalar.wait_ge(S.h["cvD"], S.c["cvD"])
                    pre, stride = fmoff[pn]
                    for cc in range(2):
                        i = scalar.activation(
                            fmbuf[pn][:, cc * stride + pre + sb8 * 512:
                                      cc * stride + pre + (sb8 + 1) * 512],
                            lsb[:, cc * 512:(cc + 1) * 512], AF.Silu)
                    S.inc(i, "silu")

                @block.vector
                def _(vector):
                    vector.wait_ge(S.h["silu"], S.c["silu"])

        for li, pn in enumerate(("q", "k")):
            for sb8 in range(NBLK):
                @block.scalar
                def _(scalar, pn=pn, sb8=sb8):
                    for cc in range(2):
                        i = scalar.activation(
                            lsb[:, cc * 512:(cc + 1) * 512],
                            fmbuf[pn][:, cc * 4096 + sb8 * 512: cc * 4096 + (sb8 + 1) * 512],
                            AF.Square)
                    S.inc(i, "l2q")

                @block.tensor
                def _(tensor):
                    tensor.wait_ge(S.h["l2q"], S.c["l2q"])
                    ppt, g = pp_group(tensor)
                    for cc in range(2):
                        i = tensor.matmul(ppt[0:1, :], C.ones_col[:],
                                          lsb[:, cc * 512:(cc + 1) * 512],
                                          start=(cc == 0), stop=(cc == 1))
                    S.inc(i, "pmm")

                @block.vector
                def _(vector):
                    g = ctr["gp"]
                    vector.wait_ge(S.h["pmm"], S.c["pmm"])
                    i = vector.tensor_copy(l2row[0:1, :], pp[g % 2][0:1, :])
                    S.inc(i, "pev")

                @block.scalar
                def _(scalar):
                    S.wait(scalar, "pev", S.c["pev"])
                    S.step(scalar, scalar.activation(
                        l2row[0:1, :], l2row[0:1, :], AF.Ln,
                        bias=C.eps[0:1, 0:1]), "sch")
                    i = scalar.activation(l2rowb[0:1, :], l2row[0:1, :],
                                          AF.Exp, scale=-0.5)
                    S.inc(i, "l2k")

                @block.tensor
                def _(tensor):
                    tensor.wait_ge(S.h["l2k"], S.c["l2k"])
                    ppt, g = pp_group(tensor)
                    i = tensor.matmul(ppt[:], C.ones_row[:], l2rowb[0:1, :],
                                      start=True, stop=True)
                    S.inc(i, "pmm")

                @block.vector
                def _(vector, pn=pn, sb8=sb8):
                    g = ctr["gp"]
                    vector.wait_ge(S.h["pmm"], S.c["pmm"])
                    for cc in range(2):
                        i = vector.tensor_mul(
                            fmbuf[pn][:, cc * 4096 + sb8 * 512: cc * 4096 + (sb8 + 1) * 512],
                            fmbuf[pn][:, cc * 4096 + sb8 * 512: cc * 4096 + (sb8 + 1) * 512],
                            pp[g % 2][:])
                    S.inc(i, "pev")

        @block.vector
        def _(vector):
            vector.wait_ge(S.h["pmm"], S.c["pmm"])
            S.step(vector, vector.tensor_copy(bcol[:], pB), "vch")
            S.inc(nc.vector.engine_nop(), "bsig")

        @block.scalar
        def _(scalar):
            S.wait(scalar, "bsig", S.c["bsig"])
            i = scalar.activation(bcol[:], bcol[:], AF.Exp, scale=-1.0)
            S.inc(i, "bsig")

        @block.vector
        def _(vector):
            S.wait(vector, "bsig", S.c["bsig"])
            S.step(vector, vector.tensor_scalar(bcol[:], bcol[:], 1.0, None, ALU.add), "vch")
            S.step(vector, vector.reciprocal(bcol[:], bcol[:]), "vch")
            i = vector.tensor_copy(bcol_bf[:], bcol[:])
            S.inc(i, "bsig")

        for w in range(32):
            @block.tensor
            def _(tensor, w=w):
                if w == 0:
                    S.wait(tensor, "bsig")
                pth, n = pt_half(tensor)
                for cc in range(2):
                    tensor.transpose(
                        pth[:, cc * 128:(cc + 1) * 128],
                        k_fm[:, cc * 4096 + w * 128: cc * 4096 + (w + 1) * 128],
                        C.ident[:])
                for cc in range(2):
                    i = tensor.transpose(
                        pth[:, 256 + cc * 128: 256 + (cc + 1) * 128],
                        v_fm[:, cc * 4128 + 32 + w * 128: cc * 4128 + 32 + (w + 1) * 128],
                        C.ident[:])
                S.inc(i, "trP")

            @block.vector
            def _(vector, w=w):
                n = ctr["tp"]
                vector.wait_ge(S.h["trP"], S.c["trP"])
                if S.c["payD"] > 0:
                    vector.wait_ge(S.h["payD"], S.c["payD"])
                pth = pt[:, (n % 2) * 1024:(n % 2 + 1) * 1024]
                for cc in range(2):
                    vector.tensor_copy(
                        k_tok[:, w * 256 + cc * 128: w * 256 + (cc + 1) * 128],
                        pth[:, cc * 128:(cc + 1) * 128])
                for cc in range(2):
                    vector.tensor_copy(
                        tokst[:, cc * 128:(cc + 1) * 128],
                        pth[:, 256 + cc * 128: 256 + (cc + 1) * 128])
                i = vector.tensor_scalar(
                    vb_tok[:, w * 256:(w + 1) * 256],
                    tokst[:, 0:256], bcol[:, w:w + 1], None, ALU.mult)
                S.inc(i, "trE")

            @block.sync
            def _(sync, w=w, ch=ch):
                sync.wait_ge(S.h["trE"], S.c["trE"])
                if S.c["payD"] > 0:
                    sync.wait_ge(S.h["payD"], S.c["payD"])
                S.inc(sync.dma_start(
                    out=a2_in[w // 4, 1][:, w % 4, ch * 256:(ch + 1) * 256],
                    in_=tokst[:, 0:256]), "payD", 16)

        for w in range(32):
            @block.vector
            def _(vector, w=w):
                if w == 0:
                    vector.wait_ge(S.h["trE"], S.c["trE"])
                i = vector.tensor_scalar(
                    tokst[:, 256:512], k_tok[:, w * 256:(w + 1) * 256],
                    bcol[:, w:w + 1], None, ALU.mult)
                S.inc(i, "kbE")

            @block.tensor
            def _(tensor, w=w):
                tensor.wait_ge(S.h["kbE"], S.c["kbE"])
                pth, n = pt_half(tensor)
                for cc in range(2):
                    i = tensor.transpose(
                        pth[:, cc * 128:(cc + 1) * 128],
                        tokst[:, 256 + cc * 128: 256 + (cc + 1) * 128],
                        C.ident[:])
                S.inc(i, "trP")

            @block.vector
            def _(vector, w=w):
                n = ctr["tp"]
                vector.wait_ge(S.h["trP"], S.c["trP"])
                pth = pt[:, (n % 2) * 1024:(n % 2 + 1) * 1024]
                for cc in range(2):
                    i = vector.tensor_copy(
                        kb_fm[:, cc * 4096 + w * 128: cc * 4096 + (w + 1) * 128],
                        pth[:, cc * 128:(cc + 1) * 128])
                S.inc(i, "trE")

        @block.vector
        def _(vector):
            vector.wait_ge(S.h["trE"], S.c["trE"])
            vector.memset(S_f[:], 0.0)
            i = vector.memset(S_bf[:], 0.0)
            S.inc(i, "scanD")

        emit_scan(ch)
        emit_fir(ch)

    @block.gpsimd
    def _(gpsimd):
        gpsimd.wait_ge(S.h["payD"], S.c["payD"])
        with allow_ici_all_to_all():
            i = gpsimd.collective_compute(
                "AllToAll", ALU.bypass,
                replica_groups=[list(range(N_CORES))],
                ins=[a2_in.opt()], outs=[a2_out.opt()])
        S.inc(i, "ccB")

    @block.gpsimd
    def _(gpsimd):
        gpsimd.wait_ge(S.h["silu"], S.c["silu"])
        for dc in range(NDC):
            gpsimd.wait_ge(S.h["clg"], S.hb("clg", 16 * dc)) if dc > 0 else None
            S.inc(gpsimd.dma_start(
                out=wo_sb[:, dc * 1024:(dc + 1) * 1024],
                in_=dr["wo_n"][dc * 128:(dc + 1) * 128, :]), "clg", 16)

    @block.vector
    def _(vector):
        vector.wait_ge(S.h["clg"], S.c["clg"])
        for dc in range(NDC):
            i = vector.tensor_scalar(
                wo_sb[:, dc * 1024:(dc + 1) * 1024],
                wo_sb[:, dc * 1024:(dc + 1) * 1024],
                rmsw_sb[:, dc % 2: dc % 2 + 1], None, ALU.mult)
            S.inc(i, "clv")

    @block.scalar
    def _(scalar):
        S.wait(scalar, "wld")
        i = scalar.activation(trow[0:1, 4:8], trow[0:1, 0:4], AF.Exp)
        S.inc(i, "cst")

    @block.vector
    def _(vector):
        S.waitb(vector, "cst", 1)
        S.step(vector, vector.reciprocal(trow[0:1, 8:12], trow[0:1, 4:8]), "vch")
        i = vector.tensor_copy(trowb[0:1, 0:4], trow[0:1, 8:12])
        S.inc(i, "cst")

    @block.tensor
    def _(tensor):
        S.waitb(tensor, "cst", 2)
        tensor.wait_ge(S.h["scanD"], S.c["scanD"])
        tensor.wait_ge(S.h["sUe"], S.c["sUe"])
        ppt, g = pp_group(tensor)
        i = tensor.matmul(ppt[:, 0:4], C.ones_row[:], trowb[0:1, 0:4],
                          start=True, stop=True)
        S.inc(i, "pmm")

    @block.vector
    def _(vector):
        g = ctr["gp"]
        vector.wait_ge(S.h["pmm"], S.c["pmm"])
        i = vector.tensor_copy(itemp3[:, 0:4], pp[g % 2][:, 0:4])
        S.inc(i, "pev")

    for t in range(NT):
        b, ti, par = t // 4, t % 4, t % 2
        ct_od = k_tok[:, par * 4096:par * 4096 + 1024]
        ct_v = k_tok[:, par * 4096 + 1024:par * 4096 + 2048]
        ct_ls = k_tok[:, par * 4096 + 2048:par * 4096 + 3072]
        ct_ll = k_tok[:, par * 4096 + 3072:par * 4096 + 4096]
        ct_xn = vb_tok[:, par * 1024:(par + 1) * 1024]

        @block.sync
        def _(sync, t=t, b=b, ti=ti, par=par, ct_od=ct_od, ct_v=ct_v,
              ct_ls=ct_ls, ct_ll=ct_ll, ct_xn=ct_xn):
            if t == 0:
                S.wait(sync, "ccB")
            if t >= 2:
                sync.wait_ge(S.h["pev"], S.c["pev"])
                S.wait(sync, "yst")
            for half in range(2):
                src = 2 * b + half
                for tix, buf in ((0, ct_od), (1, ct_v), (2, ct_ls), (3, ct_ll)):
                    S.inc(sync.dma_start(
                        out=buf[:, half * 512:(half + 1) * 512],
                        in_=a2_out[src, tix][:, ti, :]), "cl", 16)
            S.inc(sync.dma_start(
                out=ct_xn.rearrange("p (dc s) -> p dc s", s=128),
                in_=a1_in[2 * b][:, :, ti * 128:(ti + 1) * 128]), "cl", 16)
            S.inc(sync.dma_start(out=xb[:, par * D:(par + 1) * D],
                                 in_=x_src[:, t * D:(t + 1) * D]), "cl", 16)

        @block.tensor
        def _(tensor, t=t, ct_xn=ct_xn):
            tensor.wait_ge(S.h["cl"], S.c["cl"])
            ppt, g = pp_group(tensor)
            for dc in range(NDC):
                tensor.matmul(ppt[:, 0:16],
                              ct_xn[:, dc * 128:(dc + 1) * 128],
                              wgl_sb[:, dc * 16:(dc + 1) * 16],
                              start=(dc == 0), stop=False)
            i = tensor.matmul(ppt[:, 0:16], C.ones_row[:], bgl_sb[0:1, 0:16],
                              start=False, stop=True)
            S.inc(i, "gmm")

        @block.vector
        def _(vector, t=t):
            g = ctr["gp"]
            vector.wait_ge(S.h["gmm"], S.c["gmm"])
            i = vector.tensor_copy(glt[:], pp[g % 2][:, 0:16])
            S.inc(i, "pev")

        @block.scalar
        def _(scalar, t=t):
            S.wait(scalar, "pev", S.c["pev"])
            i = scalar.activation(gsc[:, 0:4], glt[:, 0:4], AF.Exp, scale=-1.0)
            S.inc(i, "cst")

        @block.vector
        def _(vector, t=t):
            S.wait(vector, "cst", S.c["cst"])
            S.step(vector, vector.tensor_scalar(gsc[:, 0:4], gsc[:, 0:4],
                                                1.0, None, ALU.add), "vch")
            S.step(vector, vector.reciprocal(gsc[:, 0:4], gsc[:, 0:4]), "vch")
            for h in range(4):
                S.step(vector, vector.tensor_scalar(
                    gsc[:, 12 + 3 * h:15 + 3 * h], glt[:, 4 + 3 * h:7 + 3 * h],
                    itemp3[:, h:h + 1], None, ALU.mult), "vch")
            S.step(vector, vector.tensor_reduce(
                gsc[:, 24:28].rearrange("p (h c) -> p h c", c=1),
                gsc[:, 12:24].rearrange("p (h c) -> p h c", c=3),
                AX.X, ALU.max), "vch")
            for j3 in range(3):
                S.step(vector, vector.tensor_copy(
                    gsc[:, 28:40].rearrange("p (h c) -> p h c", c=3)[:, :, j3:j3 + 1],
                    gsc[:, 24:28].rearrange("p (h c) -> p h c", c=1)), "vch")
            i = vector.tensor_sub(gsc[:, 12:24], gsc[:, 12:24], gsc[:, 28:40])
            S.inc(i, "mixv")

        @block.scalar
        def _(scalar, t=t):
            S.wait(scalar, "mixv", S.c["mixv"])
            i = scalar.activation(gsc[:, 28:40], gsc[:, 12:24], AF.Exp)
            S.inc(i, "cst")

        @block.vector
        def _(vector, t=t, ct_od=ct_od, ct_v=ct_v, ct_ls=ct_ls, ct_ll=ct_ll):
            S.wait(vector, "cst", S.c["cst"])
            S.step(vector, vector.tensor_reduce(
                gsc[:, 24:28].rearrange("p (h c) -> p h c", c=1),
                gsc[:, 28:40].rearrange("p (h c) -> p h c", c=3),
                AX.X, ALU.add), "vch")
            S.step(vector, vector.reciprocal(gsc[:, 24:28], gsc[:, 24:28]), "vch")
            for j3 in range(3):
                S.step(vector, vector.tensor_copy(
                    gsc[:, 12:24].rearrange("p (h c) -> p h c", c=3)[:, :, j3:j3 + 1],
                    gsc[:, 24:28].rearrange("p (h c) -> p h c", c=1)), "vch")
            S.step(vector, vector.tensor_mul(gsc[:, 12:24], gsc[:, 12:24],
                                             gsc[:, 28:40]), "vch")
            for h in range(4):
                p0 = gsc[:, 12 + 3 * h:13 + 3 * h]
                p1 = gsc[:, 13 + 3 * h:14 + 3 * h]
                p2 = gsc[:, 14 + 3 * h:15 + 3 * h]
                sl = slice(h * 256, (h + 1) * 256)
                vector.tensor_scalar(mx_t[:, sl], ct_v[:, sl], p0, None, ALU.mult)
                vector.scalar_tensor_tensor(mx_t[:, sl], ct_ls[:, sl], p1,
                                            mx_t[:, sl], ALU.mult, ALU.add)
                vector.scalar_tensor_tensor(mx_t[:, sl], ct_ll[:, sl], p2,
                                            mx_t[:, sl], ALU.mult, ALU.add)
                vector.tensor_sub(ot_t[:, sl], ct_od[:, sl], mx_t[:, sl])
                i = vector.scalar_tensor_tensor(ot_t[:, sl], ot_t[:, sl],
                                                gsc[:, h:h + 1],
                                                mx_t[:, sl], ALU.mult, ALU.add)
            S.inc(i, "mixv")

        @block.scalar
        def _(scalar, t=t, par=par):
            S.wait(scalar, "mixv", S.c["mixv"])
            for h in range(4):
                scalar.activation(T.ln_xt[:, par * D + h * 256: par * D + (h + 1) * 256],
                                  ot_t[:, h * 256:(h + 1) * 256],
                                  AF.Square, accum_out=rmsc[:, h:h + 1])
            S.step(scalar, scalar.activation(rmsc[:, 4:8], rmsc[:, 0:4],
                                             AF.Ln, bias=C.eps[:, 1:2],
                                             scale=1.0 / 256), "sch")
            i = scalar.activation(rmsc[:, 0:4], rmsc[:, 4:8], AF.Exp, scale=-0.5)
            S.inc(i, "cst")

        @block.vector
        def _(vector, t=t):
            S.wait(vector, "cst", S.c["cst"])
            for h in range(4):
                i = vector.tensor_scalar(ot_t[:, h * 256:(h + 1) * 256],
                                         ot_t[:, h * 256:(h + 1) * 256],
                                         rmsc[:, h:h + 1], None, ALU.mult)
            S.inc(i, "otr")

        @block.tensor
        def _(tensor, t=t):
            tensor.wait_ge(S.h["otr"], S.c["otr"])
            pth, n = pt_half(tensor)
            for dc in range(NDC):
                i = tensor.transpose(pth[:, dc * 128:(dc + 1) * 128],
                                     ot_t[:, dc * 128:(dc + 1) * 128], C.ident[:])
            S.inc(i, "trP")

        @block.vector
        def _(vector, t=t):
            n = ctr["tp"]
            vector.wait_ge(S.h["trP"], S.c["trP"])
            vector.wait_ge(S.h["omm"], S.c["omm"])
            pth = pt[:, (n % 2) * 1024:(n % 2 + 1) * 1024]
            for dc in range(NDC):
                i = vector.tensor_copy(mx_t[:, dc * 128:(dc + 1) * 128],
                                       pth[:, dc * 128:(dc + 1) * 128])
            S.inc(i, "trE")

        for dh in range(2):
            @block.tensor
            def _(tensor, t=t, dh=dh):
                tensor.wait_ge(S.h["trE"], S.c["trE"])
                if dh == 0 and t == 0:
                    tensor.wait_ge(S.h["clv"], S.c["clv"])
                ppt, g = pp_group(tensor)
                for dc in range(NDC):
                    i = tensor.matmul(
                        ppt[:],
                        mx_t[:, dc * 128:(dc + 1) * 128],
                        wo_sb[:, dc * 1024 + dh * 512: dc * 1024 + (dh + 1) * 512],
                        start=(dc == 0), stop=(dc == NDC - 1))
                S.inc(i, "omm")

            @block.vector
            def _(vector, t=t, dh=dh, par=par):
                g = ctr["gp"]
                vector.wait_ge(S.h["omm"], S.c["omm"])
                i = vector.tensor_add(
                    xb[:, par * D + dh * 512: par * D + (dh + 1) * 512],
                    xb[:, par * D + dh * 512: par * D + (dh + 1) * 512],
                    pp[g % 2][:])
                S.inc(i, "pev")

        @block.sync
        def _(sync, t=t, par=par):
            sync.wait_ge(S.h["pev"], S.c["pev"])
            S.inc(sync.dma_start(out=y_dst[:, t * D:(t + 1) * D],
                                 in_=xb[:, par * D:(par + 1) * D]), "yst", 16)

    @block.sync
    def _(sync):
        sync.wait_ge(S.h["yst"], S.c["yst"])


def host_attn1_weights(Wq, Wk, Wv, conv_q, conv_k, conv_v, Wb, fir_short,
                       fir_long, Wg, bg, Wl, bl, log_temp, rms_w, Wo):
    masks = np.zeros((128, 96), np.float32)
    su = -np.triu(np.ones((32, 32), np.float32), 1)
    iu = np.triu(np.ones((32, 32), np.float32), 0)
    ey = np.eye(32, dtype=np.float32)
    for j in range(4):
        masks[32 * j:32 * (j + 1), 0:32] = su
        masks[32 * j:32 * (j + 1), 32:64] = iu
        masks[32 * j:32 * (j + 1), 64:96] = ey
    wgl_np = np.concatenate([Wg, Wl], axis=1)
    return {
        "wgl": np.ascontiguousarray(wgl_np.reshape(8, 128, 16).transpose(1, 0, 2)
                                    .reshape(128, 128)),
        "bgl": np.ascontiguousarray(np.concatenate([bg, bl])[None, :]),
        "ltemp": np.ascontiguousarray(log_temp[None, :]),
        "rmsw": np.ascontiguousarray(rms_w.reshape(2, 128).T),
        "wo_n": np.ascontiguousarray(Wo),
        "masks": masks,
        "identw": np.eye(128, dtype=BBF16),
        "wbc_full": Wb, "wq_full": Wq, "wk_full": Wk, "wv_full": Wv,
        "cvq_full": conv_q, "cvk_full": conv_k, "cvv_full": conv_v,
        "firs_full": fir_short, "firl_full": fir_long,
    }


def host_attn1_core_inputs(wd, c):
    h0 = 2 * (c % 2)
    cs, ce = h0 * 256, (h0 + 2) * 256
    out = {k: v for k, v in wd.items() if not k.endswith("_full")}
    out["wqc"] = np.ascontiguousarray(wd["wq_full"][:, cs:ce])
    out["wkc"] = np.ascontiguousarray(wd["wk_full"][:, cs:ce])
    out["wvc"] = np.ascontiguousarray(wd["wv_full"][:, cs:ce])
    for name, key in (("cvq", "cvq_full"), ("cvk", "cvk_full"), ("cvv", "cvv_full")):
        sl = wd[key][cs:ce]
        out[name] = np.ascontiguousarray(sl.reshape(4, 128, 4).transpose(1, 0, 2)
                                         .reshape(128, 16))
    out["wbc"] = np.ascontiguousarray(
        wd["wbc_full"][:, h0:h0 + 2].reshape(8, 128, 2).transpose(1, 0, 2)
        .reshape(128, 16))
    fs = wd["firs_full"][h0:h0 + 2].reshape(512, 3)
    fl = wd["firl_full"][h0:h0 + 2].reshape(512, 31)
    out["firs"] = np.ascontiguousarray(fs.reshape(4, 128, 3).transpose(1, 0, 2)
                                       .reshape(128, 12))
    out["firl"] = np.ascontiguousarray(fl.reshape(4, 128, 31).transpose(1, 0, 2)
                                       .reshape(128, 124))
    return out



def _build_standalone(which):
    nc = bass.Bass("TRN2", target_bir_lowering=False)
    x_in = nc.dram_tensor("x", [128, NT * D], F32, kind="ExternalInput").ap()
    y_out = nc.dram_tensor("y", [128, NT * D], F32, kind="ExternalOutput").ap()
    shapes = {
        "ffn0": [("wfc", [8, 32, 128, 8, 128]), ("wpr", [8, 2, 32, 128, 512]),
                 ("bfc", [128, 256]), ("bpr", [8, 1024]), ("gwT", [1024, 8])],
        "attn0": [("wqT", [1024, 1024]), ("wkT", [1024, 1024]),
                  ("wvT", [1024, 1024]), ("woT", [1024, 1024]), ("bo", [1, 1024])],
        "ffn1": [("w1r", [64, 128, 8, 128]), ("w2r", [2, 64, 128, 512]),
                 ("ew", [128, 8])],
        "attn1": [("wqc", [1024, 512]), ("wkc", [1024, 512]), ("wvc", [1024, 512]),
                  ("cvq", [128, 16]), ("cvk", [128, 16]), ("cvv", [128, 16]),
                  ("wbc", [128, 16]), ("firs", [128, 12]), ("firl", [128, 124]),
                  ("wgl", [128, 128]), ("bgl", [1, 16]), ("ltemp", [1, 4]),
                  ("rmsw", [128, 2]), ("wo_n", [1024, 1024]), ("masks", [128, 96])],
    }
    dr = {n: nc.dram_tensor(n, shp, F32, kind="ExternalInput").ap()
          for n, shp in shapes[which]}
    dr["identw"] = nc.dram_tensor("identw", [128, 128], BF16,
                                  kind="ExternalInput").ap()
    if which == "ffn0":
        dr["wfc_bf"] = nc.dram_tensor("wfc_bf", [8, 32, 128, 1024], BF16).ap()
        dr["wpr_bf"] = nc.dram_tensor("wpr_bf", [8, 2, 32, 128, 512], BF16).ap()
        dr["f0scr"] = nc.dram_tensor("f0scr", [1, 2], F32).ap()
    if which == "attn0":
        dr["kv_cc_in"] = nc.dram_tensor("kv_cc_in", [128, 32 * 130], F32).ap()
        dr["kv_cc_out"] = nc.dram_tensor("kv_cc_out", [128, 32 * 130], F32).ap()
    if which == "attn1":
        dr["a1_in"] = nc.dram_tensor("a1_in", [8, 128, 8, 512], BF16).ap()
        dr["a1_out"] = nc.dram_tensor("a1_out", [8, 128, 8, 512], BF16).ap()
        dr["a2_in"] = nc.dram_tensor("a2_in", [8, 4, 128, 4, 512], BF16).ap()
        dr["a2_out"] = nc.dram_tensor("a2_out", [8, 4, 128, 4, 512], BF16).ap()
    alloc = {"ffn0": alloc_ffn0, "attn0": alloc_attn0,
             "ffn1": alloc_ffn1, "attn1": alloc_attn1}[which]
    with contextlib.ExitStack() as stack:
        P = Pool(nc, stack)
        C = alloc_consts(nc, stack)
        PS = alloc_psums(nc, stack)
        T = alloc(nc, stack, P)
        S = Sems(nc, stack, FUSED_SEMS)
        with nc.Block() as block:
            emit_const_init(nc, block, S, C, dr)
            if which == "ffn0":
                emit_ffn0(nc, block, S, T, C, PS, dr, x_in, y_out)
            elif which == "attn0":
                emit_attn0(nc, block, S, T, C, PS, dr, x_in, y_out)
            elif which == "ffn1":
                emit_ffn1(nc, block, S, T, C, PS, dr, x_in, y_out, copy_rest=True)
            else:
                emit_attn1(nc, block, S, T, C, PS, dr, x_in, y_out)
    return nc


def _shard_x(x, c):
    s = x[:, 512 * c:512 * (c + 1), :].reshape(T_LOC, D)
    return np.ascontiguousarray(s.reshape(NT, 128, D).transpose(1, 0, 2)
                                .reshape(128, NT * D))


def _run_spmd(nc, in_maps):
    res = run_bass_kernel_spmd(nc, in_maps, core_ids=list(range(N_CORES)))
    out = np.empty((4, 4096, D), np.float32)
    for c in range(N_CORES):
        y = np.asarray(res.results[c]["y"]).reshape(128, NT, D) \
            .transpose(1, 0, 2).reshape(T_LOC, D)
        out[:, 512 * c:512 * (c + 1), :] = y.reshape(4, 512, D)
    return out


def run_ffn0(x, wd):
    nc = _build_standalone("ffn0")
    return _run_spmd(nc, [{"x": _shard_x(x, c), **wd} for c in range(N_CORES)])


def run_attn0(x, wd):
    nc = _build_standalone("attn0")
    return _run_spmd(nc, [{"x": _shard_x(x, c), **wd} for c in range(N_CORES)])


def run_ffn1(x, wd, ew):
    nc = _build_standalone("ffn1")
    return _run_spmd(nc, [{"x": _shard_x(x, c), "ew": host_ffn1_ew(ew, c), **wd}
                          for c in range(N_CORES)])


def run_attn1(x, wd):
    nc = _build_standalone("attn1")
    return _run_spmd(nc, [{"x": _shard_x(x, c), **host_attn1_core_inputs(wd, c)}
                          for c in range(N_CORES)])



def build_fused(phases=("f0", "a0", "f0", "f1", "a1", "f1")):
    nc = bass.Bass("TRN2", target_bir_lowering=False)
    dr = {}
    x_in = nc.dram_tensor("x", [128, NT * D], F32, kind="ExternalInput").ap()
    y_out = nc.dram_tensor("y", [128, NT * D], F32, kind="ExternalOutput").ap()
    xbuf = nc.dram_tensor("xbuf", [128, NT * D], F32).ap()

    for n, shp in [("wfc", [8, 32, 128, 8, 128]), ("wpr", [8, 2, 32, 128, 512]),
                   ("bfc", [128, 256]), ("bpr", [8, 1024]), ("gwT", [1024, 8]),
                   ("wqT", [1024, 1024]), ("wkT", [1024, 1024]),
                   ("wvT", [1024, 1024]), ("woT", [1024, 1024]), ("bo", [1, 1024]),
                   ("w1r", [64, 128, 8, 128]), ("w2r", [2, 64, 128, 512]),
                   ("ew", [128, 8]),
                   ("wqc", [1024, 512]), ("wkc", [1024, 512]), ("wvc", [1024, 512]),
                   ("cvq", [128, 16]), ("cvk", [128, 16]), ("cvv", [128, 16]),
                   ("wbc", [128, 16]), ("firs", [128, 12]), ("firl", [128, 124]),
                   ("wgl", [128, 128]), ("bgl", [1, 16]), ("ltemp", [1, 4]),
                   ("rmsw", [128, 2]), ("wo_n", [1024, 1024]),
                   ("masks", [128, 96])]:
        dr[n] = nc.dram_tensor(n, shp, F32, kind="ExternalInput").ap()
    dr["identw"] = nc.dram_tensor("identw", [128, 128], BF16,
                                  kind="ExternalInput").ap()
    dr["wfc_bf"] = nc.dram_tensor("wfc_bf", [8, 32, 128, 1024], BF16).ap()
    dr["wpr_bf"] = nc.dram_tensor("wpr_bf", [8, 2, 32, 128, 512], BF16).ap()
    dr["w1_bf"] = nc.dram_tensor("w1_bf", [64, 128, 1024], BF16).ap()
    dr["w2_bf"] = nc.dram_tensor("w2_bf", [2, 64, 128, 512], BF16).ap()
    dr["f0scr"] = nc.dram_tensor("f0scr", [1, 2], F32).ap()
    dr["kv_cc_in"] = nc.dram_tensor("kv_cc_in", [128, 32 * 130], F32).ap()
    dr["kv_cc_out"] = nc.dram_tensor("kv_cc_out", [128, 32 * 130], F32).ap()
    dr["a1_in"] = nc.dram_tensor("a1_in", [8, 128, 8, 512], BF16).ap()
    dr["a1_out"] = nc.dram_tensor("a1_out", [8, 128, 8, 512], BF16).ap()
    dr["a2_in"] = nc.dram_tensor("a2_in", [8, 4, 128, 4, 512], BF16).ap()
    dr["a2_out"] = nc.dram_tensor("a2_out", [8, 4, 128, 4, 512], BF16).ap()

    with contextlib.ExitStack() as stack:
        P = Pool(nc, stack, fused=True, f32_cols=10848, bf16_cols=72784)
        C = alloc_consts(nc, stack)
        PS = alloc_psums(nc, stack)
        T_f0 = alloc_ffn0(nc, stack, P)
        P.reset()
        T_a0 = alloc_attn0(nc, stack, P)
        P.reset()
        T_f1 = alloc_ffn1(nc, stack, P)
        P.reset()
        T_a1 = alloc_attn1(nc, stack, P)
        P.reset()
        assert P.hw32 <= 10848 and P.hw16 <= 72784, (P.hw32, P.hw16)
        S = Sems(nc, stack, FUSED_SEMS)
        with nc.Block() as block:
            emit_const_init(nc, block, S, C, dr)
            last = len(phases) - 1
            src = x_in
            f0_seen = f1_seen = False
            for idx, ph in enumerate(phases):
                dst = y_out if idx == last else xbuf
                if ph == "f0":
                    emit_ffn0(nc, block, S, T_f0, C, PS, dr, src, dst,
                              conv=not f0_seen)
                    f0_seen = True
                elif ph == "a0":
                    emit_attn0(nc, block, S, T_a0, C, PS, dr, src, dst)
                elif ph == "f1":
                    emit_ffn1(nc, block, S, T_f1, C, PS, dr, src, dst,
                              copy_rest=(idx == last), src_bf=f1_seen)
                    f1_seen = True
                elif ph == "a1":
                    emit_attn1(nc, block, S, T_a1, C, PS, dr, src, dst)
                src = dst
                if idx == 0 and phases.count("f1") > 1:
                    emit_ffn1_conv(nc, block, S, T_f1, dr)

            @block.sync
            def _(sync):
                sync.wait_ge(S.h["yst"], S.c["yst"])
    return nc


_FUSED_CACHE = {}


def _fused_inmaps(inputs):
    f32 = np.float32
    x = np.ascontiguousarray(np.asarray(inputs["L0_attn_x"], f32))

    wd_ffn0 = host_ffn0_weights(
        np.asarray(inputs["L0_ffn_w_fc"], f32), np.asarray(inputs["L0_ffn_b_fc"], f32),
        np.asarray(inputs["L0_ffn_w_proj"], f32), np.asarray(inputs["L0_ffn_b_proj"], f32),
        np.asarray(inputs["L0_ffn_gate_w"], f32))
    wd_attn0 = host_attn0_weights(
        np.asarray(inputs["L0_attn_wq"], f32), np.asarray(inputs["L0_attn_wk"], f32),
        np.asarray(inputs["L0_attn_wv"], f32), np.asarray(inputs["L0_attn_wo"], f32),
        np.asarray(inputs["L0_attn_bo"], f32))
    wd_ffn1 = host_ffn1_weights(
        np.asarray(inputs["L1_ffn_w1"], f32), np.asarray(inputs["L1_ffn_w2"], f32))
    ew = np.asarray(inputs["L1_ffn_expert_weights"], f32)
    wd_attn1 = host_attn1_weights(
        np.asarray(inputs["L1_attn_Wq"], f32), np.asarray(inputs["L1_attn_Wk"], f32),
        np.asarray(inputs["L1_attn_Wv"], f32), np.asarray(inputs["L1_attn_conv_q"], f32),
        np.asarray(inputs["L1_attn_conv_k"], f32), np.asarray(inputs["L1_attn_conv_v"], f32),
        np.asarray(inputs["L1_attn_Wb"], f32), np.asarray(inputs["L1_attn_fir_short"], f32),
        np.asarray(inputs["L1_attn_fir_long"], f32), np.asarray(inputs["L1_attn_Wg"], f32),
        np.asarray(inputs["L1_attn_bg"], f32), np.asarray(inputs["L1_attn_Wl"], f32),
        np.asarray(inputs["L1_attn_bl"], f32), np.asarray(inputs["L1_attn_log_temp"], f32),
        np.asarray(inputs["L1_attn_rms_w"], f32), np.asarray(inputs["L1_attn_Wo"], f32))

    in_maps = []
    shared = {**{k: v for k, v in wd_ffn0.items()},
              **{k: v for k, v in wd_attn0.items() if k != "identw"},
              **{k: v for k, v in wd_ffn1.items() if k != "identw"}}
    for c in range(N_CORES):
        m = {"x": _shard_x(x, c), "ew": host_ffn1_ew(ew, c), **shared,
             **host_attn1_core_inputs(wd_attn1, c)}
        in_maps.append(m)
    return in_maps


def kernel(**inputs):
    _ = inputs.get("L1_ffn_scores"), inputs.get("L1_ffn_top_experts")
    nc = build_fused()
    return _run_spmd(nc, _fused_inmaps(inputs))
